# version 10
# baseline (speedup 1.0000x reference)
import numpy as np
import ml_dtypes
import concourse.bass as bass
import concourse.mybir as mybir
from concourse.bass_utils import run_bass_kernel_spmd
from concourse.bass_core_rust import IndirectOffsetOnAxis

F32 = mybir.dt.float32
I32 = mybir.dt.int32
BF16 = mybir.dt.bfloat16
AF = mybir.ActivationFunctionType
ALU = mybir.AluOpType
AX = mybir.AxisListType

D = 1024
SEQ = 4096
LN_EPS = 1e-5
QK_EPS = 1e-6
SM_OFF = 10.0
MOE_LN_EPS = 1e-5

ENGINES = ("tensor", "scalar", "vector", "gpsimd", "sync")


class Tok:
    __slots__ = ("eng", "sem", "seq", "inc", "flag", "value")

    def __init__(self, eng, sem, seq, inc, value=0):
        self.eng, self.sem, self.seq, self.inc = eng, sem, seq, inc
        self.flag, self.value = False, value


NDMA_SEMS = 4


class PB:

    def __init__(self, nc):
        self.nc = nc
        self.streams = {e: [] for e in ENGINES}
        self.dma_idx = {}
        self.pending = {e: [] for e in ENGINES}

    def op(self, eng, fn, waits=(), dma=False):
        ws = [w for w in waits if w is not None]
        if self.pending[eng]:
            ws.extend(self.pending[eng])
            self.pending[eng] = []
        if dma:
            j = self.dma_idx[eng] = self.dma_idx.get(eng, -1) + 1
            sem = f"{eng}dma{j % NDMA_SEMS}"
            tok = Tok(eng, sem, len(self.streams[eng]), 16, 16 * (j // NDMA_SEMS + 1))
            tok.flag = True
            if j >= NDMA_SEMS:
                prev = Tok(eng, sem, -1, 16, 16 * (j // NDMA_SEMS))
                prev.flag = True
                ws.append(prev)
        else:
            sem = eng
            tok = Tok(eng, sem, len(self.streams[eng]), 1)
        for w in ws:
            if w.eng != eng or w.inc == 16:
                w.flag = True
        self.streams[eng].append((fn, ws, tok))
        return tok

    def barrier(self):
        toks = []
        for e in ENGINES:
            if self.streams[e]:
                toks.append(self.streams[e][-1][2])
        for e in ENGINES:
            self.pending[e] = [t for t in toks if t.eng != e or t.inc == 16]

    def emit(self, block):
        sem_names = set()
        for e in ENGINES:
            for (_, ws, tok) in self.streams[e]:
                if tok.flag:
                    sem_names.add(tok.sem)
                for w in ws:
                    if w.flag:
                        sem_names.add(w.sem)
        self.cum = {}
        for e in ENGINES:
            cnt, cums = 0, []
            for (_, _, tok) in self.streams[e]:
                if tok.flag and tok.inc == 1:
                    cnt += 1
                cums.append(cnt)
            self.cum[e] = cums
        import contextlib
        self._stack = contextlib.ExitStack()
        sems = {s: self._stack.enter_context(self.nc.semaphore(f"pb_{s}"))
                for s in sorted(sem_names)}
        self.sems = sems

        def emitter(eng):
            def run(h):
                waited = {}
                drained = 0
                for i, (fn, ws, tok) in enumerate(self.streams[eng]):
                    need_drain = False
                    for w in ws:
                        if w.inc == 16:
                            if w.value > waited.get(w.sem, 0):
                                h.wait_ge(sems[w.sem], w.value)
                                waited[w.sem] = w.value
                        elif w.eng == eng:
                            if w.seq >= drained:
                                need_drain = True
                        else:
                            val = self.cum[w.eng][w.seq]
                            if val > waited.get(w.eng, 0):
                                h.wait_ge(sems[w.eng], val)
                                waited[w.eng] = val
                    if need_drain:
                        h.drain()
                        drained = i
                    inst = fn(h)
                    if tok.flag:
                        inst.then_inc(sems[tok.sem], tok.inc)
            return run

        block.tensor(emitter("tensor"))
        block.scalar(emitter("scalar"))
        block.vector(emitter("vector"))
        block.gpsimd(emitter("gpsimd"))
        block.sync(emitter("sync"))


def bf(x):
    return np.ascontiguousarray(x).astype(ml_dtypes.bfloat16)


def f32(x):
    return np.ascontiguousarray(x, dtype=np.float32)



_ALLOC_N = [0]


class Alloc:

    def __init__(self, nc):
        import contextlib
        self.nc = nc
        self.stack = contextlib.ExitStack()

    def _u(self, name):
        _ALLOC_N[0] += 1
        return f"{name}_{_ALLOC_N[0]}"

    def sb(self, name, shape, dtype, side=None):
        return self.stack.enter_context(
            self.nc.sbuf_tensor(self._u(name), shape, dtype, side=side)
            if side else self.nc.sbuf_tensor(self._u(name), shape, dtype))

    def ps(self, name, shape, dtype):
        return self.stack.enter_context(self.nc.psum_tensor(self._u(name), shape, dtype))

    def close(self):
        self.stack.close()


def build_l0a(nqc):
    nc = bass.Bass("TRN2", target_bir_lowering=False)
    NKC = nqc + 2
    io = {
        'x_pad': nc.dram_tensor("x_pad", [NKC * 128, D], F32, kind="ExternalInput").ap(),
        'qkv_wT': nc.dram_tensor("qkv_wT", [D, 3 * D], F32, kind="ExternalInput").ap(),
        'qkv_b': nc.dram_tensor("qkv_b", [1, 3 * D], F32, kind="ExternalInput").ap(),
        'out_wT': nc.dram_tensor("out_wT", [D, D], F32, kind="ExternalInput").ap(),
        'out_b': nc.dram_tensor("out_b", [1, D], F32, kind="ExternalInput").ap(),
        'band_mask': nc.dram_tensor("band_mask", [128, nqc * 3 * 128], BF16,
                                    kind="ExternalInput").ap(),
        'identm': nc.dram_tensor("identm", [128, 128], BF16, kind="ExternalInput").ap(),
        'y1': nc.dram_tensor("y1", [nqc * 128, D], F32, kind="ExternalOutput").ap(),
    }
    with nc.Block() as blk:
        pb = PB(nc)
        emit_l0a(nc, pb, blk, io, nqc)
        pb.emit(blk)
    return nc


def emit_l0a(nc, pb, blk, io, nqc, tag=''):
    NKC = nqc + 2
    x_pad, qkv_wT, qkv_b, out_wT, out_b, band_mask, ident_in, y1 = (
        io['x_pad'], io['qkv_wT'], io['qkv_b'], io['out_wT'], io['out_b'],
        io['band_mask'], io['identm'], io['y1'])
    qk_dram = nc.dram_tensor("qkv_scr" + tag, [NKC * 128, 3 * D], BF16).ap()

    glob = Alloc(nc)
    ident = glob.sb("ident", [128, 128], BF16)
    masks = glob.sb("masks", [128, nqc * 3 * 128], BF16)
    onesrow = glob.sb("onesrow", [1, 128], BF16)
    onesf = glob.sb("onesf", [1, 64], F32)
    consts = glob.sb("consts", [128, 4], F32)
    t_ones = pb.op("gpsimd", lambda g: g.memset(onesrow[:], 1.0))
    t_onesf = pb.op("gpsimd", lambda g: g.memset(onesf[:], 1.0))
    pb.op("gpsimd", lambda g: g.memset(consts[:, 0:1], LN_EPS))
    pb.op("gpsimd", lambda g: g.memset(consts[:, 1:2], 1e-12))
    t_lne = pb.op("gpsimd", lambda g: g.memset(consts[:, 2:3], -0.5))
    t_eps = t_lne
    t_id = pb.op("gpsimd", lambda g: g.dma_start(out=ident[:], in_=ident_in[:]), dma=True)
    t_msk = pb.op("gpsimd", lambda g: g.dma_start(out=masks[:], in_=band_mask[:]), dma=True)

    ph = Alloc(nc)
    wqkv = ph.sb("wqkv", [128, 8 * 3 * D], BF16)
    brow = ph.sb("brow", [1, 3 * D], BF16)
    xc = ph.sb("xc", [128, 2 * D], F32)
    lnc = ph.sb("lnc", [128, 2 * D], BF16)
    lnT = ph.sb("lnT", [128, 2 * 8 * 128], BF16)
    qkf = ph.sb("qkf", [128, 2048], F32)
    sqs = ph.sb("sqs", [128, 2048], F32)
    st_ = ph.sb("st_", [128, 16], F32)
    nrm = ph.sb("nrm", [128, 128], F32)
    qkn = ph.sb("qkn", [128, 2 * 3072], BF16)
    pT_ = [ph.ps("pT0", [128, 1024], BF16), ph.ps("pT1", [128, 1024], BF16)]
    pqkv = ph.ps("pqkv", [128, 2048], F32)
    pv = ph.ps("pv", [128, 1024], F32)

    t_w = [pb.op("gpsimd", lambda g, j=j: g.dma_start(
        out=wqkv[:, j * 3072:(j + 1) * 3072],
        in_=qkv_wT[j * 128:(j + 1) * 128, :]), dma=True) for j in range(8)]
    t_b = pb.op("gpsimd", lambda g: g.dma_start(out=brow[:], in_=qkv_b[:]), dma=True)

    t_qkst = []
    SAB = ph.sb("SAB", [128, 4 * NKC], F32)
    SA, SB = SAB[:, 0:NKC], SAB[:, NKC:2 * NKC]
    MEAN, RSTD = SAB[:, 2 * NKC:3 * NKC], SAB[:, 3 * NKC:4 * NKC]
    t_sfree = [None, None]
    GS = 8
    t_gstats = {}
    for i in range(NKC):
        p = i % 2
        xs = xc[:, p * D:(p + 1) * D]
        t_x = pb.op("sync", lambda s, i=i, xs=xs: s.dma_start(
            out=xs, in_=x_pad[i * 128:(i + 1) * 128, :]),
            waits=[t_sfree[p]], dma=True)
        t_sfree[p] = emit_sums(pb, xs, SA[:, i:i + 1], SB[:, i:i + 1], sqs,
                               [t_x, t_sfree[1 - p]])
        if i % GS == GS - 1 or i == NKC - 1:
            g = i // GS
            lo, hi = g * GS, i + 1
            t_gstats[g] = emit_stats_batch(
                pb, SA[:, lo:hi], SB[:, lo:hi], MEAN[:, lo:hi], RSTD[:, lo:hi],
                consts[:, 0:1], consts[:, 2:3], [t_sfree[0], t_sfree[1], t_lne])
    t_xfree = None
    t_lnTfree = [None, None]
    t_lncfree = [None, None]
    t_qkffree = [None, None]
    t_pvfree = None
    t_pqkvfree = None
    t_qknfree = [None, None]
    for i in range(NKC):
        p = i % 2
        xs = xc[:, p * D:(p + 1) * D]
        lns = lnc[:, p * D:(p + 1) * D]
        NR = nrm[:, p * 64:(p + 1) * 64]
        t_x = pb.op("sync", lambda s, i=i, xs=xs: s.dma_start(
            out=xs, in_=x_pad[i * 128:(i + 1) * 128, :]),
            waits=[t_xfree] if i >= 2 else [t_sfree[0], t_sfree[1]], dma=True)
        t_ln = pb.op("vector", lambda v, xs=xs, lns=lns, i=i: v.tensor_scalar(
            lns, xs, MEAN[:, i:i + 1], RSTD[:, i:i + 1], ALU.subtract, ALU.mult),
            waits=[t_x, t_gstats[i // GS], t_lncfree[p]])
        t_xfree = t_ln
        t_lnTs = []
        t_tp = None
        for j in range(8):
            pj = (i * 8 + j) % 2
            t_tp = pb.op("tensor", lambda t, lns=lns, j=j, pj=pj: t.transpose(
                pT_[pj][:, 0:128], lns[:, j * 128:(j + 1) * 128], ident[:]),
                waits=[t_ln, t_id, t_lnTfree[pj]])
            t_ev = pb.op("vector", lambda v, j=j, p=p, pj=pj: v.tensor_copy(
                lnT[:, (p * 8 + j) * 128:(p * 8 + j + 1) * 128], pT_[pj][:, 0:128]),
                waits=[t_tp])
            t_lnTfree[pj] = t_ev
            t_lnTs.append(t_ev)
        t_lncfree[p] = t_tp

        def qk_mms(t, p=p):
            for piece in range(4):
                for j in range(8):
                    t.matmul(pqkv[:, piece * 512:(piece + 1) * 512],
                             lnT[:, (p * 8 + j) * 128:(p * 8 + j + 1) * 128],
                             wqkv[:, j * 3072 + piece * 512:j * 3072 + (piece + 1) * 512],
                             start=(j == 0), stop=False)
                last = t.matmul(pqkv[:, piece * 512:(piece + 1) * 512], onesrow[0:1, :],
                                brow[0:1, piece * 512:(piece + 1) * 512],
                                start=False, stop=True)
            return last
        t_qk = pb.op("tensor", qk_mms, waits=t_lnTs + t_w + [t_b, t_ones, t_pqkvfree])

        def v_mms(t, p=p):
            for piece in range(2):
                for j in range(8):
                    t.matmul(pv[:, piece * 512:(piece + 1) * 512],
                             lnT[:, (p * 8 + j) * 128:(p * 8 + j + 1) * 128],
                             wqkv[:, j * 3072 + 2048 + piece * 512:
                                  j * 3072 + 2048 + (piece + 1) * 512],
                             start=(j == 0), stop=False)
                last = t.matmul(pv[:, piece * 512:(piece + 1) * 512], onesrow[0:1, :],
                                brow[0:1, 2048 + piece * 512:2048 + (piece + 1) * 512],
                                start=False, stop=True)
            return last
        t_v = pb.op("tensor", v_mms, waits=[t_pvfree])
        t_qkev = pb.op("vector", lambda v: v.tensor_copy(qkf[:], pqkv[:]),
                       waits=[t_qk, t_qkffree[0], t_qkffree[1]])
        t_pqkvfree = t_qkev
        t_vev = pb.op("vector", lambda v, p=p: v.tensor_copy(
            qkn[:, p * 3072 + 2048:(p + 1) * 3072], pv[:]),
            waits=[t_v, t_qknfree[p]])
        t_pvfree = t_vev
        t_sqp = pb.op("scalar", lambda s: s.activation(sqs[:], qkf[:], AF.Square),
                      waits=[t_qkev])
        t_red = pb.op("vector", lambda v, NR=NR: v.tensor_reduce(
            NR[:, 0:32], sqs[:].rearrange("p (h e) -> p h e", e=64), AX.X, ALU.add),
            waits=[t_sqp])
        t_nsq = pb.op("scalar", lambda s, NR=NR: s.activation(
            NR[:, 32:64], NR[:, 0:32], AF.Ln, bias=consts[:, 1:2]), waits=[t_red, t_lne])
        t_nr = pb.op("scalar", lambda s, NR=NR: s.activation(
            NR[:, 0:32], NR[:, 32:64], AF.Exp, scale=consts[:, 2:3]), waits=[t_nsq])
        t_qknv = pb.op("vector", lambda v, p=p, NR=NR: v.tensor_tensor(
            qkn[:, p * 3072:p * 3072 + 2048].rearrange("p (h e) -> p h e", e=64),
            qkf[:].rearrange("p (h e) -> p h e", e=64),
            NR[:, 0:32].unsqueeze(2).broadcast_to([128, 32, 64]), ALU.mult),
            waits=[t_nr, t_qknfree[p]])
        t_qkffree = [t_sqp, t_qknv]
        t_st = pb.op("sync", lambda s, i=i, p=p: s.dma_start(
            out=qk_dram[i * 128:(i + 1) * 128, :], in_=qkn[:, p * 3072:(p + 1) * 3072]),
            waits=[t_qknv, t_vev], dma=True)
        t_qknfree[p] = t_st
        t_qkst.append(t_st)
    pb.barrier()
    ph.close()

    for r0 in range(0, nqc, 16):
        nqh = min(16, nqc - r0)
        nkh = nqh + 2
        ph2 = Alloc(nc)
        kT = ph2.sb("kT", [128, 8 * 18 * 128], BF16)
        qT = ph2.sb("qT", [128, 8 * 16 * 128], BF16)
        v_aug = ph2.sb("v_aug", [128, 18 * 16 * 65], BF16)
        wout = ph2.sb("wout", [128, 8 * D], BF16)
        obrow = ph2.sb("obrow", [1, D], BF16)
        expt = ph2.sb("expt", [128, 2 * 512], BF16)
        rl = ph2.sb("rl", [1, 2 * 512], F32)
        rlbc = ph2.sb("rlbc", [64, 2 * 512], F32)
        outT = ph2.sb("outT", [128, 2 * 8 * 128], BF16)
        xre = ph2.sb("xre", [128, 2 * D], F32)
        y1c = ph2.sb("y1c", [128, 2 * D], F32)
        psc = ph2.ps("psc", [128, 2 * 1024], F32)
        pav = ph2.ps("pav", [65, 2 * 512], F32)
        pbc = ph2.ps("pbc", [64, 512], F32)
        pproj = psc[:, 0:1024]

        t_kT = [pb.op("sync", lambda s, j=j, r0=r0, nkh=nkh: s.dma_start(
            out=kT[:, j * 18 * 128:j * 18 * 128 + nkh * 128],
            in_=qk_dram[r0 * 128:(r0 + nkh) * 128, D + j * 128:D + (j + 1) * 128],
            transpose=True), waits=t_qkst[:r0 + nkh], dma=True) for j in range(8)]
        t_qT = [pb.op("sync", lambda s, j=j, r0=r0, nqh=nqh: s.dma_start(
            out=qT[:, j * 16 * 128:j * 16 * 128 + nqh * 128],
            in_=qk_dram[(r0 + 1) * 128:(r0 + 1 + nqh) * 128, j * 128:(j + 1) * 128],
            transpose=True), waits=t_qkst[:r0 + 1 + nqh], dma=True) for j in range(8)]
        t_vl, t_vone = [], []
        for c in range(nkh):
            t_ = pb.op("gpsimd", lambda g, c=c, r0=r0: g.dma_start(
                out=v_aug[:, c * 1040:(c + 1) * 1040]
                .rearrange("p (h e) -> p h e", e=65)[:, :, 0:64],
                in_=qk_dram[(r0 + c) * 128:(r0 + c + 1) * 128, 2 * D:3 * D]
                .rearrange("p (h e) -> p h e", e=64)),
                waits=t_qkst[:r0 + nkh], dma=True)
            t_vl.append(t_)
            t_vone.append(pb.op("gpsimd", lambda g, c=c: g.memset(
                v_aug[:, c * 1040 + 64:(c + 1) * 1040:65], 1.0), waits=[t_]))
        t_wo = [pb.op("gpsimd", lambda g, j=j: g.dma_start(
            out=wout[:, j * D:(j + 1) * D],
            in_=out_wT[j * 128:(j + 1) * 128, :]), dma=True) for j in range(8)]
        t_ob = pb.op("gpsimd", lambda g: g.dma_start(out=obrow[:], in_=out_b[:]), dma=True)

        t_expfree = [None, None]
        t_pprojfree = None
        t_pscfree = [None, None]
        t_pavfree = [None, None]
        t_rlf = None
        t_bcf = None
        t_y1 = [None, None]
        for qc in range(nqh):
            gqc = r0 + qc
            p = qc % 2
            t_xr = pb.op("sync", lambda s, gqc=gqc, p=p: s.dma_start(
                out=xre[:, p * D:(p + 1) * D],
                in_=x_pad[(gqc + 1) * 128:(gqc + 2) * 128, :]),
                waits=[t_y1[p]], dma=True)
            t_fins = []
            for hg in range(4):
                u = qc * 4 + hg
                pu = u % 2
                t_avs = []
                for kc in range(3):
                    kchunk = qc + kc
                    pp = (u * 3 + kc) % 2

                    def sc_mms(t, hg=hg, pp=pp, kchunk=kchunk, qc=qc):
                        for hh in range(4):
                            h = hg * 4 + hh
                            tile, row0 = h // 2, 64 * (h % 2)
                            col = pp * 1024 + (hh % 2) * 512 + (hh // 2) * 128
                            last = t.matmul(
                                psc[:, col:col + 128],
                                kT[row0:row0 + 64,
                                   tile * 18 * 128 + kchunk * 128:
                                   tile * 18 * 128 + (kchunk + 1) * 128],
                                qT[row0:row0 + 64,
                                   tile * 16 * 128 + qc * 128:
                                   tile * 16 * 128 + (qc + 1) * 128],
                                start=(hh < 2), stop=(hh >= 2))
                        return last
                    t_sc = pb.op("tensor", sc_mms, waits=t_kT + t_qT + [t_pscfree[pp]])
                    t_exp = pb.op("scalar", lambda s, pp=pp: s.activation(
                        expt[:, pp * 512:(pp + 1) * 512],
                        psc[:, pp * 1024:(pp + 1) * 1024]
                        .rearrange("p (b c) -> p b c", c=512)[:, :, 0:256], AF.Exp),
                        waits=[t_sc, t_expfree[pp]])
                    t_pscfree[pp] = t_exp
                    t_m = pb.op("vector", lambda v, pp=pp, gqc=gqc, kc=kc: v.tensor_tensor(
                        expt[:, pp * 512:(pp + 1) * 512].rearrange("p (h q) -> p h q", q=128),
                        expt[:, pp * 512:(pp + 1) * 512].rearrange("p (h q) -> p h q", q=128),
                        masks[:, (gqc * 3 + kc) * 128:(gqc * 3 + kc + 1) * 128]
                        .unsqueeze(1).broadcast_to([128, 4, 128]), ALU.mult),
                        waits=[t_exp, t_msk])

                    def av_mms(t, hg=hg, pp=pp, pu=pu, kchunk=kchunk, kc=kc):
                        for hh in range(4):
                            h = hg * 4 + hh
                            ec = ((hh % 2) * 2 + hh // 2) * 128
                            last = t.matmul(
                                pav[:, pu * 512 + hh * 128:pu * 512 + (hh + 1) * 128],
                                v_aug[:, kchunk * 1040 + h * 65:
                                      kchunk * 1040 + (h + 1) * 65],
                                expt[:, pp * 512 + ec:pp * 512 + ec + 128],
                                start=(kc == 0 and hh == 0), stop=(kc == 2 and hh == 3))
                        return last
                    t_av = pb.op("tensor", av_mms,
                                 waits=[t_m, t_pavfree[pu]] + t_vone if kc == 0 else [t_m])
                    t_expfree[pp] = t_av
                    t_avs.append(t_av)
                t_rl = pb.op("scalar", lambda s, pu=pu: s.activation(
                    rl[0:1, pu * 512:(pu + 1) * 512], pav[64:65, pu * 512:(pu + 1) * 512],
                    AF.Ln), waits=[t_avs[2], t_rlf])
                t_rl = pb.op("scalar", lambda s, pu=pu: s.activation(
                    rl[0:1, pu * 512:(pu + 1) * 512], rl[0:1, pu * 512:(pu + 1) * 512],
                    AF.Exp, scale=-1.0), waits=[t_rl])
                t_bc = pb.op("tensor", lambda t, pu=pu: t.matmul(
                    pbc[:], onesf[0:1, 0:64],
                    rl[0:1, pu * 512:(pu + 1) * 512], start=True, stop=True),
                    waits=[t_rl, t_bcf, t_onesf])
                t_rlf = t_bc
                t_bcev = pb.op("vector", lambda v, pu=pu: v.tensor_copy(
                    rlbc[:, pu * 512:(pu + 1) * 512], pbc[:]), waits=[t_bc])
                t_bcf = t_bcev

                def fin(v, hg=hg, pu=pu, p=p):
                    for hh in range(4):
                        h = hg * 4 + hh
                        tile, row0 = h // 2, 64 * (h % 2)
                        last = v.scalar_tensor_tensor(
                            outT[row0:row0 + 64, (p * 8 + tile) * 128:(p * 8 + tile + 1) * 128],
                            pav[0:64, pu * 512 + hh * 128:pu * 512 + (hh + 1) * 128], 1.0,
                            rlbc[:, pu * 512 + hh * 128:pu * 512 + (hh + 1) * 128],
                            ALU.mult, ALU.mult)
                    return last
                t_fin = pb.op("vector", fin, waits=[t_bcev] + ([t_y1[p]] if hg == 0 else []))
                t_pavfree[pu] = t_fin
                t_fins.append(t_fin)

            def proj_mms(t, p=p):
                for piece in range(2):
                    for j in range(8):
                        t.matmul(pproj[:, piece * 512:(piece + 1) * 512],
                                 outT[:, (p * 8 + j) * 128:(p * 8 + j + 1) * 128],
                                 wout[:, j * D + piece * 512:j * D + (piece + 1) * 512],
                                 start=(j == 0), stop=False)
                    last = t.matmul(pproj[:, piece * 512:(piece + 1) * 512], onesrow[0:1, :],
                                    obrow[0:1, piece * 512:(piece + 1) * 512],
                                    start=False, stop=True)
                return last
            t_proj = pb.op("tensor", proj_mms,
                           waits=t_fins + t_wo + [t_ob, t_pprojfree,
                                                  t_pscfree[0], t_pscfree[1]])
            t_pscfree[0] = None
            t_y1c = pb.op("vector", lambda v, p=p: v.scalar_tensor_tensor(
                y1c[:, p * D:(p + 1) * D], pproj[:], 1.0,
                xre[:, p * D:(p + 1) * D], ALU.mult, ALU.add),
                waits=[t_proj, t_xr])
            t_pprojfree = t_y1c
            t_pscfree[0] = t_y1c
            t_st = pb.op("sync", lambda s, gqc=gqc, p=p: s.dma_start(
                out=y1[gqc * 128:(gqc + 1) * 128, :], in_=y1c[:, p * D:(p + 1) * D]),
                waits=[t_y1c], dma=True)
            t_y1[p] = t_st
        pb.op("sync", lambda s: s.nop(), waits=[t_y1[0], t_y1[1]])
        pb.barrier()
        ph2.close()
    glob.close()



def l0a_inputs(x, core):
    b, h = core // 2, core % 2
    q0 = h * 2048
    lo, hi = q0 - 128, q0 + 2048 + 128
    xp = np.zeros((2048 + 256, D), dtype=np.float32)
    s, e = max(lo, 0), min(hi, SEQ)
    xp[s - lo:e - lo] = x[b, s:e]
    return {"x_pad": xp}


def l0a_masks(nqc, q0):
    m = np.zeros((nqc, 3, 128, 128), dtype=np.float32)
    for qc in range(nqc):
        gq = q0 + qc * 128 + np.arange(128)[None, :]
        for kc in range(3):
            gk = q0 + (qc + kc - 1) * 128 + np.arange(128)[:, None]
            diff = gk - gq
            ok = (np.abs(diff) <= 16) & (diff % 2 == 0) & (gk >= 0) & (gk < SEQ)
            m[qc, kc] = ok.astype(np.float32)
    return bf(m.reshape(nqc * 3, 128, 128).transpose(1, 0, 2).reshape(128, nqc * 3 * 128))


def emit_sums(pb, xs, sa_col, sb_col, sqs, waits):
    t1 = pb.op("scalar", lambda s: s.activation(
        sqs[:, 0:D], xs, AF.Square, accum_out=sa_col), waits=waits)
    t2 = pb.op("scalar", lambda s: s.activation(
        sqs[:, 0:D], xs, AF.Copy, accum_out=sb_col), waits=[t1])
    return t2


def emit_stats_batch(pb, SA, SB, MEAN, RSTD, eps_col, mhalf_col, waits):
    t = pb.op("vector", lambda v: v.tensor_scalar_mul(MEAN, SB, 1.0 / D), waits=waits)
    t = pb.op("vector", lambda v: v.scalar_tensor_tensor(
        RSTD, MEAN, 1.0, MEAN, ALU.mult, ALU.mult), waits=[t])
    t = pb.op("vector", lambda v: v.scalar_tensor_tensor(
        RSTD, SA, 1.0 / D, RSTD, ALU.mult, ALU.subtract), waits=[t])
    t = pb.op("scalar", lambda s: s.activation(RSTD, RSTD, AF.Ln, bias=eps_col),
              waits=[t])
    t = pb.op("scalar", lambda s: s.activation(RSTD, RSTD, AF.Exp, scale=mhalf_col),
              waits=[t])
    return t


def emit_ln_stats(pb, sc_eng_bufs, xs, S, consts_eps_col, waits, eps_tok=None,
                  mhalf=None):
    (sqs,) = sc_eng_bufs
    t_s1 = pb.op("scalar", lambda s: s.activation(
        sqs[:, 0:D], xs, AF.Square, accum_out=S[:, 1:2]), waits=waits)
    t_s0 = pb.op("scalar", lambda s: s.activation(
        sqs[:, 0:D], xs, AF.Copy, accum_out=S[:, 0:1]), waits=[t_s1])
    t_mv = pb.op("vector", lambda v: v.tensor_scalar_mul(S[:, 2:3], S[:, 0:1], 1.0 / D),
                 waits=[t_s0])
    t_mv2 = pb.op("vector", lambda v: v.scalar_tensor_tensor(
        S[:, 3:4], S[:, 2:3], 1.0, S[:, 2:3], ALU.mult, ALU.mult), waits=[t_mv])
    t_var = pb.op("vector", lambda v: v.scalar_tensor_tensor(
        S[:, 3:4], S[:, 1:2], 1.0 / D, S[:, 3:4], ALU.mult, ALU.subtract), waits=[t_mv2])
    t_sq = pb.op("scalar", lambda s: s.activation(
        S[:, 4:5], S[:, 3:4], AF.Ln, bias=consts_eps_col), waits=[t_var, eps_tok])
    t_r = pb.op("scalar", lambda s: s.activation(
        S[:, 5:6], S[:, 4:5], AF.Exp, scale=mhalf), waits=[t_sq])
    return t_var, t_r



def build_l0f(nt_chunks, npass=2):
    nc = bass.Bass("TRN2", target_bir_lowering=False)
    NT = nt_chunks * 128
    io = {
        'y1_in': nc.dram_tensor("y1_in", [NT, D], F32, kind="ExternalInput").ap(),
        'g_bc': nc.dram_tensor("g_bc", [128, D], F32, kind="ExternalInput").ap(),
        'b_bc': nc.dram_tensor("b_bc", [128, D], F32, kind="ExternalInput").ap(),
        'rW': nc.dram_tensor("rW", [128, 8 * 8], F32, kind="ExternalInput").ap(),
        'rb': nc.dram_tensor("rb", [1, 8], F32, kind="ExternalInput").ap(),
        'W1': nc.dram_tensor("W1", [8, D, 2048], F32, kind="ExternalInput").ap(),
        'b1t': nc.dram_tensor("b1t", [128, 8 * 16], F32, kind="ExternalInput").ap(),
        'W2': nc.dram_tensor("W2", [8, 2048, D], F32, kind="ExternalInput").ap(),
        'b2r': nc.dram_tensor("b2r", [1, 8 * D], F32, kind="ExternalInput").ap(),
        'identm': nc.dram_tensor("identm", [128, 128], BF16, kind="ExternalInput").ap(),
        'y2': nc.dram_tensor("y2", [NT, D], F32, kind="ExternalOutput").ap(),
    }
    with nc.Block() as blk:
        pb = PB(nc)
        emit_l0f(nc, pb, blk, io, nt_chunks, npass)
        pb.emit(blk)
    return nc


def emit_l0f(nc, pb, blk, io, nt_chunks, npass=2, tag=''):
    NT = nt_chunks * 128
    (y1_in, g_bc, b_bc, rW, rb, W1, b1t, W2, b2r, ident_in, y2) = (
        io['y1_in'], io['g_bc'], io['b_bc'], io['rW'], io['rb'], io['W1'],
        io['b1t'], io['W2'], io['b2r'], io['identm'], io['y2'])
    w1_bf = nc.dram_tensor("w1_bf" + tag, [8, D, 2048], BF16).ap()
    w2_bf = nc.dram_tensor("w2_bf" + tag, [8, 2048, D], BF16).ap()

    pc = nt_chunks // npass
    TGP = (pc * 128) // 512

    glob = Alloc(nc)
    ident = glob.sb("ident", [128, 128], BF16)
    gb = glob.sb("gb", [128, 2 * D], F32)
    b1s = glob.sb("b1s", [128, 8 * 16], F32)
    b2row = glob.sb("b2row", [1, 8 * D], BF16)
    rWs = glob.sb("rWs", [128, 8 * 8], BF16)
    rbs = glob.sb("rbs", [1, 8], BF16)
    onesrow = glob.sb("onesrow", [1, 128], BF16)
    consts = glob.sb("consts", [128, 4], F32)
    stats = glob.sb("stats", [128, nt_chunks * 4], F32)
    wts = glob.sb("wts", [128, nt_chunks * 8], F32)
    zT = glob.sb("zT", [128, 8 * NT], BF16)
    acc = glob.sb("acc", [128, pc * D], F32)
    SAB = glob.sb("SAB", [128, 7 * nt_chunks], F32)
    if True:
        t_ones = pb.op("gpsimd", lambda g: g.memset(onesrow[:], 1.0))
        pb.op("gpsimd", lambda g: g.memset(consts[:, 0:1], LN_EPS))
        pb.op("gpsimd", lambda g: g.memset(consts[:, 1:2], MOE_LN_EPS))
        t_eps = pb.op("gpsimd", lambda g: g.memset(consts[:, 2:3], -0.5))
        t_id = pb.op("gpsimd", lambda g: g.dma_start(out=ident[:], in_=ident_in[:]), dma=True)
        t_gb = pb.op("gpsimd", lambda g: g.dma_start(out=gb[:, 0:D], in_=g_bc[:]), dma=True)
        t_bb = pb.op("gpsimd", lambda g: g.dma_start(out=gb[:, D:2 * D], in_=b_bc[:]), dma=True)
        t_b1 = pb.op("gpsimd", lambda g: g.dma_start(out=b1s[:], in_=b1t[:]), dma=True)
        t_b2 = pb.op("gpsimd", lambda g: g.dma_start(out=b2row[:], in_=b2r[:]), dma=True)
        t_rw = pb.op("gpsimd", lambda g: g.dma_start(out=rWs[:], in_=rW[:]), dma=True)
        t_rb = pb.op("gpsimd", lambda g: g.dma_start(out=rbs[:], in_=rb[:]), dma=True)

        ph = Alloc(nc)
        yc = ph.sb("yc", [128, 2 * D], F32)
        sqs = ph.sb("sqs", [128, D], F32)
        S8 = ph.sb("S8", [128, 16], F32)
        az = ph.sb("az", [128, 2 * D], F32)
        zc = ph.sb("zc", [128, 2 * D], BF16)
        es = ph.sb("es", [128, 2 * 8], F32)
        r8 = ph.sb("r8", [128, 2 * 8], F32)
        pTz = [ph.ps("pTz0", [128, 1024], BF16), ph.ps("pTz1", [128, 1024], BF16)]
        plog = ph.ps("plog", [128, 1024], F32)

        NT_C = nt_chunks
        SA, SB = SAB[:, 0:NT_C], SAB[:, NT_C:2 * NT_C]
        MEAN, R1 = SAB[:, 2 * NT_C:3 * NT_C], SAB[:, 3 * NT_C:4 * NT_C]
        SC12 = SAB[:, 4 * NT_C:5 * NT_C]
        P0, P1 = SAB[:, 5 * NT_C:6 * NT_C], SAB[:, 6 * NT_C:7 * NT_C]
        lg_all = ph.sb("lg_all", [128, NT_C * 8], F32)
        t_sfree = [None, None]
        for i in range(NT_C):
            p = i % 2
            ys = yc[:, p * D:(p + 1) * D]
            t_y = pb.op("sync", lambda s, i=i, ys=ys: s.dma_start(
                out=ys, in_=y1_in[i * 128:(i + 1) * 128, :]),
                waits=[t_sfree[p]], dma=True)
            t_sfree[p] = emit_sums(pb, ys, SA[:, i:i + 1], SB[:, i:i + 1], sqs,
                                   [t_y, t_sfree[1 - p]])
        t_st1 = emit_stats_batch(pb, SA, SB, MEAN, R1, consts[:, 0:1],
                                 consts[:, 2:3], [t_sfree[0], t_sfree[1], t_eps])
        t_b1_ = pb.op("vector", lambda v: v.tensor_scalar_add(P0, R1, 1.0), waits=[t_st1])
        t_b2_ = pb.op("vector", lambda v: v.tensor_tensor(P1, MEAN, R1, ALU.mult),
                      waits=[t_b1_])
        t_b3_ = pb.op("vector", lambda v: v.scalar_tensor_tensor(
            SC12, MEAN, 1.0, MEAN, ALU.mult, ALU.mult), waits=[t_b2_])
        t_b4_ = pb.op("vector", lambda v: v.scalar_tensor_tensor(
            SC12, SA, 1.0 / D, SC12, ALU.mult, ALU.subtract), waits=[t_b3_])
        t_b5_ = pb.op("vector", lambda v: v.tensor_tensor(SA, R1, R1, ALU.mult),
                      waits=[t_b4_])
        t_b6_ = pb.op("vector", lambda v: v.tensor_tensor(SC12, SC12, SA, ALU.mult),
                      waits=[t_b5_])
        t_b7_ = pb.op("scalar", lambda s: s.activation(SC12, SC12, AF.Ln,
                                                       bias=consts[:, 1:2]), waits=[t_b6_])
        t_b8_ = pb.op("scalar", lambda s: s.activation(SC12, SC12, AF.Exp,
                                                       scale=consts[:, 2:3]), waits=[t_b7_])
        t_stats = pb.op("vector", lambda v: v.tensor_tensor(SC12, SC12, R1, ALU.mult),
                        waits=[t_b8_])
        t_yfree = None
        t_lnTfree = [None, None]
        t_zfree = [None, None]
        t_azfree = [None, None]
        t_lgfree = None
        t_plogfree = [None, None]
        for i in range(NT_C):
            p = i % 2
            ys = yc[:, p * D:(p + 1) * D]
            t_y = pb.op("sync", lambda s, i=i, ys=ys: s.dma_start(
                out=ys, in_=y1_in[i * 128:(i + 1) * 128, :]),
                waits=[t_yfree] if i >= 2 else [t_sfree[0], t_sfree[1]], dma=True)
            t_az = pb.op("vector", lambda v, ys=ys, i=i, p=p: v.tensor_scalar(
                az[:, p * D:(p + 1) * D], ys, MEAN[:, i:i + 1], SC12[:, i:i + 1],
                ALU.subtract, ALU.mult), waits=[t_y, t_stats, t_azfree[p]])
            t_yfree = t_az
            t_z1 = pb.op("vector", lambda v, p=p: v.tensor_tensor(
                az[:, p * D:(p + 1) * D], az[:, p * D:(p + 1) * D], gb[:, 0:D], ALU.mult),
                waits=[t_az, t_gb])
            t_z = pb.op("vector", lambda v, p=p: v.tensor_tensor(
                zc[:, p * D:(p + 1) * D], az[:, p * D:(p + 1) * D], gb[:, D:2 * D], ALU.add),
                waits=[t_z1, t_bb, t_zfree[p]])
            t_azfree[p] = t_z
            t_zTs = []
            t_tp = None
            for j in range(8):
                pj = (i * 8 + j) % 2
                t_tp = pb.op("tensor", lambda t, p=p, j=j, pj=pj: t.transpose(
                    pTz[pj][:, 0:128], zc[:, p * D + j * 128:p * D + (j + 1) * 128], ident[:]),
                    waits=[t_z, t_id, t_lnTfree[pj]])
                t_ev = pb.op("vector", lambda v, i=i, j=j, pj=pj: v.tensor_copy(
                    zT[:, j * NT + i * 128:j * NT + (i + 1) * 128], pTz[pj][:, 0:128]),
                    waits=[t_tp])
                t_lnTfree[pj] = t_ev
                t_zTs.append(t_ev)
            t_zfree[p] = t_tp

            def r_mms(t, i=i, p=p):
                for j in range(8):
                    t.matmul(plog[:, p * 512:p * 512 + 8],
                             zT[:, j * NT + i * 128:j * NT + (i + 1) * 128],
                             rWs[:, j * 8:(j + 1) * 8], start=(j == 0), stop=False)
                return t.matmul(plog[:, p * 512:p * 512 + 8], onesrow[0:1, :],
                                rbs[0:1, :], start=False, stop=True)
            t_lg = pb.op("tensor", r_mms,
                         waits=t_zTs + [t_rw, t_rb, t_ones, t_plogfree[p]])
            t_lge = pb.op("vector", lambda v, i=i, p=p: v.tensor_copy(
                lg_all[:, i * 8:(i + 1) * 8], plog[:, p * 512:p * 512 + 8]),
                waits=[t_lg, t_lgfree])
            t_plogfree[p] = t_lge
        L3 = lg_all[:].rearrange("p (c e) -> p c e", e=8)
        W3 = wts[:].rearrange("p (c e) -> p c e", e=8)
        t_d1 = pb.op("vector", lambda v: v.tensor_reduce(
            SA[:, 0:NT_C], L3, AX.X, ALU.max, negate=True),
            waits=[t_plogfree[0], t_plogfree[1]])
        t_d2 = pb.op("vector", lambda v: v.tensor_tensor(
            L3, L3, SA[:, 0:NT_C].unsqueeze(2).broadcast_to([128, NT_C, 8]), ALU.add),
            waits=[t_d1])
        t_d3 = pb.op("scalar", lambda s: s.activation(lg_all[:], lg_all[:], AF.Exp),
                     waits=[t_d2])
        t_d4 = pb.op("vector", lambda v: v.tensor_reduce(
            SA[:, 0:NT_C], L3, AX.X, ALU.max), waits=[t_d3])
        t_d5 = pb.op("vector", lambda v: v.tensor_tensor(
            W3, L3, SA[:, 0:NT_C].unsqueeze(2).broadcast_to([128, NT_C, 8]), ALU.is_ge),
            waits=[t_d4])
        t_d6 = pb.op("vector", lambda v: v.scalar_tensor_tensor(
            wts[:], wts[:], -1e30, lg_all[:], ALU.mult, ALU.add), waits=[t_d5])
        t_d7 = pb.op("vector", lambda v: v.tensor_reduce(
            SB[:, 0:NT_C], W3, AX.X, ALU.max), waits=[t_d6])
        t_d8 = pb.op("vector", lambda v: v.tensor_tensor(
            W3, L3, SB[:, 0:NT_C].unsqueeze(2).broadcast_to([128, NT_C, 8]), ALU.is_ge),
            waits=[t_d7])
        t_d9 = pb.op("vector", lambda v: v.tensor_tensor(
            wts[:], wts[:], lg_all[:], ALU.mult), waits=[t_d8])
        t_da = pb.op("vector", lambda v: v.tensor_reduce(
            SB[:, 0:NT_C], W3, AX.X, ALU.add), waits=[t_d9])
        t_db_ = pb.op("scalar", lambda s: s.activation(SB[:, 0:NT_C], SB[:, 0:NT_C],
                                                       AF.Ln), waits=[t_da])
        t_dc = pb.op("scalar", lambda s: s.activation(SB[:, 0:NT_C], SB[:, 0:NT_C],
                                                      AF.Exp, scale=-1.0), waits=[t_db_])
        t_dd = pb.op("vector", lambda v: v.tensor_tensor(
            W3, W3, SB[:, 0:NT_C].unsqueeze(2).broadcast_to([128, NT_C, 8]), ALU.mult),
            waits=[t_dc])
        pb.barrier()
        ph.close()

        ph2 = Alloc(nc)
        NWB = 4
        wbuf = [ph2.sb(f"wbuf{k}", [128, 8 * 1024], BF16) for k in range(NWB)]
        hb = ph2.sb("hb", [128, 16 * 512], BF16)
        yre = ph2.sb("yre", [128, D], F32)
        y2c = ph2.sb("y2c", [128, 2 * D], F32)
        phid = ph2.ps("phid", [128, 2048], F32)
        po = ph2.ps("po", [128, 2 * 1024], F32)

        t_wload = []
        t_wuse = []
        t_wecho = {}
        cache_tok = {}
        nloads = npass * 8 * 4

        def emit_wload(k):
            P, rem = divmod(k, 32)
            e, q = divmod(rem, 4)
            buf = wbuf[k % NWB]
            waits = []
            if k >= NWB:
                waits.append(t_wuse[k - NWB])
                waits.extend(t_wecho.get(k - NWB, []))
            if P == 0:
                if q < 2:
                    srcs = [W1[e, j * 128:(j + 1) * 128, q * 1024:(q + 1) * 1024]
                            for j in range(8)]
                    dsts = [w1_bf[e, j * 128:(j + 1) * 128, q * 1024:(q + 1) * 1024]
                            for j in range(8)]
                else:
                    h = q - 2
                    srcs = [W2[e, h * 1024 + j * 128:h * 1024 + (j + 1) * 128, :]
                            for j in range(8)]
                    dsts = [w2_bf[e, h * 1024 + j * 128:h * 1024 + (j + 1) * 128, :]
                            for j in range(8)]
                toks = [pb.op("gpsimd", lambda g, j=j, buf=buf, src=srcs[j]: g.dma_start(
                    out=buf[:, j * 1024:(j + 1) * 1024], in_=src), waits=waits, dma=True)
                    for j in range(8)]
                if npass > 1:
                    echo = [pb.op("sync", lambda s, j=j, buf=buf, dst=dsts[j]: s.dma_start(
                        out=dst, in_=buf[:, j * 1024:(j + 1) * 1024]),
                        waits=[toks[j]], dma=True) for j in range(8)]
                    cache_tok[(e, q)] = echo
                    t_wecho[k] = echo
            else:
                if q < 2:
                    srcs = [w1_bf[e, j * 128:(j + 1) * 128, q * 1024:(q + 1) * 1024]
                            for j in range(8)]
                else:
                    h = q - 2
                    srcs = [w2_bf[e, h * 1024 + j * 128:h * 1024 + (j + 1) * 128, :]
                            for j in range(8)]
                waits = waits + cache_tok[(e, q)]
                toks = [pb.op("sync", lambda s, j=j, buf=buf, src=srcs[j]: s.dma_start(
                    out=buf[:, j * 1024:(j + 1) * 1024], in_=src), waits=waits, dma=True)
                    for j in range(8)]
            t_wload.append(toks)
            t_wuse.append(None)

        for k in range(min(NWB, nloads)):
            emit_wload(k)

        t_accfree = None
        t_hbfree = [None]
        t_pofree = [None, None]
        t_phidfree = [None] * 16
        t_yrefree = None
        t_y2free = [None, None]
        t_y2st = []
        for P in range(npass):
            c0 = P * pc
            t_az0 = pb.op("vector", lambda v: v.memset(acc[:], 0.0),
                          waits=[t_accfree])
            t_accfree = None
            t_evacs = []
            for e in range(8):
                kbase = P * 32 + e * 4
                for tg in range(TGP):
                    tp = 0
                    HB = hb[:]
                    t_gelus = []
                    for fo in range(16):
                        half = fo // 8
                        fol = fo % 8
                        kW1 = kbase + half
                        bank = fo % 4

                        def w1_mms(t, e=e, tg=tg, fo=fo, fol=fol, kW1=kW1, bank=bank,
                                   c0=c0, P=P):
                            buf = wbuf[kW1 % NWB]
                            for j in range(8):
                                last = t.matmul(
                                    phid[:, bank * 512:(bank + 1) * 512],
                                    buf[:, j * 1024 + fol * 128:j * 1024 + (fol + 1) * 128],
                                    zT[:, j * NT + c0 * 128 + tg * 512:
                                       j * NT + c0 * 128 + (tg + 1) * 512],
                                    start=(j == 0), stop=(j == 7))
                            return last
                        t_mm = pb.op("tensor", w1_mms,
                                     waits=t_wload[kW1] + [t_phidfree[bank],
                                                           t_gelus[-4] if fo >= 4 else None])
                        t_g = pb.op("scalar", lambda s, fo=fo, bank=bank, e=e, HB=HB:
                                    s.activation(HB[:, fo * 512:(fo + 1) * 512],
                                                 phid[:, bank * 512:(bank + 1) * 512],
                                                 AF.Gelu, bias=b1s[:, e * 16 + fo:e * 16 + fo + 1]),
                                    waits=[t_mm, t_b1, t_hbfree[tp]] if fo == 0 else [t_mm])
                        t_phidfree[bank] = t_g
                        t_gelus.append(t_g)
                    if e == 0 and tg < 2:
                        pass
                    t_wuse[kbase] = t_gelus[7]
                    t_wuse[kbase + 1] = t_gelus[15]
                    t_w2ev = []
                    for t128 in range(4):
                        il = tg * 4 + t128
                        pop = (tg * 4 + t128) % 2

                        def w2_mms(t, e=e, kbase=kbase, t128=t128, pop=pop, HB=HB):
                            for piece in range(2):
                                for i16 in range(16):
                                    buf = wbuf[(kbase + 2 + i16 // 8) % NWB]
                                    t.matmul(
                                        po[:, pop * 1024 + piece * 512:
                                           pop * 1024 + (piece + 1) * 512],
                                        HB[:, i16 * 512 + t128 * 128:i16 * 512 + t128 * 128 + 128],
                                        buf[:, (i16 % 8) * 1024 + piece * 512:
                                            (i16 % 8) * 1024 + (piece + 1) * 512],
                                        start=(i16 == 0), stop=False)
                                last = t.matmul(
                                    po[:, pop * 1024 + piece * 512:pop * 1024 + (piece + 1) * 512],
                                    onesrow[0:1, :],
                                    b2row[0:1, e * D + piece * 512:e * D + (piece + 1) * 512],
                                    start=False, stop=True)
                            return last
                        t_w2 = pb.op("tensor", w2_mms,
                                     waits=[t_gelus[15], t_pofree[pop], t_b2]
                                     + t_wload[kbase + 2] + t_wload[kbase + 3])

                        def evac(v, e=e, il=il, pop=pop, c0=c0):
                            i_glob = c0 + il
                            for piece in range(2):
                                last = v.scalar_tensor_tensor(
                                    acc[:, il * D + piece * 512:il * D + (piece + 1) * 512],
                                    po[:, pop * 1024 + piece * 512:
                                       pop * 1024 + (piece + 1) * 512],
                                    wts[:, i_glob * 8 + e:i_glob * 8 + e + 1],
                                    acc[:, il * D + piece * 512:il * D + (piece + 1) * 512],
                                    ALU.mult, ALU.add)
                            return last
                        t_ev = pb.op("vector", evac, waits=[t_w2, t_az0])
                        t_pofree[pop] = t_ev
                        t_evacs.append(t_ev)
                    t_hbfree[tp] = t_w2
                t_wuse[kbase + 2] = t_w2
                t_wuse[kbase + 3] = t_w2
                for k in range(kbase + NWB, min(kbase + 4 + NWB, nloads)):
                    if len(t_wload) <= k:
                        emit_wload(k)
            for il in range(pc):
                i = c0 + il
                p = il % 2
                t_yre = pb.op("sync", lambda s, i=i: s.dma_start(
                    out=yre[:], in_=y1_in[i * 128:(i + 1) * 128, :]),
                    waits=[t_yrefree], dma=True)
                t_t1 = pb.op("vector", lambda v, p=p, i=i: v.tensor_scalar(
                    y2c[:, p * D:(p + 1) * D], yre[:],
                    P0[:, i:i + 1], P1[:, i:i + 1], ALU.mult, ALU.subtract),
                    waits=[t_yre, t_y2free[p]] + t_evacs)
                t_t2 = pb.op("vector", lambda v, p=p, il=il: v.tensor_tensor(
                    y2c[:, p * D:(p + 1) * D], y2c[:, p * D:(p + 1) * D],
                    acc[:, il * D:(il + 1) * D], ALU.add), waits=[t_t1])
                t_yrefree = t_t2
                t_st = pb.op("sync", lambda s, i=i, p=p: s.dma_start(
                    out=y2[i * 128:(i + 1) * 128, :], in_=y2c[:, p * D:(p + 1) * D]),
                    waits=[t_t2], dma=True)
                t_y2free[p] = t_st
                t_y2st.append(t_st)
            t_accfree = t_y2st[-1]
        pb.op("sync", lambda s: s.nop(), waits=t_y2st[-2:])
        pb.barrier()
        ph2.close()
    glob.close()



def ceil_div(a, b):
    return -(-a // b)


def build_l1a(nqg=4, skc=None):
    if skc is None:
        skc = nqg * 8
    nc = bass.Bass("TRN2", target_bir_lowering=False)
    NQ = nqg * 512
    NS = skc * 128
    io = {
        'y2_seq': nc.dram_tensor("y2_seq", [NS, D], F32, kind="ExternalInput").ap(),
        'y2_own': nc.dram_tensor("y2_own", [NQ, D], F32, kind="ExternalInput").ap(),
        'w_q': nc.dram_tensor("w_q", [D, D], F32, kind="ExternalInput").ap(),
        'w_kv': nc.dram_tensor("w_kv", [D, 2 * D], F32, kind="ExternalInput").ap(),
        'w_out': nc.dram_tensor("w_out", [D, D], F32, kind="ExternalInput").ap(),
        'kcs': nc.dram_tensor("kcs", [128, skc * 64], F32, kind="ExternalInput").ap(),
        'qcs': nc.dram_tensor("qcs", [128, (NQ // 128) * 64], F32, kind="ExternalInput").ap(),
        'cmask': nc.dram_tensor("cmask", [128, nqg * 8 * 128], BF16, kind="ExternalInput").ap(),
        'identm': nc.dram_tensor("identm", [128, 128], BF16, kind="ExternalInput").ap(),
        'y3': nc.dram_tensor("y3", [NQ, D], F32, kind="ExternalOutput").ap(),
    }
    with nc.Block() as blk:
        pb = PB(nc)
        emit_l1a(nc, pb, blk, io, nqg, skc)
        pb.emit(blk)
    return nc


def emit_l1a(nc, pb, blk, io, nqg=4, skc=None, tag='', barrier=True, mid_hook=None, bg=None):
    if skc is None:
        skc = nqg * 8
    NQ = nqg * 512
    NS = skc * 128
    (y2_seq, y2_own, w_q, w_kv, w_out, kcs, qcs, cmask, ident_in, y3) = (
        io['y2_seq'], io['y2_own'], io['w_q'], io['w_kv'], io['w_out'],
        io['kcs'], io['qcs'], io['cmask'], io['identm'], io['y3'])
    qk_dram = nc.dram_tensor("l1qk_scr" + tag, [NS, D], BF16).ap()
    q_dram = nc.dram_tensor("q_scr" + tag, [NQ, D], BF16).ap()
    o_dram = nc.dram_tensor("o_scr" + tag, [D, NQ], BF16).ap()

    glob = Alloc(nc)
    ident = glob.sb("ident", [128, 128], BF16)
    onesrow = glob.sb("onesrow", [1, 128], BF16)
    onesf = glob.sb("onesf", [1, 64], F32)
    consts = glob.sb("consts", [128, 4], F32)
    v_aug = glob.sb("v_aug", [128, skc * 16 * 65], BF16)
    kT = glob.sb("kT", [128, 8 * NS], BF16)
    if True:
        t_ones = pb.op("gpsimd", lambda g: g.memset(onesrow[:], 1.0))
        t_onesf = pb.op("gpsimd", lambda g: g.memset(onesf[:], 1.0))
        pb.op("gpsimd", lambda g: g.memset(consts[:, 0:1], LN_EPS))
        t_lne = pb.op("gpsimd", lambda g: g.memset(consts[:, 2:3], -0.5))
        t_off = pb.op("gpsimd", lambda g: g.memset(consts[:, 1:2], -SM_OFF))
        t_id = pb.op("gpsimd", lambda g: g.dma_start(out=ident[:], in_=ident_in[:]), dma=True)

        ph = Alloc(nc)
        wkv = ph.sb("wkv", [128, 8 * 2 * D], BF16)
        css = ph.sb("css", [128, 2 * 64], F32)
        yc = ph.sb("yc", [128, 2 * D], F32)
        lnc = ph.sb("lnc", [128, 2 * D], BF16)
        lnT = ph.sb("lnT", [128, 2 * 8 * 128], BF16)
        sqs = ph.sb("sqs", [128, D], F32)
        S8 = ph.sb("S8", [128, 16], F32)
        kf = ph.sb("kf", [128, 2 * D], F32)
        tmp = ph.sb("tmp", [128, 512], F32)
        kb = ph.sb("kb", [128, 2 * D], BF16)
        pT = [ph.ps("pT0", [128, 1024], BF16), ph.ps("pT1", [128, 1024], BF16)]
        pkv = ph.ps("pkv", [128, 2048], F32)
        pq = ph.ps("pq", [128, 1024], F32)

        t_wkv = [pb.op("gpsimd", lambda g, j=j: g.dma_start(
            out=wkv[:, j * 2048:(j + 1) * 2048],
            in_=w_kv[j * 128:(j + 1) * 128, :]), dma=True) for j in range(8)]
        t_wq = []
        t_css = [None, None]

        rope_chain = [None]

        def rope_inplace(pb_, X, CS, c, waits):
            X3 = X.rearrange("p (h e) -> p h e", e=64)
            C = CS[:, 0:32]
            Sn = CS[:, 32:64]

            def b(col0, n):
                return lambda q: q
            t1 = pb_.op("vector", lambda v: v.tensor_tensor(
                tmp[:, 0:256].rearrange("p (h e) -> p h e", e=16),
                X3[:, :, 16:32],
                Sn[:, 0:16].unsqueeze(1).broadcast_to([128, 16, 16]), ALU.mult),
                waits=list(waits) + [rope_chain[0]])
            t2 = pb_.op("vector", lambda v: v.tensor_tensor(
                tmp[:, 256:512].rearrange("p (h e) -> p h e", e=16),
                X3[:, :, 0:16],
                Sn[:, 16:32].unsqueeze(1).broadcast_to([128, 16, 16]), ALU.mult),
                waits=waits)
            t3 = pb_.op("vector", lambda v: v.tensor_tensor(
                X3[:, :, 0:16], X3[:, :, 0:16],
                C[:, 0:16].unsqueeze(1).broadcast_to([128, 16, 16]), ALU.mult),
                waits=[t1, t2])
            t4 = pb_.op("vector", lambda v: v.tensor_tensor(
                X3[:, :, 16:32], X3[:, :, 16:32],
                C[:, 16:32].unsqueeze(1).broadcast_to([128, 16, 16]), ALU.mult),
                waits=[t3])
            t5 = pb_.op("vector", lambda v: v.tensor_tensor(
                X3[:, :, 0:16], X3[:, :, 0:16],
                tmp[:, 0:256].rearrange("p (h e) -> p h e", e=16), ALU.subtract),
                waits=[t4])
            t6 = pb_.op("vector", lambda v: v.tensor_tensor(
                X3[:, :, 16:32], X3[:, :, 16:32],
                tmp[:, 256:512].rearrange("p (h e) -> p h e", e=16), ALU.add),
                waits=[t5])
            rope_chain[0] = t6
            return t6

        t_kst = []
        t_qst = []
        NCH = skc + NQ // 128
        SAB = ph.sb("SAB", [128, 4 * (NCH)], F32)
        SA, SB = SAB[:, 0:NCH], SAB[:, NCH:2 * NCH]
        MEAN, RSTD = SAB[:, 2 * NCH:3 * NCH], SAB[:, 3 * NCH:4 * NCH]
        t_sfree = [None, None]
        GS = 8
        t_gstats = {}
        for i in range(NCH):
            p = i % 2
            ys = yc[:, p * D:(p + 1) * D]
            src_ = y2_seq[i * 128:(i + 1) * 128, :] if i < skc else \
                y2_own[(i - skc) * 128:(i - skc + 1) * 128, :]
            t_y = pb.op("sync", lambda s, src_=src_, ys=ys: s.dma_start(out=ys, in_=src_),
                        waits=[t_sfree[p]], dma=True)
            t_sfree[p] = emit_sums(pb, ys, SA[:, i:i + 1], SB[:, i:i + 1], sqs,
                                   [t_y, t_sfree[1 - p]])
            if i % GS == GS - 1 or i == NCH - 1:
                g = i // GS
                lo, hi = g * GS, i + 1
                t_gstats[g] = emit_stats_batch(
                    pb, SA[:, lo:hi], SB[:, lo:hi], MEAN[:, lo:hi], RSTD[:, lo:hi],
                    consts[:, 0:1], consts[:, 2:3],
                    [t_sfree[0], t_sfree[1], t_lne])
        t_yfree = None
        t_lnTfree = [None, None]
        t_lncfree = [None, None]
        t_kffree = [None, None]
        t_kbfree = [None, None]
        t_pkvfree = None
        t_pqfree = [None, None]
        t_vone = None
        for i in range(NCH):
            isq = i < skc
            io = i - skc
            p = i % 2
            ys = yc[:, p * D:(p + 1) * D]
            lns = lnc[:, p * D:(p + 1) * D]
            src_ = y2_seq[i * 128:(i + 1) * 128, :] if isq else \
                y2_own[io * 128:(io + 1) * 128, :]
            t_y = pb.op("sync", lambda s, src_=src_, ys=ys: s.dma_start(out=ys, in_=src_),
                        waits=[t_yfree] if i >= 2 else [t_sfree[0], t_sfree[1]], dma=True)
            t_ln = pb.op("vector", lambda v, ys=ys, lns=lns, i=i: v.tensor_scalar(
                lns, ys, MEAN[:, i:i + 1], RSTD[:, i:i + 1], ALU.subtract, ALU.mult),
                waits=[t_y, t_gstats[i // GS], t_lncfree[p]])
            t_yfree = t_ln
            t_lnTs = []
            t_tp = None
            for j in range(8):
                pj = (i * 8 + j) % 2
                t_tp = pb.op("tensor", lambda t, lns=lns, j=j, pj=pj: t.transpose(
                    pT[pj][:, 0:128], lns[:, j * 128:(j + 1) * 128], ident[:]),
                    waits=[t_ln, t_id, t_lnTfree[pj]])
                t_ev = pb.op("vector", lambda v, j=j, p=p, pj=pj: v.tensor_copy(
                    lnT[:, (p * 8 + j) * 128:(p * 8 + j + 1) * 128], pT[pj][:, 0:128]),
                    waits=[t_tp])
                t_lnTfree[pj] = t_ev
                t_lnTs.append(t_ev)
            t_lncfree[p] = t_tp
            if isq:
                def kv_mms(t, p=p):
                    pass
                def kv_mms(t, p=p):
                    for piece in range(4):
                        for j in range(8):
                            last = t.matmul(
                                pkv[:, piece * 512:(piece + 1) * 512],
                                lnT[:, (p * 8 + j) * 128:(p * 8 + j + 1) * 128],
                                wkv[:, j * 2048 + piece * 512:j * 2048 + (piece + 1) * 512],
                                start=(j == 0), stop=(j == 7))
                    return last
                t_kv = pb.op("tensor", kv_mms, waits=t_lnTs + t_wkv + [t_pkvfree])
                if i == skc - 1:
                    t_kv_last = t_kv
                t_vev = pb.op("vector", lambda v, i=i: v.tensor_copy(
                    v_aug[:, i * 1040:(i + 1) * 1040].rearrange("p (h e) -> p h e", e=65)[:, :, 0:64],
                    pkv[:, 1024:2048].rearrange("p (h e) -> p h e", e=64)), waits=[t_kv])
                t_vone = pb.op("vector", lambda v, i=i: v.memset(
                    v_aug[:, i * 1040 + 64:(i + 1) * 1040:65], 1.0), waits=[t_vev])
                t_kev = pb.op("vector", lambda v, p=p: v.tensor_copy(
                    kf[:, p * D:(p + 1) * D], pkv[:, 0:1024]),
                    waits=[t_kv, t_kffree[p]])
                t_pkvfree = t_vev if False else t_kev
                t_cl = pb.op("scalar", lambda g, i=i, p=p: g.dma_start(
                    out=css[:, p * 64:(p + 1) * 64], in_=kcs[:, i * 64:(i + 1) * 64]),
                    waits=[t_css[p]], dma=True)
                t_rope = rope_inplace(pb, kf[:, p * D:(p + 1) * D],
                                      css[:, p * 64:(p + 1) * 64], i, [t_kev, t_cl])
                t_css[p] = t_rope
                t_kb = pb.op("vector", lambda v, p=p: v.tensor_copy(
                    kb[:, p * D:(p + 1) * D], kf[:, p * D:(p + 1) * D]),
                    waits=[t_rope, t_kbfree[p]])
                t_kffree[p] = t_kb
                t_st = pb.op("sync", lambda s, i=i, p=p: s.dma_start(
                    out=qk_dram[i * 128:(i + 1) * 128, 0:1024], in_=kb[:, p * D:(p + 1) * D]),
                    waits=[t_kb], dma=True)
                t_kbfree[p] = t_st
                t_kst.append(t_st)
            else:
                if io == 0:
                    t_wq = [pb.op("gpsimd", lambda g, j=j: g.dma_start(
                        out=wkv[:, j * 2048:j * 2048 + 1024],
                        in_=w_q[j * 128:(j + 1) * 128, :]),
                        waits=[t_kv_last], dma=True) for j in range(8)]

                def q_mms(t, p=p):
                    for piece in range(2):
                        for j in range(8):
                            last = t.matmul(
                                pq[:, piece * 512:(piece + 1) * 512],
                                lnT[:, (p * 8 + j) * 128:(p * 8 + j + 1) * 128],
                                wkv[:, j * 2048 + piece * 512:j * 2048 + (piece + 1) * 512],
                                start=(j == 0), stop=(j == 7))
                    return last
                t_q = pb.op("tensor", q_mms, waits=t_lnTs + t_wq + [t_pqfree[0]])
                t_qev = pb.op("vector", lambda v, p=p: v.tensor_scalar_mul(
                    kf[:, p * D:(p + 1) * D], pq[:], 0.125),
                    waits=[t_q, t_kffree[p]])
                t_pqfree[0] = t_qev
                t_cl = pb.op("scalar", lambda g, io=io, p=p: g.dma_start(
                    out=css[:, p * 64:(p + 1) * 64], in_=qcs[:, io * 64:(io + 1) * 64]),
                    waits=[t_css[p]], dma=True)
                t_rope = rope_inplace(pb, kf[:, p * D:(p + 1) * D],
                                      css[:, p * 64:(p + 1) * 64], io, [t_qev, t_cl])
                t_css[p] = t_rope
                t_kb = pb.op("vector", lambda v, p=p: v.tensor_copy(
                    kb[:, p * D:(p + 1) * D], kf[:, p * D:(p + 1) * D]),
                    waits=[t_rope, t_kbfree[p]])
                t_kffree[p] = t_kb
                t_st = pb.op("sync", lambda s, io=io, p=p: s.dma_start(
                    out=q_dram[io * 128:(io + 1) * 128, :], in_=kb[:, p * D:(p + 1) * D]),
                    waits=[t_kb], dma=True)
                t_kbfree[p] = t_st
                t_qst.append(t_st)
        t_kT = [pb.op("sync", lambda s, j=j: s.dma_start(
            out=kT[:, j * NS:(j + 1) * NS],
            in_=qk_dram[:, j * 128:(j + 1) * 128], transpose=True),
            waits=t_kst, dma=True) for j in range(8)]
        pb.barrier()
        ph.close()
        if mid_hook is not None:
            tails = [pb.streams[e][-1][2] for e in ENGINES if pb.streams[e]]
            mid_hook(tails)

        ph3 = Alloc(nc)
        qTg = ph3.sb("qTg", [128, 2 * 8 * 512], BF16)
        mk = ph3.sb("mk", [128, 2 * 8 * 128], BF16)
        expt = ph3.sb("expt", [128, 2 * 1024], BF16)
        rl = ph3.sb("rl", [1, 3 * 512], F32)
        rlbc = ph3.sb("rlbc", [64, 3 * 512], F32)
        oT = ph3.sb("oT", [128, 3 * 512], BF16)
        psc = ph3.ps("psc", [128, 2 * 1024], F32)
        pav = ph3.ps("pav", [65, 2 * 512], F32)
        pbc = ph3.ps("pbc", [64, 512], F32)

        t_qT = [[None] * 8 for _ in range(2)]
        t_mkl = [None, None]
        t_qTfree = [[None] * 8 for _ in range(2)]
        t_mkfree = [None, None]
        t_pscfree = [None, None]
        t_expfree = [None, None]
        t_pavfree = [None, None]
        pb.pair_ctr = 0
        t_rlf = None
        t_bcf = None
        t_oTfree = [None, None]
        t_ost = []
        for qg in range(nqg):
            gp = qg % 2
            t_qT[gp] = [pb.op("sync", lambda s, j=j, qg=qg, gp=gp: s.dma_start(
                out=qTg[:, (gp * 8 + j) * 512:(gp * 8 + j + 1) * 512],
                in_=q_dram[qg * 512:(qg + 1) * 512, j * 128:(j + 1) * 128], transpose=True),
                waits=t_qst + [t_qTfree[gp][j]], dma=True) for j in range(8)]
            t_mkl[gp] = pb.op("scalar", lambda g, qg=qg, gp=gp: g.dma_start(
                out=mk[:, gp * 1024:(gp + 1) * 1024],
                in_=cmask[:, qg * 1024:(qg + 1) * 1024]),
                waits=[t_mkfree[gp]], dma=True)
            for h in range(16):
                u = qg * 16 + h
                pu = u % 2
                tile, row0 = h // 2, 64 * (h % 2)
                nkc = 8 * qg + 8
                t_avs = []
                for k0 in range(0, nkc, 2):
                    pr = (pb.pair_ctr) % 2
                    pb.pair_ctr += 1
                    kcs_ = [kc for kc in (k0, k0 + 1) if kc < nkc]
                    t_mms = []
                    cols = []
                    for ii, kc in enumerate(kcs_):
                        thresh = max(0, min(4, ceil_div(kc - 1 - 8 * qg, 2)))
                        ncol = (4 - thresh) * 128
                        col0 = ii * 512 + thresh * 128
                        cols.append((kc, col0, ncol, thresh))
                        t_sc = pb.op("tensor", lambda t, tile=tile, row0=row0, kc=kc,
                                     qg=qg, gp=gp, pr=pr, col0=col0, ncol=ncol,
                                     ii=ii: t.matmul(
                                         psc[:, pr * 1024 + col0:pr * 1024 + col0 + ncol],
                                         kT[row0:row0 + 64, tile * NS + kc * 128:
                                            tile * NS + (kc + 1) * 128],
                                         qTg[row0:row0 + 64,
                                             (gp * 8 + tile) * 512 + col0 - ii * 512:
                                             (gp * 8 + tile) * 512 + col0 - ii * 512 + ncol],
                                         start=True, stop=True),
                                     waits=t_kT + [t_qT[gp][tile], t_pscfree[pr]])
                        t_mms.append(t_sc)
                    spans = []
                    for (kc, col0, ncol, thresh) in cols:
                        if spans and spans[-1][1] == col0:
                            spans[-1][1] = col0 + ncol
                        else:
                            spans.append([col0, col0 + ncol])
                    t_exp = None
                    for (e0, e1) in spans:
                        t_exp = pb.op("scalar", lambda s, pr=pr, e0=e0, e1=e1: s.activation(
                            expt[:, pr * 1024 + e0:pr * 1024 + e1],
                            psc[:, pr * 1024 + e0:pr * 1024 + e1],
                            AF.Exp, bias=consts[:, 1:2]),
                            waits=t_mms + [t_off, t_expfree[pr]])
                    t_pscfree[pr] = t_exp
                    t_last = t_exp
                    for (kc, col0, ncol, thresh) in cols:
                        if kc >= 8 * qg:
                            t_m = pb.op("vector", lambda v, pr=pr, gp=gp, kc=kc, qg=qg,
                                        col0=col0: v.tensor_tensor(
                                            expt[:, pr * 1024 + col0:pr * 1024 + col0 + 128],
                                            expt[:, pr * 1024 + col0:pr * 1024 + col0 + 128],
                                            mk[:, gp * 1024 + (kc - 8 * qg) * 128:
                                               gp * 1024 + (kc - 8 * qg) * 128 + 128],
                                            ALU.mult),
                                        waits=[t_exp, t_mkl[gp]])
                        else:
                            t_m = t_exp
                        t_av = pb.op("tensor", lambda t, kc=kc, h=h, pr=pr, pu=pu,
                                     col0=col0, ncol=ncol: t.matmul(
                                         pav[:, pu * 512 + col0 % 512:
                                             pu * 512 + col0 % 512 + ncol],
                                         v_aug[:, kc * 1040 + h * 65:kc * 1040 + (h + 1) * 65],
                                         expt[:, pr * 1024 + col0:pr * 1024 + col0 + ncol],
                                         start=(kc == 0), stop=(kc == nkc - 1)),
                                     waits=[t_m, t_vone, t_pavfree[pu]] if kc == 0 else [t_m])
                        t_last = t_av
                        t_avs.append(t_av)
                    t_expfree[pr] = t_last
                t_rl = pb.op("scalar", lambda s, pu=pu: s.activation(
                    rl[0:1, pu * 512:(pu + 1) * 512], pav[64:65, pu * 512:(pu + 1) * 512],
                    AF.Ln), waits=[t_avs[-1], t_rlf])
                t_rl = pb.op("scalar", lambda s, pu=pu: s.activation(
                    rl[0:1, pu * 512:(pu + 1) * 512], rl[0:1, pu * 512:(pu + 1) * 512],
                    AF.Exp, scale=-1.0), waits=[t_rl])
                t_bc = pb.op("tensor", lambda t, pu=pu: t.matmul(
                    pbc[:], onesf[0:1, 0:64], rl[0:1, pu * 512:(pu + 1) * 512],
                    start=True, stop=True), waits=[t_rl, t_bcf, t_onesf])
                t_rlf = t_bc
                t_bcev = pb.op("vector", lambda v, pu=pu: v.tensor_copy(
                    rlbc[:, pu * 512:(pu + 1) * 512], pbc[:]), waits=[t_bc])
                t_bcf = t_bcev
                t_fin = pb.op("vector", lambda v, pu=pu: v.scalar_tensor_tensor(
                    oT[row0:row0 + 64, pu * 512:(pu + 1) * 512] if False else
                    oT[0:64, pu * 512:(pu + 1) * 512],
                    pav[0:64, pu * 512:(pu + 1) * 512], 1.0,
                    rlbc[:, pu * 512:(pu + 1) * 512], ALU.mult, ALU.mult),
                    waits=[t_bcev, t_oTfree[pu]])
                t_pavfree[pu] = t_fin
                t_o = pb.op("sync", lambda s, h=h, qg=qg, pu=pu: s.dma_start(
                    out=o_dram[h * 64:(h + 1) * 64, qg * 512:(qg + 1) * 512],
                    in_=oT[0:64, pu * 512:(pu + 1) * 512]), waits=[t_fin], dma=True)
                t_oTfree[pu] = t_o
                t_ost.append(t_o)
                if bg is not None:
                    next(bg, None)
            for j in range(8):
                t_qTfree[gp][j] = t_avs[-1]
            t_mkfree[gp] = t_avs[-1]
        pb.barrier()
        ph3.close()

        ph4 = Alloc(nc)
        wos = ph4.sb("wos", [128, 8 * D], BF16)
        oTl = ph4.sb("oTl", [128, 2 * 8 * 128], BF16)
        yre = ph4.sb("yre", [128, 2 * D], F32)
        y3c = ph4.sb("y3c", [128, 2 * D], F32)
        pproj = ph4.ps("pproj", [128, 1024], F32)
        t_wo = [pb.op("gpsimd", lambda g, j=j: g.dma_start(
            out=wos[:, j * D:(j + 1) * D], in_=w_out[j * 128:(j + 1) * 128, :]),
            dma=True) for j in range(8)]
        t_y3 = [None, None]
        t_oTl = [None, None]
        t_pprojfree = None
        for qc in range(NQ // 128):
            p = qc % 2
            t_ol = [pb.op("scalar", lambda g, j=j, qc=qc, p=p: g.dma_start(
                out=oTl[:, (p * 8 + j) * 128:(p * 8 + j + 1) * 128],
                in_=o_dram[j * 128:(j + 1) * 128, qc * 128:(qc + 1) * 128]),
                waits=t_ost + [t_oTl[p]], dma=True) for j in range(8)]
            t_xr = pb.op("sync", lambda s, qc=qc, p=p: s.dma_start(
                out=yre[:, p * D:(p + 1) * D], in_=y2_own[qc * 128:(qc + 1) * 128, :]),
                waits=[t_y3[p]], dma=True)

            def proj_mms(t, p=p):
                for piece in range(2):
                    for j in range(8):
                        last = t.matmul(pproj[:, piece * 512:(piece + 1) * 512],
                                        oTl[:, (p * 8 + j) * 128:(p * 8 + j + 1) * 128],
                                        wos[:, j * D + piece * 512:j * D + (piece + 1) * 512],
                                        start=(j == 0), stop=(j == 7))
                return last
            t_proj = pb.op("tensor", proj_mms, waits=t_ol + t_wo + [t_pprojfree])
            t_y3c = pb.op("vector", lambda v, p=p: v.scalar_tensor_tensor(
                y3c[:, p * D:(p + 1) * D], pproj[:], 1.0,
                yre[:, p * D:(p + 1) * D], ALU.mult, ALU.add), waits=[t_proj, t_xr])
            t_pprojfree = t_y3c
            t_st = pb.op("sync", lambda s, qc=qc, p=p: s.dma_start(
                out=y3[qc * 128:(qc + 1) * 128, :], in_=y3c[:, p * D:(p + 1) * D]),
                waits=[t_y3c], dma=True)
            t_y3[p] = t_st
            t_oTl[p] = t_proj
        pb.op("sync", lambda s: s.nop(), waits=[t_y3[0], t_y3[1]])
        if barrier:
            pb.barrier()
        ph4.close()
    glob.close()


def l1a_rope_tables(positions):
    ROT = 32
    inv = 1.0 / (10000.0 ** (np.arange(0, ROT, 2) / ROT))
    f = positions[:, None] * inv[None, :]
    fr = np.concatenate([f, f], axis=-1)
    cs = np.concatenate([np.cos(fr), np.sin(fr)], axis=-1)
    n = len(positions) // 128
    return f32(cs.reshape(n, 128, 64).transpose(1, 0, 2).reshape(128, n * 64))


def l1a_masks(h, nqg=4):
    out = np.zeros((nqg, 8, 128, 128), dtype=np.float32)
    for qg in range(nqg):
        for dk in range(8):
            kc = 8 * qg + dk
            thresh = max(0, min(4, ceil_div(kc - 1 - 8 * qg, 2)))
            j0 = thresh
            gqc = 8 * qg + 2 * j0 + h
            kpos = kc * 128 + np.arange(128)[:, None]
            qpos = gqc * 128 + np.arange(128)[None, :]
            out[qg, dk] = (kpos <= qpos).astype(np.float32)
            for j in range(thresh + 1, 4):
                gq = 8 * qg + 2 * j + h
                assert kc * 128 + 127 <= gq * 128, (qg, kc, j, h)
    return bf(out.reshape(nqg * 8, 128, 128).transpose(1, 0, 2).reshape(128, nqg * 8 * 128))


def own_rows(h, noc=16):
    idx = []
    for o in range(noc):
        g = 2 * o + h
        idx.extend(range(g * 128, (g + 1) * 128))
    return np.array(idx)



def emit_quant(pb, eng, X, W, scr, waits, out=None):
    NB = W // 16
    bm, sc, rsc = scr['bm'][:, 0:NB], scr['bm'][:, NB:2 * NB], scr['bm'][:, 2 * NB:3 * NB]
    ax, idx = scr['ax'][:, 0:W], scr['idx'][:, 0:W]
    s2 = scr['s2'][:, 0:W]
    dst = out if out is not None else X
    dst3 = dst.rearrange("p (n b) -> p n b", b=16)
    ax3 = ax.rearrange("p (n b) -> p n b", b=16)
    idx3 = idx.rearrange("p (n b) -> p n b", b=16)
    mids = [0.25, 0.75, 1.25, 1.75, 2.5, 3.5, 5.0]
    ea = pb.op("scalar", lambda s: s.activation(ax, X, AF.Abs), waits=waits)
    es = pb.op("scalar", lambda s: s.activation(s2, X, AF.Sign), waits=waits)
    e = pb.op("vector", lambda v: v.tensor_reduce(bm, ax3, AX.X, ALU.max), waits=[ea])
    e = pb.op("vector", lambda v: v.tensor_scalar(sc, bm, 1.0 / 6.0, 1e-12,
                                                  ALU.mult, ALU.max), waits=[e])
    er = pb.op("vector", lambda v: v.reciprocal(rsc, sc), waits=[e])
    e = pb.op("vector", lambda v: v.tensor_tensor(
        ax3, ax3, rsc.unsqueeze(2).broadcast_to([128, NB, 16]), ALU.mult), waits=[er])
    e = pb.op("vector", lambda v: v.tensor_scalar(idx, ax, mids[0], None, ALU.is_gt),
              waits=[e])
    for m in mids[1:]:
        e = pb.op("vector", lambda v, m=m: v.scalar_tensor_tensor(
            idx, ax, m, idx, ALU.is_gt, ALU.add), waits=[e])
    e1 = pb.op("scalar", lambda s: s.activation(ax, idx, AF.Relu, bias=scr['c4']),
               waits=[e, scr['ctok']])
    e3 = pb.op("vector", lambda v: v.tensor_scalar_mul(idx, idx, 0.5), waits=[e1])
    e4 = pb.op("vector", lambda v: v.scalar_tensor_tensor(idx, ax, 0.5, idx,
                                                          ALU.mult, ALU.add), waits=[e3])
    e2 = pb.op("scalar", lambda s: s.activation(ax, ax, AF.Relu, bias=scr['c2']),
               waits=[e4])
    e5 = pb.op("vector", lambda v: v.tensor_tensor(idx, idx, ax, ALU.add), waits=[e2])
    e6 = pb.op("vector", lambda v: v.tensor_tensor(idx, idx, s2, ALU.mult), waits=[e5, es])
    e7 = pb.op("vector", lambda v: v.tensor_tensor(
        dst3, idx3, sc.unsqueeze(2).broadcast_to([128, NB, 16]), ALU.mult), waits=[e6])
    return e7


def build_l1f(ntc=16):
    nc = bass.Bass("TRN2", target_bir_lowering=False)
    NT = ntc * 128
    I = 2816
    io = {
        'y3_in': nc.dram_tensor("y3_in", [NT, D], F32, kind="ExternalInput").ap(),
        'w_gate': nc.dram_tensor("w_gate", [I, D], F32, kind="ExternalInput").ap(),
        'w_up': nc.dram_tensor("w_up", [I, D], F32, kind="ExternalInput").ap(),
        'w_down': nc.dram_tensor("w_down", [D, I], F32, kind="ExternalInput").ap(),
        'identm': nc.dram_tensor("identm", [128, 128], BF16, kind="ExternalInput").ap(),
        'y4': nc.dram_tensor("y4", [NT, D], F32, kind="ExternalOutput").ap(),
    }
    with nc.Block() as blk:
        pb = PB(nc)
        emit_l1f(nc, pb, blk, io, ntc)
        pb.emit(blk)
    return nc


def gen_l1f_weights(nc, pb, io, scr, wq_dram, t_wst, tag='', waits0=()):
    I = 2816
    NI = I // 128
    w_gate, w_up, w_down = io['w_gate'], io['w_up'], io['w_down']
    wc, wb = scr['wc'], scr['wb']
    t_wqchain = None
    t_wcfree = [None, None]
    t_wbfree = [None, None]
    jobs = [(w_gate, 0, NI, 1024, 0), (w_up, I, NI, 1024, 0),
            (w_down, 2 * I, 8, 1408, 0), (w_down, 2 * I, 8, 1408, 1408)]
    for (wsrc, row0, nch, wid, co) in jobs:
        for ch in range(nch):
            p = ch % 2
            WC = wc[:, p * 1408:p * 1408 + wid]
            t_l = pb.op("sync", lambda s, wsrc=wsrc, ch=ch, WC=WC, wid=wid, co=co:
                        s.dma_start(out=WC, in_=wsrc[ch * 128:(ch + 1) * 128, co:co + wid]),
                        waits=[t_wcfree[p]] + list(waits0), dma=True)
            t_q = emit_quant(pb, "vector", WC, wid, scr,
                             [t_l, t_wqchain, t_wbfree[p]],
                             out=wb[:, p * 1408:p * 1408 + wid])
            t_wqchain = t_q
            t_wcfree[p] = t_q
            t_s = pb.op("sync", lambda s, row0=row0, ch=ch, p=p, wid=wid, co=co:
                        s.dma_start(out=wq_dram[row0 + ch * 128:row0 + (ch + 1) * 128,
                                                co:co + wid],
                                    in_=wb[:, p * 1408:p * 1408 + wid]),
                        waits=[t_q], dma=True)
            t_wbfree[p] = t_s
            t_wst.append(t_s)
            yield


def emit_l1f_weights(nc, pb, io, scr, wq_dram, tag='', waits0=()):
    t_wst = []
    for _ in gen_l1f_weights(nc, pb, io, scr, wq_dram, t_wst, tag, waits0):
        pass
    return t_wst


def emit_l1f(nc, pb, blk, io, ntc=16, tag='', t_wst=None, wq_dram=None):
    NT = ntc * 128
    I = 2816
    NI = I // 128
    (y3_in, w_gate, w_up, w_down, ident_in, y4) = (
        io['y3_in'], io['w_gate'], io['w_up'], io['w_down'], io['identm'], io['y4'])
    if wq_dram is None:
        wq_dram = nc.dram_tensor("wq_scr" + tag, [2 * I + D, max(D, I)], BF16).ap()

    glob = Alloc(nc)
    ident = glob.sb("ident", [128, 128], BF16)
    consts = glob.sb("consts", [128, 4], F32)
    wgT = glob.sb("wgT", [128, 8 * I], BF16)
    wuT = glob.sb("wuT", [128, 8 * I], BF16)
    wdT = glob.sb("wdT", [128, NI * D], BF16)
    if True:
        pb.op("gpsimd", lambda g: g.memset(consts[:, 0:1], LN_EPS))
        t_lne = pb.op("gpsimd", lambda g: g.memset(consts[:, 2:3], -0.5))
        t_id = pb.op("gpsimd", lambda g: g.dma_start(out=ident[:], in_=ident_in[:]), dma=True)

        if t_wst is None:
            phw = Alloc(nc)
            scrw = {
                'wc': phw.sb("wc", [128, 2 * 1408], F32),
                'wb': phw.sb("wb", [128, 2 * 1408], BF16),
                'bm': phw.sb("qbm", [128, 3 * 88], F32),
                'ax': phw.sb("qax", [128, 1408], F32),
                'idx': phw.sb("qidx", [128, 1408], F32),
                's2': phw.sb("qs2", [128, 1408], F32),
            }
            cq = phw.sb("qc", [128, 2], F32)
            scrw['c4'] = cq[:, 0:1]
            scrw['c2'] = cq[:, 1:2]
            pb.op("gpsimd", lambda g: g.memset(cq[:, 0:1], -4.0))
            scrw['ctok'] = pb.op("gpsimd", lambda g: g.memset(cq[:, 1:2], -2.0))
            t_wst = emit_l1f_weights(nc, pb, io, scrw, wq_dram, tag)
            pb.barrier()
            phw.close()
        t_wgT = [pb.op("sync", lambda s, j=j: s.dma_start(
            out=wgT[:, j * I:(j + 1) * I],
            in_=wq_dram[0:I, j * 128:(j + 1) * 128], transpose=True),
            waits=t_wst, dma=True) for j in range(8)]
        t_wuT = [pb.op("sync", lambda s, j=j: s.dma_start(
            out=wuT[:, j * I:(j + 1) * I],
            in_=wq_dram[I:2 * I, j * 128:(j + 1) * 128], transpose=True),
            waits=t_wst, dma=True) for j in range(8)]
        t_wdT = [pb.op("sync", lambda s, j=j: s.dma_start(
            out=wdT[:, j * D:(j + 1) * D],
            in_=wq_dram[2 * I:2 * I + D, j * 128:(j + 1) * 128], transpose=True),
            waits=t_wst, dma=True) for j in range(NI)]

        pht = Alloc(nc)
        yc = pht.sb("yc", [128, D], F32)
        sqs = pht.sb("sqs", [128, D], F32)
        S8 = pht.sb("S8", [128, 16], F32)
        xf = pht.sb("xf", [128, D], F32)
        xqT = pht.sb("xqT", [128, 8 * 128], BF16)
        hq = pht.sb("hq", [128, I], F32)
        hqb = pht.sb("hqb", [128, I], BF16)
        qbm2 = pht.sb("qbm2", [128, 3 * (I // 16)], F32)
        qax = pht.sb("qax", [128, I], F32)
        qidx2 = pht.sb("qidx2", [128, I], F32)
        qs2 = pht.sb("qs2", [128, I], F32)
        xbm = pht.sb("xbm", [128, 3 * 64], F32)
        xidx = pht.sb("xidx", [128, D], F32)
        xs2 = pht.sb("xs2", [128, D], F32)
        cq2 = pht.sb("cq2", [128, 2], F32)
        pT = [pht.ps("pT0", [128, 1024], BF16), pht.ps("pT1", [128, 1024], BF16)]
        pgu = pht.ps("pgu", [128, 2048], F32)
        pd = pht.ps("pd", [128, 1024], F32)

        scr2 = {'bm': qbm2, 'ax': qax, 'idx': qidx2, 's2': qs2,
                'c4': cq2[:, 0:1], 'c2': cq2[:, 1:2]}
        pb.op("gpsimd", lambda g: g.memset(cq2[:, 0:1], -4.0))
        scr2['ctok'] = pb.op("gpsimd", lambda g: g.memset(cq2[:, 1:2], -2.0))
        scrx = {'bm': xbm, 'ax': sqs, 'idx': xidx, 's2': xs2,
                'c4': cq2[:, 0:1], 'c2': cq2[:, 1:2], 'ctok': scr2['ctok']}
        segs = [(0, 1024), (1024, 1024), (2048, 768)]
        SABt = pht.sb("SABt", [128, 4 * ntc], F32)
        SAt, SBt = SABt[:, 0:ntc], SABt[:, ntc:2 * ntc]
        MEANt, RSTDt = SABt[:, 2 * ntc:3 * ntc], SABt[:, 3 * ntc:4 * ntc]
        t_sfree = [None, None]
        for i in range(ntc):
            p = 0
            ys = yc[:]
            t_y = pb.op("sync", lambda s, i=i, ys=ys: s.dma_start(
                out=ys, in_=y3_in[i * 128:(i + 1) * 128, :]),
                waits=[t_sfree[0]], dma=True)
            t_sfree[0] = emit_sums(pb, ys, SAt[:, i:i + 1], SBt[:, i:i + 1], sqs,
                                   [t_y, t_sfree[0]])
        t_stats = emit_stats_batch(pb, SAt, SBt, MEANt, RSTDt, consts[:, 0:1],
                                   consts[:, 2:3], [t_sfree[0], t_sfree[1], t_lne])
        t_yfree = None
        t_qchain = None
        t_xchain = None
        t_xqTfree = [None]
        t_hqT_free = None
        t_lnTfree = [None, None]
        t_xffree = [None]
        t_hqbfree = None
        t_hqfree = [None]
        t_pgufree = [None, None]
        t_pdfree = None
        t_y4free = [None, None]
        t_y4st = []
        for i in range(ntc):
            p = i % 2
            ys = yc[:]
            XF = xf[:]
            t_y = pb.op("sync", lambda s, i=i, ys=ys: s.dma_start(
                out=ys, in_=y3_in[i * 128:(i + 1) * 128, :]),
                waits=[t_yfree] if i >= 1 else [t_sfree[0]], dma=True)
            t_ln = pb.op("vector", lambda v, ys=ys, XF=XF, i=i: v.tensor_scalar(
                XF, ys, MEANt[:, i:i + 1], RSTDt[:, i:i + 1], ALU.subtract, ALU.mult),
                waits=[t_y, t_stats, t_xffree[0]])
            t_xq = emit_quant(pb, "vector", XF, D, scrx, [t_ln, t_xchain, t_hqbfree],
                              out=hqb[:, 0:D])
            t_xchain = t_xq
            t_cb = t_xq
            t_xqTs = []
            t_tp = None
            for j in range(8):
                pj = (i * 8 + j) % 2
                t_tp = pb.op("tensor", lambda t, j=j, pj=pj: t.transpose(
                    pT[pj][:, 0:128], hqb[:, j * 128:(j + 1) * 128], ident[:]),
                    waits=[t_cb, t_id, t_lnTfree[pj]])
                t_ev = pb.op("vector", lambda v, j=j, pj=pj: v.tensor_copy(
                    xqT[:, j * 128:(j + 1) * 128], pT[pj][:, 0:128]),
                    waits=[t_tp] + ([t_xqTfree[0]] if j == 0 else []))
                t_lnTfree[pj] = t_ev
                t_xqTs.append(t_ev)
            t_hqbfree = t_tp
            t_hqs = []
            for (s0, sw) in segs:
                sp = (s0 // 1024) % 2

                def gu_mms(t, s0=s0, sw=sw, W_=None, off=0):
                    for half in range(ceil_div(sw, 512)):
                        cw = min(512, sw - half * 512)
                        for j in range(8):
                            last = t.matmul(
                                pgu[:, off + half * 512:off + half * 512 + cw],
                                xqT[:, j * 128:(j + 1) * 128],
                                W_[:, j * I + s0 + half * 512:j * I + s0 + half * 512 + cw],
                                start=(j == 0), stop=(j == 7))
                    return last
                t_g = pb.op("tensor", lambda t, f=gu_mms: f(t, W_=wgT, off=0),
                            waits=t_xqTs + t_wgT + [t_pgufree[0]])
                t_u = pb.op("tensor", lambda t, f=gu_mms: f(t, W_=wuT, off=1024),
                            waits=t_xqTs + t_wuT + [t_pgufree[1]])
                t_xqTfree[0] = t_u
                t_si = pb.op("scalar", lambda s, s0=s0, sw=sw: s.activation(
                    hq[:, s0:s0 + sw], pgu[:, 0:sw], AF.Silu),
                    waits=[t_g, t_hqfree[0]])
                t_pgufree[0] = t_si
                t_h = pb.op("vector", lambda v, s0=s0, sw=sw: v.scalar_tensor_tensor(
                    hq[:, s0:s0 + sw], pgu[:, 1024:1024 + sw], 1.0,
                    hq[:, s0:s0 + sw], ALU.mult, ALU.mult),
                    waits=[t_si, t_u])
                t_pgufree[1] = t_h
                t_hqs.append(t_h)
            t_qh = emit_quant(pb, "vector", hq[:, 0:I], I, scr2,
                              t_hqs + [t_qchain, t_hqbfree], out=hqb[:, 0:I])
            t_qchain = t_qh
            t_cbh = t_qh
            t_hqTs = []
            for j in range(NI):
                pj = j % 2
                t_tp2 = pb.op("tensor", lambda t, j=j, pj=pj: t.transpose(
                    pT[pj][:, 128:256], hqb[:, j * 128:(j + 1) * 128], ident[:]),
                    waits=[t_cbh, t_lnTfree[pj]])
                t_ev2 = pb.op("vector", lambda v, j=j, pj=pj: v.tensor_copy(
                    hq[:].bitcast(BF16)[:, j * 128:(j + 1) * 128],
                    pT[pj][:, 128:256]),
                    waits=[t_tp2] + ([t_hqT_free] if j == 0 else []))
                t_lnTfree[pj] = t_ev2
                t_hqTs.append(t_ev2)
            t_hqbfree = t_tp2
            def d_mms(t):
                for piece in range(2):
                    for j in range(NI):
                        last = t.matmul(
                            pd[:, piece * 512:(piece + 1) * 512],
                            hq[:].bitcast(BF16)[:, j * 128:(j + 1) * 128],
                            wdT[:, j * D + piece * 512:j * D + (piece + 1) * 512],
                            start=(j == 0), stop=(j == NI - 1))
                return last
            t_d = pb.op("tensor", d_mms, waits=t_hqTs + t_wdT + [t_pdfree])
            t_hqT_free = t_d
            t_hqfree[0] = t_d
            t_y4 = pb.op("vector", lambda v, ys=ys: v.scalar_tensor_tensor(
                xf[:], pd[:], 1.0, ys, ALU.mult, ALU.add),
                waits=[t_d, t_y4free[p], t_xffree[0]])
            t_pdfree = t_y4
            t_yfree = t_y4
            t_st = pb.op("sync", lambda s, i=i: s.dma_start(
                out=y4[i * 128:(i + 1) * 128, :], in_=xf[:]),
                waits=[t_y4], dma=True)
            t_y4free[p] = t_st
            t_xffree[0] = t_st
            t_y4st.append(t_st)
        pb.op("sync", lambda s: s.nop(), waits=t_y4st[-2:])
        pb.barrier()
        pht.close()
    glob.close()





def emit_gather_rows(nc, pb, src_dram, dst_dram, ownidx_in, nch):
    al = Alloc(nc)
    idxs = al.sb("gidx", [128, nch], I32)
    buf = al.sb("gbuf", [128, 2 * D], F32)
    t_i = pb.op("gpsimd", lambda g: g.dma_start(out=idxs[:], in_=ownidx_in[:]), dma=True)
    t_pf = [None, None]
    for o in range(nch):
        p = o % 2
        t_g = pb.op("gpsimd", lambda g, o=o, p=p: g.indirect_dma_start(
            buf[:, p * D:(p + 1) * D], None, src_dram[:],
            IndirectOffsetOnAxis(idxs[:, o:o + 1], 0)),
            waits=[t_i, t_pf[p]], dma=True)
        t_s = pb.op("sync", lambda s, o=o, p=p: s.dma_start(
            out=dst_dram[o * 128:(o + 1) * 128, :], in_=buf[:, p * D:(p + 1) * D]),
            waits=[t_g], dma=True)
        t_pf[p] = t_s
    pb.barrier()
    al.close()


def build_fused():
    nc = bass.Bass("TRN2", target_bir_lowering=False)
    NQC = 32
    NKC = NQC + 2
    I = 2816
    ext = lambda n, sh, dt=F32: nc.dram_tensor(n, sh, dt, kind="ExternalInput").ap()
    io = {
        'x_pad': ext("x_pad", [NKC * 128, D]),
        'qkv_wT': ext("qkv_wT", [D, 3 * D]),
        'qkv_b': ext("qkv_b", [1, 3 * D]),
        'out_wT': ext("out_wT", [D, D]),
        'out_b': ext("out_b", [1, D]),
        'band_mask': ext("band_mask", [128, NQC * 3 * 128], BF16),
        'identm': ext("identm", [128, 128], BF16),
        'g_bc': ext("g_bc", [128, D]),
        'b_bc': ext("b_bc", [128, D]),
        'rW': ext("rW", [128, 8 * 8]),
        'rb': ext("rb", [1, 8]),
        'W1': ext("W1", [8, D, 2048]),
        'b1t': ext("b1t", [128, 8 * 16]),
        'W2': ext("W2", [8, 2048, D]),
        'b2r': ext("b2r", [1, 8 * D]),
        'w_q': ext("w_q", [D, D]),
        'w_kv': ext("w_kv", [D, 2 * D]),
        'w_out': ext("w_out", [D, D]),
        'kcs': ext("kcs", [128, 32 * 64]),
        'qcs': ext("qcs", [128, 16 * 64]),
        'cmask': ext("cmask", [128, 4 * 8 * 128], BF16),
        'ownidx': ext("ownidx", [128, 16], I32),
        'w_gate': ext("w_gate", [I, D]),
        'w_up': ext("w_up", [I, D]),
        'w_down': ext("w_down", [D, I]),
        'y4': nc.dram_tensor("y4", [2048, D], F32, kind="ExternalOutput").ap(),
        'y1': nc.dram_tensor("y1_scr", [4096, D], F32).ap(),
        'y2': nc.dram_tensor("y2_scr", [4096, D], F32).ap(),
        'y2own': nc.dram_tensor("y2own_scr", [2048, D], F32).ap(),
        'y3': nc.dram_tensor("y3_scr", [2048, D], F32).ap(),
    }
    with nc.Block() as blk:
        pb = PB(nc)
        emit_l0a(nc, pb, blk, io, NQC)
        io2 = dict(io)
        io2['y1_in'] = io['y1']
        emit_l0f(nc, pb, blk, io2, NQC, npass=8)
        emit_gather_rows(nc, pb, io['y2'], io['y2own'], io['ownidx'], 16)
        I = 2816
        wq_dram = nc.dram_tensor("wq_scr", [2 * I + D, max(D, I)], BF16).ap()
        phw = Alloc(nc)
        scrw = {}
        box = {}

        def mid_hook(tails):
            scrw['wc'] = phw.sb("wc", [128, 2 * 1408], F32, side='right')
            scrw['wb'] = phw.sb("wb", [128, 2 * 1408], BF16, side='right')
            scrw['bm'] = phw.sb("qbm", [128, 3 * 88], F32, side='right')
            scrw['ax'] = phw.sb("qax", [128, 1408], F32, side='right')
            scrw['idx'] = phw.sb("qidx", [128, 1408], F32, side='right')
            scrw['s2'] = phw.sb("qs2", [128, 1408], F32, side='right')
            cq = phw.sb("qc", [128, 2], F32, side='right')
            scrw['c4'] = cq[:, 0:1]
            scrw['c2'] = cq[:, 1:2]
            t1 = pb.op("gpsimd", lambda g: g.memset(cq[:, 0:1], -4.0))
            scrw['ctok'] = pb.op("gpsimd", lambda g: g.memset(cq[:, 1:2], -2.0))
            box['tails'] = tails
        io3 = dict(io)
        io3['y2_seq'] = io['y2']
        io3['y2_own'] = io['y2own']
        io4 = dict(io)
        io4['y3_in'] = io['y3']
        t_wst = []
        bgen = gen_l1f_weights(nc, pb, io4, scrw, wq_dram, t_wst)
        emit_l1a(nc, pb, blk, io3, nqg=4, skc=32, barrier=False, mid_hook=mid_hook,
                 bg=bgen)
        for _ in bgen:
            pass
        pb.barrier()
        phw.close()
        emit_l1f(nc, pb, blk, io4, 16, t_wst=t_wst, wq_dram=wq_dram)
        for e in ENGINES:
            n = len(pb.streams[e])
            print(f"fused stream {e}: {n} ops")
        pb.emit(blk)
    return nc


_programs = {}


def _get_prog(name):
    if name not in _programs:
        if name == 'l0a':
            _programs[name] = build_l0a(16)
        elif name == 'l0f':
            _programs[name] = build_l0f(16, 2)
        elif name == 'l1a':
            _programs[name] = build_l1a(4)
        elif name == 'l1f':
            _programs[name] = build_l1f(16)
    return _programs[name]


_IDENT = None


def kernel_unfused(L0_attn_x, L0_attn_qkv_w, L0_attn_qkv_b, L0_attn_out_w, L0_attn_out_b,
           L0_ffn_ln_g, L0_ffn_ln_b, L0_ffn_rW, L0_ffn_rb, L0_ffn_W1, L0_ffn_b1,
           L0_ffn_W2, L0_ffn_b2, L1_attn_w_q, L1_attn_w_kv, L1_attn_w_out,
           L1_ffn_w_gate, L1_ffn_w_up, L1_ffn_w_down):
    global _IDENT
    if _IDENT is None:
        _IDENT = bf(np.eye(128, dtype=np.float32))
    x = f32(L0_attn_x)
    B, N, Dm = x.shape

    nc = _get_prog('l0a')
    qkv_wT = f32(L0_attn_qkv_w.T)
    out_wT = f32(L0_attn_out_w.T)
    qkv_b = f32(np.asarray(L0_attn_qkv_b)[None, :])
    out_b = f32(np.asarray(L0_attn_out_b)[None, :])
    in_maps = []
    for core in range(8):
        b, h = core // 2, core % 2
        d = l0a_inputs(x, core)
        d.update({"qkv_wT": qkv_wT, "qkv_b": qkv_b, "out_wT": out_wT, "out_b": out_b,
                  "band_mask": l0a_masks(16, h * 2048), "identm": _IDENT})
        in_maps.append(d)
    r1 = run_bass_kernel_spmd(nc, in_maps, core_ids=list(range(8)))

    nc = _get_prog('l0f')
    g_bc = f32(np.broadcast_to(np.asarray(L0_ffn_ln_g), (128, Dm)))
    b_bc = f32(np.broadcast_to(np.asarray(L0_ffn_ln_b), (128, Dm)))
    rW = f32(np.asarray(L0_ffn_rW).reshape(8, 128, 8).transpose(1, 0, 2).reshape(128, 64))
    rb = f32(np.asarray(L0_ffn_rb)[None, :])
    W1 = f32(L0_ffn_W1)
    b1t = f32(np.asarray(L0_ffn_b1).reshape(8, 16, 128).transpose(2, 0, 1).reshape(128, 128))
    W2 = f32(L0_ffn_W2)
    b2r = f32(np.asarray(L0_ffn_b2).reshape(1, 8 * 1024))
    in_maps = []
    for core in range(8):
        in_maps.append({"y1_in": r1.results[core]["y1"], "g_bc": g_bc, "b_bc": b_bc,
                        "rW": rW, "rb": rb, "W1": W1, "b1t": b1t, "W2": W2, "b2r": b2r,
                        "identm": _IDENT})
    r2 = run_bass_kernel_spmd(nc, in_maps, core_ids=list(range(8)))

    y2 = np.empty((B, N, Dm), dtype=np.float32)
    for core in range(8):
        b, h = core // 2, core % 2
        y2[b, h * 2048:(h + 1) * 2048] = r2.results[core]["y2"]

    nc = _get_prog('l1a')
    w_q = f32(L1_attn_w_q)
    w_kv = f32(L1_attn_w_kv)
    w_out = f32(L1_attn_w_out)
    kcs = l1a_rope_tables(np.arange(N).astype(np.float64))
    qcs_h = [l1a_rope_tables(own_rows(h).astype(np.float64)) for h in (0, 1)]
    msk_h = [l1a_masks(h) for h in (0, 1)]
    in_maps = []
    for core in range(8):
        b, h = core // 2, core % 2
        in_maps.append({"y2_seq": y2[b], "y2_own": f32(y2[b][own_rows(h)]),
                        "w_q": w_q, "w_kv": w_kv, "w_out": w_out, "kcs": kcs,
                        "qcs": qcs_h[h], "cmask": msk_h[h], "identm": _IDENT})
    r3 = run_bass_kernel_spmd(nc, in_maps, core_ids=list(range(8)))

    nc = _get_prog('l1f')
    wg = f32(L1_ffn_w_gate)
    wu = f32(L1_ffn_w_up)
    wd = f32(L1_ffn_w_down)
    in_maps = []
    for core in range(8):
        in_maps.append({"y3_in": r3.results[core]["y3"], "w_gate": wg, "w_up": wu,
                        "w_down": wd, "identm": _IDENT})
    r4 = run_bass_kernel_spmd(nc, in_maps, core_ids=list(range(8)))

    out = np.empty((B, N, Dm), dtype=np.float32)
    for core in range(8):
        b, h = core // 2, core % 2
        out[b][own_rows(h)] = r4.results[core]["y4"]
    return out


def kernel(L0_attn_x, L0_attn_qkv_w, L0_attn_qkv_b, L0_attn_out_w, L0_attn_out_b,
           L0_ffn_ln_g, L0_ffn_ln_b, L0_ffn_rW, L0_ffn_rb, L0_ffn_W1, L0_ffn_b1,
           L0_ffn_W2, L0_ffn_b2, L1_attn_w_q, L1_attn_w_kv, L1_attn_w_out,
           L1_ffn_w_gate, L1_ffn_w_up, L1_ffn_w_down):
    global _IDENT
    if _IDENT is None:
        _IDENT = bf(np.eye(128, dtype=np.float32))
    if 'fused' not in _programs:
        _programs['fused'] = build_fused()
    nc = _programs['fused']
    x = f32(L0_attn_x)
    B, N, Dm = x.shape

    shared = {
        "qkv_wT": f32(np.asarray(L0_attn_qkv_w).T),
        "qkv_b": f32(np.asarray(L0_attn_qkv_b)[None, :]),
        "out_wT": f32(np.asarray(L0_attn_out_w).T),
        "out_b": f32(np.asarray(L0_attn_out_b)[None, :]),
        "band_mask": l0a_masks(32, 0),
        "identm": _IDENT,
        "g_bc": f32(np.broadcast_to(np.asarray(L0_ffn_ln_g), (128, Dm))),
        "b_bc": f32(np.broadcast_to(np.asarray(L0_ffn_ln_b), (128, Dm))),
        "rW": f32(np.asarray(L0_ffn_rW).reshape(8, 128, 8).transpose(1, 0, 2).reshape(128, 64)),
        "rb": f32(np.asarray(L0_ffn_rb)[None, :]),
        "W1": f32(L0_ffn_W1),
        "b1t": f32(np.asarray(L0_ffn_b1).reshape(8, 16, 128).transpose(2, 0, 1).reshape(128, 128)),
        "W2": f32(L0_ffn_W2),
        "b2r": f32(np.asarray(L0_ffn_b2).reshape(1, 8 * 1024)),
        "w_q": f32(L1_attn_w_q),
        "w_kv": f32(L1_attn_w_kv),
        "w_out": f32(L1_attn_w_out),
        "kcs": l1a_rope_tables(np.arange(N).astype(np.float64)),
        "w_gate": f32(L1_ffn_w_gate),
        "w_up": f32(L1_ffn_w_up),
        "w_down": f32(L1_ffn_w_down),
    }
    qcs_h = [l1a_rope_tables(own_rows(h).astype(np.float64)) for h in (0, 1)]
    msk_h = [l1a_masks(h) for h in (0, 1)]
    idx_h = []
    for h in (0, 1):
        rows = own_rows(h).astype(np.int32)
        idx_h.append(np.ascontiguousarray(rows.reshape(16, 128).T))
    in_maps = []
    for core in range(8):
        b, h = core // 2, core % 2
        xp = np.zeros((34 * 128, Dm), dtype=np.float32)
        xp[128:128 + N] = x[b]
        d = dict(shared)
        d.update({"x_pad": xp, "qcs": qcs_h[h], "cmask": msk_h[h], "ownidx": idx_h[h]})
        in_maps.append(d)
    res = run_bass_kernel_spmd(nc, in_maps, core_ids=list(range(8)))
    out = np.empty((B, N, Dm), dtype=np.float32)
    for core in range(8):
        b, h = core // 2, core % 2
        out[b][own_rows(h)] = res.results[core]["y4"]
    return out


# version 12
# speedup vs baseline: 1.0132x; 1.0132x over previous
import numpy as np
import ml_dtypes
import concourse.bass as bass
import concourse.mybir as mybir
from concourse.bass_utils import run_bass_kernel_spmd
from concourse.bass_core_rust import IndirectOffsetOnAxis

F32 = mybir.dt.float32
I32 = mybir.dt.int32
BF16 = mybir.dt.bfloat16
AF = mybir.ActivationFunctionType
ALU = mybir.AluOpType
AX = mybir.AxisListType

D = 1024
SEQ = 4096
LN_EPS = 1e-5
QK_EPS = 1e-6
SM_OFF = 10.0
MOE_LN_EPS = 1e-5

ENGINES = ("tensor", "scalar", "vector", "gpsimd", "sync")


class Tok:
    __slots__ = ("eng", "sem", "seq", "inc", "flag", "value")

    def __init__(self, eng, sem, seq, inc, value=0):
        self.eng, self.sem, self.seq, self.inc = eng, sem, seq, inc
        self.flag, self.value = False, value


NDMA_SEMS = 4


class PB:

    def __init__(self, nc):
        self.nc = nc
        self.streams = {e: [] for e in ENGINES}
        self.dma_idx = {}
        self.pending = {e: [] for e in ENGINES}

    def op(self, eng, fn, waits=(), dma=False):
        ws = [w for w in waits if w is not None]
        if self.pending[eng]:
            ws.extend(self.pending[eng])
            self.pending[eng] = []
        if dma:
            j = self.dma_idx[eng] = self.dma_idx.get(eng, -1) + 1
            sem = f"{eng}dma{j % NDMA_SEMS}"
            tok = Tok(eng, sem, len(self.streams[eng]), 16, 16 * (j // NDMA_SEMS + 1))
            tok.flag = True
            if j >= NDMA_SEMS:
                prev = Tok(eng, sem, -1, 16, 16 * (j // NDMA_SEMS))
                prev.flag = True
                ws.append(prev)
        else:
            sem = eng
            tok = Tok(eng, sem, len(self.streams[eng]), 1)
        for w in ws:
            if w.eng != eng or w.inc == 16:
                w.flag = True
        self.streams[eng].append((fn, ws, tok))
        return tok

    def barrier(self):
        toks = []
        for e in ENGINES:
            if self.streams[e]:
                toks.append(self.streams[e][-1][2])
        for e in ENGINES:
            self.pending[e] = [t for t in toks if t.eng != e or t.inc == 16]

    def emit(self, block):
        sem_names = set()
        for e in ENGINES:
            for (_, ws, tok) in self.streams[e]:
                if tok.flag:
                    sem_names.add(tok.sem)
                for w in ws:
                    if w.flag:
                        sem_names.add(w.sem)
        self.cum = {}
        for e in ENGINES:
            cnt, cums = 0, []
            for (_, _, tok) in self.streams[e]:
                if tok.flag and tok.inc == 1:
                    cnt += 1
                cums.append(cnt)
            self.cum[e] = cums
        import contextlib
        self._stack = contextlib.ExitStack()
        sems = {s: self._stack.enter_context(self.nc.semaphore(f"pb_{s}"))
                for s in sorted(sem_names)}
        self.sems = sems

        def emitter(eng):
            def run(h):
                waited = {}
                drained = 0
                for i, (fn, ws, tok) in enumerate(self.streams[eng]):
                    need_drain = False
                    for w in ws:
                        if w.inc == 16:
                            if w.value > waited.get(w.sem, 0):
                                h.wait_ge(sems[w.sem], w.value)
                                waited[w.sem] = w.value
                        elif w.eng == eng:
                            if w.seq >= drained:
                                need_drain = True
                        else:
                            val = self.cum[w.eng][w.seq]
                            if val > waited.get(w.eng, 0):
                                h.wait_ge(sems[w.eng], val)
                                waited[w.eng] = val
                    if need_drain:
                        h.drain()
                        drained = i
                    inst = fn(h)
                    if tok.flag:
                        inst.then_inc(sems[tok.sem], tok.inc)
            return run

        block.tensor(emitter("tensor"))
        block.scalar(emitter("scalar"))
        block.vector(emitter("vector"))
        block.gpsimd(emitter("gpsimd"))
        block.sync(emitter("sync"))


def bf(x):
    return np.ascontiguousarray(x).astype(ml_dtypes.bfloat16)


def f32(x):
    return np.ascontiguousarray(x, dtype=np.float32)



_ALLOC_N = [0]


class Alloc:

    def __init__(self, nc):
        import contextlib
        self.nc = nc
        self.stack = contextlib.ExitStack()

    def _u(self, name):
        _ALLOC_N[0] += 1
        return f"{name}_{_ALLOC_N[0]}"

    def sb(self, name, shape, dtype, side=None):
        return self.stack.enter_context(
            self.nc.sbuf_tensor(self._u(name), shape, dtype, side=side)
            if side else self.nc.sbuf_tensor(self._u(name), shape, dtype))

    def ps(self, name, shape, dtype):
        return self.stack.enter_context(self.nc.psum_tensor(self._u(name), shape, dtype))

    def close(self):
        self.stack.close()


def build_l0a(nqc):
    nc = bass.Bass("TRN2", target_bir_lowering=False)
    NKC = nqc + 2
    io = {
        'x_pad': nc.dram_tensor("x_pad", [NKC * 128, D], F32, kind="ExternalInput").ap(),
        'qkv_wT': nc.dram_tensor("qkv_wT", [D, 3 * D], F32, kind="ExternalInput").ap(),
        'qkv_b': nc.dram_tensor("qkv_b", [1, 3 * D], F32, kind="ExternalInput").ap(),
        'out_wT': nc.dram_tensor("out_wT", [D, D], F32, kind="ExternalInput").ap(),
        'out_b': nc.dram_tensor("out_b", [1, D], F32, kind="ExternalInput").ap(),
        'band_mask': nc.dram_tensor("band_mask", [128, nqc * 3 * 128], BF16,
                                    kind="ExternalInput").ap(),
        'identm': nc.dram_tensor("identm", [128, 128], BF16, kind="ExternalInput").ap(),
        'y1': nc.dram_tensor("y1", [nqc * 128, D], F32, kind="ExternalOutput").ap(),
    }
    with nc.Block() as blk:
        pb = PB(nc)
        emit_l0a(nc, pb, blk, io, nqc)
        pb.emit(blk)
    return nc


def emit_l0a(nc, pb, blk, io, nqc, tag=''):
    NKC = nqc + 2
    x_pad, qkv_wT, qkv_b, out_wT, out_b, band_mask, ident_in, y1 = (
        io['x_pad'], io['qkv_wT'], io['qkv_b'], io['out_wT'], io['out_b'],
        io['band_mask'], io['identm'], io['y1'])
    qk_dram = nc.dram_tensor("qkv_scr" + tag, [NKC * 128, 3 * D], BF16).ap()

    glob = Alloc(nc)
    ident = glob.sb("ident", [128, 128], BF16)
    masks = glob.sb("masks", [128, nqc * 3 * 128], BF16)
    onesrow = glob.sb("onesrow", [1, 128], BF16)
    onesf = glob.sb("onesf", [1, 64], F32)
    consts = glob.sb("consts", [128, 4], F32)
    t_ones = pb.op("gpsimd", lambda g: g.memset(onesrow[:], 1.0))
    t_onesf = pb.op("gpsimd", lambda g: g.memset(onesf[:], 1.0))
    pb.op("gpsimd", lambda g: g.memset(consts[:, 0:1], LN_EPS))
    pb.op("gpsimd", lambda g: g.memset(consts[:, 1:2], 1e-12))
    t_lne = pb.op("gpsimd", lambda g: g.memset(consts[:, 2:3], -0.5))
    t_eps = t_lne
    t_id = pb.op("gpsimd", lambda g: g.dma_start(out=ident[:], in_=ident_in[:]), dma=True)
    t_msk = pb.op("gpsimd", lambda g: g.dma_start(out=masks[:], in_=band_mask[:]), dma=True)

    ph = Alloc(nc)
    wqkv = ph.sb("wqkv", [128, 8 * 3 * D], BF16)
    brow = ph.sb("brow", [1, 3 * D], BF16)
    xc = ph.sb("xc", [128, 2 * D], F32)
    lnc = ph.sb("lnc", [128, 2 * D], BF16)
    lnT = ph.sb("lnT", [128, 2 * 8 * 128], BF16)
    qkf = ph.sb("qkf", [128, 2048], F32)
    sqs = ph.sb("sqs", [128, 2048], F32)
    st_ = ph.sb("st_", [128, 16], F32)
    nrm = ph.sb("nrm", [128, 128], F32)
    qkn = ph.sb("qkn", [128, 2 * 3072], BF16)
    pT_ = [ph.ps("pT0", [128, 1024], BF16), ph.ps("pT1", [128, 1024], BF16)]
    pqkv = ph.ps("pqkv", [128, 2048], F32)
    pv = ph.ps("pv", [128, 1024], F32)

    t_w = [pb.op("gpsimd", lambda g, j=j: g.dma_start(
        out=wqkv[:, j * 3072:(j + 1) * 3072],
        in_=qkv_wT[j * 128:(j + 1) * 128, :]), dma=True) for j in range(8)]
    t_b = pb.op("gpsimd", lambda g: g.dma_start(out=brow[:], in_=qkv_b[:]), dma=True)

    t_qkst = []
    SAB = ph.sb("SAB", [128, 4 * NKC], F32)
    SA, SB = SAB[:, 0:NKC], SAB[:, NKC:2 * NKC]
    MEAN, RSTD = SAB[:, 2 * NKC:3 * NKC], SAB[:, 3 * NKC:4 * NKC]
    t_sfree = [None, None]
    GS = 8
    t_gstats = {}
    for i in range(NKC):
        p = i % 2
        xs = xc[:, p * D:(p + 1) * D]
        t_x = pb.op("sync", lambda s, i=i, xs=xs: s.dma_start(
            out=xs, in_=x_pad[i * 128:(i + 1) * 128, :]),
            waits=[t_sfree[p]], dma=True)
        t_sfree[p] = emit_sums(pb, xs, SA[:, i:i + 1], SB[:, i:i + 1], sqs,
                               [t_x, t_sfree[1 - p]])
        if i % GS == GS - 1 or i == NKC - 1:
            g = i // GS
            lo, hi = g * GS, i + 1
            t_gstats[g] = emit_stats_batch(
                pb, SA[:, lo:hi], SB[:, lo:hi], MEAN[:, lo:hi], RSTD[:, lo:hi],
                consts[:, 0:1], consts[:, 2:3], [t_sfree[0], t_sfree[1], t_lne])
    t_xfree = None
    t_lnTfree = [None, None]
    t_lncfree = [None, None]
    t_qkffree = [None, None]
    t_pvfree = None
    t_pqkvfree = None
    t_qknfree = [None, None]
    for i in range(NKC):
        p = i % 2
        xs = xc[:, p * D:(p + 1) * D]
        lns = lnc[:, p * D:(p + 1) * D]
        NR = nrm[:, p * 64:(p + 1) * 64]
        t_x = pb.op("sync", lambda s, i=i, xs=xs: s.dma_start(
            out=xs, in_=x_pad[i * 128:(i + 1) * 128, :]),
            waits=[t_xfree] if i >= 2 else [t_sfree[0], t_sfree[1]], dma=True)
        t_ln = pb.op("vector", lambda v, xs=xs, lns=lns, i=i: v.tensor_scalar(
            lns, xs, MEAN[:, i:i + 1], RSTD[:, i:i + 1], ALU.subtract, ALU.mult),
            waits=[t_x, t_gstats[i // GS], t_lncfree[p]])
        t_xfree = t_ln
        t_lnTs = []
        t_tp = None
        for j in range(8):
            pj = (i * 8 + j) % 2
            t_tp = pb.op("tensor", lambda t, lns=lns, j=j, pj=pj: t.transpose(
                pT_[pj][:, 0:128], lns[:, j * 128:(j + 1) * 128], ident[:]),
                waits=[t_ln, t_id, t_lnTfree[pj]])
            t_ev = pb.op("vector", lambda v, j=j, p=p, pj=pj: v.tensor_copy(
                lnT[:, (p * 8 + j) * 128:(p * 8 + j + 1) * 128], pT_[pj][:, 0:128]),
                waits=[t_tp])
            t_lnTfree[pj] = t_ev
            t_lnTs.append(t_ev)
        t_lncfree[p] = t_tp

        def qk_mms(t, p=p):
            for piece in range(4):
                for j in range(8):
                    t.matmul(pqkv[:, piece * 512:(piece + 1) * 512],
                             lnT[:, (p * 8 + j) * 128:(p * 8 + j + 1) * 128],
                             wqkv[:, j * 3072 + piece * 512:j * 3072 + (piece + 1) * 512],
                             start=(j == 0), stop=False)
                last = t.matmul(pqkv[:, piece * 512:(piece + 1) * 512], onesrow[0:1, :],
                                brow[0:1, piece * 512:(piece + 1) * 512],
                                start=False, stop=True)
            return last
        t_qk = pb.op("tensor", qk_mms, waits=t_lnTs + t_w + [t_b, t_ones, t_pqkvfree])

        def v_mms(t, p=p):
            for piece in range(2):
                for j in range(8):
                    t.matmul(pv[:, piece * 512:(piece + 1) * 512],
                             lnT[:, (p * 8 + j) * 128:(p * 8 + j + 1) * 128],
                             wqkv[:, j * 3072 + 2048 + piece * 512:
                                  j * 3072 + 2048 + (piece + 1) * 512],
                             start=(j == 0), stop=False)
                last = t.matmul(pv[:, piece * 512:(piece + 1) * 512], onesrow[0:1, :],
                                brow[0:1, 2048 + piece * 512:2048 + (piece + 1) * 512],
                                start=False, stop=True)
            return last
        t_v = pb.op("tensor", v_mms, waits=[t_pvfree])
        t_qkev = pb.op("vector", lambda v: v.tensor_copy(qkf[:], pqkv[:]),
                       waits=[t_qk, t_qkffree[0], t_qkffree[1]])
        t_pqkvfree = t_qkev
        t_vev = pb.op("vector", lambda v, p=p: v.tensor_copy(
            qkn[:, p * 3072 + 2048:(p + 1) * 3072], pv[:]),
            waits=[t_v, t_qknfree[p]])
        t_pvfree = t_vev
        t_sqp = pb.op("scalar", lambda s: s.activation(sqs[:], qkf[:], AF.Square),
                      waits=[t_qkev])
        t_red = pb.op("vector", lambda v, NR=NR: v.tensor_reduce(
            NR[:, 0:32], sqs[:].rearrange("p (h e) -> p h e", e=64), AX.X, ALU.add),
            waits=[t_sqp])
        t_nsq = pb.op("scalar", lambda s, NR=NR: s.activation(
            NR[:, 32:64], NR[:, 0:32], AF.Ln, bias=consts[:, 1:2]), waits=[t_red, t_lne])
        t_nr = pb.op("scalar", lambda s, NR=NR: s.activation(
            NR[:, 0:32], NR[:, 32:64], AF.Exp, scale=consts[:, 2:3]), waits=[t_nsq])
        t_qknv = pb.op("vector", lambda v, p=p, NR=NR: v.tensor_tensor(
            qkn[:, p * 3072:p * 3072 + 2048].rearrange("p (h e) -> p h e", e=64),
            qkf[:].rearrange("p (h e) -> p h e", e=64),
            NR[:, 0:32].unsqueeze(2).broadcast_to([128, 32, 64]), ALU.mult),
            waits=[t_nr, t_qknfree[p]])
        t_qkffree = [t_sqp, t_qknv]
        t_st = pb.op("sync", lambda s, i=i, p=p: s.dma_start(
            out=qk_dram[i * 128:(i + 1) * 128, :], in_=qkn[:, p * 3072:(p + 1) * 3072]),
            waits=[t_qknv, t_vev], dma=True)
        t_qknfree[p] = t_st
        t_qkst.append(t_st)
    pb.barrier()
    ph.close()

    for r0 in range(0, nqc, 16):
        nqh = min(16, nqc - r0)
        nkh = nqh + 2
        ph2 = Alloc(nc)
        kT = ph2.sb("kT", [128, 8 * 18 * 128], BF16)
        qT = ph2.sb("qT", [128, 8 * 16 * 128], BF16)
        v_aug = ph2.sb("v_aug", [128, 18 * 16 * 65], BF16)
        wout = ph2.sb("wout", [128, 8 * D], BF16)
        obrow = ph2.sb("obrow", [1, D], BF16)
        expt = ph2.sb("expt", [128, 2 * 512], BF16)
        rl = ph2.sb("rl", [1, 2 * 512], F32)
        rlbc = ph2.sb("rlbc", [64, 2 * 512], F32)
        outT = ph2.sb("outT", [128, 2 * 8 * 128], BF16)
        xre = ph2.sb("xre", [128, 2 * D], F32)
        y1c = ph2.sb("y1c", [128, 2 * D], F32)
        psc = ph2.ps("psc", [128, 2 * 1024], F32)
        pav = ph2.ps("pav", [65, 2 * 512], F32)
        pbc = ph2.ps("pbc", [64, 512], F32)
        pproj = psc[:, 0:1024]

        t_kT = [pb.op("sync", lambda s, j=j, r0=r0, nkh=nkh: s.dma_start(
            out=kT[:, j * 18 * 128:j * 18 * 128 + nkh * 128],
            in_=qk_dram[r0 * 128:(r0 + nkh) * 128, D + j * 128:D + (j + 1) * 128],
            transpose=True), waits=t_qkst[:r0 + nkh], dma=True) for j in range(8)]
        t_qT = [pb.op("sync", lambda s, j=j, r0=r0, nqh=nqh: s.dma_start(
            out=qT[:, j * 16 * 128:j * 16 * 128 + nqh * 128],
            in_=qk_dram[(r0 + 1) * 128:(r0 + 1 + nqh) * 128, j * 128:(j + 1) * 128],
            transpose=True), waits=t_qkst[:r0 + 1 + nqh], dma=True) for j in range(8)]
        t_vl, t_vone = [], []
        for c in range(nkh):
            t_ = pb.op("gpsimd", lambda g, c=c, r0=r0: g.dma_start(
                out=v_aug[:, c * 1040:(c + 1) * 1040]
                .rearrange("p (h e) -> p h e", e=65)[:, :, 0:64],
                in_=qk_dram[(r0 + c) * 128:(r0 + c + 1) * 128, 2 * D:3 * D]
                .rearrange("p (h e) -> p h e", e=64)),
                waits=t_qkst[:r0 + nkh], dma=True)
            t_vl.append(t_)
            t_vone.append(pb.op("gpsimd", lambda g, c=c: g.memset(
                v_aug[:, c * 1040 + 64:(c + 1) * 1040:65], 1.0), waits=[t_]))
        t_wo = [pb.op("gpsimd", lambda g, j=j: g.dma_start(
            out=wout[:, j * D:(j + 1) * D],
            in_=out_wT[j * 128:(j + 1) * 128, :]), dma=True) for j in range(8)]
        t_ob = pb.op("gpsimd", lambda g: g.dma_start(out=obrow[:], in_=out_b[:]), dma=True)

        t_expfree = [None, None]
        t_pprojfree = None
        t_pscfree = [None, None]
        t_pavfree = [None, None]
        t_rlf = None
        t_bcf = None
        t_y1 = [None, None]
        for qc in range(nqh):
            gqc = r0 + qc
            p = qc % 2
            t_xr = pb.op("sync", lambda s, gqc=gqc, p=p: s.dma_start(
                out=xre[:, p * D:(p + 1) * D],
                in_=x_pad[(gqc + 1) * 128:(gqc + 2) * 128, :]),
                waits=[t_y1[p]], dma=True)
            t_fins = []
            for hg in range(4):
                u = qc * 4 + hg
                pu = u % 2
                t_avs = []
                for kc in range(3):
                    kchunk = qc + kc
                    pp = (u * 3 + kc) % 2

                    def sc_mms(t, hg=hg, pp=pp, kchunk=kchunk, qc=qc):
                        for hh in range(4):
                            h = hg * 4 + hh
                            tile, row0 = h // 2, 64 * (h % 2)
                            col = pp * 1024 + (hh % 2) * 512 + (hh // 2) * 128
                            last = t.matmul(
                                psc[:, col:col + 128],
                                kT[row0:row0 + 64,
                                   tile * 18 * 128 + kchunk * 128:
                                   tile * 18 * 128 + (kchunk + 1) * 128],
                                qT[row0:row0 + 64,
                                   tile * 16 * 128 + qc * 128:
                                   tile * 16 * 128 + (qc + 1) * 128],
                                start=(hh < 2), stop=(hh >= 2))
                        return last
                    t_sc = pb.op("tensor", sc_mms, waits=t_kT + t_qT + [t_pscfree[pp]])
                    t_exp = pb.op("scalar", lambda s, pp=pp: s.activation(
                        expt[:, pp * 512:(pp + 1) * 512],
                        psc[:, pp * 1024:(pp + 1) * 1024]
                        .rearrange("p (b c) -> p b c", c=512)[:, :, 0:256], AF.Exp),
                        waits=[t_sc, t_expfree[pp]])
                    t_pscfree[pp] = t_exp
                    t_m = pb.op("vector", lambda v, pp=pp, gqc=gqc, kc=kc: v.tensor_tensor(
                        expt[:, pp * 512:(pp + 1) * 512].rearrange("p (h q) -> p h q", q=128),
                        expt[:, pp * 512:(pp + 1) * 512].rearrange("p (h q) -> p h q", q=128),
                        masks[:, (gqc * 3 + kc) * 128:(gqc * 3 + kc + 1) * 128]
                        .unsqueeze(1).broadcast_to([128, 4, 128]), ALU.mult),
                        waits=[t_exp, t_msk])

                    def av_mms(t, hg=hg, pp=pp, pu=pu, kchunk=kchunk, kc=kc):
                        for hh in range(4):
                            h = hg * 4 + hh
                            ec = ((hh % 2) * 2 + hh // 2) * 128
                            last = t.matmul(
                                pav[:, pu * 512 + hh * 128:pu * 512 + (hh + 1) * 128],
                                v_aug[:, kchunk * 1040 + h * 65:
                                      kchunk * 1040 + (h + 1) * 65],
                                expt[:, pp * 512 + ec:pp * 512 + ec + 128],
                                start=(kc == 0 and hh == 0), stop=(kc == 2 and hh == 3))
                        return last
                    t_av = pb.op("tensor", av_mms,
                                 waits=[t_m, t_pavfree[pu]] + t_vone if kc == 0 else [t_m])
                    t_expfree[pp] = t_av
                    t_avs.append(t_av)
                t_rl = pb.op("scalar", lambda s, pu=pu: s.activation(
                    rl[0:1, pu * 512:(pu + 1) * 512], pav[64:65, pu * 512:(pu + 1) * 512],
                    AF.Ln), waits=[t_avs[2], t_rlf])
                t_rl = pb.op("scalar", lambda s, pu=pu: s.activation(
                    rl[0:1, pu * 512:(pu + 1) * 512], rl[0:1, pu * 512:(pu + 1) * 512],
                    AF.Exp, scale=-1.0), waits=[t_rl])
                t_bc = pb.op("tensor", lambda t, pu=pu: t.matmul(
                    pbc[:], onesf[0:1, 0:64],
                    rl[0:1, pu * 512:(pu + 1) * 512], start=True, stop=True),
                    waits=[t_rl, t_bcf, t_onesf])
                t_rlf = t_bc
                t_bcev = pb.op("vector", lambda v, pu=pu: v.tensor_copy(
                    rlbc[:, pu * 512:(pu + 1) * 512], pbc[:]), waits=[t_bc])
                t_bcf = t_bcev

                def fin(v, hg=hg, pu=pu, p=p):
                    for hh in range(4):
                        h = hg * 4 + hh
                        tile, row0 = h // 2, 64 * (h % 2)
                        last = v.scalar_tensor_tensor(
                            outT[row0:row0 + 64, (p * 8 + tile) * 128:(p * 8 + tile + 1) * 128],
                            pav[0:64, pu * 512 + hh * 128:pu * 512 + (hh + 1) * 128], 1.0,
                            rlbc[:, pu * 512 + hh * 128:pu * 512 + (hh + 1) * 128],
                            ALU.mult, ALU.mult)
                    return last
                t_fin = pb.op("vector", fin, waits=[t_bcev] + ([t_y1[p]] if hg == 0 else []))
                t_pavfree[pu] = t_fin
                t_fins.append(t_fin)

            def proj_mms(t, p=p):
                for piece in range(2):
                    for j in range(8):
                        t.matmul(pproj[:, piece * 512:(piece + 1) * 512],
                                 outT[:, (p * 8 + j) * 128:(p * 8 + j + 1) * 128],
                                 wout[:, j * D + piece * 512:j * D + (piece + 1) * 512],
                                 start=(j == 0), stop=False)
                    last = t.matmul(pproj[:, piece * 512:(piece + 1) * 512], onesrow[0:1, :],
                                    obrow[0:1, piece * 512:(piece + 1) * 512],
                                    start=False, stop=True)
                return last
            t_proj = pb.op("tensor", proj_mms,
                           waits=t_fins + t_wo + [t_ob, t_pprojfree,
                                                  t_pscfree[0], t_pscfree[1]])
            t_pscfree[0] = None
            t_y1c = pb.op("vector", lambda v, p=p: v.scalar_tensor_tensor(
                y1c[:, p * D:(p + 1) * D], pproj[:], 1.0,
                xre[:, p * D:(p + 1) * D], ALU.mult, ALU.add),
                waits=[t_proj, t_xr])
            t_pprojfree = t_y1c
            t_pscfree[0] = t_y1c
            t_st = pb.op("sync", lambda s, gqc=gqc, p=p: s.dma_start(
                out=y1[gqc * 128:(gqc + 1) * 128, :], in_=y1c[:, p * D:(p + 1) * D]),
                waits=[t_y1c], dma=True)
            t_y1[p] = t_st
        pb.op("sync", lambda s: s.nop(), waits=[t_y1[0], t_y1[1]])
        pb.barrier()
        ph2.close()
    glob.close()



def l0a_inputs(x, core):
    b, h = core // 2, core % 2
    q0 = h * 2048
    lo, hi = q0 - 128, q0 + 2048 + 128
    xp = np.zeros((2048 + 256, D), dtype=np.float32)
    s, e = max(lo, 0), min(hi, SEQ)
    xp[s - lo:e - lo] = x[b, s:e]
    return {"x_pad": xp}


def l0a_masks(nqc, q0):
    m = np.zeros((nqc, 3, 128, 128), dtype=np.float32)
    for qc in range(nqc):
        gq = q0 + qc * 128 + np.arange(128)[None, :]
        for kc in range(3):
            gk = q0 + (qc + kc - 1) * 128 + np.arange(128)[:, None]
            diff = gk - gq
            ok = (np.abs(diff) <= 16) & (diff % 2 == 0) & (gk >= 0) & (gk < SEQ)
            m[qc, kc] = ok.astype(np.float32)
    return bf(m.reshape(nqc * 3, 128, 128).transpose(1, 0, 2).reshape(128, nqc * 3 * 128))


def emit_sums(pb, xs, sa_col, sb_col, sqs, waits):
    t1 = pb.op("scalar", lambda s: s.activation(
        sqs[:, 0:D], xs, AF.Square, accum_out=sa_col), waits=waits)
    t2 = pb.op("scalar", lambda s: s.activation(
        sqs[:, 0:D], xs, AF.Copy, accum_out=sb_col), waits=[t1])
    return t2


def emit_stats_batch(pb, SA, SB, MEAN, RSTD, eps_col, mhalf_col, waits):
    t = pb.op("vector", lambda v: v.tensor_scalar_mul(MEAN, SB, 1.0 / D), waits=waits)
    t = pb.op("vector", lambda v: v.scalar_tensor_tensor(
        RSTD, MEAN, 1.0, MEAN, ALU.mult, ALU.mult), waits=[t])
    t = pb.op("vector", lambda v: v.scalar_tensor_tensor(
        RSTD, SA, 1.0 / D, RSTD, ALU.mult, ALU.subtract), waits=[t])
    t = pb.op("scalar", lambda s: s.activation(RSTD, RSTD, AF.Ln, bias=eps_col),
              waits=[t])
    t = pb.op("scalar", lambda s: s.activation(RSTD, RSTD, AF.Exp, scale=mhalf_col),
              waits=[t])
    return t


def emit_ln_stats(pb, sc_eng_bufs, xs, S, consts_eps_col, waits, eps_tok=None,
                  mhalf=None):
    (sqs,) = sc_eng_bufs
    t_s1 = pb.op("scalar", lambda s: s.activation(
        sqs[:, 0:D], xs, AF.Square, accum_out=S[:, 1:2]), waits=waits)
    t_s0 = pb.op("scalar", lambda s: s.activation(
        sqs[:, 0:D], xs, AF.Copy, accum_out=S[:, 0:1]), waits=[t_s1])
    t_mv = pb.op("vector", lambda v: v.tensor_scalar_mul(S[:, 2:3], S[:, 0:1], 1.0 / D),
                 waits=[t_s0])
    t_mv2 = pb.op("vector", lambda v: v.scalar_tensor_tensor(
        S[:, 3:4], S[:, 2:3], 1.0, S[:, 2:3], ALU.mult, ALU.mult), waits=[t_mv])
    t_var = pb.op("vector", lambda v: v.scalar_tensor_tensor(
        S[:, 3:4], S[:, 1:2], 1.0 / D, S[:, 3:4], ALU.mult, ALU.subtract), waits=[t_mv2])
    t_sq = pb.op("scalar", lambda s: s.activation(
        S[:, 4:5], S[:, 3:4], AF.Ln, bias=consts_eps_col), waits=[t_var, eps_tok])
    t_r = pb.op("scalar", lambda s: s.activation(
        S[:, 5:6], S[:, 4:5], AF.Exp, scale=mhalf), waits=[t_sq])
    return t_var, t_r



def build_l0f(nt_chunks, npass=2):
    nc = bass.Bass("TRN2", target_bir_lowering=False)
    NT = nt_chunks * 128
    io = {
        'y1_in': nc.dram_tensor("y1_in", [NT, D], F32, kind="ExternalInput").ap(),
        'g_bc': nc.dram_tensor("g_bc", [128, D], F32, kind="ExternalInput").ap(),
        'b_bc': nc.dram_tensor("b_bc", [128, D], F32, kind="ExternalInput").ap(),
        'rW': nc.dram_tensor("rW", [128, 8 * 8], F32, kind="ExternalInput").ap(),
        'rb': nc.dram_tensor("rb", [1, 8], F32, kind="ExternalInput").ap(),
        'W1': nc.dram_tensor("W1", [8, D, 2048], F32, kind="ExternalInput").ap(),
        'b1t': nc.dram_tensor("b1t", [128, 8 * 16], F32, kind="ExternalInput").ap(),
        'W2': nc.dram_tensor("W2", [8, 2048, D], F32, kind="ExternalInput").ap(),
        'b2r': nc.dram_tensor("b2r", [1, 8 * D], F32, kind="ExternalInput").ap(),
        'identm': nc.dram_tensor("identm", [128, 128], BF16, kind="ExternalInput").ap(),
        'y2': nc.dram_tensor("y2", [NT, D], F32, kind="ExternalOutput").ap(),
    }
    with nc.Block() as blk:
        pb = PB(nc)
        emit_l0f(nc, pb, blk, io, nt_chunks, npass)
        pb.emit(blk)
    return nc


def emit_l0f(nc, pb, blk, io, nt_chunks, npass=2, tag=''):
    NT = nt_chunks * 128
    (y1_in, g_bc, b_bc, rW, rb, W1, b1t, W2, b2r, ident_in, y2) = (
        io['y1_in'], io['g_bc'], io['b_bc'], io['rW'], io['rb'], io['W1'],
        io['b1t'], io['W2'], io['b2r'], io['identm'], io['y2'])
    w1_bf = nc.dram_tensor("w1_bf" + tag, [8, D, 2048], BF16).ap()
    w2_bf = nc.dram_tensor("w2_bf" + tag, [8, 2048, D], BF16).ap()

    pc = nt_chunks // npass
    TGP = (pc * 128) // 512

    glob = Alloc(nc)
    ident = glob.sb("ident", [128, 128], BF16)
    gb = glob.sb("gb", [128, 2 * D], F32)
    b1s = glob.sb("b1s", [128, 8 * 16], F32)
    b2row = glob.sb("b2row", [1, 8 * D], BF16)
    rWs = glob.sb("rWs", [128, 8 * 8], BF16)
    rbs = glob.sb("rbs", [1, 8], BF16)
    onesrow = glob.sb("onesrow", [1, 128], BF16)
    consts = glob.sb("consts", [128, 4], F32)
    stats = glob.sb("stats", [128, nt_chunks * 4], F32)
    wts = glob.sb("wts", [128, nt_chunks * 8], F32)
    zT = glob.sb("zT", [128, 8 * NT], BF16)
    acc = glob.sb("acc", [128, pc * D], F32)
    SAB = glob.sb("SAB", [128, 7 * nt_chunks], F32)
    if True:
        t_ones = pb.op("gpsimd", lambda g: g.memset(onesrow[:], 1.0))
        pb.op("gpsimd", lambda g: g.memset(consts[:, 0:1], LN_EPS))
        pb.op("gpsimd", lambda g: g.memset(consts[:, 1:2], MOE_LN_EPS))
        t_eps = pb.op("gpsimd", lambda g: g.memset(consts[:, 2:3], -0.5))
        t_id = pb.op("gpsimd", lambda g: g.dma_start(out=ident[:], in_=ident_in[:]), dma=True)
        t_gb = pb.op("gpsimd", lambda g: g.dma_start(out=gb[:, 0:D], in_=g_bc[:]), dma=True)
        t_bb = pb.op("gpsimd", lambda g: g.dma_start(out=gb[:, D:2 * D], in_=b_bc[:]), dma=True)
        t_b1 = pb.op("gpsimd", lambda g: g.dma_start(out=b1s[:], in_=b1t[:]), dma=True)
        t_b2 = pb.op("gpsimd", lambda g: g.dma_start(out=b2row[:], in_=b2r[:]), dma=True)
        t_rw = pb.op("gpsimd", lambda g: g.dma_start(out=rWs[:], in_=rW[:]), dma=True)
        t_rb = pb.op("gpsimd", lambda g: g.dma_start(out=rbs[:], in_=rb[:]), dma=True)

        ph = Alloc(nc)
        yc = ph.sb("yc", [128, 2 * D], F32)
        sqs = ph.sb("sqs", [128, D], F32)
        S8 = ph.sb("S8", [128, 16], F32)
        az = ph.sb("az", [128, 2 * D], F32)
        zc = ph.sb("zc", [128, 2 * D], BF16)
        es = ph.sb("es", [128, 2 * 8], F32)
        r8 = ph.sb("r8", [128, 2 * 8], F32)
        pTz = [ph.ps("pTz0", [128, 1024], BF16), ph.ps("pTz1", [128, 1024], BF16)]
        plog = ph.ps("plog", [128, 1024], F32)

        NT_C = nt_chunks
        SA, SB = SAB[:, 0:NT_C], SAB[:, NT_C:2 * NT_C]
        MEAN, R1 = SAB[:, 2 * NT_C:3 * NT_C], SAB[:, 3 * NT_C:4 * NT_C]
        SC12 = SAB[:, 4 * NT_C:5 * NT_C]
        P0, P1 = SAB[:, 5 * NT_C:6 * NT_C], SAB[:, 6 * NT_C:7 * NT_C]
        lg_all = ph.sb("lg_all", [128, NT_C * 8], F32)
        t_sfree = [None, None]
        for i in range(NT_C):
            p = i % 2
            ys = yc[:, p * D:(p + 1) * D]
            t_y = pb.op("sync", lambda s, i=i, ys=ys: s.dma_start(
                out=ys, in_=y1_in[i * 128:(i + 1) * 128, :]),
                waits=[t_sfree[p]], dma=True)
            t_sfree[p] = emit_sums(pb, ys, SA[:, i:i + 1], SB[:, i:i + 1], sqs,
                                   [t_y, t_sfree[1 - p]])
        t_st1 = emit_stats_batch(pb, SA, SB, MEAN, R1, consts[:, 0:1],
                                 consts[:, 2:3], [t_sfree[0], t_sfree[1], t_eps])
        t_b1_ = pb.op("vector", lambda v: v.tensor_scalar_add(P0, R1, 1.0), waits=[t_st1])
        t_b2_ = pb.op("vector", lambda v: v.tensor_tensor(P1, MEAN, R1, ALU.mult),
                      waits=[t_b1_])
        t_b3_ = pb.op("vector", lambda v: v.scalar_tensor_tensor(
            SC12, MEAN, 1.0, MEAN, ALU.mult, ALU.mult), waits=[t_b2_])
        t_b4_ = pb.op("vector", lambda v: v.scalar_tensor_tensor(
            SC12, SA, 1.0 / D, SC12, ALU.mult, ALU.subtract), waits=[t_b3_])
        t_b5_ = pb.op("vector", lambda v: v.tensor_tensor(SA, R1, R1, ALU.mult),
                      waits=[t_b4_])
        t_b6_ = pb.op("vector", lambda v: v.tensor_tensor(SC12, SC12, SA, ALU.mult),
                      waits=[t_b5_])
        t_b7_ = pb.op("scalar", lambda s: s.activation(SC12, SC12, AF.Ln,
                                                       bias=consts[:, 1:2]), waits=[t_b6_])
        t_b8_ = pb.op("scalar", lambda s: s.activation(SC12, SC12, AF.Exp,
                                                       scale=consts[:, 2:3]), waits=[t_b7_])
        t_stats = pb.op("vector", lambda v: v.tensor_tensor(SC12, SC12, R1, ALU.mult),
                        waits=[t_b8_])
        t_yfree = None
        t_lnTfree = [None, None]
        t_zfree = [None, None]
        t_azfree = [None, None]
        t_lgfree = None
        t_plogfree = [None, None]
        for i in range(NT_C):
            p = i % 2
            ys = yc[:, p * D:(p + 1) * D]
            t_y = pb.op("sync", lambda s, i=i, ys=ys: s.dma_start(
                out=ys, in_=y1_in[i * 128:(i + 1) * 128, :]),
                waits=[t_yfree] if i >= 2 else [t_sfree[0], t_sfree[1]], dma=True)
            t_az = pb.op("vector", lambda v, ys=ys, i=i, p=p: v.tensor_scalar(
                az[:, p * D:(p + 1) * D], ys, MEAN[:, i:i + 1], SC12[:, i:i + 1],
                ALU.subtract, ALU.mult), waits=[t_y, t_stats, t_azfree[p]])
            t_yfree = t_az
            t_z1 = pb.op("vector", lambda v, p=p: v.tensor_tensor(
                az[:, p * D:(p + 1) * D], az[:, p * D:(p + 1) * D], gb[:, 0:D], ALU.mult),
                waits=[t_az, t_gb])
            t_z = pb.op("vector", lambda v, p=p: v.tensor_tensor(
                zc[:, p * D:(p + 1) * D], az[:, p * D:(p + 1) * D], gb[:, D:2 * D], ALU.add),
                waits=[t_z1, t_bb, t_zfree[p]])
            t_azfree[p] = t_z
            t_zTs = []
            t_tp = None
            for j in range(8):
                pj = (i * 8 + j) % 2
                t_tp = pb.op("tensor", lambda t, p=p, j=j, pj=pj: t.transpose(
                    pTz[pj][:, 0:128], zc[:, p * D + j * 128:p * D + (j + 1) * 128], ident[:]),
                    waits=[t_z, t_id, t_lnTfree[pj]])
                t_ev = pb.op("vector", lambda v, i=i, j=j, pj=pj: v.tensor_copy(
                    zT[:, j * NT + i * 128:j * NT + (i + 1) * 128], pTz[pj][:, 0:128]),
                    waits=[t_tp])
                t_lnTfree[pj] = t_ev
                t_zTs.append(t_ev)
            t_zfree[p] = t_tp

            def r_mms(t, i=i, p=p):
                for j in range(8):
                    t.matmul(plog[:, p * 512:p * 512 + 8],
                             zT[:, j * NT + i * 128:j * NT + (i + 1) * 128],
                             rWs[:, j * 8:(j + 1) * 8], start=(j == 0), stop=False)
                return t.matmul(plog[:, p * 512:p * 512 + 8], onesrow[0:1, :],
                                rbs[0:1, :], start=False, stop=True)
            t_lg = pb.op("tensor", r_mms,
                         waits=t_zTs + [t_rw, t_rb, t_ones, t_plogfree[p]])
            t_lge = pb.op("vector", lambda v, i=i, p=p: v.tensor_copy(
                lg_all[:, i * 8:(i + 1) * 8], plog[:, p * 512:p * 512 + 8]),
                waits=[t_lg, t_lgfree])
            t_plogfree[p] = t_lge
        L3 = lg_all[:].rearrange("p (c e) -> p c e", e=8)
        W3 = wts[:].rearrange("p (c e) -> p c e", e=8)
        t_d1 = pb.op("vector", lambda v: v.tensor_reduce(
            SA[:, 0:NT_C], L3, AX.X, ALU.max, negate=True),
            waits=[t_plogfree[0], t_plogfree[1]])
        t_d2 = pb.op("vector", lambda v: v.tensor_tensor(
            L3, L3, SA[:, 0:NT_C].unsqueeze(2).broadcast_to([128, NT_C, 8]), ALU.add),
            waits=[t_d1])
        t_d3 = pb.op("scalar", lambda s: s.activation(lg_all[:], lg_all[:], AF.Exp),
                     waits=[t_d2])
        t_d4 = pb.op("vector", lambda v: v.tensor_reduce(
            SA[:, 0:NT_C], L3, AX.X, ALU.max), waits=[t_d3])
        t_d5 = pb.op("vector", lambda v: v.tensor_tensor(
            W3, L3, SA[:, 0:NT_C].unsqueeze(2).broadcast_to([128, NT_C, 8]), ALU.is_ge),
            waits=[t_d4])
        t_d6 = pb.op("vector", lambda v: v.scalar_tensor_tensor(
            wts[:], wts[:], -1e30, lg_all[:], ALU.mult, ALU.add), waits=[t_d5])
        t_d7 = pb.op("vector", lambda v: v.tensor_reduce(
            SB[:, 0:NT_C], W3, AX.X, ALU.max), waits=[t_d6])
        t_d8 = pb.op("vector", lambda v: v.tensor_tensor(
            W3, L3, SB[:, 0:NT_C].unsqueeze(2).broadcast_to([128, NT_C, 8]), ALU.is_ge),
            waits=[t_d7])
        t_d9 = pb.op("vector", lambda v: v.tensor_tensor(
            wts[:], wts[:], lg_all[:], ALU.mult), waits=[t_d8])
        t_da = pb.op("vector", lambda v: v.tensor_reduce(
            SB[:, 0:NT_C], W3, AX.X, ALU.add), waits=[t_d9])
        t_db_ = pb.op("scalar", lambda s: s.activation(SB[:, 0:NT_C], SB[:, 0:NT_C],
                                                       AF.Ln), waits=[t_da])
        t_dc = pb.op("scalar", lambda s: s.activation(SB[:, 0:NT_C], SB[:, 0:NT_C],
                                                      AF.Exp, scale=-1.0), waits=[t_db_])
        t_dd = pb.op("vector", lambda v: v.tensor_tensor(
            W3, W3, SB[:, 0:NT_C].unsqueeze(2).broadcast_to([128, NT_C, 8]), ALU.mult),
            waits=[t_dc])
        pb.barrier()
        ph.close()

        ph2 = Alloc(nc)
        NWB = 4
        wbuf = [ph2.sb(f"wbuf{k}", [128, 8 * 1024], BF16) for k in range(NWB)]
        hb = ph2.sb("hb", [128, 16 * 512], BF16)
        yre = ph2.sb("yre", [128, D], F32)
        y2c = ph2.sb("y2c", [128, 2 * D], F32)
        phid = ph2.ps("phid", [128, 2048], F32)
        po = ph2.ps("po", [128, 2 * 1024], F32)

        t_wload = []
        t_wuse = []
        t_wecho = {}
        cache_tok = {}
        nloads = npass * 8 * 4

        def emit_wload(k):
            P, rem = divmod(k, 32)
            e, q = divmod(rem, 4)
            buf = wbuf[k % NWB]
            waits = []
            if k >= NWB:
                waits.append(t_wuse[k - NWB])
                waits.extend(t_wecho.get(k - NWB, []))
            if P == 0:
                if q < 2:
                    srcs = [W1[e, j * 128:(j + 1) * 128, q * 1024:(q + 1) * 1024]
                            for j in range(8)]
                    dsts = [w1_bf[e, j * 128:(j + 1) * 128, q * 1024:(q + 1) * 1024]
                            for j in range(8)]
                else:
                    h = q - 2
                    srcs = [W2[e, h * 1024 + j * 128:h * 1024 + (j + 1) * 128, :]
                            for j in range(8)]
                    dsts = [w2_bf[e, h * 1024 + j * 128:h * 1024 + (j + 1) * 128, :]
                            for j in range(8)]
                toks = [pb.op("gpsimd", lambda g, j=j, buf=buf, src=srcs[j]: g.dma_start(
                    out=buf[:, j * 1024:(j + 1) * 1024], in_=src), waits=waits, dma=True)
                    for j in range(8)]
                if npass > 1:
                    echo = [pb.op("sync", lambda s, j=j, buf=buf, dst=dsts[j]: s.dma_start(
                        out=dst, in_=buf[:, j * 1024:(j + 1) * 1024]),
                        waits=[toks[j]], dma=True) for j in range(8)]
                    cache_tok[(e, q)] = echo
                    t_wecho[k] = echo
            else:
                if q < 2:
                    srcs = [w1_bf[e, j * 128:(j + 1) * 128, q * 1024:(q + 1) * 1024]
                            for j in range(8)]
                else:
                    h = q - 2
                    srcs = [w2_bf[e, h * 1024 + j * 128:h * 1024 + (j + 1) * 128, :]
                            for j in range(8)]
                waits = waits + cache_tok[(e, q)]
                toks = [pb.op("sync", lambda s, j=j, buf=buf, src=srcs[j]: s.dma_start(
                    out=buf[:, j * 1024:(j + 1) * 1024], in_=src), waits=waits, dma=True)
                    for j in range(8)]
            t_wload.append(toks)
            t_wuse.append(None)

        for k in range(min(NWB, nloads)):
            emit_wload(k)

        t_accfree = None
        t_hbfree = [None]
        t_pofree = [None, None]
        t_phidfree = [None] * 16
        t_yrefree = None
        t_y2free = [None, None]
        t_y2st = []
        for P in range(npass):
            c0 = P * pc
            t_az0 = pb.op("vector", lambda v: v.memset(acc[:], 0.0),
                          waits=[t_accfree])
            t_accfree = None
            t_evacs = []
            for e in range(8):
                kbase = P * 32 + e * 4
                for tg in range(TGP):
                    tp = 0
                    HB = hb[:]
                    t_gelus = []
                    for fo in range(16):
                        half = fo // 8
                        fol = fo % 8
                        kW1 = kbase + half
                        bank = fo % 4

                        def w1_mms(t, e=e, tg=tg, fo=fo, fol=fol, kW1=kW1, bank=bank,
                                   c0=c0, P=P):
                            buf = wbuf[kW1 % NWB]
                            for j in range(8):
                                last = t.matmul(
                                    phid[:, bank * 512:(bank + 1) * 512],
                                    buf[:, j * 1024 + fol * 128:j * 1024 + (fol + 1) * 128],
                                    zT[:, j * NT + c0 * 128 + tg * 512:
                                       j * NT + c0 * 128 + (tg + 1) * 512],
                                    start=(j == 0), stop=(j == 7))
                            return last
                        t_mm = pb.op("tensor", w1_mms,
                                     waits=t_wload[kW1] + [t_phidfree[bank],
                                                           t_gelus[-4] if fo >= 4 else None])
                        t_g = pb.op("scalar", lambda s, fo=fo, bank=bank, e=e, HB=HB:
                                    s.activation(HB[:, fo * 512:(fo + 1) * 512],
                                                 phid[:, bank * 512:(bank + 1) * 512],
                                                 AF.Gelu, bias=b1s[:, e * 16 + fo:e * 16 + fo + 1]),
                                    waits=[t_mm, t_b1, t_hbfree[tp]] if fo == 0 else [t_mm])
                        t_phidfree[bank] = t_g
                        t_gelus.append(t_g)
                    if e == 0 and tg < 2:
                        pass
                    t_wuse[kbase] = t_gelus[7]
                    t_wuse[kbase + 1] = t_gelus[15]
                    t_w2ev = []
                    for t128 in range(4):
                        il = tg * 4 + t128
                        pop = (tg * 4 + t128) % 2

                        def w2_mms(t, e=e, kbase=kbase, t128=t128, pop=pop, HB=HB):
                            for piece in range(2):
                                for i16 in range(16):
                                    buf = wbuf[(kbase + 2 + i16 // 8) % NWB]
                                    t.matmul(
                                        po[:, pop * 1024 + piece * 512:
                                           pop * 1024 + (piece + 1) * 512],
                                        HB[:, i16 * 512 + t128 * 128:i16 * 512 + t128 * 128 + 128],
                                        buf[:, (i16 % 8) * 1024 + piece * 512:
                                            (i16 % 8) * 1024 + (piece + 1) * 512],
                                        start=(i16 == 0), stop=False)
                                last = t.matmul(
                                    po[:, pop * 1024 + piece * 512:pop * 1024 + (piece + 1) * 512],
                                    onesrow[0:1, :],
                                    b2row[0:1, e * D + piece * 512:e * D + (piece + 1) * 512],
                                    start=False, stop=True)
                            return last
                        t_w2 = pb.op("tensor", w2_mms,
                                     waits=[t_gelus[15], t_pofree[pop], t_b2]
                                     + t_wload[kbase + 2] + t_wload[kbase + 3])

                        def evac(v, e=e, il=il, pop=pop, c0=c0):
                            i_glob = c0 + il
                            for piece in range(2):
                                last = v.scalar_tensor_tensor(
                                    acc[:, il * D + piece * 512:il * D + (piece + 1) * 512],
                                    po[:, pop * 1024 + piece * 512:
                                       pop * 1024 + (piece + 1) * 512],
                                    wts[:, i_glob * 8 + e:i_glob * 8 + e + 1],
                                    acc[:, il * D + piece * 512:il * D + (piece + 1) * 512],
                                    ALU.mult, ALU.add)
                            return last
                        t_ev = pb.op("vector", evac, waits=[t_w2, t_az0])
                        t_pofree[pop] = t_ev
                        t_evacs.append(t_ev)
                    t_hbfree[tp] = t_w2
                t_wuse[kbase + 2] = t_w2
                t_wuse[kbase + 3] = t_w2
                for k in range(kbase + NWB, min(kbase + 4 + NWB, nloads)):
                    if len(t_wload) <= k:
                        emit_wload(k)
            for il in range(pc):
                i = c0 + il
                p = il % 2
                t_yre = pb.op("sync", lambda s, i=i: s.dma_start(
                    out=yre[:], in_=y1_in[i * 128:(i + 1) * 128, :]),
                    waits=[t_yrefree], dma=True)
                t_t1 = pb.op("vector", lambda v, p=p, i=i: v.tensor_scalar(
                    y2c[:, p * D:(p + 1) * D], yre[:],
                    P0[:, i:i + 1], P1[:, i:i + 1], ALU.mult, ALU.subtract),
                    waits=[t_yre, t_y2free[p]] + t_evacs)
                t_t2 = pb.op("vector", lambda v, p=p, il=il: v.tensor_tensor(
                    y2c[:, p * D:(p + 1) * D], y2c[:, p * D:(p + 1) * D],
                    acc[:, il * D:(il + 1) * D], ALU.add), waits=[t_t1])
                t_yrefree = t_t2
                t_st = pb.op("sync", lambda s, i=i, p=p: s.dma_start(
                    out=y2[i * 128:(i + 1) * 128, :], in_=y2c[:, p * D:(p + 1) * D]),
                    waits=[t_t2], dma=True)
                t_y2free[p] = t_st
                t_y2st.append(t_st)
            t_accfree = t_y2st[-1]
        pb.op("sync", lambda s: s.nop(), waits=t_y2st[-2:])
        pb.barrier()
        ph2.close()
    glob.close()



def ceil_div(a, b):
    return -(-a // b)


def build_l1a(nqg=4, skc=None):
    if skc is None:
        skc = nqg * 8
    nc = bass.Bass("TRN2", target_bir_lowering=False)
    NQ = nqg * 512
    NS = skc * 128
    io = {
        'y2_seq': nc.dram_tensor("y2_seq", [NS, D], F32, kind="ExternalInput").ap(),
        'y2_own': nc.dram_tensor("y2_own", [NQ, D], F32, kind="ExternalInput").ap(),
        'w_q': nc.dram_tensor("w_q", [D, D], F32, kind="ExternalInput").ap(),
        'w_kv': nc.dram_tensor("w_kv", [D, 2 * D], F32, kind="ExternalInput").ap(),
        'w_out': nc.dram_tensor("w_out", [D, D], F32, kind="ExternalInput").ap(),
        'kcs': nc.dram_tensor("kcs", [128, skc * 64], F32, kind="ExternalInput").ap(),
        'qcs': nc.dram_tensor("qcs", [128, (NQ // 128) * 64], F32, kind="ExternalInput").ap(),
        'cmask': nc.dram_tensor("cmask", [128, nqg * 8 * 128], BF16, kind="ExternalInput").ap(),
        'identm': nc.dram_tensor("identm", [128, 128], BF16, kind="ExternalInput").ap(),
        'y3': nc.dram_tensor("y3", [NQ, D], F32, kind="ExternalOutput").ap(),
    }
    with nc.Block() as blk:
        pb = PB(nc)
        emit_l1a(nc, pb, blk, io, nqg, skc)
        pb.emit(blk)
    return nc


def emit_l1a(nc, pb, blk, io, nqg=4, skc=None, tag='', barrier=True, mid_hook=None, bg=None):
    if skc is None:
        skc = nqg * 8
    NQ = nqg * 512
    NS = skc * 128
    (y2_seq, y2_own, w_q, w_kv, w_out, kcs, qcs, cmask, ident_in, y3) = (
        io['y2_seq'], io['y2_own'], io['w_q'], io['w_kv'], io['w_out'],
        io['kcs'], io['qcs'], io['cmask'], io['identm'], io['y3'])
    qk_dram = nc.dram_tensor("l1qk_scr" + tag, [NS, D], BF16).ap()
    q_dram = nc.dram_tensor("q_scr" + tag, [NQ, D], BF16).ap()
    o_dram = nc.dram_tensor("o_scr" + tag, [D, NQ], BF16).ap()

    glob = Alloc(nc)
    ident = glob.sb("ident", [128, 128], BF16)
    onesrow = glob.sb("onesrow", [1, 128], BF16)
    onesf = glob.sb("onesf", [1, 64], F32)
    consts = glob.sb("consts", [128, 4], F32)
    v_aug = glob.sb("v_aug", [128, skc * 16 * 65], BF16)
    kT = glob.sb("kT", [128, 8 * NS], BF16)
    if True:
        t_ones = pb.op("gpsimd", lambda g: g.memset(onesrow[:], 1.0))
        t_onesf = pb.op("gpsimd", lambda g: g.memset(onesf[:], 1.0))
        pb.op("gpsimd", lambda g: g.memset(consts[:, 0:1], LN_EPS))
        t_lne = pb.op("gpsimd", lambda g: g.memset(consts[:, 2:3], -0.5))
        t_off = pb.op("gpsimd", lambda g: g.memset(consts[:, 1:2], -SM_OFF))
        t_id = pb.op("gpsimd", lambda g: g.dma_start(out=ident[:], in_=ident_in[:]), dma=True)

        ph = Alloc(nc)
        wkv = ph.sb("wkv", [128, 8 * 2 * D], BF16)
        css = ph.sb("css", [128, 2 * 64], F32)
        yc = ph.sb("yc", [128, 2 * D], F32)
        lnc = ph.sb("lnc", [128, 2 * D], BF16)
        lnT = ph.sb("lnT", [128, 2 * 8 * 128], BF16)
        sqs = ph.sb("sqs", [128, D], F32)
        S8 = ph.sb("S8", [128, 16], F32)
        kf = ph.sb("kf", [128, 2 * D], F32)
        tmp = ph.sb("tmp", [128, 512], F32)
        kb = ph.sb("kb", [128, 2 * D], BF16)
        pT = [ph.ps("pT0", [128, 1024], BF16), ph.ps("pT1", [128, 1024], BF16)]
        pkv = ph.ps("pkv", [128, 2048], F32)
        pq = ph.ps("pq", [128, 1024], F32)

        t_wkv = [pb.op("gpsimd", lambda g, j=j: g.dma_start(
            out=wkv[:, j * 2048:(j + 1) * 2048],
            in_=w_kv[j * 128:(j + 1) * 128, :]), dma=True) for j in range(8)]
        t_wq = []
        t_css = [None, None]

        rope_chain = [None]

        def rope_inplace(pb_, X, CS, c, waits):
            X3 = X.rearrange("p (h e) -> p h e", e=64)
            C = CS[:, 0:32]
            Sn = CS[:, 32:64]

            def b(col0, n):
                return lambda q: q
            t1 = pb_.op("vector", lambda v: v.tensor_tensor(
                tmp[:, 0:256].rearrange("p (h e) -> p h e", e=16),
                X3[:, :, 16:32],
                Sn[:, 0:16].unsqueeze(1).broadcast_to([128, 16, 16]), ALU.mult),
                waits=list(waits) + [rope_chain[0]])
            t2 = pb_.op("vector", lambda v: v.tensor_tensor(
                tmp[:, 256:512].rearrange("p (h e) -> p h e", e=16),
                X3[:, :, 0:16],
                Sn[:, 16:32].unsqueeze(1).broadcast_to([128, 16, 16]), ALU.mult),
                waits=waits)
            t3 = pb_.op("vector", lambda v: v.tensor_tensor(
                X3[:, :, 0:16], X3[:, :, 0:16],
                C[:, 0:16].unsqueeze(1).broadcast_to([128, 16, 16]), ALU.mult),
                waits=[t1, t2])
            t4 = pb_.op("vector", lambda v: v.tensor_tensor(
                X3[:, :, 16:32], X3[:, :, 16:32],
                C[:, 16:32].unsqueeze(1).broadcast_to([128, 16, 16]), ALU.mult),
                waits=[t3])
            t5 = pb_.op("vector", lambda v: v.tensor_tensor(
                X3[:, :, 0:16], X3[:, :, 0:16],
                tmp[:, 0:256].rearrange("p (h e) -> p h e", e=16), ALU.subtract),
                waits=[t4])
            t6 = pb_.op("vector", lambda v: v.tensor_tensor(
                X3[:, :, 16:32], X3[:, :, 16:32],
                tmp[:, 256:512].rearrange("p (h e) -> p h e", e=16), ALU.add),
                waits=[t5])
            rope_chain[0] = t6
            return t6

        t_kst = []
        t_qst = []
        NCH = skc + NQ // 128
        SAB = ph.sb("SAB", [128, 4 * (NCH)], F32)
        SA, SB = SAB[:, 0:NCH], SAB[:, NCH:2 * NCH]
        MEAN, RSTD = SAB[:, 2 * NCH:3 * NCH], SAB[:, 3 * NCH:4 * NCH]
        t_sfree = [None, None]
        GS = 8
        t_gstats = {}
        for i in range(NCH):
            p = i % 2
            ys = yc[:, p * D:(p + 1) * D]
            src_ = y2_seq[i * 128:(i + 1) * 128, :] if i < skc else \
                y2_own[(i - skc) * 128:(i - skc + 1) * 128, :]
            t_y = pb.op("sync", lambda s, src_=src_, ys=ys: s.dma_start(out=ys, in_=src_),
                        waits=[t_sfree[p]], dma=True)
            t_sfree[p] = emit_sums(pb, ys, SA[:, i:i + 1], SB[:, i:i + 1], sqs,
                                   [t_y, t_sfree[1 - p]])
            if i % GS == GS - 1 or i == NCH - 1:
                g = i // GS
                lo, hi = g * GS, i + 1
                t_gstats[g] = emit_stats_batch(
                    pb, SA[:, lo:hi], SB[:, lo:hi], MEAN[:, lo:hi], RSTD[:, lo:hi],
                    consts[:, 0:1], consts[:, 2:3],
                    [t_sfree[0], t_sfree[1], t_lne])
        t_yfree = None
        t_lnTfree = [None, None]
        t_lncfree = [None, None]
        t_kffree = [None, None]
        t_kbfree = [None, None]
        t_pkvfree = None
        t_pqfree = [None, None]
        t_vone = None
        for i in range(NCH):
            isq = i < skc
            io = i - skc
            p = i % 2
            ys = yc[:, p * D:(p + 1) * D]
            lns = lnc[:, p * D:(p + 1) * D]
            src_ = y2_seq[i * 128:(i + 1) * 128, :] if isq else \
                y2_own[io * 128:(io + 1) * 128, :]
            t_y = pb.op("sync", lambda s, src_=src_, ys=ys: s.dma_start(out=ys, in_=src_),
                        waits=[t_yfree] if i >= 2 else [t_sfree[0], t_sfree[1]], dma=True)
            t_ln = pb.op("vector", lambda v, ys=ys, lns=lns, i=i: v.tensor_scalar(
                lns, ys, MEAN[:, i:i + 1], RSTD[:, i:i + 1], ALU.subtract, ALU.mult),
                waits=[t_y, t_gstats[i // GS], t_lncfree[p]])
            t_yfree = t_ln
            t_lnTs = []
            t_tp = None
            for j in range(8):
                pj = (i * 8 + j) % 2
                t_tp = pb.op("tensor", lambda t, lns=lns, j=j, pj=pj: t.transpose(
                    pT[pj][:, 0:128], lns[:, j * 128:(j + 1) * 128], ident[:]),
                    waits=[t_ln, t_id, t_lnTfree[pj]])
                t_ev = pb.op("vector", lambda v, j=j, p=p, pj=pj: v.tensor_copy(
                    lnT[:, (p * 8 + j) * 128:(p * 8 + j + 1) * 128], pT[pj][:, 0:128]),
                    waits=[t_tp])
                t_lnTfree[pj] = t_ev
                t_lnTs.append(t_ev)
            t_lncfree[p] = t_tp
            if isq:
                def kv_mms(t, p=p):
                    pass
                def kv_mms(t, p=p):
                    for piece in range(4):
                        for j in range(8):
                            last = t.matmul(
                                pkv[:, piece * 512:(piece + 1) * 512],
                                lnT[:, (p * 8 + j) * 128:(p * 8 + j + 1) * 128],
                                wkv[:, j * 2048 + piece * 512:j * 2048 + (piece + 1) * 512],
                                start=(j == 0), stop=(j == 7))
                    return last
                t_kv = pb.op("tensor", kv_mms, waits=t_lnTs + t_wkv + [t_pkvfree])
                if i == skc - 1:
                    t_kv_last = t_kv
                t_vev = pb.op("vector", lambda v, i=i: v.tensor_copy(
                    v_aug[:, i * 1040:(i + 1) * 1040].rearrange("p (h e) -> p h e", e=65)[:, :, 0:64],
                    pkv[:, 1024:2048].rearrange("p (h e) -> p h e", e=64)), waits=[t_kv])
                t_vone = pb.op("vector", lambda v, i=i: v.memset(
                    v_aug[:, i * 1040 + 64:(i + 1) * 1040:65], 1.0), waits=[t_vev])
                t_kev = pb.op("vector", lambda v, p=p: v.tensor_copy(
                    kf[:, p * D:(p + 1) * D], pkv[:, 0:1024]),
                    waits=[t_kv, t_kffree[p]])
                t_pkvfree = t_vev if False else t_kev
                t_cl = pb.op("scalar", lambda g, i=i, p=p: g.dma_start(
                    out=css[:, p * 64:(p + 1) * 64], in_=kcs[:, i * 64:(i + 1) * 64]),
                    waits=[t_css[p]], dma=True)
                t_rope = rope_inplace(pb, kf[:, p * D:(p + 1) * D],
                                      css[:, p * 64:(p + 1) * 64], i, [t_kev, t_cl])
                t_css[p] = t_rope
                t_kb = pb.op("vector", lambda v, p=p: v.tensor_copy(
                    kb[:, p * D:(p + 1) * D], kf[:, p * D:(p + 1) * D]),
                    waits=[t_rope, t_kbfree[p]])
                t_kffree[p] = t_kb
                t_st = pb.op("sync", lambda s, i=i, p=p: s.dma_start(
                    out=qk_dram[i * 128:(i + 1) * 128, 0:1024], in_=kb[:, p * D:(p + 1) * D]),
                    waits=[t_kb], dma=True)
                t_kbfree[p] = t_st
                t_kst.append(t_st)
            else:
                if io == 0:
                    t_wq = [pb.op("gpsimd", lambda g, j=j: g.dma_start(
                        out=wkv[:, j * 2048:j * 2048 + 1024],
                        in_=w_q[j * 128:(j + 1) * 128, :]),
                        waits=[t_kv_last], dma=True) for j in range(8)]

                def q_mms(t, p=p):
                    for piece in range(2):
                        for j in range(8):
                            last = t.matmul(
                                pq[:, piece * 512:(piece + 1) * 512],
                                lnT[:, (p * 8 + j) * 128:(p * 8 + j + 1) * 128],
                                wkv[:, j * 2048 + piece * 512:j * 2048 + (piece + 1) * 512],
                                start=(j == 0), stop=(j == 7))
                    return last
                t_q = pb.op("tensor", q_mms, waits=t_lnTs + t_wq + [t_pqfree[0]])
                t_qev = pb.op("vector", lambda v, p=p: v.tensor_scalar_mul(
                    kf[:, p * D:(p + 1) * D], pq[:], 0.125),
                    waits=[t_q, t_kffree[p]])
                t_pqfree[0] = t_qev
                t_cl = pb.op("scalar", lambda g, io=io, p=p: g.dma_start(
                    out=css[:, p * 64:(p + 1) * 64], in_=qcs[:, io * 64:(io + 1) * 64]),
                    waits=[t_css[p]], dma=True)
                t_rope = rope_inplace(pb, kf[:, p * D:(p + 1) * D],
                                      css[:, p * 64:(p + 1) * 64], io, [t_qev, t_cl])
                t_css[p] = t_rope
                t_kb = pb.op("vector", lambda v, p=p: v.tensor_copy(
                    kb[:, p * D:(p + 1) * D], kf[:, p * D:(p + 1) * D]),
                    waits=[t_rope, t_kbfree[p]])
                t_kffree[p] = t_kb
                t_st = pb.op("sync", lambda s, io=io, p=p: s.dma_start(
                    out=q_dram[io * 128:(io + 1) * 128, :], in_=kb[:, p * D:(p + 1) * D]),
                    waits=[t_kb], dma=True)
                t_kbfree[p] = t_st
                t_qst.append(t_st)
        t_kT = [pb.op("sync", lambda s, j=j: s.dma_start(
            out=kT[:, j * NS:(j + 1) * NS],
            in_=qk_dram[:, j * 128:(j + 1) * 128], transpose=True),
            waits=t_kst, dma=True) for j in range(8)]
        pb.barrier()
        ph.close()
        if mid_hook is not None:
            tails = [pb.streams[e][-1][2] for e in ENGINES if pb.streams[e]]
            mid_hook(tails)

        ph3 = Alloc(nc)
        qTg = ph3.sb("qTg", [128, 2 * 8 * 512], BF16)
        mk = ph3.sb("mk", [128, 2 * 8 * 128], BF16)
        expt = ph3.sb("expt", [128, 2 * 1024], BF16)
        rl = ph3.sb("rl", [1, 3 * 512], F32)
        rlbc = ph3.sb("rlbc", [64, 3 * 512], F32)
        oT = ph3.sb("oT", [128, 3 * 512], BF16)
        psc = ph3.ps("psc", [128, 2 * 1024], F32)
        pav = ph3.ps("pav", [65, 2 * 512], F32)
        pbc = ph3.ps("pbc", [64, 512], F32)

        t_qT = [[None] * 8 for _ in range(2)]
        t_mkl = [None, None]
        t_qTfree = [[None] * 8 for _ in range(2)]
        t_mkfree = [None, None]
        t_pscfree = [None, None]
        t_expfree = [None, None]
        t_pavfree = [None, None]
        pb.pair_ctr = 0
        t_rlf = None
        t_bcf = None
        t_oTfree = [None, None]
        t_ost = []
        for qg in range(nqg):
            gp = qg % 2
            t_qT[gp] = [pb.op("sync", lambda s, j=j, qg=qg, gp=gp: s.dma_start(
                out=qTg[:, (gp * 8 + j) * 512:(gp * 8 + j + 1) * 512],
                in_=q_dram[qg * 512:(qg + 1) * 512, j * 128:(j + 1) * 128], transpose=True),
                waits=t_qst + [t_qTfree[gp][j]], dma=True) for j in range(8)]
            t_mkl[gp] = pb.op("scalar", lambda g, qg=qg, gp=gp: g.dma_start(
                out=mk[:, gp * 1024:(gp + 1) * 1024],
                in_=cmask[:, qg * 1024:(qg + 1) * 1024]),
                waits=[t_mkfree[gp]], dma=True)
            qg_last_avs = []

            def head_pipeline(h, qg=qg, gp=gp):
                nonlocal t_rlf, t_bcf
                u = qg * 16 + h
                pu = u % 2
                tile, row0 = h // 2, 64 * (h % 2)
                nkc = 8 * qg + 8
                t_avs = []
                for k0 in range(0, nkc, 2):
                    pr = (pb.pair_ctr) % 2
                    pb.pair_ctr += 1
                    kcs_ = [kc for kc in (k0, k0 + 1) if kc < nkc]
                    t_mms = []
                    cols = []
                    for ii, kc in enumerate(kcs_):
                        thresh = max(0, min(4, ceil_div(kc - 1 - 8 * qg, 2)))
                        ncol = (4 - thresh) * 128
                        col0 = ii * 512 + thresh * 128
                        cols.append((kc, col0, ncol, thresh))
                        t_sc = pb.op("tensor", lambda t, tile=tile, row0=row0, kc=kc,
                                     qg=qg, gp=gp, pr=pr, col0=col0, ncol=ncol,
                                     ii=ii: t.matmul(
                                         psc[:, pr * 1024 + col0:pr * 1024 + col0 + ncol],
                                         kT[row0:row0 + 64, tile * NS + kc * 128:
                                            tile * NS + (kc + 1) * 128],
                                         qTg[row0:row0 + 64,
                                             (gp * 8 + tile) * 512 + col0 - ii * 512:
                                             (gp * 8 + tile) * 512 + col0 - ii * 512 + ncol],
                                         start=True, stop=True),
                                     waits=t_kT + [t_qT[gp][tile], t_pscfree[pr]])
                        t_mms.append(t_sc)
                    spans = []
                    for (kc, col0, ncol, thresh) in cols:
                        if spans and spans[-1][1] == col0:
                            spans[-1][1] = col0 + ncol
                        else:
                            spans.append([col0, col0 + ncol])
                    t_exp = None
                    for (e0, e1) in spans:
                        t_exp = pb.op("scalar", lambda s, pr=pr, e0=e0, e1=e1: s.activation(
                            expt[:, pr * 1024 + e0:pr * 1024 + e1],
                            psc[:, pr * 1024 + e0:pr * 1024 + e1],
                            AF.Exp, bias=consts[:, 1:2]),
                            waits=t_mms + [t_off, t_expfree[pr]])
                    t_pscfree[pr] = t_exp
                    t_last = t_exp
                    for (kc, col0, ncol, thresh) in cols:
                        if kc >= 8 * qg:
                            t_m = pb.op("vector", lambda v, pr=pr, gp=gp, kc=kc, qg=qg,
                                        col0=col0: v.tensor_tensor(
                                            expt[:, pr * 1024 + col0:pr * 1024 + col0 + 128],
                                            expt[:, pr * 1024 + col0:pr * 1024 + col0 + 128],
                                            mk[:, gp * 1024 + (kc - 8 * qg) * 128:
                                               gp * 1024 + (kc - 8 * qg) * 128 + 128],
                                            ALU.mult),
                                        waits=[t_exp, t_mkl[gp]])
                        else:
                            t_m = t_exp
                        t_av = pb.op("tensor", lambda t, kc=kc, h=h, pr=pr, pu=pu,
                                     col0=col0, ncol=ncol: t.matmul(
                                         pav[:, pu * 512 + col0 % 512:
                                             pu * 512 + col0 % 512 + ncol],
                                         v_aug[:, kc * 1040 + h * 65:kc * 1040 + (h + 1) * 65],
                                         expt[:, pr * 1024 + col0:pr * 1024 + col0 + ncol],
                                         start=(kc == 0), stop=(kc == nkc - 1)),
                                     waits=[t_m, t_vone, t_pavfree[pu]] if kc == 0 else [t_m])
                        t_last = t_av
                        t_avs.append(t_av)
                    t_expfree[pr] = t_last
                    yield
                t_rl = pb.op("scalar", lambda s, pu=pu: s.activation(
                    rl[0:1, pu * 512:(pu + 1) * 512], pav[64:65, pu * 512:(pu + 1) * 512],
                    AF.Ln), waits=[t_avs[-1], t_rlf])
                t_rl = pb.op("scalar", lambda s, pu=pu: s.activation(
                    rl[0:1, pu * 512:(pu + 1) * 512], rl[0:1, pu * 512:(pu + 1) * 512],
                    AF.Exp, scale=-1.0), waits=[t_rl])
                t_bc = pb.op("tensor", lambda t, pu=pu: t.matmul(
                    pbc[:], onesf[0:1, 0:64], rl[0:1, pu * 512:(pu + 1) * 512],
                    start=True, stop=True), waits=[t_rl, t_bcf, t_onesf])
                t_rlf = t_bc
                t_bcev = pb.op("vector", lambda v, pu=pu: v.tensor_copy(
                    rlbc[:, pu * 512:(pu + 1) * 512], pbc[:]), waits=[t_bc])
                t_bcf = t_bcev
                t_fin = pb.op("vector", lambda v, pu=pu: v.scalar_tensor_tensor(
                    oT[row0:row0 + 64, pu * 512:(pu + 1) * 512] if False else
                    oT[0:64, pu * 512:(pu + 1) * 512],
                    pav[0:64, pu * 512:(pu + 1) * 512], 1.0,
                    rlbc[:, pu * 512:(pu + 1) * 512], ALU.mult, ALU.mult),
                    waits=[t_bcev, t_oTfree[pu]])
                t_pavfree[pu] = t_fin
                t_o = pb.op("sync", lambda s, h=h, qg=qg, pu=pu: s.dma_start(
                    out=o_dram[h * 64:(h + 1) * 64, qg * 512:(qg + 1) * 512],
                    in_=oT[0:64, pu * 512:(pu + 1) * 512]), waits=[t_fin], dma=True)
                t_oTfree[pu] = t_o
                t_ost.append(t_o)
                qg_last_avs.append(t_avs[-1])
                if bg is not None:
                    next(bg, None)

            for h0 in range(0, 16, 2):
                gens = [head_pipeline(h0), head_pipeline(h0 + 1)]
                while gens:
                    gens = [g for g in gens if next(g, 'done') != 'done']
            for j in range(8):
                t_qTfree[gp][j] = qg_last_avs[-1]
            t_mkfree[gp] = qg_last_avs[-1]
        pb.barrier()
        ph3.close()

        ph4 = Alloc(nc)
        wos = ph4.sb("wos", [128, 8 * D], BF16)
        oTl = ph4.sb("oTl", [128, 2 * 8 * 128], BF16)
        yre = ph4.sb("yre", [128, 2 * D], F32)
        y3c = ph4.sb("y3c", [128, 2 * D], F32)
        pproj = ph4.ps("pproj", [128, 1024], F32)
        t_wo = [pb.op("gpsimd", lambda g, j=j: g.dma_start(
            out=wos[:, j * D:(j + 1) * D], in_=w_out[j * 128:(j + 1) * 128, :]),
            dma=True) for j in range(8)]
        t_y3 = [None, None]
        t_oTl = [None, None]
        t_pprojfree = None
        for qc in range(NQ // 128):
            p = qc % 2
            t_ol = [pb.op("scalar", lambda g, j=j, qc=qc, p=p: g.dma_start(
                out=oTl[:, (p * 8 + j) * 128:(p * 8 + j + 1) * 128],
                in_=o_dram[j * 128:(j + 1) * 128, qc * 128:(qc + 1) * 128]),
                waits=t_ost + [t_oTl[p]], dma=True) for j in range(8)]
            t_xr = pb.op("sync", lambda s, qc=qc, p=p: s.dma_start(
                out=yre[:, p * D:(p + 1) * D], in_=y2_own[qc * 128:(qc + 1) * 128, :]),
                waits=[t_y3[p]], dma=True)

            def proj_mms(t, p=p):
                for piece in range(2):
                    for j in range(8):
                        last = t.matmul(pproj[:, piece * 512:(piece + 1) * 512],
                                        oTl[:, (p * 8 + j) * 128:(p * 8 + j + 1) * 128],
                                        wos[:, j * D + piece * 512:j * D + (piece + 1) * 512],
                                        start=(j == 0), stop=(j == 7))
                return last
            t_proj = pb.op("tensor", proj_mms, waits=t_ol + t_wo + [t_pprojfree])
            t_y3c = pb.op("vector", lambda v, p=p: v.scalar_tensor_tensor(
                y3c[:, p * D:(p + 1) * D], pproj[:], 1.0,
                yre[:, p * D:(p + 1) * D], ALU.mult, ALU.add), waits=[t_proj, t_xr])
            t_pprojfree = t_y3c
            t_st = pb.op("sync", lambda s, qc=qc, p=p: s.dma_start(
                out=y3[qc * 128:(qc + 1) * 128, :], in_=y3c[:, p * D:(p + 1) * D]),
                waits=[t_y3c], dma=True)
            t_y3[p] = t_st
            t_oTl[p] = t_proj
        pb.op("sync", lambda s: s.nop(), waits=[t_y3[0], t_y3[1]])
        if barrier:
            pb.barrier()
        ph4.close()
    glob.close()


def l1a_rope_tables(positions):
    ROT = 32
    inv = 1.0 / (10000.0 ** (np.arange(0, ROT, 2) / ROT))
    f = positions[:, None] * inv[None, :]
    fr = np.concatenate([f, f], axis=-1)
    cs = np.concatenate([np.cos(fr), np.sin(fr)], axis=-1)
    n = len(positions) // 128
    return f32(cs.reshape(n, 128, 64).transpose(1, 0, 2).reshape(128, n * 64))


def l1a_masks(h, nqg=4):
    out = np.zeros((nqg, 8, 128, 128), dtype=np.float32)
    for qg in range(nqg):
        for dk in range(8):
            kc = 8 * qg + dk
            thresh = max(0, min(4, ceil_div(kc - 1 - 8 * qg, 2)))
            j0 = thresh
            gqc = 8 * qg + 2 * j0 + h
            kpos = kc * 128 + np.arange(128)[:, None]
            qpos = gqc * 128 + np.arange(128)[None, :]
            out[qg, dk] = (kpos <= qpos).astype(np.float32)
            for j in range(thresh + 1, 4):
                gq = 8 * qg + 2 * j + h
                assert kc * 128 + 127 <= gq * 128, (qg, kc, j, h)
    return bf(out.reshape(nqg * 8, 128, 128).transpose(1, 0, 2).reshape(128, nqg * 8 * 128))


def own_rows(h, noc=16):
    idx = []
    for o in range(noc):
        g = 2 * o + h
        idx.extend(range(g * 128, (g + 1) * 128))
    return np.array(idx)



def emit_quant(pb, eng, X, W, scr, waits, out=None):
    NB = W // 16
    bm, sc, rsc = scr['bm'][:, 0:NB], scr['bm'][:, NB:2 * NB], scr['bm'][:, 2 * NB:3 * NB]
    ax, idx = scr['ax'][:, 0:W], scr['idx'][:, 0:W]
    s2 = scr['s2'][:, 0:W]
    dst = out if out is not None else X
    dst3 = dst.rearrange("p (n b) -> p n b", b=16)
    ax3 = ax.rearrange("p (n b) -> p n b", b=16)
    idx3 = idx.rearrange("p (n b) -> p n b", b=16)
    mids = [0.25, 0.75, 1.25, 1.75, 2.5, 3.5, 5.0]
    ea = pb.op("scalar", lambda s: s.activation(ax, X, AF.Abs), waits=waits)
    es = pb.op("scalar", lambda s: s.activation(s2, X, AF.Sign), waits=waits)
    e = pb.op("vector", lambda v: v.tensor_reduce(bm, ax3, AX.X, ALU.max), waits=[ea])
    e = pb.op("vector", lambda v: v.tensor_scalar(sc, bm, 1.0 / 6.0, 1e-12,
                                                  ALU.mult, ALU.max), waits=[e])
    er = pb.op("vector", lambda v: v.reciprocal(rsc, sc), waits=[e])
    e = pb.op("vector", lambda v: v.tensor_tensor(
        ax3, ax3, rsc.unsqueeze(2).broadcast_to([128, NB, 16]), ALU.mult), waits=[er])
    e = pb.op("vector", lambda v: v.tensor_scalar(idx, ax, mids[0], None, ALU.is_gt),
              waits=[e])
    for m in mids[1:]:
        e = pb.op("vector", lambda v, m=m: v.scalar_tensor_tensor(
            idx, ax, m, idx, ALU.is_gt, ALU.add), waits=[e])
    e1 = pb.op("scalar", lambda s: s.activation(ax, idx, AF.Relu, bias=scr['c4']),
               waits=[e, scr['ctok']])
    e3 = pb.op("vector", lambda v: v.tensor_scalar_mul(idx, idx, 0.5), waits=[e1])
    e4 = pb.op("vector", lambda v: v.scalar_tensor_tensor(idx, ax, 0.5, idx,
                                                          ALU.mult, ALU.add), waits=[e3])
    e2 = pb.op("scalar", lambda s: s.activation(ax, ax, AF.Relu, bias=scr['c2']),
               waits=[e4])
    e5 = pb.op("vector", lambda v: v.tensor_tensor(idx, idx, ax, ALU.add), waits=[e2])
    e6 = pb.op("vector", lambda v: v.tensor_tensor(idx, idx, s2, ALU.mult), waits=[e5, es])
    e7 = pb.op("vector", lambda v: v.tensor_tensor(
        dst3, idx3, sc.unsqueeze(2).broadcast_to([128, NB, 16]), ALU.mult), waits=[e6])
    return e7


def build_l1f(ntc=16):
    nc = bass.Bass("TRN2", target_bir_lowering=False)
    NT = ntc * 128
    I = 2816
    io = {
        'y3_in': nc.dram_tensor("y3_in", [NT, D], F32, kind="ExternalInput").ap(),
        'w_gate': nc.dram_tensor("w_gate", [I, D], F32, kind="ExternalInput").ap(),
        'w_up': nc.dram_tensor("w_up", [I, D], F32, kind="ExternalInput").ap(),
        'w_down': nc.dram_tensor("w_down", [D, I], F32, kind="ExternalInput").ap(),
        'identm': nc.dram_tensor("identm", [128, 128], BF16, kind="ExternalInput").ap(),
        'y4': nc.dram_tensor("y4", [NT, D], F32, kind="ExternalOutput").ap(),
    }
    with nc.Block() as blk:
        pb = PB(nc)
        emit_l1f(nc, pb, blk, io, ntc)
        pb.emit(blk)
    return nc


def gen_l1f_weights(nc, pb, io, scr, wq_dram, t_wst, tag='', waits0=()):
    I = 2816
    NI = I // 128
    w_gate, w_up, w_down = io['w_gate'], io['w_up'], io['w_down']
    wc, wb = scr['wc'], scr['wb']
    t_wqchain = None
    t_wcfree = [None, None]
    t_wbfree = [None, None]
    jobs = [(w_gate, 0, NI, 1024, 0), (w_up, I, NI, 1024, 0),
            (w_down, 2 * I, 8, 1408, 0), (w_down, 2 * I, 8, 1408, 1408)]
    for (wsrc, row0, nch, wid, co) in jobs:
        for ch in range(nch):
            p = ch % 2
            WC = wc[:, p * 1408:p * 1408 + wid]
            t_l = pb.op("sync", lambda s, wsrc=wsrc, ch=ch, WC=WC, wid=wid, co=co:
                        s.dma_start(out=WC, in_=wsrc[ch * 128:(ch + 1) * 128, co:co + wid]),
                        waits=[t_wcfree[p]] + list(waits0), dma=True)
            t_q = emit_quant(pb, "vector", WC, wid, scr,
                             [t_l, t_wqchain, t_wbfree[p]],
                             out=wb[:, p * 1408:p * 1408 + wid])
            t_wqchain = t_q
            t_wcfree[p] = t_q
            t_s = pb.op("sync", lambda s, row0=row0, ch=ch, p=p, wid=wid, co=co:
                        s.dma_start(out=wq_dram[row0 + ch * 128:row0 + (ch + 1) * 128,
                                                co:co + wid],
                                    in_=wb[:, p * 1408:p * 1408 + wid]),
                        waits=[t_q], dma=True)
            t_wbfree[p] = t_s
            t_wst.append(t_s)
            yield


def emit_l1f_weights(nc, pb, io, scr, wq_dram, tag='', waits0=()):
    t_wst = []
    for _ in gen_l1f_weights(nc, pb, io, scr, wq_dram, t_wst, tag, waits0):
        pass
    return t_wst


def emit_l1f(nc, pb, blk, io, ntc=16, tag='', t_wst=None, wq_dram=None):
    NT = ntc * 128
    I = 2816
    NI = I // 128
    (y3_in, w_gate, w_up, w_down, ident_in, y4) = (
        io['y3_in'], io['w_gate'], io['w_up'], io['w_down'], io['identm'], io['y4'])
    if wq_dram is None:
        wq_dram = nc.dram_tensor("wq_scr" + tag, [2 * I + D, max(D, I)], BF16).ap()

    glob = Alloc(nc)
    ident = glob.sb("ident", [128, 128], BF16)
    consts = glob.sb("consts", [128, 4], F32)
    wgT = glob.sb("wgT", [128, 8 * I], BF16)
    wuT = glob.sb("wuT", [128, 8 * I], BF16)
    wdT = glob.sb("wdT", [128, NI * D], BF16)
    if True:
        pb.op("gpsimd", lambda g: g.memset(consts[:, 0:1], LN_EPS))
        t_lne = pb.op("gpsimd", lambda g: g.memset(consts[:, 2:3], -0.5))
        t_id = pb.op("gpsimd", lambda g: g.dma_start(out=ident[:], in_=ident_in[:]), dma=True)

        if t_wst is None:
            phw = Alloc(nc)
            scrw = {
                'wc': phw.sb("wc", [128, 2 * 1408], F32),
                'wb': phw.sb("wb", [128, 2 * 1408], BF16),
                'bm': phw.sb("qbm", [128, 3 * 88], F32),
                'ax': phw.sb("qax", [128, 1408], F32),
                'idx': phw.sb("qidx", [128, 1408], F32),
                's2': phw.sb("qs2", [128, 1408], F32),
            }
            cq = phw.sb("qc", [128, 2], F32)
            scrw['c4'] = cq[:, 0:1]
            scrw['c2'] = cq[:, 1:2]
            pb.op("gpsimd", lambda g: g.memset(cq[:, 0:1], -4.0))
            scrw['ctok'] = pb.op("gpsimd", lambda g: g.memset(cq[:, 1:2], -2.0))
            t_wst = emit_l1f_weights(nc, pb, io, scrw, wq_dram, tag)
            pb.barrier()
            phw.close()
        t_wgT = [pb.op("sync", lambda s, j=j: s.dma_start(
            out=wgT[:, j * I:(j + 1) * I],
            in_=wq_dram[0:I, j * 128:(j + 1) * 128], transpose=True),
            waits=t_wst, dma=True) for j in range(8)]
        t_wuT = [pb.op("sync", lambda s, j=j: s.dma_start(
            out=wuT[:, j * I:(j + 1) * I],
            in_=wq_dram[I:2 * I, j * 128:(j + 1) * 128], transpose=True),
            waits=t_wst, dma=True) for j in range(8)]
        t_wdT = [pb.op("sync", lambda s, j=j: s.dma_start(
            out=wdT[:, j * D:(j + 1) * D],
            in_=wq_dram[2 * I:2 * I + D, j * 128:(j + 1) * 128], transpose=True),
            waits=t_wst, dma=True) for j in range(NI)]

        pht = Alloc(nc)
        yc = pht.sb("yc", [128, D], F32)
        sqs = pht.sb("sqs", [128, D], F32)
        S8 = pht.sb("S8", [128, 16], F32)
        xf = pht.sb("xf", [128, D], F32)
        xqT = pht.sb("xqT", [128, 8 * 128], BF16)
        hq = pht.sb("hq", [128, I], F32)
        hqb = pht.sb("hqb", [128, I], BF16)
        qbm2 = pht.sb("qbm2", [128, 3 * (I // 16)], F32)
        qax = pht.sb("qax", [128, I], F32)
        qidx2 = pht.sb("qidx2", [128, I], F32)
        qs2 = pht.sb("qs2", [128, I], F32)
        xbm = pht.sb("xbm", [128, 3 * 64], F32)
        xidx = pht.sb("xidx", [128, D], F32)
        xs2 = pht.sb("xs2", [128, D], F32)
        cq2 = pht.sb("cq2", [128, 2], F32)
        pT = [pht.ps("pT0", [128, 1024], BF16), pht.ps("pT1", [128, 1024], BF16)]
        pgu = pht.ps("pgu", [128, 2048], F32)
        pd = pht.ps("pd", [128, 1024], F32)

        scr2 = {'bm': qbm2, 'ax': qax, 'idx': qidx2, 's2': qs2,
                'c4': cq2[:, 0:1], 'c2': cq2[:, 1:2]}
        pb.op("gpsimd", lambda g: g.memset(cq2[:, 0:1], -4.0))
        scr2['ctok'] = pb.op("gpsimd", lambda g: g.memset(cq2[:, 1:2], -2.0))
        scrx = {'bm': xbm, 'ax': sqs, 'idx': xidx, 's2': xs2,
                'c4': cq2[:, 0:1], 'c2': cq2[:, 1:2], 'ctok': scr2['ctok']}
        segs = [(0, 1024), (1024, 1024), (2048, 768)]
        SABt = pht.sb("SABt", [128, 4 * ntc], F32)
        SAt, SBt = SABt[:, 0:ntc], SABt[:, ntc:2 * ntc]
        MEANt, RSTDt = SABt[:, 2 * ntc:3 * ntc], SABt[:, 3 * ntc:4 * ntc]
        t_sfree = [None, None]
        for i in range(ntc):
            p = 0
            ys = yc[:]
            t_y = pb.op("sync", lambda s, i=i, ys=ys: s.dma_start(
                out=ys, in_=y3_in[i * 128:(i + 1) * 128, :]),
                waits=[t_sfree[0]], dma=True)
            t_sfree[0] = emit_sums(pb, ys, SAt[:, i:i + 1], SBt[:, i:i + 1], sqs,
                                   [t_y, t_sfree[0]])
        t_stats = emit_stats_batch(pb, SAt, SBt, MEANt, RSTDt, consts[:, 0:1],
                                   consts[:, 2:3], [t_sfree[0], t_sfree[1], t_lne])
        t_yfree = None
        t_qchain = None
        t_xchain = None
        t_xqTfree = [None]
        t_hqT_free = None
        t_lnTfree = [None, None]
        t_xffree = [None]
        t_hqbfree = None
        t_hqfree = [None]
        t_pgufree = [None, None]
        t_pdfree = None
        t_y4free = [None, None]
        t_y4st = []
        for i in range(ntc):
            p = i % 2
            ys = yc[:]
            XF = xf[:]
            t_y = pb.op("sync", lambda s, i=i, ys=ys: s.dma_start(
                out=ys, in_=y3_in[i * 128:(i + 1) * 128, :]),
                waits=[t_yfree] if i >= 1 else [t_sfree[0]], dma=True)
            t_ln = pb.op("vector", lambda v, ys=ys, XF=XF, i=i: v.tensor_scalar(
                XF, ys, MEANt[:, i:i + 1], RSTDt[:, i:i + 1], ALU.subtract, ALU.mult),
                waits=[t_y, t_stats, t_xffree[0]])
            t_xq = emit_quant(pb, "vector", XF, D, scrx, [t_ln, t_xchain, t_hqbfree],
                              out=hqb[:, 0:D])
            t_xchain = t_xq
            t_cb = t_xq
            t_xqTs = []
            t_tp = None
            for j in range(8):
                pj = (i * 8 + j) % 2
                t_tp = pb.op("tensor", lambda t, j=j, pj=pj: t.transpose(
                    pT[pj][:, 0:128], hqb[:, j * 128:(j + 1) * 128], ident[:]),
                    waits=[t_cb, t_id, t_lnTfree[pj]])
                t_ev = pb.op("vector", lambda v, j=j, pj=pj: v.tensor_copy(
                    xqT[:, j * 128:(j + 1) * 128], pT[pj][:, 0:128]),
                    waits=[t_tp] + ([t_xqTfree[0]] if j == 0 else []))
                t_lnTfree[pj] = t_ev
                t_xqTs.append(t_ev)
            t_hqbfree = t_tp
            t_hqs = []
            for (s0, sw) in segs:
                sp = (s0 // 1024) % 2

                def gu_mms(t, s0=s0, sw=sw, W_=None, off=0):
                    for half in range(ceil_div(sw, 512)):
                        cw = min(512, sw - half * 512)
                        for j in range(8):
                            last = t.matmul(
                                pgu[:, off + half * 512:off + half * 512 + cw],
                                xqT[:, j * 128:(j + 1) * 128],
                                W_[:, j * I + s0 + half * 512:j * I + s0 + half * 512 + cw],
                                start=(j == 0), stop=(j == 7))
                    return last
                t_g = pb.op("tensor", lambda t, f=gu_mms: f(t, W_=wgT, off=0),
                            waits=t_xqTs + t_wgT + [t_pgufree[0]])
                t_u = pb.op("tensor", lambda t, f=gu_mms: f(t, W_=wuT, off=1024),
                            waits=t_xqTs + t_wuT + [t_pgufree[1]])
                t_xqTfree[0] = t_u
                t_si = pb.op("scalar", lambda s, s0=s0, sw=sw: s.activation(
                    hq[:, s0:s0 + sw], pgu[:, 0:sw], AF.Silu),
                    waits=[t_g, t_hqfree[0]])
                t_pgufree[0] = t_si
                t_h = pb.op("vector", lambda v, s0=s0, sw=sw: v.scalar_tensor_tensor(
                    hq[:, s0:s0 + sw], pgu[:, 1024:1024 + sw], 1.0,
                    hq[:, s0:s0 + sw], ALU.mult, ALU.mult),
                    waits=[t_si, t_u])
                t_pgufree[1] = t_h
                t_hqs.append(t_h)
            t_qh = emit_quant(pb, "vector", hq[:, 0:I], I, scr2,
                              t_hqs + [t_qchain, t_hqbfree], out=hqb[:, 0:I])
            t_qchain = t_qh
            t_cbh = t_qh
            t_hqTs = []
            for j in range(NI):
                pj = j % 2
                t_tp2 = pb.op("tensor", lambda t, j=j, pj=pj: t.transpose(
                    pT[pj][:, 128:256], hqb[:, j * 128:(j + 1) * 128], ident[:]),
                    waits=[t_cbh, t_lnTfree[pj]])
                t_ev2 = pb.op("vector", lambda v, j=j, pj=pj: v.tensor_copy(
                    hq[:].bitcast(BF16)[:, j * 128:(j + 1) * 128],
                    pT[pj][:, 128:256]),
                    waits=[t_tp2] + ([t_hqT_free] if j == 0 else []))
                t_lnTfree[pj] = t_ev2
                t_hqTs.append(t_ev2)
            t_hqbfree = t_tp2
            def d_mms(t):
                for piece in range(2):
                    for j in range(NI):
                        last = t.matmul(
                            pd[:, piece * 512:(piece + 1) * 512],
                            hq[:].bitcast(BF16)[:, j * 128:(j + 1) * 128],
                            wdT[:, j * D + piece * 512:j * D + (piece + 1) * 512],
                            start=(j == 0), stop=(j == NI - 1))
                return last
            t_d = pb.op("tensor", d_mms, waits=t_hqTs + t_wdT + [t_pdfree])
            t_hqT_free = t_d
            t_hqfree[0] = t_d
            t_y4 = pb.op("vector", lambda v, ys=ys: v.scalar_tensor_tensor(
                xf[:], pd[:], 1.0, ys, ALU.mult, ALU.add),
                waits=[t_d, t_y4free[p], t_xffree[0]])
            t_pdfree = t_y4
            t_yfree = t_y4
            t_st = pb.op("sync", lambda s, i=i: s.dma_start(
                out=y4[i * 128:(i + 1) * 128, :], in_=xf[:]),
                waits=[t_y4], dma=True)
            t_y4free[p] = t_st
            t_xffree[0] = t_st
            t_y4st.append(t_st)
        pb.op("sync", lambda s: s.nop(), waits=t_y4st[-2:])
        pb.barrier()
        pht.close()
    glob.close()





def emit_gather_rows(nc, pb, src_dram, dst_dram, ownidx_in, nch):
    al = Alloc(nc)
    idxs = al.sb("gidx", [128, nch], I32)
    buf = al.sb("gbuf", [128, 2 * D], F32)
    t_i = pb.op("gpsimd", lambda g: g.dma_start(out=idxs[:], in_=ownidx_in[:]), dma=True)
    t_pf = [None, None]
    for o in range(nch):
        p = o % 2
        t_g = pb.op("gpsimd", lambda g, o=o, p=p: g.indirect_dma_start(
            buf[:, p * D:(p + 1) * D], None, src_dram[:],
            IndirectOffsetOnAxis(idxs[:, o:o + 1], 0)),
            waits=[t_i, t_pf[p]], dma=True)
        t_s = pb.op("sync", lambda s, o=o, p=p: s.dma_start(
            out=dst_dram[o * 128:(o + 1) * 128, :], in_=buf[:, p * D:(p + 1) * D]),
            waits=[t_g], dma=True)
        t_pf[p] = t_s
    pb.barrier()
    al.close()


def build_fused():
    nc = bass.Bass("TRN2", target_bir_lowering=False)
    NQC = 32
    NKC = NQC + 2
    I = 2816
    ext = lambda n, sh, dt=F32: nc.dram_tensor(n, sh, dt, kind="ExternalInput").ap()
    io = {
        'x_pad': ext("x_pad", [NKC * 128, D]),
        'qkv_wT': ext("qkv_wT", [D, 3 * D]),
        'qkv_b': ext("qkv_b", [1, 3 * D]),
        'out_wT': ext("out_wT", [D, D]),
        'out_b': ext("out_b", [1, D]),
        'band_mask': ext("band_mask", [128, NQC * 3 * 128], BF16),
        'identm': ext("identm", [128, 128], BF16),
        'g_bc': ext("g_bc", [128, D]),
        'b_bc': ext("b_bc", [128, D]),
        'rW': ext("rW", [128, 8 * 8]),
        'rb': ext("rb", [1, 8]),
        'W1': ext("W1", [8, D, 2048]),
        'b1t': ext("b1t", [128, 8 * 16]),
        'W2': ext("W2", [8, 2048, D]),
        'b2r': ext("b2r", [1, 8 * D]),
        'w_q': ext("w_q", [D, D]),
        'w_kv': ext("w_kv", [D, 2 * D]),
        'w_out': ext("w_out", [D, D]),
        'kcs': ext("kcs", [128, 32 * 64]),
        'qcs': ext("qcs", [128, 16 * 64]),
        'cmask': ext("cmask", [128, 4 * 8 * 128], BF16),
        'ownidx': ext("ownidx", [128, 16], I32),
        'w_gate': ext("w_gate", [I, D]),
        'w_up': ext("w_up", [I, D]),
        'w_down': ext("w_down", [D, I]),
        'y4': nc.dram_tensor("y4", [2048, D], F32, kind="ExternalOutput").ap(),
        'y1': nc.dram_tensor("y1_scr", [4096, D], F32).ap(),
        'y2': nc.dram_tensor("y2_scr", [4096, D], F32).ap(),
        'y2own': nc.dram_tensor("y2own_scr", [2048, D], F32).ap(),
        'y3': nc.dram_tensor("y3_scr", [2048, D], F32).ap(),
    }
    with nc.Block() as blk:
        pb = PB(nc)
        emit_l0a(nc, pb, blk, io, NQC)
        io2 = dict(io)
        io2['y1_in'] = io['y1']
        emit_l0f(nc, pb, blk, io2, NQC, npass=8)
        emit_gather_rows(nc, pb, io['y2'], io['y2own'], io['ownidx'], 16)
        I = 2816
        wq_dram = nc.dram_tensor("wq_scr", [2 * I + D, max(D, I)], BF16).ap()
        phw = Alloc(nc)
        scrw = {}
        box = {}

        def mid_hook(tails):
            scrw['wc'] = phw.sb("wc", [128, 2 * 1408], F32, side='right')
            scrw['wb'] = phw.sb("wb", [128, 2 * 1408], BF16, side='right')
            scrw['bm'] = phw.sb("qbm", [128, 3 * 88], F32, side='right')
            scrw['ax'] = phw.sb("qax", [128, 1408], F32, side='right')
            scrw['idx'] = phw.sb("qidx", [128, 1408], F32, side='right')
            scrw['s2'] = phw.sb("qs2", [128, 1408], F32, side='right')
            cq = phw.sb("qc", [128, 2], F32, side='right')
            scrw['c4'] = cq[:, 0:1]
            scrw['c2'] = cq[:, 1:2]
            t1 = pb.op("gpsimd", lambda g: g.memset(cq[:, 0:1], -4.0))
            scrw['ctok'] = pb.op("gpsimd", lambda g: g.memset(cq[:, 1:2], -2.0))
            box['tails'] = tails
        io3 = dict(io)
        io3['y2_seq'] = io['y2']
        io3['y2_own'] = io['y2own']
        io4 = dict(io)
        io4['y3_in'] = io['y3']
        t_wst = []
        bgen = gen_l1f_weights(nc, pb, io4, scrw, wq_dram, t_wst)
        emit_l1a(nc, pb, blk, io3, nqg=4, skc=32, barrier=False, mid_hook=mid_hook,
                 bg=bgen)
        for _ in bgen:
            pass
        pb.barrier()
        phw.close()
        emit_l1f(nc, pb, blk, io4, 16, t_wst=t_wst, wq_dram=wq_dram)
        for e in ENGINES:
            n = len(pb.streams[e])
            print(f"fused stream {e}: {n} ops")
        pb.emit(blk)
    return nc


_programs = {}


def _get_prog(name):
    if name not in _programs:
        if name == 'l0a':
            _programs[name] = build_l0a(16)
        elif name == 'l0f':
            _programs[name] = build_l0f(16, 2)
        elif name == 'l1a':
            _programs[name] = build_l1a(4)
        elif name == 'l1f':
            _programs[name] = build_l1f(16)
    return _programs[name]


_IDENT = None


def kernel_unfused(L0_attn_x, L0_attn_qkv_w, L0_attn_qkv_b, L0_attn_out_w, L0_attn_out_b,
           L0_ffn_ln_g, L0_ffn_ln_b, L0_ffn_rW, L0_ffn_rb, L0_ffn_W1, L0_ffn_b1,
           L0_ffn_W2, L0_ffn_b2, L1_attn_w_q, L1_attn_w_kv, L1_attn_w_out,
           L1_ffn_w_gate, L1_ffn_w_up, L1_ffn_w_down):
    global _IDENT
    if _IDENT is None:
        _IDENT = bf(np.eye(128, dtype=np.float32))
    x = f32(L0_attn_x)
    B, N, Dm = x.shape

    nc = _get_prog('l0a')
    qkv_wT = f32(L0_attn_qkv_w.T)
    out_wT = f32(L0_attn_out_w.T)
    qkv_b = f32(np.asarray(L0_attn_qkv_b)[None, :])
    out_b = f32(np.asarray(L0_attn_out_b)[None, :])
    in_maps = []
    for core in range(8):
        b, h = core // 2, core % 2
        d = l0a_inputs(x, core)
        d.update({"qkv_wT": qkv_wT, "qkv_b": qkv_b, "out_wT": out_wT, "out_b": out_b,
                  "band_mask": l0a_masks(16, h * 2048), "identm": _IDENT})
        in_maps.append(d)
    r1 = run_bass_kernel_spmd(nc, in_maps, core_ids=list(range(8)))

    nc = _get_prog('l0f')
    g_bc = f32(np.broadcast_to(np.asarray(L0_ffn_ln_g), (128, Dm)))
    b_bc = f32(np.broadcast_to(np.asarray(L0_ffn_ln_b), (128, Dm)))
    rW = f32(np.asarray(L0_ffn_rW).reshape(8, 128, 8).transpose(1, 0, 2).reshape(128, 64))
    rb = f32(np.asarray(L0_ffn_rb)[None, :])
    W1 = f32(L0_ffn_W1)
    b1t = f32(np.asarray(L0_ffn_b1).reshape(8, 16, 128).transpose(2, 0, 1).reshape(128, 128))
    W2 = f32(L0_ffn_W2)
    b2r = f32(np.asarray(L0_ffn_b2).reshape(1, 8 * 1024))
    in_maps = []
    for core in range(8):
        in_maps.append({"y1_in": r1.results[core]["y1"], "g_bc": g_bc, "b_bc": b_bc,
                        "rW": rW, "rb": rb, "W1": W1, "b1t": b1t, "W2": W2, "b2r": b2r,
                        "identm": _IDENT})
    r2 = run_bass_kernel_spmd(nc, in_maps, core_ids=list(range(8)))

    y2 = np.empty((B, N, Dm), dtype=np.float32)
    for core in range(8):
        b, h = core // 2, core % 2
        y2[b, h * 2048:(h + 1) * 2048] = r2.results[core]["y2"]

    nc = _get_prog('l1a')
    w_q = f32(L1_attn_w_q)
    w_kv = f32(L1_attn_w_kv)
    w_out = f32(L1_attn_w_out)
    kcs = l1a_rope_tables(np.arange(N).astype(np.float64))
    qcs_h = [l1a_rope_tables(own_rows(h).astype(np.float64)) for h in (0, 1)]
    msk_h = [l1a_masks(h) for h in (0, 1)]
    in_maps = []
    for core in range(8):
        b, h = core // 2, core % 2
        in_maps.append({"y2_seq": y2[b], "y2_own": f32(y2[b][own_rows(h)]),
                        "w_q": w_q, "w_kv": w_kv, "w_out": w_out, "kcs": kcs,
                        "qcs": qcs_h[h], "cmask": msk_h[h], "identm": _IDENT})
    r3 = run_bass_kernel_spmd(nc, in_maps, core_ids=list(range(8)))

    nc = _get_prog('l1f')
    wg = f32(L1_ffn_w_gate)
    wu = f32(L1_ffn_w_up)
    wd = f32(L1_ffn_w_down)
    in_maps = []
    for core in range(8):
        in_maps.append({"y3_in": r3.results[core]["y3"], "w_gate": wg, "w_up": wu,
                        "w_down": wd, "identm": _IDENT})
    r4 = run_bass_kernel_spmd(nc, in_maps, core_ids=list(range(8)))

    out = np.empty((B, N, Dm), dtype=np.float32)
    for core in range(8):
        b, h = core // 2, core % 2
        out[b][own_rows(h)] = r4.results[core]["y4"]
    return out


def kernel(L0_attn_x, L0_attn_qkv_w, L0_attn_qkv_b, L0_attn_out_w, L0_attn_out_b,
           L0_ffn_ln_g, L0_ffn_ln_b, L0_ffn_rW, L0_ffn_rb, L0_ffn_W1, L0_ffn_b1,
           L0_ffn_W2, L0_ffn_b2, L1_attn_w_q, L1_attn_w_kv, L1_attn_w_out,
           L1_ffn_w_gate, L1_ffn_w_up, L1_ffn_w_down):
    global _IDENT
    if _IDENT is None:
        _IDENT = bf(np.eye(128, dtype=np.float32))
    if 'fused' not in _programs:
        _programs['fused'] = build_fused()
    nc = _programs['fused']
    x = f32(L0_attn_x)
    B, N, Dm = x.shape

    shared = {
        "qkv_wT": f32(np.asarray(L0_attn_qkv_w).T),
        "qkv_b": f32(np.asarray(L0_attn_qkv_b)[None, :]),
        "out_wT": f32(np.asarray(L0_attn_out_w).T),
        "out_b": f32(np.asarray(L0_attn_out_b)[None, :]),
        "band_mask": l0a_masks(32, 0),
        "identm": _IDENT,
        "g_bc": f32(np.broadcast_to(np.asarray(L0_ffn_ln_g), (128, Dm))),
        "b_bc": f32(np.broadcast_to(np.asarray(L0_ffn_ln_b), (128, Dm))),
        "rW": f32(np.asarray(L0_ffn_rW).reshape(8, 128, 8).transpose(1, 0, 2).reshape(128, 64)),
        "rb": f32(np.asarray(L0_ffn_rb)[None, :]),
        "W1": f32(L0_ffn_W1),
        "b1t": f32(np.asarray(L0_ffn_b1).reshape(8, 16, 128).transpose(2, 0, 1).reshape(128, 128)),
        "W2": f32(L0_ffn_W2),
        "b2r": f32(np.asarray(L0_ffn_b2).reshape(1, 8 * 1024)),
        "w_q": f32(L1_attn_w_q),
        "w_kv": f32(L1_attn_w_kv),
        "w_out": f32(L1_attn_w_out),
        "kcs": l1a_rope_tables(np.arange(N).astype(np.float64)),
        "w_gate": f32(L1_ffn_w_gate),
        "w_up": f32(L1_ffn_w_up),
        "w_down": f32(L1_ffn_w_down),
    }
    qcs_h = [l1a_rope_tables(own_rows(h).astype(np.float64)) for h in (0, 1)]
    msk_h = [l1a_masks(h) for h in (0, 1)]
    idx_h = []
    for h in (0, 1):
        rows = own_rows(h).astype(np.int32)
        idx_h.append(np.ascontiguousarray(rows.reshape(16, 128).T))
    in_maps = []
    for core in range(8):
        b, h = core // 2, core % 2
        xp = np.zeros((34 * 128, Dm), dtype=np.float32)
        xp[128:128 + N] = x[b]
        d = dict(shared)
        d.update({"x_pad": xp, "qcs": qcs_h[h], "cmask": msk_h[h], "ownidx": idx_h[h]})
        in_maps.append(d)
    res = run_bass_kernel_spmd(nc, in_maps, core_ids=list(range(8)))
    out = np.empty((B, N, Dm), dtype=np.float32)
    for core in range(8):
        b, h = core // 2, core % 2
        out[b][own_rows(h)] = res.results[core]["y4"]
    return out


# version 14
# speedup vs baseline: 1.0590x; 1.0452x over previous
import numpy as np
import ml_dtypes
import concourse.bass as bass
import concourse.mybir as mybir
from concourse.bass_utils import run_bass_kernel_spmd
from concourse.bass_core_rust import IndirectOffsetOnAxis

F32 = mybir.dt.float32
I32 = mybir.dt.int32
BF16 = mybir.dt.bfloat16
AF = mybir.ActivationFunctionType
ALU = mybir.AluOpType
AX = mybir.AxisListType

D = 1024
SEQ = 4096
LN_EPS = 1e-5
QK_EPS = 1e-6
SM_OFF = 10.0
MOE_LN_EPS = 1e-5

ENGINES = ("tensor", "scalar", "vector", "gpsimd", "sync")


class Tok:
    __slots__ = ("eng", "sem", "seq", "inc", "flag", "value")

    def __init__(self, eng, sem, seq, inc, value=0):
        self.eng, self.sem, self.seq, self.inc = eng, sem, seq, inc
        self.flag, self.value = False, value


NDMA_SEMS = 4


class PB:

    def __init__(self, nc):
        self.nc = nc
        self.streams = {e: [] for e in ENGINES}
        self.dma_idx = {}
        self.pending = {e: [] for e in ENGINES}

    def op(self, eng, fn, waits=(), dma=False):
        ws = [w for w in waits if w is not None]
        if self.pending[eng]:
            ws.extend(self.pending[eng])
            self.pending[eng] = []
        if dma:
            j = self.dma_idx[eng] = self.dma_idx.get(eng, -1) + 1
            sem = f"{eng}dma{j % NDMA_SEMS}"
            tok = Tok(eng, sem, len(self.streams[eng]), 16, 16 * (j // NDMA_SEMS + 1))
            tok.flag = True
            if j >= NDMA_SEMS:
                prev = Tok(eng, sem, -1, 16, 16 * (j // NDMA_SEMS))
                prev.flag = True
                ws.append(prev)
        else:
            sem = eng
            tok = Tok(eng, sem, len(self.streams[eng]), 1)
        for w in ws:
            if w.eng != eng or w.inc == 16:
                w.flag = True
        self.streams[eng].append((fn, ws, tok))
        return tok

    def barrier(self):
        toks = []
        for e in ENGINES:
            if self.streams[e]:
                toks.append(self.streams[e][-1][2])
        for e in ENGINES:
            self.pending[e] = [t for t in toks if t.eng != e or t.inc == 16]

    def emit(self, block):
        sem_names = set()
        for e in ENGINES:
            for (_, ws, tok) in self.streams[e]:
                if tok.flag:
                    sem_names.add(tok.sem)
                for w in ws:
                    if w.flag:
                        sem_names.add(w.sem)
        self.cum = {}
        for e in ENGINES:
            cnt, cums = 0, []
            for (_, _, tok) in self.streams[e]:
                if tok.flag and tok.inc == 1:
                    cnt += 1
                cums.append(cnt)
            self.cum[e] = cums
        import contextlib
        self._stack = contextlib.ExitStack()
        sems = {s: self._stack.enter_context(self.nc.semaphore(f"pb_{s}"))
                for s in sorted(sem_names)}
        self.sems = sems

        def emitter(eng):
            def run(h):
                waited = {}
                drained = 0
                for i, (fn, ws, tok) in enumerate(self.streams[eng]):
                    need_drain = False
                    for w in ws:
                        if w.inc == 16:
                            if w.value > waited.get(w.sem, 0):
                                h.wait_ge(sems[w.sem], w.value)
                                waited[w.sem] = w.value
                        elif w.eng == eng:
                            if w.seq >= drained:
                                need_drain = True
                        else:
                            val = self.cum[w.eng][w.seq]
                            if val > waited.get(w.eng, 0):
                                h.wait_ge(sems[w.eng], val)
                                waited[w.eng] = val
                    if need_drain:
                        h.drain()
                        drained = i
                    inst = fn(h)
                    if tok.flag:
                        inst.then_inc(sems[tok.sem], tok.inc)
            return run

        block.tensor(emitter("tensor"))
        block.scalar(emitter("scalar"))
        block.vector(emitter("vector"))
        block.gpsimd(emitter("gpsimd"))
        block.sync(emitter("sync"))


def bf(x):
    return np.ascontiguousarray(x).astype(ml_dtypes.bfloat16)


def f32(x):
    return np.ascontiguousarray(x, dtype=np.float32)



_ALLOC_N = [0]


class Alloc:

    def __init__(self, nc):
        import contextlib
        self.nc = nc
        self.stack = contextlib.ExitStack()

    def _u(self, name):
        _ALLOC_N[0] += 1
        return f"{name}_{_ALLOC_N[0]}"

    def sb(self, name, shape, dtype, side=None):
        return self.stack.enter_context(
            self.nc.sbuf_tensor(self._u(name), shape, dtype, side=side)
            if side else self.nc.sbuf_tensor(self._u(name), shape, dtype))

    def ps(self, name, shape, dtype):
        return self.stack.enter_context(self.nc.psum_tensor(self._u(name), shape, dtype))

    def close(self):
        self.stack.close()


def build_l0a(nqc):
    nc = bass.Bass("TRN2", target_bir_lowering=False)
    NKC = nqc + 2
    io = {
        'x_pad': nc.dram_tensor("x_pad", [NKC * 128, D], F32, kind="ExternalInput").ap(),
        'qkv_wT': nc.dram_tensor("qkv_wT", [D, 3 * D], F32, kind="ExternalInput").ap(),
        'qkv_b': nc.dram_tensor("qkv_b", [1, 3 * D], F32, kind="ExternalInput").ap(),
        'out_wT': nc.dram_tensor("out_wT", [D, D], F32, kind="ExternalInput").ap(),
        'out_b': nc.dram_tensor("out_b", [1, D], F32, kind="ExternalInput").ap(),
        'band_mask': nc.dram_tensor("band_mask", [128, nqc * 3 * 128], BF16,
                                    kind="ExternalInput").ap(),
        'identm': nc.dram_tensor("identm", [128, 128], BF16, kind="ExternalInput").ap(),
        'y1': nc.dram_tensor("y1", [nqc * 128, D], F32, kind="ExternalOutput").ap(),
    }
    with nc.Block() as blk:
        pb = PB(nc)
        emit_l0a(nc, pb, blk, io, nqc)
        pb.emit(blk)
    return nc


def emit_l0a(nc, pb, blk, io, nqc, tag=''):
    NKC = nqc + 2
    x_pad, qkv_wT, qkv_b, out_wT, out_b, band_mask, ident_in, y1 = (
        io['x_pad'], io['qkv_wT'], io['qkv_b'], io['out_wT'], io['out_b'],
        io['band_mask'], io['identm'], io['y1'])
    qk_dram = nc.dram_tensor("qkv_scr" + tag, [NKC * 128, 3 * D], BF16).ap()

    glob = Alloc(nc)
    ident = glob.sb("ident", [128, 128], BF16)
    masks = glob.sb("masks", [128, nqc * 3 * 128], BF16)
    onesrow = glob.sb("onesrow", [1, 128], BF16)
    onesf = glob.sb("onesf", [1, 64], F32)
    consts = glob.sb("consts", [128, 4], F32)
    t_ones = pb.op("gpsimd", lambda g: g.memset(onesrow[:], 1.0))
    t_onesf = pb.op("gpsimd", lambda g: g.memset(onesf[:], 1.0))
    pb.op("gpsimd", lambda g: g.memset(consts[:, 0:1], LN_EPS))
    pb.op("gpsimd", lambda g: g.memset(consts[:, 1:2], 1e-12))
    t_lne = pb.op("gpsimd", lambda g: g.memset(consts[:, 2:3], -0.5))
    t_eps = t_lne
    t_id = pb.op("gpsimd", lambda g: g.dma_start(out=ident[:], in_=ident_in[:]), dma=True)
    t_msk = pb.op("gpsimd", lambda g: g.dma_start(out=masks[:], in_=band_mask[:]), dma=True)

    ph = Alloc(nc)
    wqkv = ph.sb("wqkv", [128, 8 * 3 * D], BF16)
    brow = ph.sb("brow", [1, 3 * D], BF16)
    xc = ph.sb("xc", [128, 2 * D], F32)
    lnc = ph.sb("lnc", [128, 2 * D], BF16)
    lnT = ph.sb("lnT", [128, 2 * 8 * 128], BF16)
    qkf = ph.sb("qkf", [128, 2048], F32)
    sqs = ph.sb("sqs", [128, 2048], F32)
    st_ = ph.sb("st_", [128, 16], F32)
    nrm = ph.sb("nrm", [128, 128], F32)
    qkn = ph.sb("qkn", [128, 2 * 3072], BF16)
    pT_ = [ph.ps("pT0", [128, 1024], BF16), ph.ps("pT1", [128, 1024], BF16)]
    pqkv = ph.ps("pqkv", [128, 2048], F32)
    pv = ph.ps("pv", [128, 1024], F32)

    t_w = [pb.op("gpsimd", lambda g, j=j: g.dma_start(
        out=wqkv[:, j * 3072:(j + 1) * 3072],
        in_=qkv_wT[j * 128:(j + 1) * 128, :]), dma=True) for j in range(8)]
    t_b = pb.op("gpsimd", lambda g: g.dma_start(out=brow[:], in_=qkv_b[:]), dma=True)

    t_qkst = []
    SAB = ph.sb("SAB", [128, 4 * NKC], F32)
    SA, SB = SAB[:, 0:NKC], SAB[:, NKC:2 * NKC]
    MEAN, RSTD = SAB[:, 2 * NKC:3 * NKC], SAB[:, 3 * NKC:4 * NKC]
    t_sfree = [None, None]
    GS = 8
    t_gstats = {}
    for i in range(NKC):
        p = i % 2
        xs = xc[:, p * D:(p + 1) * D]
        t_x = pb.op("sync", lambda s, i=i, xs=xs: s.dma_start(
            out=xs, in_=x_pad[i * 128:(i + 1) * 128, :]),
            waits=[t_sfree[p]], dma=True)
        t_sfree[p] = emit_sums(pb, xs, SA[:, i:i + 1], SB[:, i:i + 1], sqs,
                               [t_x, t_sfree[1 - p]])
        if i % GS == GS - 1 or i == NKC - 1:
            g = i // GS
            lo, hi = g * GS, i + 1
            t_gstats[g] = emit_stats_batch(
                pb, SA[:, lo:hi], SB[:, lo:hi], MEAN[:, lo:hi], RSTD[:, lo:hi],
                consts[:, 0:1], consts[:, 2:3], [t_sfree[0], t_sfree[1], t_lne])
    t_xfree = None
    t_lnTfree = [None, None]
    t_lncfree = [None, None]
    t_qkffree = [None, None]
    t_pvfree = None
    t_pqkvfree = None
    t_qknfree = [None, None]
    for i in range(NKC):
        p = i % 2
        xs = xc[:, p * D:(p + 1) * D]
        lns = lnc[:, p * D:(p + 1) * D]
        NR = nrm[:, p * 64:(p + 1) * 64]
        t_x = pb.op("sync", lambda s, i=i, xs=xs: s.dma_start(
            out=xs, in_=x_pad[i * 128:(i + 1) * 128, :]),
            waits=[t_xfree] if i >= 2 else [t_sfree[0], t_sfree[1]], dma=True)
        t_ln = pb.op("vector", lambda v, xs=xs, lns=lns, i=i: v.tensor_scalar(
            lns, xs, MEAN[:, i:i + 1], RSTD[:, i:i + 1], ALU.subtract, ALU.mult),
            waits=[t_x, t_gstats[i // GS], t_lncfree[p]])
        t_xfree = t_ln
        t_lnTs = []
        t_tp = None
        for j in range(8):
            pj = (i * 8 + j) % 2
            t_tp = pb.op("tensor", lambda t, lns=lns, j=j, pj=pj: t.transpose(
                pT_[pj][:, 0:128], lns[:, j * 128:(j + 1) * 128], ident[:]),
                waits=[t_ln, t_id, t_lnTfree[pj]])
            t_ev = pb.op("vector", lambda v, j=j, p=p, pj=pj: v.tensor_copy(
                lnT[:, (p * 8 + j) * 128:(p * 8 + j + 1) * 128], pT_[pj][:, 0:128]),
                waits=[t_tp])
            t_lnTfree[pj] = t_ev
            t_lnTs.append(t_ev)
        t_lncfree[p] = t_tp

        def qk_mms(t, p=p):
            for piece in range(4):
                for j in range(8):
                    t.matmul(pqkv[:, piece * 512:(piece + 1) * 512],
                             lnT[:, (p * 8 + j) * 128:(p * 8 + j + 1) * 128],
                             wqkv[:, j * 3072 + piece * 512:j * 3072 + (piece + 1) * 512],
                             start=(j == 0), stop=False)
                last = t.matmul(pqkv[:, piece * 512:(piece + 1) * 512], onesrow[0:1, :],
                                brow[0:1, piece * 512:(piece + 1) * 512],
                                start=False, stop=True)
            return last
        t_qk = pb.op("tensor", qk_mms, waits=t_lnTs + t_w + [t_b, t_ones, t_pqkvfree])

        def v_mms(t, p=p):
            for piece in range(2):
                for j in range(8):
                    t.matmul(pv[:, piece * 512:(piece + 1) * 512],
                             lnT[:, (p * 8 + j) * 128:(p * 8 + j + 1) * 128],
                             wqkv[:, j * 3072 + 2048 + piece * 512:
                                  j * 3072 + 2048 + (piece + 1) * 512],
                             start=(j == 0), stop=False)
                last = t.matmul(pv[:, piece * 512:(piece + 1) * 512], onesrow[0:1, :],
                                brow[0:1, 2048 + piece * 512:2048 + (piece + 1) * 512],
                                start=False, stop=True)
            return last
        t_v = pb.op("tensor", v_mms, waits=[t_pvfree])
        t_qkev = pb.op("vector", lambda v: v.tensor_copy(qkf[:], pqkv[:]),
                       waits=[t_qk, t_qkffree[0], t_qkffree[1]])
        t_pqkvfree = t_qkev
        t_vev = pb.op("vector", lambda v, p=p: v.tensor_copy(
            qkn[:, p * 3072 + 2048:(p + 1) * 3072], pv[:]),
            waits=[t_v, t_qknfree[p]])
        t_pvfree = t_vev
        t_sqp = pb.op("scalar", lambda s: s.activation(sqs[:], qkf[:], AF.Square),
                      waits=[t_qkev])
        t_red = pb.op("vector", lambda v, NR=NR: v.tensor_reduce(
            NR[:, 0:32], sqs[:].rearrange("p (h e) -> p h e", e=64), AX.X, ALU.add),
            waits=[t_sqp])
        t_nsq = pb.op("scalar", lambda s, NR=NR: s.activation(
            NR[:, 32:64], NR[:, 0:32], AF.Ln, bias=consts[:, 1:2]), waits=[t_red, t_lne])
        t_nr = pb.op("scalar", lambda s, NR=NR: s.activation(
            NR[:, 0:32], NR[:, 32:64], AF.Exp, scale=consts[:, 2:3]), waits=[t_nsq])
        t_qknv = pb.op("vector", lambda v, p=p, NR=NR: v.tensor_tensor(
            qkn[:, p * 3072:p * 3072 + 2048].rearrange("p (h e) -> p h e", e=64),
            qkf[:].rearrange("p (h e) -> p h e", e=64),
            NR[:, 0:32].unsqueeze(2).broadcast_to([128, 32, 64]), ALU.mult),
            waits=[t_nr, t_qknfree[p]])
        t_qkffree = [t_sqp, t_qknv]
        t_st = pb.op("sync", lambda s, i=i, p=p: s.dma_start(
            out=qk_dram[i * 128:(i + 1) * 128, :], in_=qkn[:, p * 3072:(p + 1) * 3072]),
            waits=[t_qknv, t_vev], dma=True)
        t_qknfree[p] = t_st
        t_qkst.append(t_st)
    pb.barrier()
    ph.close()

    for r0 in range(0, nqc, 16):
        nqh = min(16, nqc - r0)
        nkh = nqh + 2
        ph2 = Alloc(nc)
        kT = ph2.sb("kT", [128, 8 * 18 * 128], BF16)
        qT = ph2.sb("qT", [128, 8 * 16 * 128], BF16)
        v_aug = ph2.sb("v_aug", [128, 18 * 16 * 65], BF16)
        wout = ph2.sb("wout", [128, 8 * D], BF16)
        obrow = ph2.sb("obrow", [1, D], BF16)
        expt = ph2.sb("expt", [128, 2 * 512], BF16)
        rl = ph2.sb("rl", [1, 2 * 512], F32)
        rlbc = ph2.sb("rlbc", [64, 2 * 512], F32)
        outT = ph2.sb("outT", [128, 2 * 8 * 128], BF16)
        xre = ph2.sb("xre", [128, 2 * D], F32)
        y1c = ph2.sb("y1c", [128, 2 * D], F32)
        psc = ph2.ps("psc", [128, 2 * 1024], F32)
        pav = ph2.ps("pav", [65, 2 * 512], F32)
        pbc = ph2.ps("pbc", [64, 512], F32)
        pproj = psc[:, 0:1024]

        t_kT = [pb.op("sync", lambda s, j=j, r0=r0, nkh=nkh: s.dma_start(
            out=kT[:, j * 18 * 128:j * 18 * 128 + nkh * 128],
            in_=qk_dram[r0 * 128:(r0 + nkh) * 128, D + j * 128:D + (j + 1) * 128],
            transpose=True), waits=t_qkst[:r0 + nkh], dma=True) for j in range(8)]
        t_qT = [pb.op("sync", lambda s, j=j, r0=r0, nqh=nqh: s.dma_start(
            out=qT[:, j * 16 * 128:j * 16 * 128 + nqh * 128],
            in_=qk_dram[(r0 + 1) * 128:(r0 + 1 + nqh) * 128, j * 128:(j + 1) * 128],
            transpose=True), waits=t_qkst[:r0 + 1 + nqh], dma=True) for j in range(8)]
        t_vl, t_vone = [], []
        for c in range(nkh):
            t_ = pb.op("gpsimd", lambda g, c=c, r0=r0: g.dma_start(
                out=v_aug[:, c * 1040:(c + 1) * 1040]
                .rearrange("p (h e) -> p h e", e=65)[:, :, 0:64],
                in_=qk_dram[(r0 + c) * 128:(r0 + c + 1) * 128, 2 * D:3 * D]
                .rearrange("p (h e) -> p h e", e=64)),
                waits=t_qkst[:r0 + nkh], dma=True)
            t_vl.append(t_)
            t_vone.append(pb.op("gpsimd", lambda g, c=c: g.memset(
                v_aug[:, c * 1040 + 64:(c + 1) * 1040:65], 1.0), waits=[t_]))
        t_wo = [pb.op("gpsimd", lambda g, j=j: g.dma_start(
            out=wout[:, j * D:(j + 1) * D],
            in_=out_wT[j * 128:(j + 1) * 128, :]), dma=True) for j in range(8)]
        t_ob = pb.op("gpsimd", lambda g: g.dma_start(out=obrow[:], in_=out_b[:]), dma=True)

        t_expfree = [None, None]
        t_pprojfree = None
        t_pscfree = [None, None]
        t_pavfree = [None, None]
        t_rlf = None
        t_bcf = None
        t_y1 = [None, None]
        for qc in range(nqh):
            gqc = r0 + qc
            p = qc % 2
            t_xr = pb.op("sync", lambda s, gqc=gqc, p=p: s.dma_start(
                out=xre[:, p * D:(p + 1) * D],
                in_=x_pad[(gqc + 1) * 128:(gqc + 2) * 128, :]),
                waits=[t_y1[p]], dma=True)
            t_fins = []
            for hg in range(4):
                u = qc * 4 + hg
                pu = u % 2
                t_avs = []
                for kc in range(3):
                    kchunk = qc + kc
                    pp = (u * 3 + kc) % 2

                    def sc_mms(t, hg=hg, pp=pp, kchunk=kchunk, qc=qc):
                        for hh in range(4):
                            h = hg * 4 + hh
                            tile, row0 = h // 2, 64 * (h % 2)
                            col = pp * 1024 + (hh % 2) * 512 + (hh // 2) * 128
                            last = t.matmul(
                                psc[:, col:col + 128],
                                kT[row0:row0 + 64,
                                   tile * 18 * 128 + kchunk * 128:
                                   tile * 18 * 128 + (kchunk + 1) * 128],
                                qT[row0:row0 + 64,
                                   tile * 16 * 128 + qc * 128:
                                   tile * 16 * 128 + (qc + 1) * 128],
                                start=(hh < 2), stop=(hh >= 2))
                        return last
                    t_sc = pb.op("tensor", sc_mms, waits=t_kT + t_qT + [t_pscfree[pp]])
                    t_exp = pb.op("scalar", lambda s, pp=pp: s.activation(
                        expt[:, pp * 512:(pp + 1) * 512],
                        psc[:, pp * 1024:(pp + 1) * 1024]
                        .rearrange("p (b c) -> p b c", c=512)[:, :, 0:256], AF.Exp),
                        waits=[t_sc, t_expfree[pp]])
                    t_pscfree[pp] = t_exp
                    t_m = pb.op("vector", lambda v, pp=pp, gqc=gqc, kc=kc: v.tensor_tensor(
                        expt[:, pp * 512:(pp + 1) * 512].rearrange("p (h q) -> p h q", q=128),
                        expt[:, pp * 512:(pp + 1) * 512].rearrange("p (h q) -> p h q", q=128),
                        masks[:, (gqc * 3 + kc) * 128:(gqc * 3 + kc + 1) * 128]
                        .unsqueeze(1).broadcast_to([128, 4, 128]), ALU.mult),
                        waits=[t_exp, t_msk])

                    def av_mms(t, hg=hg, pp=pp, pu=pu, kchunk=kchunk, kc=kc):
                        for hh in range(4):
                            h = hg * 4 + hh
                            ec = ((hh % 2) * 2 + hh // 2) * 128
                            last = t.matmul(
                                pav[:, pu * 512 + hh * 128:pu * 512 + (hh + 1) * 128],
                                v_aug[:, kchunk * 1040 + h * 65:
                                      kchunk * 1040 + (h + 1) * 65],
                                expt[:, pp * 512 + ec:pp * 512 + ec + 128],
                                start=(kc == 0 and hh == 0), stop=(kc == 2 and hh == 3))
                        return last
                    t_av = pb.op("tensor", av_mms,
                                 waits=[t_m, t_pavfree[pu]] + t_vone if kc == 0 else [t_m])
                    t_expfree[pp] = t_av
                    t_avs.append(t_av)
                t_rl = pb.op("scalar", lambda s, pu=pu: s.activation(
                    rl[0:1, pu * 512:(pu + 1) * 512], pav[64:65, pu * 512:(pu + 1) * 512],
                    AF.Ln), waits=[t_avs[2], t_rlf])
                t_rl = pb.op("scalar", lambda s, pu=pu: s.activation(
                    rl[0:1, pu * 512:(pu + 1) * 512], rl[0:1, pu * 512:(pu + 1) * 512],
                    AF.Exp, scale=-1.0), waits=[t_rl])
                t_bc = pb.op("tensor", lambda t, pu=pu: t.matmul(
                    pbc[:], onesf[0:1, 0:64],
                    rl[0:1, pu * 512:(pu + 1) * 512], start=True, stop=True),
                    waits=[t_rl, t_bcf, t_onesf])
                t_rlf = t_bc
                t_bcev = pb.op("vector", lambda v, pu=pu: v.tensor_copy(
                    rlbc[:, pu * 512:(pu + 1) * 512], pbc[:]), waits=[t_bc])
                t_bcf = t_bcev

                def fin(v, hg=hg, pu=pu, p=p):
                    for hh in range(4):
                        h = hg * 4 + hh
                        tile, row0 = h // 2, 64 * (h % 2)
                        last = v.scalar_tensor_tensor(
                            outT[row0:row0 + 64, (p * 8 + tile) * 128:(p * 8 + tile + 1) * 128],
                            pav[0:64, pu * 512 + hh * 128:pu * 512 + (hh + 1) * 128], 1.0,
                            rlbc[:, pu * 512 + hh * 128:pu * 512 + (hh + 1) * 128],
                            ALU.mult, ALU.mult)
                    return last
                t_fin = pb.op("vector", fin, waits=[t_bcev] + ([t_y1[p]] if hg == 0 else []))
                t_pavfree[pu] = t_fin
                t_fins.append(t_fin)

            def proj_mms(t, p=p):
                for piece in range(2):
                    for j in range(8):
                        t.matmul(pproj[:, piece * 512:(piece + 1) * 512],
                                 outT[:, (p * 8 + j) * 128:(p * 8 + j + 1) * 128],
                                 wout[:, j * D + piece * 512:j * D + (piece + 1) * 512],
                                 start=(j == 0), stop=False)
                    last = t.matmul(pproj[:, piece * 512:(piece + 1) * 512], onesrow[0:1, :],
                                    obrow[0:1, piece * 512:(piece + 1) * 512],
                                    start=False, stop=True)
                return last
            t_proj = pb.op("tensor", proj_mms,
                           waits=t_fins + t_wo + [t_ob, t_pprojfree,
                                                  t_pscfree[0], t_pscfree[1]])
            t_pscfree[0] = None
            t_y1c = pb.op("vector", lambda v, p=p: v.scalar_tensor_tensor(
                y1c[:, p * D:(p + 1) * D], pproj[:], 1.0,
                xre[:, p * D:(p + 1) * D], ALU.mult, ALU.add),
                waits=[t_proj, t_xr])
            t_pprojfree = t_y1c
            t_pscfree[0] = t_y1c
            t_st = pb.op("sync", lambda s, gqc=gqc, p=p: s.dma_start(
                out=y1[gqc * 128:(gqc + 1) * 128, :], in_=y1c[:, p * D:(p + 1) * D]),
                waits=[t_y1c], dma=True)
            t_y1[p] = t_st
        pb.op("sync", lambda s: s.nop(), waits=[t_y1[0], t_y1[1]])
        pb.barrier()
        ph2.close()
    glob.close()



def l0a_inputs(x, core):
    b, h = core // 2, core % 2
    q0 = h * 2048
    lo, hi = q0 - 128, q0 + 2048 + 128
    xp = np.zeros((2048 + 256, D), dtype=np.float32)
    s, e = max(lo, 0), min(hi, SEQ)
    xp[s - lo:e - lo] = x[b, s:e]
    return {"x_pad": xp}


def l0a_masks(nqc, q0):
    m = np.zeros((nqc, 3, 128, 128), dtype=np.float32)
    for qc in range(nqc):
        gq = q0 + qc * 128 + np.arange(128)[None, :]
        for kc in range(3):
            gk = q0 + (qc + kc - 1) * 128 + np.arange(128)[:, None]
            diff = gk - gq
            ok = (np.abs(diff) <= 16) & (diff % 2 == 0) & (gk >= 0) & (gk < SEQ)
            m[qc, kc] = ok.astype(np.float32)
    return bf(m.reshape(nqc * 3, 128, 128).transpose(1, 0, 2).reshape(128, nqc * 3 * 128))


def emit_sums(pb, xs, sa_col, sb_col, sqs, waits):
    t1 = pb.op("scalar", lambda s: s.activation(
        sqs[:, 0:D], xs, AF.Square, accum_out=sa_col), waits=waits)
    t2 = pb.op("scalar", lambda s: s.activation(
        sqs[:, 0:D], xs, AF.Copy, accum_out=sb_col), waits=[t1])
    return t2


def emit_stats_batch(pb, SA, SB, MEAN, RSTD, eps_col, mhalf_col, waits):
    t = pb.op("vector", lambda v: v.tensor_scalar_mul(MEAN, SB, 1.0 / D), waits=waits)
    t = pb.op("vector", lambda v: v.scalar_tensor_tensor(
        RSTD, MEAN, 1.0, MEAN, ALU.mult, ALU.mult), waits=[t])
    t = pb.op("vector", lambda v: v.scalar_tensor_tensor(
        RSTD, SA, 1.0 / D, RSTD, ALU.mult, ALU.subtract), waits=[t])
    t = pb.op("scalar", lambda s: s.activation(RSTD, RSTD, AF.Ln, bias=eps_col),
              waits=[t])
    t = pb.op("scalar", lambda s: s.activation(RSTD, RSTD, AF.Exp, scale=mhalf_col),
              waits=[t])
    return t


def emit_ln_stats(pb, sc_eng_bufs, xs, S, consts_eps_col, waits, eps_tok=None,
                  mhalf=None):
    (sqs,) = sc_eng_bufs
    t_s1 = pb.op("scalar", lambda s: s.activation(
        sqs[:, 0:D], xs, AF.Square, accum_out=S[:, 1:2]), waits=waits)
    t_s0 = pb.op("scalar", lambda s: s.activation(
        sqs[:, 0:D], xs, AF.Copy, accum_out=S[:, 0:1]), waits=[t_s1])
    t_mv = pb.op("vector", lambda v: v.tensor_scalar_mul(S[:, 2:3], S[:, 0:1], 1.0 / D),
                 waits=[t_s0])
    t_mv2 = pb.op("vector", lambda v: v.scalar_tensor_tensor(
        S[:, 3:4], S[:, 2:3], 1.0, S[:, 2:3], ALU.mult, ALU.mult), waits=[t_mv])
    t_var = pb.op("vector", lambda v: v.scalar_tensor_tensor(
        S[:, 3:4], S[:, 1:2], 1.0 / D, S[:, 3:4], ALU.mult, ALU.subtract), waits=[t_mv2])
    t_sq = pb.op("scalar", lambda s: s.activation(
        S[:, 4:5], S[:, 3:4], AF.Ln, bias=consts_eps_col), waits=[t_var, eps_tok])
    t_r = pb.op("scalar", lambda s: s.activation(
        S[:, 5:6], S[:, 4:5], AF.Exp, scale=mhalf), waits=[t_sq])
    return t_var, t_r



def build_l0f(nt_chunks, npass=2):
    nc = bass.Bass("TRN2", target_bir_lowering=False)
    NT = nt_chunks * 128
    io = {
        'y1_in': nc.dram_tensor("y1_in", [NT, D], F32, kind="ExternalInput").ap(),
        'g_bc': nc.dram_tensor("g_bc", [128, D], F32, kind="ExternalInput").ap(),
        'b_bc': nc.dram_tensor("b_bc", [128, D], F32, kind="ExternalInput").ap(),
        'rW': nc.dram_tensor("rW", [128, 8 * 8], F32, kind="ExternalInput").ap(),
        'rb': nc.dram_tensor("rb", [1, 8], F32, kind="ExternalInput").ap(),
        'W1': nc.dram_tensor("W1", [8, D, 2048], F32, kind="ExternalInput").ap(),
        'b1t': nc.dram_tensor("b1t", [128, 8 * 16], F32, kind="ExternalInput").ap(),
        'W2': nc.dram_tensor("W2", [8, 2048, D], F32, kind="ExternalInput").ap(),
        'b2r': nc.dram_tensor("b2r", [1, 8 * D], F32, kind="ExternalInput").ap(),
        'identm': nc.dram_tensor("identm", [128, 128], BF16, kind="ExternalInput").ap(),
        'y2': nc.dram_tensor("y2", [NT, D], F32, kind="ExternalOutput").ap(),
    }
    with nc.Block() as blk:
        pb = PB(nc)
        emit_l0f(nc, pb, blk, io, nt_chunks, npass)
        pb.emit(blk)
    return nc


def emit_l0f(nc, pb, blk, io, nt_chunks, npass=2, tag=''):
    NT = nt_chunks * 128
    (y1_in, g_bc, b_bc, rW, rb, W1, b1t, W2, b2r, ident_in, y2) = (
        io['y1_in'], io['g_bc'], io['b_bc'], io['rW'], io['rb'], io['W1'],
        io['b1t'], io['W2'], io['b2r'], io['identm'], io['y2'])
    w1_bf = nc.dram_tensor("w1_bf" + tag, [8, D, 2048], BF16).ap()
    w2_bf = nc.dram_tensor("w2_bf" + tag, [8, 2048, D], BF16).ap()

    pc = nt_chunks // npass
    TGP = (pc * 128) // 512

    glob = Alloc(nc)
    ident = glob.sb("ident", [128, 128], BF16)
    gb = glob.sb("gb", [128, 2 * D], F32)
    b1s = glob.sb("b1s", [128, 8 * 16], F32)
    b2row = glob.sb("b2row", [1, 8 * D], BF16)
    rWs = glob.sb("rWs", [128, 8 * 8], BF16)
    rbs = glob.sb("rbs", [1, 8], BF16)
    onesrow = glob.sb("onesrow", [1, 128], BF16)
    consts = glob.sb("consts", [128, 4], F32)
    stats = glob.sb("stats", [128, nt_chunks * 4], F32)
    wts = glob.sb("wts", [128, nt_chunks * 8], F32)
    zT = glob.sb("zT", [128, 8 * NT], BF16)
    acc = glob.sb("acc", [128, pc * D], F32)
    SAB = glob.sb("SAB", [128, 7 * nt_chunks], F32)
    if True:
        t_ones = pb.op("gpsimd", lambda g: g.memset(onesrow[:], 1.0))
        pb.op("gpsimd", lambda g: g.memset(consts[:, 0:1], LN_EPS))
        pb.op("gpsimd", lambda g: g.memset(consts[:, 1:2], MOE_LN_EPS))
        t_eps = pb.op("gpsimd", lambda g: g.memset(consts[:, 2:3], -0.5))
        t_id = pb.op("gpsimd", lambda g: g.dma_start(out=ident[:], in_=ident_in[:]), dma=True)
        t_gb = pb.op("gpsimd", lambda g: g.dma_start(out=gb[:, 0:D], in_=g_bc[:]), dma=True)
        t_bb = pb.op("gpsimd", lambda g: g.dma_start(out=gb[:, D:2 * D], in_=b_bc[:]), dma=True)
        t_b1 = pb.op("gpsimd", lambda g: g.dma_start(out=b1s[:], in_=b1t[:]), dma=True)
        t_b2 = pb.op("gpsimd", lambda g: g.dma_start(out=b2row[:], in_=b2r[:]), dma=True)
        t_rw = pb.op("gpsimd", lambda g: g.dma_start(out=rWs[:], in_=rW[:]), dma=True)
        t_rb = pb.op("gpsimd", lambda g: g.dma_start(out=rbs[:], in_=rb[:]), dma=True)

        ph = Alloc(nc)
        yc = ph.sb("yc", [128, 2 * D], F32)
        sqs = ph.sb("sqs", [128, D], F32)
        S8 = ph.sb("S8", [128, 16], F32)
        az = ph.sb("az", [128, 2 * D], F32)
        zc = ph.sb("zc", [128, 2 * D], BF16)
        es = ph.sb("es", [128, 2 * 8], F32)
        r8 = ph.sb("r8", [128, 2 * 8], F32)
        pTz = [ph.ps("pTz0", [128, 1024], BF16), ph.ps("pTz1", [128, 1024], BF16)]
        plog = ph.ps("plog", [128, 1024], F32)

        NT_C = nt_chunks
        SA, SB = SAB[:, 0:NT_C], SAB[:, NT_C:2 * NT_C]
        MEAN, R1 = SAB[:, 2 * NT_C:3 * NT_C], SAB[:, 3 * NT_C:4 * NT_C]
        SC12 = SAB[:, 4 * NT_C:5 * NT_C]
        P0, P1 = SAB[:, 5 * NT_C:6 * NT_C], SAB[:, 6 * NT_C:7 * NT_C]
        lg_all = ph.sb("lg_all", [128, NT_C * 8], F32)
        t_sfree = [None, None]
        for i in range(NT_C):
            p = i % 2
            ys = yc[:, p * D:(p + 1) * D]
            t_y = pb.op("sync", lambda s, i=i, ys=ys: s.dma_start(
                out=ys, in_=y1_in[i * 128:(i + 1) * 128, :]),
                waits=[t_sfree[p]], dma=True)
            t_sfree[p] = emit_sums(pb, ys, SA[:, i:i + 1], SB[:, i:i + 1], sqs,
                                   [t_y, t_sfree[1 - p]])
        t_st1 = emit_stats_batch(pb, SA, SB, MEAN, R1, consts[:, 0:1],
                                 consts[:, 2:3], [t_sfree[0], t_sfree[1], t_eps])
        t_b1_ = pb.op("vector", lambda v: v.tensor_scalar_add(P0, R1, 1.0), waits=[t_st1])
        t_b2_ = pb.op("vector", lambda v: v.tensor_tensor(P1, MEAN, R1, ALU.mult),
                      waits=[t_b1_])
        t_b3_ = pb.op("vector", lambda v: v.scalar_tensor_tensor(
            SC12, MEAN, 1.0, MEAN, ALU.mult, ALU.mult), waits=[t_b2_])
        t_b4_ = pb.op("vector", lambda v: v.scalar_tensor_tensor(
            SC12, SA, 1.0 / D, SC12, ALU.mult, ALU.subtract), waits=[t_b3_])
        t_b5_ = pb.op("vector", lambda v: v.tensor_tensor(SA, R1, R1, ALU.mult),
                      waits=[t_b4_])
        t_b6_ = pb.op("vector", lambda v: v.tensor_tensor(SC12, SC12, SA, ALU.mult),
                      waits=[t_b5_])
        t_b7_ = pb.op("scalar", lambda s: s.activation(SC12, SC12, AF.Ln,
                                                       bias=consts[:, 1:2]), waits=[t_b6_])
        t_b8_ = pb.op("scalar", lambda s: s.activation(SC12, SC12, AF.Exp,
                                                       scale=consts[:, 2:3]), waits=[t_b7_])
        t_stats = pb.op("vector", lambda v: v.tensor_tensor(SC12, SC12, R1, ALU.mult),
                        waits=[t_b8_])
        t_yfree = None
        t_lnTfree = [None, None]
        t_zfree = [None, None]
        t_azfree = [None, None]
        t_lgfree = None
        t_plogfree = [None, None]
        for i in range(NT_C):
            p = i % 2
            ys = yc[:, p * D:(p + 1) * D]
            t_y = pb.op("sync", lambda s, i=i, ys=ys: s.dma_start(
                out=ys, in_=y1_in[i * 128:(i + 1) * 128, :]),
                waits=[t_yfree] if i >= 2 else [t_sfree[0], t_sfree[1]], dma=True)
            t_az = pb.op("vector", lambda v, ys=ys, i=i, p=p: v.tensor_scalar(
                az[:, p * D:(p + 1) * D], ys, MEAN[:, i:i + 1], SC12[:, i:i + 1],
                ALU.subtract, ALU.mult), waits=[t_y, t_stats, t_azfree[p]])
            t_yfree = t_az
            t_z1 = pb.op("vector", lambda v, p=p: v.tensor_tensor(
                az[:, p * D:(p + 1) * D], az[:, p * D:(p + 1) * D], gb[:, 0:D], ALU.mult),
                waits=[t_az, t_gb])
            t_z = pb.op("vector", lambda v, p=p: v.tensor_tensor(
                zc[:, p * D:(p + 1) * D], az[:, p * D:(p + 1) * D], gb[:, D:2 * D], ALU.add),
                waits=[t_z1, t_bb, t_zfree[p]])
            t_azfree[p] = t_z
            t_zTs = []
            t_tp = None
            for j in range(8):
                pj = (i * 8 + j) % 2
                t_tp = pb.op("tensor", lambda t, p=p, j=j, pj=pj: t.transpose(
                    pTz[pj][:, 0:128], zc[:, p * D + j * 128:p * D + (j + 1) * 128], ident[:]),
                    waits=[t_z, t_id, t_lnTfree[pj]])
                t_ev = pb.op("vector", lambda v, i=i, j=j, pj=pj: v.tensor_copy(
                    zT[:, j * NT + i * 128:j * NT + (i + 1) * 128], pTz[pj][:, 0:128]),
                    waits=[t_tp])
                t_lnTfree[pj] = t_ev
                t_zTs.append(t_ev)
            t_zfree[p] = t_tp

            def r_mms(t, i=i, p=p):
                for j in range(8):
                    t.matmul(plog[:, p * 512:p * 512 + 8],
                             zT[:, j * NT + i * 128:j * NT + (i + 1) * 128],
                             rWs[:, j * 8:(j + 1) * 8], start=(j == 0), stop=False)
                return t.matmul(plog[:, p * 512:p * 512 + 8], onesrow[0:1, :],
                                rbs[0:1, :], start=False, stop=True)
            t_lg = pb.op("tensor", r_mms,
                         waits=t_zTs + [t_rw, t_rb, t_ones, t_plogfree[p]])
            t_lge = pb.op("vector", lambda v, i=i, p=p: v.tensor_copy(
                lg_all[:, i * 8:(i + 1) * 8], plog[:, p * 512:p * 512 + 8]),
                waits=[t_lg, t_lgfree])
            t_plogfree[p] = t_lge
        L3 = lg_all[:].rearrange("p (c e) -> p c e", e=8)
        W3 = wts[:].rearrange("p (c e) -> p c e", e=8)
        t_d1 = pb.op("vector", lambda v: v.tensor_reduce(
            SA[:, 0:NT_C], L3, AX.X, ALU.max, negate=True),
            waits=[t_plogfree[0], t_plogfree[1]])
        t_d2 = pb.op("vector", lambda v: v.tensor_tensor(
            L3, L3, SA[:, 0:NT_C].unsqueeze(2).broadcast_to([128, NT_C, 8]), ALU.add),
            waits=[t_d1])
        t_d3 = pb.op("scalar", lambda s: s.activation(lg_all[:], lg_all[:], AF.Exp),
                     waits=[t_d2])
        t_d4 = pb.op("vector", lambda v: v.tensor_reduce(
            SA[:, 0:NT_C], L3, AX.X, ALU.max), waits=[t_d3])
        t_d5 = pb.op("vector", lambda v: v.tensor_tensor(
            W3, L3, SA[:, 0:NT_C].unsqueeze(2).broadcast_to([128, NT_C, 8]), ALU.is_ge),
            waits=[t_d4])
        t_d6 = pb.op("vector", lambda v: v.scalar_tensor_tensor(
            wts[:], wts[:], -1e30, lg_all[:], ALU.mult, ALU.add), waits=[t_d5])
        t_d7 = pb.op("vector", lambda v: v.tensor_reduce(
            SB[:, 0:NT_C], W3, AX.X, ALU.max), waits=[t_d6])
        t_d8 = pb.op("vector", lambda v: v.tensor_tensor(
            W3, L3, SB[:, 0:NT_C].unsqueeze(2).broadcast_to([128, NT_C, 8]), ALU.is_ge),
            waits=[t_d7])
        t_d9 = pb.op("vector", lambda v: v.tensor_tensor(
            wts[:], wts[:], lg_all[:], ALU.mult), waits=[t_d8])
        t_da = pb.op("vector", lambda v: v.tensor_reduce(
            SB[:, 0:NT_C], W3, AX.X, ALU.add), waits=[t_d9])
        t_db_ = pb.op("scalar", lambda s: s.activation(SB[:, 0:NT_C], SB[:, 0:NT_C],
                                                       AF.Ln), waits=[t_da])
        t_dc = pb.op("scalar", lambda s: s.activation(SB[:, 0:NT_C], SB[:, 0:NT_C],
                                                      AF.Exp, scale=-1.0), waits=[t_db_])
        t_dd = pb.op("vector", lambda v: v.tensor_tensor(
            W3, W3, SB[:, 0:NT_C].unsqueeze(2).broadcast_to([128, NT_C, 8]), ALU.mult),
            waits=[t_dc])
        pb.barrier()
        ph.close()

        ph2 = Alloc(nc)
        NWB = 4
        wbuf = [ph2.sb(f"wbuf{k}", [128, 8 * 1024], BF16) for k in range(NWB)]
        hb = ph2.sb("hb", [128, 16 * 512], BF16)
        yre = ph2.sb("yre", [128, D], F32)
        y2c = ph2.sb("y2c", [128, 2 * D], F32)
        phid = ph2.ps("phid", [128, 2048], F32)
        po = ph2.ps("po", [128, 2 * 1024], F32)

        t_wload = []
        t_wuse = []
        t_wecho = {}
        cache_tok = {}
        nloads = npass * 8 * 4

        def emit_wload(k):
            P, rem = divmod(k, 32)
            e, q = divmod(rem, 4)
            buf = wbuf[k % NWB]
            waits = []
            if k >= NWB:
                waits.append(t_wuse[k - NWB])
                waits.extend(t_wecho.get(k - NWB, []))
            if P == 0:
                if q < 2:
                    srcs = [W1[e, j * 128:(j + 1) * 128, q * 1024:(q + 1) * 1024]
                            for j in range(8)]
                    dsts = [w1_bf[e, j * 128:(j + 1) * 128, q * 1024:(q + 1) * 1024]
                            for j in range(8)]
                else:
                    h = q - 2
                    srcs = [W2[e, h * 1024 + j * 128:h * 1024 + (j + 1) * 128, :]
                            for j in range(8)]
                    dsts = [w2_bf[e, h * 1024 + j * 128:h * 1024 + (j + 1) * 128, :]
                            for j in range(8)]
                toks = [pb.op("gpsimd", lambda g, j=j, buf=buf, src=srcs[j]: g.dma_start(
                    out=buf[:, j * 1024:(j + 1) * 1024], in_=src), waits=waits, dma=True)
                    for j in range(8)]
                if npass > 1:
                    echo = [pb.op("sync", lambda s, j=j, buf=buf, dst=dsts[j]: s.dma_start(
                        out=dst, in_=buf[:, j * 1024:(j + 1) * 1024]),
                        waits=[toks[j]], dma=True) for j in range(8)]
                    cache_tok[(e, q)] = echo
                    t_wecho[k] = echo
            else:
                if q < 2:
                    srcs = [w1_bf[e, j * 128:(j + 1) * 128, q * 1024:(q + 1) * 1024]
                            for j in range(8)]
                else:
                    h = q - 2
                    srcs = [w2_bf[e, h * 1024 + j * 128:h * 1024 + (j + 1) * 128, :]
                            for j in range(8)]
                waits = waits + cache_tok[(e, q)]
                toks = [pb.op("sync", lambda s, j=j, buf=buf, src=srcs[j]: s.dma_start(
                    out=buf[:, j * 1024:(j + 1) * 1024], in_=src), waits=waits, dma=True)
                    for j in range(8)]
            t_wload.append(toks)
            t_wuse.append(None)

        for k in range(min(NWB, nloads)):
            emit_wload(k)

        t_accfree = None
        t_hbfree = [None]
        t_pofree = [None, None]
        t_phidfree = [None] * 16
        t_yrefree = None
        t_y2free = [None, None]
        t_y2st = []
        for P in range(npass):
            c0 = P * pc
            t_az0 = pb.op("vector", lambda v: v.memset(acc[:], 0.0),
                          waits=[t_accfree])
            t_accfree = None
            t_evacs = []
            for e in range(8):
                kbase = P * 32 + e * 4
                for tg in range(TGP):
                    tp = 0
                    HB = hb[:]
                    t_gelus = []
                    for fo in range(16):
                        half = fo // 8
                        fol = fo % 8
                        kW1 = kbase + half
                        bank = fo % 4

                        def w1_mms(t, e=e, tg=tg, fo=fo, fol=fol, kW1=kW1, bank=bank,
                                   c0=c0, P=P):
                            buf = wbuf[kW1 % NWB]
                            for j in range(8):
                                last = t.matmul(
                                    phid[:, bank * 512:(bank + 1) * 512],
                                    buf[:, j * 1024 + fol * 128:j * 1024 + (fol + 1) * 128],
                                    zT[:, j * NT + c0 * 128 + tg * 512:
                                       j * NT + c0 * 128 + (tg + 1) * 512],
                                    start=(j == 0), stop=(j == 7))
                            return last
                        t_mm = pb.op("tensor", w1_mms,
                                     waits=t_wload[kW1] + [t_phidfree[bank],
                                                           t_gelus[-4] if fo >= 4 else None])
                        t_g = pb.op("scalar", lambda s, fo=fo, bank=bank, e=e, HB=HB:
                                    s.activation(HB[:, fo * 512:(fo + 1) * 512],
                                                 phid[:, bank * 512:(bank + 1) * 512],
                                                 AF.Gelu, bias=b1s[:, e * 16 + fo:e * 16 + fo + 1]),
                                    waits=[t_mm, t_b1, t_hbfree[tp]] if fo == 0 else [t_mm])
                        t_phidfree[bank] = t_g
                        t_gelus.append(t_g)
                    if e == 0 and tg < 2:
                        pass
                    t_wuse[kbase] = t_gelus[7]
                    t_wuse[kbase + 1] = t_gelus[15]
                    t_w2ev = []
                    for t128 in range(4):
                        il = tg * 4 + t128
                        pop = (tg * 4 + t128) % 2

                        def w2_mms(t, e=e, kbase=kbase, t128=t128, pop=pop, HB=HB):
                            for piece in range(2):
                                for i16 in range(16):
                                    buf = wbuf[(kbase + 2 + i16 // 8) % NWB]
                                    t.matmul(
                                        po[:, pop * 1024 + piece * 512:
                                           pop * 1024 + (piece + 1) * 512],
                                        HB[:, i16 * 512 + t128 * 128:i16 * 512 + t128 * 128 + 128],
                                        buf[:, (i16 % 8) * 1024 + piece * 512:
                                            (i16 % 8) * 1024 + (piece + 1) * 512],
                                        start=(i16 == 0), stop=False)
                                last = t.matmul(
                                    po[:, pop * 1024 + piece * 512:pop * 1024 + (piece + 1) * 512],
                                    onesrow[0:1, :],
                                    b2row[0:1, e * D + piece * 512:e * D + (piece + 1) * 512],
                                    start=False, stop=True)
                            return last
                        t_w2 = pb.op("tensor", w2_mms,
                                     waits=[t_gelus[15], t_pofree[pop], t_b2]
                                     + t_wload[kbase + 2] + t_wload[kbase + 3])

                        def evac(v, e=e, il=il, pop=pop, c0=c0):
                            i_glob = c0 + il
                            for piece in range(2):
                                last = v.scalar_tensor_tensor(
                                    acc[:, il * D + piece * 512:il * D + (piece + 1) * 512],
                                    po[:, pop * 1024 + piece * 512:
                                       pop * 1024 + (piece + 1) * 512],
                                    wts[:, i_glob * 8 + e:i_glob * 8 + e + 1],
                                    acc[:, il * D + piece * 512:il * D + (piece + 1) * 512],
                                    ALU.mult, ALU.add)
                            return last
                        t_ev = pb.op("vector", evac, waits=[t_w2, t_az0])
                        t_pofree[pop] = t_ev
                        t_evacs.append(t_ev)
                    t_hbfree[tp] = t_w2
                t_wuse[kbase + 2] = t_w2
                t_wuse[kbase + 3] = t_w2
                for k in range(kbase + NWB, min(kbase + 4 + NWB, nloads)):
                    if len(t_wload) <= k:
                        emit_wload(k)
            for il in range(pc):
                i = c0 + il
                p = il % 2
                t_yre = pb.op("sync", lambda s, i=i: s.dma_start(
                    out=yre[:], in_=y1_in[i * 128:(i + 1) * 128, :]),
                    waits=[t_yrefree], dma=True)
                t_t1 = pb.op("vector", lambda v, p=p, i=i: v.tensor_scalar(
                    y2c[:, p * D:(p + 1) * D], yre[:],
                    P0[:, i:i + 1], P1[:, i:i + 1], ALU.mult, ALU.subtract),
                    waits=[t_yre, t_y2free[p]] + t_evacs)
                t_t2 = pb.op("vector", lambda v, p=p, il=il: v.tensor_tensor(
                    y2c[:, p * D:(p + 1) * D], y2c[:, p * D:(p + 1) * D],
                    acc[:, il * D:(il + 1) * D], ALU.add), waits=[t_t1])
                t_yrefree = t_t2
                t_st = pb.op("sync", lambda s, i=i, p=p: s.dma_start(
                    out=y2[i * 128:(i + 1) * 128, :], in_=y2c[:, p * D:(p + 1) * D]),
                    waits=[t_t2], dma=True)
                t_y2free[p] = t_st
                t_y2st.append(t_st)
            t_accfree = t_y2st[-1]
        pb.op("sync", lambda s: s.nop(), waits=t_y2st[-2:])
        pb.barrier()
        ph2.close()
    glob.close()



def ceil_div(a, b):
    return -(-a // b)


def build_l1a(nqg=4, skc=None):
    if skc is None:
        skc = nqg * 8
    nc = bass.Bass("TRN2", target_bir_lowering=False)
    NQ = nqg * 512
    NS = skc * 128
    io = {
        'y2_seq': nc.dram_tensor("y2_seq", [NS, D], F32, kind="ExternalInput").ap(),
        'y2_own': nc.dram_tensor("y2_own", [NQ, D], F32, kind="ExternalInput").ap(),
        'w_q': nc.dram_tensor("w_q", [D, D], F32, kind="ExternalInput").ap(),
        'w_kv': nc.dram_tensor("w_kv", [D, 2 * D], F32, kind="ExternalInput").ap(),
        'w_out': nc.dram_tensor("w_out", [D, D], F32, kind="ExternalInput").ap(),
        'kcs': nc.dram_tensor("kcs", [128, skc * 64], F32, kind="ExternalInput").ap(),
        'qcs': nc.dram_tensor("qcs", [128, (NQ // 128) * 64], F32, kind="ExternalInput").ap(),
        'cmask': nc.dram_tensor("cmask", [128, nqg * 8 * 128], BF16, kind="ExternalInput").ap(),
        'identm': nc.dram_tensor("identm", [128, 128], BF16, kind="ExternalInput").ap(),
        'y3': nc.dram_tensor("y3", [NQ, D], F32, kind="ExternalOutput").ap(),
    }
    with nc.Block() as blk:
        pb = PB(nc)
        emit_l1a(nc, pb, blk, io, nqg, skc)
        pb.emit(blk)
    return nc


def emit_l1a(nc, pb, blk, io, nqg=4, skc=None, tag='', barrier=True, mid_hook=None, bg=None):
    if skc is None:
        skc = nqg * 8
    NQ = nqg * 512
    NS = skc * 128
    (y2_seq, y2_own, w_q, w_kv, w_out, kcs, qcs, cmask, ident_in, y3) = (
        io['y2_seq'], io['y2_own'], io['w_q'], io['w_kv'], io['w_out'],
        io['kcs'], io['qcs'], io['cmask'], io['identm'], io['y3'])
    qk_dram = nc.dram_tensor("l1qk_scr" + tag, [NS, D], BF16).ap()
    q_dram = nc.dram_tensor("q_scr" + tag, [NQ, D], BF16).ap()
    o_dram = nc.dram_tensor("o_scr" + tag, [D, NQ], BF16).ap()

    glob = Alloc(nc)
    ident = glob.sb("ident", [128, 128], BF16)
    onesrow = glob.sb("onesrow", [1, 128], BF16)
    onesf = glob.sb("onesf", [1, 64], F32)
    consts = glob.sb("consts", [128, 4], F32)
    v_aug = glob.sb("v_aug", [128, skc * 16 * 65], BF16)
    kT = glob.sb("kT", [128, 8 * NS], BF16)
    if True:
        t_ones = pb.op("gpsimd", lambda g: g.memset(onesrow[:], 1.0))
        t_onesf = pb.op("gpsimd", lambda g: g.memset(onesf[:], 1.0))
        pb.op("gpsimd", lambda g: g.memset(consts[:, 0:1], LN_EPS))
        t_lne = pb.op("gpsimd", lambda g: g.memset(consts[:, 2:3], -0.5))
        t_off = pb.op("gpsimd", lambda g: g.memset(consts[:, 1:2], -SM_OFF))
        t_id = pb.op("gpsimd", lambda g: g.dma_start(out=ident[:], in_=ident_in[:]), dma=True)

        ph = Alloc(nc)
        wkv = ph.sb("wkv", [128, 8 * 2 * D], BF16)
        css = ph.sb("css", [128, 2 * 64], F32)
        yc = ph.sb("yc", [128, 2 * D], F32)
        lnc = ph.sb("lnc", [128, 2 * D], BF16)
        lnT = ph.sb("lnT", [128, 2 * 8 * 128], BF16)
        sqs = ph.sb("sqs", [128, D], F32)
        S8 = ph.sb("S8", [128, 16], F32)
        kf = ph.sb("kf", [128, 2 * D], F32)
        tmp = ph.sb("tmp", [128, 512], F32)
        kb = ph.sb("kb", [128, 2 * D], BF16)
        pT = [ph.ps("pT0", [128, 1024], BF16), ph.ps("pT1", [128, 1024], BF16)]
        pkv = ph.ps("pkv", [128, 2048], F32)
        pq = ph.ps("pq", [128, 1024], F32)

        t_wkv = [pb.op("gpsimd", lambda g, j=j: g.dma_start(
            out=wkv[:, j * 2048:(j + 1) * 2048],
            in_=w_kv[j * 128:(j + 1) * 128, :]), dma=True) for j in range(8)]
        t_wq = []
        t_css = [None, None]

        rope_chain = [None]

        def rope_inplace(pb_, X, CS, c, waits):
            X3 = X.rearrange("p (h e) -> p h e", e=64)
            C = CS[:, 0:32]
            Sn = CS[:, 32:64]

            def b(col0, n):
                return lambda q: q
            t1 = pb_.op("vector", lambda v: v.tensor_tensor(
                tmp[:, 0:256].rearrange("p (h e) -> p h e", e=16),
                X3[:, :, 16:32],
                Sn[:, 0:16].unsqueeze(1).broadcast_to([128, 16, 16]), ALU.mult),
                waits=list(waits) + [rope_chain[0]])
            t2 = pb_.op("vector", lambda v: v.tensor_tensor(
                tmp[:, 256:512].rearrange("p (h e) -> p h e", e=16),
                X3[:, :, 0:16],
                Sn[:, 16:32].unsqueeze(1).broadcast_to([128, 16, 16]), ALU.mult),
                waits=waits)
            t3 = pb_.op("vector", lambda v: v.tensor_tensor(
                X3[:, :, 0:16], X3[:, :, 0:16],
                C[:, 0:16].unsqueeze(1).broadcast_to([128, 16, 16]), ALU.mult),
                waits=[t1, t2])
            t4 = pb_.op("vector", lambda v: v.tensor_tensor(
                X3[:, :, 16:32], X3[:, :, 16:32],
                C[:, 16:32].unsqueeze(1).broadcast_to([128, 16, 16]), ALU.mult),
                waits=[t3])
            t5 = pb_.op("vector", lambda v: v.tensor_tensor(
                X3[:, :, 0:16], X3[:, :, 0:16],
                tmp[:, 0:256].rearrange("p (h e) -> p h e", e=16), ALU.subtract),
                waits=[t4])
            t6 = pb_.op("vector", lambda v: v.tensor_tensor(
                X3[:, :, 16:32], X3[:, :, 16:32],
                tmp[:, 256:512].rearrange("p (h e) -> p h e", e=16), ALU.add),
                waits=[t5])
            rope_chain[0] = t6
            return t6

        t_kst = []
        t_qst = []
        NCH = skc + NQ // 128
        SAB = ph.sb("SAB", [128, 4 * (NCH)], F32)
        SA, SB = SAB[:, 0:NCH], SAB[:, NCH:2 * NCH]
        MEAN, RSTD = SAB[:, 2 * NCH:3 * NCH], SAB[:, 3 * NCH:4 * NCH]
        t_sfree = [None, None]
        GS = 8
        t_gstats = {}
        for i in range(NCH):
            p = i % 2
            ys = yc[:, p * D:(p + 1) * D]
            src_ = y2_seq[i * 128:(i + 1) * 128, :] if i < skc else \
                y2_own[(i - skc) * 128:(i - skc + 1) * 128, :]
            t_y = pb.op("sync", lambda s, src_=src_, ys=ys: s.dma_start(out=ys, in_=src_),
                        waits=[t_sfree[p]], dma=True)
            t_sfree[p] = emit_sums(pb, ys, SA[:, i:i + 1], SB[:, i:i + 1], sqs,
                                   [t_y, t_sfree[1 - p]])
            if i % GS == GS - 1 or i == NCH - 1:
                g = i // GS
                lo, hi = g * GS, i + 1
                t_gstats[g] = emit_stats_batch(
                    pb, SA[:, lo:hi], SB[:, lo:hi], MEAN[:, lo:hi], RSTD[:, lo:hi],
                    consts[:, 0:1], consts[:, 2:3],
                    [t_sfree[0], t_sfree[1], t_lne])
        t_yfree = None
        t_lnTfree = [None, None]
        t_lncfree = [None, None]
        t_kffree = [None, None]
        t_kbfree = [None, None]
        t_pkvfree = None
        t_pqfree = [None, None]
        t_vone = None
        for i in range(NCH):
            isq = i < skc
            io = i - skc
            p = i % 2
            ys = yc[:, p * D:(p + 1) * D]
            lns = lnc[:, p * D:(p + 1) * D]
            src_ = y2_seq[i * 128:(i + 1) * 128, :] if isq else \
                y2_own[io * 128:(io + 1) * 128, :]
            t_y = pb.op("sync", lambda s, src_=src_, ys=ys: s.dma_start(out=ys, in_=src_),
                        waits=[t_yfree] if i >= 2 else [t_sfree[0], t_sfree[1]], dma=True)
            t_ln = pb.op("vector", lambda v, ys=ys, lns=lns, i=i: v.tensor_scalar(
                lns, ys, MEAN[:, i:i + 1], RSTD[:, i:i + 1], ALU.subtract, ALU.mult),
                waits=[t_y, t_gstats[i // GS], t_lncfree[p]])
            t_yfree = t_ln
            t_lnTs = []
            t_tp = None
            for j in range(8):
                pj = (i * 8 + j) % 2
                t_tp = pb.op("tensor", lambda t, lns=lns, j=j, pj=pj: t.transpose(
                    pT[pj][:, 0:128], lns[:, j * 128:(j + 1) * 128], ident[:]),
                    waits=[t_ln, t_id, t_lnTfree[pj]])
                t_ev = pb.op("vector", lambda v, j=j, p=p, pj=pj: v.tensor_copy(
                    lnT[:, (p * 8 + j) * 128:(p * 8 + j + 1) * 128], pT[pj][:, 0:128]),
                    waits=[t_tp])
                t_lnTfree[pj] = t_ev
                t_lnTs.append(t_ev)
            t_lncfree[p] = t_tp
            if isq:
                def kv_mms(t, p=p):
                    pass
                def kv_mms(t, p=p):
                    for piece in range(4):
                        for j in range(8):
                            last = t.matmul(
                                pkv[:, piece * 512:(piece + 1) * 512],
                                lnT[:, (p * 8 + j) * 128:(p * 8 + j + 1) * 128],
                                wkv[:, j * 2048 + piece * 512:j * 2048 + (piece + 1) * 512],
                                start=(j == 0), stop=(j == 7))
                    return last
                t_kv = pb.op("tensor", kv_mms, waits=t_lnTs + t_wkv + [t_pkvfree])
                if i == skc - 1:
                    t_kv_last = t_kv
                t_vev = pb.op("vector", lambda v, i=i: v.tensor_copy(
                    v_aug[:, i * 1040:(i + 1) * 1040].rearrange("p (h e) -> p h e", e=65)[:, :, 0:64],
                    pkv[:, 1024:2048].rearrange("p (h e) -> p h e", e=64)), waits=[t_kv])
                t_vone = pb.op("vector", lambda v, i=i: v.memset(
                    v_aug[:, i * 1040 + 64:(i + 1) * 1040:65], 1.0), waits=[t_vev])
                t_kev = pb.op("vector", lambda v, p=p: v.tensor_copy(
                    kf[:, p * D:(p + 1) * D], pkv[:, 0:1024]),
                    waits=[t_kv, t_kffree[p]])
                t_pkvfree = t_vev if False else t_kev
                t_cl = pb.op("scalar", lambda g, i=i, p=p: g.dma_start(
                    out=css[:, p * 64:(p + 1) * 64], in_=kcs[:, i * 64:(i + 1) * 64]),
                    waits=[t_css[p]], dma=True)
                t_rope = rope_inplace(pb, kf[:, p * D:(p + 1) * D],
                                      css[:, p * 64:(p + 1) * 64], i, [t_kev, t_cl])
                t_css[p] = t_rope
                t_kb = pb.op("vector", lambda v, p=p: v.tensor_copy(
                    kb[:, p * D:(p + 1) * D], kf[:, p * D:(p + 1) * D]),
                    waits=[t_rope, t_kbfree[p]])
                t_kffree[p] = t_kb
                t_st = pb.op("sync", lambda s, i=i, p=p: s.dma_start(
                    out=qk_dram[i * 128:(i + 1) * 128, 0:1024], in_=kb[:, p * D:(p + 1) * D]),
                    waits=[t_kb], dma=True)
                t_kbfree[p] = t_st
                t_kst.append(t_st)
            else:
                if io == 0:
                    t_wq = [pb.op("gpsimd", lambda g, j=j: g.dma_start(
                        out=wkv[:, j * 2048:j * 2048 + 1024],
                        in_=w_q[j * 128:(j + 1) * 128, :]),
                        waits=[t_kv_last], dma=True) for j in range(8)]

                def q_mms(t, p=p):
                    for piece in range(2):
                        for j in range(8):
                            last = t.matmul(
                                pq[:, piece * 512:(piece + 1) * 512],
                                lnT[:, (p * 8 + j) * 128:(p * 8 + j + 1) * 128],
                                wkv[:, j * 2048 + piece * 512:j * 2048 + (piece + 1) * 512],
                                start=(j == 0), stop=(j == 7))
                    return last
                t_q = pb.op("tensor", q_mms, waits=t_lnTs + t_wq + [t_pqfree[0]])
                t_qev = pb.op("vector", lambda v, p=p: v.tensor_scalar_mul(
                    kf[:, p * D:(p + 1) * D], pq[:], 0.125),
                    waits=[t_q, t_kffree[p]])
                t_pqfree[0] = t_qev
                t_cl = pb.op("scalar", lambda g, io=io, p=p: g.dma_start(
                    out=css[:, p * 64:(p + 1) * 64], in_=qcs[:, io * 64:(io + 1) * 64]),
                    waits=[t_css[p]], dma=True)
                t_rope = rope_inplace(pb, kf[:, p * D:(p + 1) * D],
                                      css[:, p * 64:(p + 1) * 64], io, [t_qev, t_cl])
                t_css[p] = t_rope
                t_kb = pb.op("vector", lambda v, p=p: v.tensor_copy(
                    kb[:, p * D:(p + 1) * D], kf[:, p * D:(p + 1) * D]),
                    waits=[t_rope, t_kbfree[p]])
                t_kffree[p] = t_kb
                t_st = pb.op("sync", lambda s, io=io, p=p: s.dma_start(
                    out=q_dram[io * 128:(io + 1) * 128, :], in_=kb[:, p * D:(p + 1) * D]),
                    waits=[t_kb], dma=True)
                t_kbfree[p] = t_st
                t_qst.append(t_st)
        t_kT = [pb.op("sync", lambda s, j=j: s.dma_start(
            out=kT[:, j * NS:(j + 1) * NS],
            in_=qk_dram[:, j * 128:(j + 1) * 128], transpose=True),
            waits=t_kst, dma=True) for j in range(8)]
        pb.barrier()
        ph.close()
        if mid_hook is not None:
            tails = [pb.streams[e][-1][2] for e in ENGINES if pb.streams[e]]
            mid_hook(tails)

        ph3 = Alloc(nc)
        qTg = ph3.sb("qTg", [128, 2 * 8 * 512], BF16)
        mk = ph3.sb("mk", [128, 2 * 8 * 128], BF16)
        expt = ph3.sb("expt", [128, 2 * 1024], BF16)
        rl = ph3.sb("rl", [1, 3 * 512], F32)
        rlbc = ph3.sb("rlbc", [64, 3 * 512], F32)
        oT = ph3.sb("oT", [128, 3 * 512], BF16)
        psc = ph3.ps("psc", [128, 2 * 1024], F32)
        pav = ph3.ps("pav", [65, 2 * 512], F32)
        pbc = ph3.ps("pbc", [64, 512], F32)

        t_qT = [[None] * 8 for _ in range(2)]
        t_mkl = [None, None]
        t_qTfree = [[None] * 8 for _ in range(2)]
        t_mkfree = [None, None]
        t_pscfree = [None, None]
        t_expfree = [None, None]
        t_pavfree = [None, None]
        pb.pair_ctr = 0
        t_rlf = None
        t_bcf = None
        t_oTfree = [None, None]
        t_ost = []
        for qg in range(nqg):
            gp = qg % 2
            t_qT[gp] = [pb.op("sync", lambda s, j=j, qg=qg, gp=gp: s.dma_start(
                out=qTg[:, (gp * 8 + j) * 512:(gp * 8 + j + 1) * 512],
                in_=q_dram[qg * 512:(qg + 1) * 512, j * 128:(j + 1) * 128], transpose=True),
                waits=t_qst + [t_qTfree[gp][j]], dma=True) for j in range(8)]
            t_mkl[gp] = pb.op("scalar", lambda g, qg=qg, gp=gp: g.dma_start(
                out=mk[:, gp * 1024:(gp + 1) * 1024],
                in_=cmask[:, qg * 1024:(qg + 1) * 1024]),
                waits=[t_mkfree[gp]], dma=True)
            qg_last_avs = []

            def head_pipeline(h, qg=qg, gp=gp):
                nonlocal t_rlf, t_bcf
                u = qg * 16 + h
                pu = u % 2
                tile, row0 = h // 2, 64 * (h % 2)
                nkc = 8 * qg + 8
                t_avs = []
                for k0 in range(0, nkc, 2):
                    pr = (pb.pair_ctr) % 2
                    pb.pair_ctr += 1
                    kcs_ = [kc for kc in (k0, k0 + 1) if kc < nkc]
                    t_mms = []
                    cols = []
                    for ii, kc in enumerate(kcs_):
                        thresh = max(0, min(4, ceil_div(kc - 1 - 8 * qg, 2)))
                        ncol = (4 - thresh) * 128
                        col0 = ii * 512 + thresh * 128
                        cols.append((kc, col0, ncol, thresh))
                        t_sc = pb.op("tensor", lambda t, tile=tile, row0=row0, kc=kc,
                                     qg=qg, gp=gp, pr=pr, col0=col0, ncol=ncol,
                                     ii=ii: t.matmul(
                                         psc[:, pr * 1024 + col0:pr * 1024 + col0 + ncol],
                                         kT[row0:row0 + 64, tile * NS + kc * 128:
                                            tile * NS + (kc + 1) * 128],
                                         qTg[row0:row0 + 64,
                                             (gp * 8 + tile) * 512 + col0 - ii * 512:
                                             (gp * 8 + tile) * 512 + col0 - ii * 512 + ncol],
                                         start=True, stop=True),
                                     waits=t_kT + [t_qT[gp][tile], t_pscfree[pr]])
                        t_mms.append(t_sc)
                    spans = []
                    for (kc, col0, ncol, thresh) in cols:
                        if spans and spans[-1][1] == col0:
                            spans[-1][1] = col0 + ncol
                        else:
                            spans.append([col0, col0 + ncol])
                    t_exp = None
                    for (e0, e1) in spans:
                        t_exp = pb.op("scalar", lambda s, pr=pr, e0=e0, e1=e1: s.activation(
                            expt[:, pr * 1024 + e0:pr * 1024 + e1],
                            psc[:, pr * 1024 + e0:pr * 1024 + e1],
                            AF.Exp, bias=consts[:, 1:2]),
                            waits=t_mms + [t_off, t_expfree[pr]])
                    t_pscfree[pr] = t_exp
                    t_last = t_exp
                    for (kc, col0, ncol, thresh) in cols:
                        if kc >= 8 * qg:
                            t_m = pb.op("vector", lambda v, pr=pr, gp=gp, kc=kc, qg=qg,
                                        col0=col0: v.tensor_tensor(
                                            expt[:, pr * 1024 + col0:pr * 1024 + col0 + 128],
                                            expt[:, pr * 1024 + col0:pr * 1024 + col0 + 128],
                                            mk[:, gp * 1024 + (kc - 8 * qg) * 128:
                                               gp * 1024 + (kc - 8 * qg) * 128 + 128],
                                            ALU.mult),
                                        waits=[t_exp, t_mkl[gp]])
                        else:
                            t_m = t_exp
                        t_av = pb.op("tensor", lambda t, kc=kc, h=h, pr=pr, pu=pu,
                                     col0=col0, ncol=ncol: t.matmul(
                                         pav[:, pu * 512 + col0 % 512:
                                             pu * 512 + col0 % 512 + ncol],
                                         v_aug[:, kc * 1040 + h * 65:kc * 1040 + (h + 1) * 65],
                                         expt[:, pr * 1024 + col0:pr * 1024 + col0 + ncol],
                                         start=(kc == 0), stop=(kc == nkc - 1)),
                                     waits=[t_m, t_vone, t_pavfree[pu]] if kc == 0 else [t_m])
                        t_last = t_av
                        t_avs.append(t_av)
                    t_expfree[pr] = t_last
                    yield
                t_rl = pb.op("scalar", lambda s, pu=pu: s.activation(
                    rl[0:1, pu * 512:(pu + 1) * 512], pav[64:65, pu * 512:(pu + 1) * 512],
                    AF.Ln), waits=[t_avs[-1], t_rlf])
                t_rl = pb.op("scalar", lambda s, pu=pu: s.activation(
                    rl[0:1, pu * 512:(pu + 1) * 512], rl[0:1, pu * 512:(pu + 1) * 512],
                    AF.Exp, scale=-1.0), waits=[t_rl])
                t_bc = pb.op("tensor", lambda t, pu=pu: t.matmul(
                    pbc[:], onesf[0:1, 0:64], rl[0:1, pu * 512:(pu + 1) * 512],
                    start=True, stop=True), waits=[t_rl, t_bcf, t_onesf])
                t_rlf = t_bc
                t_bcev = pb.op("vector", lambda v, pu=pu: v.tensor_copy(
                    rlbc[:, pu * 512:(pu + 1) * 512], pbc[:]), waits=[t_bc])
                t_bcf = t_bcev
                t_fin = pb.op("vector", lambda v, pu=pu: v.scalar_tensor_tensor(
                    oT[row0:row0 + 64, pu * 512:(pu + 1) * 512] if False else
                    oT[0:64, pu * 512:(pu + 1) * 512],
                    pav[0:64, pu * 512:(pu + 1) * 512], 1.0,
                    rlbc[:, pu * 512:(pu + 1) * 512], ALU.mult, ALU.mult),
                    waits=[t_bcev, t_oTfree[pu]])
                t_pavfree[pu] = t_fin
                t_o = pb.op("sync", lambda s, h=h, qg=qg, pu=pu: s.dma_start(
                    out=o_dram[h * 64:(h + 1) * 64, qg * 512:(qg + 1) * 512],
                    in_=oT[0:64, pu * 512:(pu + 1) * 512]), waits=[t_fin], dma=True)
                t_oTfree[pu] = t_o
                t_ost.append(t_o)
                qg_last_avs.append(t_avs[-1])
                if bg is not None:
                    next(bg, None)

            for h0 in range(0, 16, 2):
                gens = [head_pipeline(h0), head_pipeline(h0 + 1)]
                while gens:
                    gens = [g for g in gens if next(g, 'done') != 'done']
                    if bg is not None:
                        for _ in range(3):
                            next(bg, None)
            for j in range(8):
                t_qTfree[gp][j] = qg_last_avs[-1]
            t_mkfree[gp] = qg_last_avs[-1]
        pb.barrier()
        ph3.close()

        ph4 = Alloc(nc)
        wos = ph4.sb("wos", [128, 8 * D], BF16)
        oTl = ph4.sb("oTl", [128, 2 * 8 * 128], BF16)
        yre = ph4.sb("yre", [128, 2 * D], F32)
        y3c = ph4.sb("y3c", [128, 2 * D], F32)
        pproj = ph4.ps("pproj", [128, 1024], F32)
        t_wo = [pb.op("gpsimd", lambda g, j=j: g.dma_start(
            out=wos[:, j * D:(j + 1) * D], in_=w_out[j * 128:(j + 1) * 128, :]),
            dma=True) for j in range(8)]
        t_y3 = [None, None]
        t_oTl = [None, None]
        t_pprojfree = None
        for qc in range(NQ // 128):
            p = qc % 2
            t_ol = [pb.op("scalar", lambda g, j=j, qc=qc, p=p: g.dma_start(
                out=oTl[:, (p * 8 + j) * 128:(p * 8 + j + 1) * 128],
                in_=o_dram[j * 128:(j + 1) * 128, qc * 128:(qc + 1) * 128]),
                waits=t_ost + [t_oTl[p]], dma=True) for j in range(8)]
            t_xr = pb.op("sync", lambda s, qc=qc, p=p: s.dma_start(
                out=yre[:, p * D:(p + 1) * D], in_=y2_own[qc * 128:(qc + 1) * 128, :]),
                waits=[t_y3[p]], dma=True)

            def proj_mms(t, p=p):
                for piece in range(2):
                    for j in range(8):
                        last = t.matmul(pproj[:, piece * 512:(piece + 1) * 512],
                                        oTl[:, (p * 8 + j) * 128:(p * 8 + j + 1) * 128],
                                        wos[:, j * D + piece * 512:j * D + (piece + 1) * 512],
                                        start=(j == 0), stop=(j == 7))
                return last
            t_proj = pb.op("tensor", proj_mms, waits=t_ol + t_wo + [t_pprojfree])
            t_y3c = pb.op("vector", lambda v, p=p: v.scalar_tensor_tensor(
                y3c[:, p * D:(p + 1) * D], pproj[:], 1.0,
                yre[:, p * D:(p + 1) * D], ALU.mult, ALU.add), waits=[t_proj, t_xr])
            t_pprojfree = t_y3c
            t_st = pb.op("sync", lambda s, qc=qc, p=p: s.dma_start(
                out=y3[qc * 128:(qc + 1) * 128, :], in_=y3c[:, p * D:(p + 1) * D]),
                waits=[t_y3c], dma=True)
            t_y3[p] = t_st
            t_oTl[p] = t_proj
        pb.op("sync", lambda s: s.nop(), waits=[t_y3[0], t_y3[1]])
        if barrier:
            pb.barrier()
        ph4.close()
    glob.close()


def l1a_rope_tables(positions):
    ROT = 32
    inv = 1.0 / (10000.0 ** (np.arange(0, ROT, 2) / ROT))
    f = positions[:, None] * inv[None, :]
    fr = np.concatenate([f, f], axis=-1)
    cs = np.concatenate([np.cos(fr), np.sin(fr)], axis=-1)
    n = len(positions) // 128
    return f32(cs.reshape(n, 128, 64).transpose(1, 0, 2).reshape(128, n * 64))


def l1a_masks(h, nqg=4):
    out = np.zeros((nqg, 8, 128, 128), dtype=np.float32)
    for qg in range(nqg):
        for dk in range(8):
            kc = 8 * qg + dk
            thresh = max(0, min(4, ceil_div(kc - 1 - 8 * qg, 2)))
            j0 = thresh
            gqc = 8 * qg + 2 * j0 + h
            kpos = kc * 128 + np.arange(128)[:, None]
            qpos = gqc * 128 + np.arange(128)[None, :]
            out[qg, dk] = (kpos <= qpos).astype(np.float32)
            for j in range(thresh + 1, 4):
                gq = 8 * qg + 2 * j + h
                assert kc * 128 + 127 <= gq * 128, (qg, kc, j, h)
    return bf(out.reshape(nqg * 8, 128, 128).transpose(1, 0, 2).reshape(128, nqg * 8 * 128))


def own_rows(h, noc=16):
    idx = []
    for o in range(noc):
        g = 2 * o + h
        idx.extend(range(g * 128, (g + 1) * 128))
    return np.array(idx)



def gen_quant(pb, eng, X, W, scr, waits, out, box):
    for step in _quant_steps(pb, eng, X, W, scr, waits, out, box):
        yield


def emit_quant(pb, eng, X, W, scr, waits, out=None):
    box = {}
    for _ in _quant_steps(pb, eng, X, W, scr, waits, out, box):
        pass
    return box['tok']


def _quant_steps(pb, eng, X, W, scr, waits, out, box):
    NB = W // 16
    bm, sc, rsc = scr['bm'][:, 0:NB], scr['bm'][:, NB:2 * NB], scr['bm'][:, 2 * NB:3 * NB]
    ax, idx = scr['ax'][:, 0:W], scr['idx'][:, 0:W]
    s2 = scr['s2'][:, 0:W]
    dst = out if out is not None else X
    dst3 = dst.rearrange("p (n b) -> p n b", b=16)
    ax3 = ax.rearrange("p (n b) -> p n b", b=16)
    idx3 = idx.rearrange("p (n b) -> p n b", b=16)
    mids = [0.25, 0.75, 1.25, 1.75, 2.5, 3.5, 5.0]
    ea = pb.op("scalar", lambda s: s.activation(ax, X, AF.Abs), waits=waits)
    yield
    es = pb.op("scalar", lambda s: s.activation(s2, X, AF.Sign), waits=waits)
    yield
    e = pb.op(eng, lambda v: v.tensor_reduce(bm, ax3, AX.X, ALU.max), waits=[ea])
    yield
    e = pb.op(eng, lambda v: v.tensor_scalar(sc, bm, 1.0 / 6.0, 1e-12,
                                             ALU.mult, ALU.max), waits=[e])
    er = pb.op("vector", lambda v: v.reciprocal(rsc, sc), waits=[e])
    yield
    e = pb.op(eng, lambda v: v.tensor_tensor(
        ax3, ax3, rsc.unsqueeze(2).broadcast_to([128, NB, 16]), ALU.mult), waits=[er])
    yield
    e = pb.op(eng, lambda v: v.tensor_scalar(idx, ax, mids[0], None, ALU.is_gt),
              waits=[e])
    yield
    for m in mids[1:]:
        e = pb.op(eng, lambda v, m=m: v.scalar_tensor_tensor(
            idx, ax, m, idx, ALU.is_gt, ALU.add), waits=[e])
        yield
    e1 = pb.op("scalar", lambda s: s.activation(ax, idx, AF.Relu, bias=scr['c4']),
               waits=[e, scr['ctok']])
    yield
    e3 = pb.op(eng, lambda v: v.tensor_scalar_mul(idx, idx, 0.5), waits=[e1])
    yield
    e4 = pb.op(eng, lambda v: v.scalar_tensor_tensor(idx, ax, 0.5, idx,
                                                     ALU.mult, ALU.add), waits=[e3])
    yield
    e2 = pb.op("scalar", lambda s: s.activation(ax, ax, AF.Relu, bias=scr['c2']),
               waits=[e4])
    yield
    e5 = pb.op(eng, lambda v: v.tensor_tensor(idx, idx, ax, ALU.add), waits=[e2])
    yield
    e6 = pb.op(eng, lambda v: v.tensor_tensor(idx, idx, s2, ALU.mult), waits=[e5, es])
    yield
    e7 = pb.op(eng, lambda v: v.tensor_tensor(
        dst3, idx3, sc.unsqueeze(2).broadcast_to([128, NB, 16]), ALU.mult), waits=[e6])
    box['tok'] = e7


def build_l1f(ntc=16):
    nc = bass.Bass("TRN2", target_bir_lowering=False)
    NT = ntc * 128
    I = 2816
    io = {
        'y3_in': nc.dram_tensor("y3_in", [NT, D], F32, kind="ExternalInput").ap(),
        'w_gate': nc.dram_tensor("w_gate", [I, D], F32, kind="ExternalInput").ap(),
        'w_up': nc.dram_tensor("w_up", [I, D], F32, kind="ExternalInput").ap(),
        'w_down': nc.dram_tensor("w_down", [D, I], F32, kind="ExternalInput").ap(),
        'identm': nc.dram_tensor("identm", [128, 128], BF16, kind="ExternalInput").ap(),
        'y4': nc.dram_tensor("y4", [NT, D], F32, kind="ExternalOutput").ap(),
    }
    with nc.Block() as blk:
        pb = PB(nc)
        emit_l1f(nc, pb, blk, io, ntc)
        pb.emit(blk)
    return nc


def gen_l1f_weights(nc, pb, io, scr, wq_dram, t_wst, tag='', waits0=()):
    I = 2816
    NI = I // 128
    w_gate, w_up, w_down = io['w_gate'], io['w_up'], io['w_down']
    wc, wb = scr['wc'], scr['wb']
    t_wqchain = None
    t_wcfree = [None, None]
    t_wbfree = [None, None]
    jobs = [(w_gate, 0, NI, 1024, 0), (w_up, I, NI, 1024, 0),
            (w_down, 2 * I, 8, 1408, 0), (w_down, 2 * I, 8, 1408, 1408)]
    for (wsrc, row0, nch, wid, co) in jobs:
        for ch in range(nch):
            p = ch % 2
            WC = wc[:, p * 1408:p * 1408 + wid]
            t_l = pb.op("sync", lambda s, wsrc=wsrc, ch=ch, WC=WC, wid=wid, co=co:
                        s.dma_start(out=WC, in_=wsrc[ch * 128:(ch + 1) * 128, co:co + wid]),
                        waits=[t_wcfree[p]] + list(waits0), dma=True)
            box = {}
            for _ in gen_quant(pb, "vector", WC, wid, scr,
                               [t_l, t_wqchain, t_wbfree[p]],
                               wb[:, p * 1408:p * 1408 + wid], box):
                yield
            t_q = box['tok']
            t_wqchain = t_q
            t_wcfree[p] = t_q
            t_s = pb.op("sync", lambda s, row0=row0, ch=ch, p=p, wid=wid, co=co:
                        s.dma_start(out=wq_dram[row0 + ch * 128:row0 + (ch + 1) * 128,
                                                co:co + wid],
                                    in_=wb[:, p * 1408:p * 1408 + wid]),
                        waits=[t_q], dma=True)
            t_wbfree[p] = t_s
            t_wst.append(t_s)
            yield


def emit_l1f_weights(nc, pb, io, scr, wq_dram, tag='', waits0=()):
    t_wst = []
    for _ in gen_l1f_weights(nc, pb, io, scr, wq_dram, t_wst, tag, waits0):
        pass
    return t_wst


def emit_l1f(nc, pb, blk, io, ntc=16, tag='', t_wst=None, wq_dram=None):
    NT = ntc * 128
    I = 2816
    NI = I // 128
    (y3_in, w_gate, w_up, w_down, ident_in, y4) = (
        io['y3_in'], io['w_gate'], io['w_up'], io['w_down'], io['identm'], io['y4'])
    if wq_dram is None:
        wq_dram = nc.dram_tensor("wq_scr" + tag, [2 * I + D, max(D, I)], BF16).ap()

    glob = Alloc(nc)
    ident = glob.sb("ident", [128, 128], BF16)
    consts = glob.sb("consts", [128, 4], F32)
    wgT = glob.sb("wgT", [128, 8 * I], BF16)
    wuT = glob.sb("wuT", [128, 8 * I], BF16)
    wdT = glob.sb("wdT", [128, NI * D], BF16)
    if True:
        pb.op("gpsimd", lambda g: g.memset(consts[:, 0:1], LN_EPS))
        t_lne = pb.op("gpsimd", lambda g: g.memset(consts[:, 2:3], -0.5))
        t_id = pb.op("gpsimd", lambda g: g.dma_start(out=ident[:], in_=ident_in[:]), dma=True)

        if t_wst is None:
            phw = Alloc(nc)
            scrw = {
                'wc': phw.sb("wc", [128, 2 * 1408], F32),
                'wb': phw.sb("wb", [128, 2 * 1408], BF16),
                'bm': phw.sb("qbm", [128, 3 * 88], F32),
                'ax': phw.sb("qax", [128, 1408], F32),
                'idx': phw.sb("qidx", [128, 1408], F32),
                's2': phw.sb("qs2", [128, 1408], F32),
            }
            cq = phw.sb("qc", [128, 2], F32)
            scrw['c4'] = cq[:, 0:1]
            scrw['c2'] = cq[:, 1:2]
            pb.op("gpsimd", lambda g: g.memset(cq[:, 0:1], -4.0))
            scrw['ctok'] = pb.op("gpsimd", lambda g: g.memset(cq[:, 1:2], -2.0))
            t_wst = emit_l1f_weights(nc, pb, io, scrw, wq_dram, tag)
            pb.barrier()
            phw.close()
        t_wgT = [pb.op("sync", lambda s, j=j: s.dma_start(
            out=wgT[:, j * I:(j + 1) * I],
            in_=wq_dram[0:I, j * 128:(j + 1) * 128], transpose=True),
            waits=t_wst, dma=True) for j in range(8)]
        t_wuT = [pb.op("sync", lambda s, j=j: s.dma_start(
            out=wuT[:, j * I:(j + 1) * I],
            in_=wq_dram[I:2 * I, j * 128:(j + 1) * 128], transpose=True),
            waits=t_wst, dma=True) for j in range(8)]
        t_wdT = [pb.op("sync", lambda s, j=j: s.dma_start(
            out=wdT[:, j * D:(j + 1) * D],
            in_=wq_dram[2 * I:2 * I + D, j * 128:(j + 1) * 128], transpose=True),
            waits=t_wst, dma=True) for j in range(NI)]

        pht = Alloc(nc)
        yc = pht.sb("yc", [128, D], F32)
        sqs = pht.sb("sqs", [128, D], F32)
        S8 = pht.sb("S8", [128, 16], F32)
        xf = pht.sb("xf", [128, D], F32)
        xqT = pht.sb("xqT", [128, 8 * 128], BF16)
        hq = pht.sb("hq", [128, I], F32)
        hqb = pht.sb("hqb", [128, I], BF16)
        qbm2 = pht.sb("qbm2", [128, 3 * (I // 16)], F32)
        qax = pht.sb("qax", [128, I], F32)
        qidx2 = pht.sb("qidx2", [128, I], F32)
        qs2 = pht.sb("qs2", [128, I], F32)
        xbm = pht.sb("xbm", [128, 3 * 64], F32)
        xidx = pht.sb("xidx", [128, D], F32)
        xs2 = pht.sb("xs2", [128, D], F32)
        cq2 = pht.sb("cq2", [128, 2], F32)
        pT = [pht.ps("pT0", [128, 1024], BF16), pht.ps("pT1", [128, 1024], BF16)]
        pgu = pht.ps("pgu", [128, 2048], F32)
        pd = pht.ps("pd", [128, 1024], F32)

        scr2 = {'bm': qbm2, 'ax': qax, 'idx': qidx2, 's2': qs2,
                'c4': cq2[:, 0:1], 'c2': cq2[:, 1:2]}
        pb.op("gpsimd", lambda g: g.memset(cq2[:, 0:1], -4.0))
        scr2['ctok'] = pb.op("gpsimd", lambda g: g.memset(cq2[:, 1:2], -2.0))
        scrx = {'bm': xbm, 'ax': sqs, 'idx': xidx, 's2': xs2,
                'c4': cq2[:, 0:1], 'c2': cq2[:, 1:2], 'ctok': scr2['ctok']}
        segs = [(0, 1024), (1024, 1024), (2048, 768)]
        SABt = pht.sb("SABt", [128, 4 * ntc], F32)
        SAt, SBt = SABt[:, 0:ntc], SABt[:, ntc:2 * ntc]
        MEANt, RSTDt = SABt[:, 2 * ntc:3 * ntc], SABt[:, 3 * ntc:4 * ntc]
        t_sfree = [None, None]
        for i in range(ntc):
            p = 0
            ys = yc[:]
            t_y = pb.op("sync", lambda s, i=i, ys=ys: s.dma_start(
                out=ys, in_=y3_in[i * 128:(i + 1) * 128, :]),
                waits=[t_sfree[0]], dma=True)
            t_sfree[0] = emit_sums(pb, ys, SAt[:, i:i + 1], SBt[:, i:i + 1], sqs,
                                   [t_y, t_sfree[0]])
        t_stats = emit_stats_batch(pb, SAt, SBt, MEANt, RSTDt, consts[:, 0:1],
                                   consts[:, 2:3], [t_sfree[0], t_sfree[1], t_lne])
        t_yfree = None
        t_qchain = None
        t_xchain = None
        t_xqTfree = [None]
        t_hqT_free = None
        t_lnTfree = [None, None]
        t_xffree = [None]
        t_hqbfree = None
        t_hqfree = [None]
        t_pgufree = [None, None]
        t_pdfree = None
        t_y4free = [None, None]
        t_y4st = []
        for i in range(ntc):
            p = i % 2
            ys = yc[:]
            XF = xf[:]
            t_y = pb.op("sync", lambda s, i=i, ys=ys: s.dma_start(
                out=ys, in_=y3_in[i * 128:(i + 1) * 128, :]),
                waits=[t_yfree] if i >= 1 else [t_sfree[0]], dma=True)
            t_ln = pb.op("vector", lambda v, ys=ys, XF=XF, i=i: v.tensor_scalar(
                XF, ys, MEANt[:, i:i + 1], RSTDt[:, i:i + 1], ALU.subtract, ALU.mult),
                waits=[t_y, t_stats, t_xffree[0]])
            t_xq = emit_quant(pb, "vector", XF, D, scrx, [t_ln, t_xchain, t_hqbfree],
                              out=hqb[:, 0:D])
            t_xchain = t_xq
            t_cb = t_xq
            t_xqTs = []
            t_tp = None
            for j in range(8):
                pj = (i * 8 + j) % 2
                t_tp = pb.op("tensor", lambda t, j=j, pj=pj: t.transpose(
                    pT[pj][:, 0:128], hqb[:, j * 128:(j + 1) * 128], ident[:]),
                    waits=[t_cb, t_id, t_lnTfree[pj]])
                t_ev = pb.op("vector", lambda v, j=j, pj=pj: v.tensor_copy(
                    xqT[:, j * 128:(j + 1) * 128], pT[pj][:, 0:128]),
                    waits=[t_tp] + ([t_xqTfree[0]] if j == 0 else []))
                t_lnTfree[pj] = t_ev
                t_xqTs.append(t_ev)
            t_hqbfree = t_tp
            t_hqs = []
            for (s0, sw) in segs:
                sp = (s0 // 1024) % 2

                def gu_mms(t, s0=s0, sw=sw, W_=None, off=0):
                    for half in range(ceil_div(sw, 512)):
                        cw = min(512, sw - half * 512)
                        for j in range(8):
                            last = t.matmul(
                                pgu[:, off + half * 512:off + half * 512 + cw],
                                xqT[:, j * 128:(j + 1) * 128],
                                W_[:, j * I + s0 + half * 512:j * I + s0 + half * 512 + cw],
                                start=(j == 0), stop=(j == 7))
                    return last
                t_g = pb.op("tensor", lambda t, f=gu_mms: f(t, W_=wgT, off=0),
                            waits=t_xqTs + t_wgT + [t_pgufree[0]])
                t_u = pb.op("tensor", lambda t, f=gu_mms: f(t, W_=wuT, off=1024),
                            waits=t_xqTs + t_wuT + [t_pgufree[1]])
                t_xqTfree[0] = t_u
                t_si = pb.op("scalar", lambda s, s0=s0, sw=sw: s.activation(
                    hq[:, s0:s0 + sw], pgu[:, 0:sw], AF.Silu),
                    waits=[t_g, t_hqfree[0]])
                t_pgufree[0] = t_si
                t_h = pb.op("vector", lambda v, s0=s0, sw=sw: v.scalar_tensor_tensor(
                    hq[:, s0:s0 + sw], pgu[:, 1024:1024 + sw], 1.0,
                    hq[:, s0:s0 + sw], ALU.mult, ALU.mult),
                    waits=[t_si, t_u])
                t_pgufree[1] = t_h
                t_hqs.append(t_h)
            t_qh = emit_quant(pb, "vector", hq[:, 0:I], I, scr2,
                              t_hqs + [t_qchain, t_hqbfree], out=hqb[:, 0:I])
            t_qchain = t_qh
            t_cbh = t_qh
            t_hqTs = []
            for j in range(NI):
                pj = j % 2
                t_tp2 = pb.op("tensor", lambda t, j=j, pj=pj: t.transpose(
                    pT[pj][:, 128:256], hqb[:, j * 128:(j + 1) * 128], ident[:]),
                    waits=[t_cbh, t_lnTfree[pj]])
                t_ev2 = pb.op("vector", lambda v, j=j, pj=pj: v.tensor_copy(
                    hq[:].bitcast(BF16)[:, j * 128:(j + 1) * 128],
                    pT[pj][:, 128:256]),
                    waits=[t_tp2] + ([t_hqT_free] if j == 0 else []))
                t_lnTfree[pj] = t_ev2
                t_hqTs.append(t_ev2)
            t_hqbfree = t_tp2
            def d_mms(t):
                for piece in range(2):
                    for j in range(NI):
                        last = t.matmul(
                            pd[:, piece * 512:(piece + 1) * 512],
                            hq[:].bitcast(BF16)[:, j * 128:(j + 1) * 128],
                            wdT[:, j * D + piece * 512:j * D + (piece + 1) * 512],
                            start=(j == 0), stop=(j == NI - 1))
                return last
            t_d = pb.op("tensor", d_mms, waits=t_hqTs + t_wdT + [t_pdfree])
            t_hqT_free = t_d
            t_hqfree[0] = t_d
            t_y4 = pb.op("vector", lambda v, ys=ys: v.scalar_tensor_tensor(
                xf[:], pd[:], 1.0, ys, ALU.mult, ALU.add),
                waits=[t_d, t_y4free[p], t_xffree[0]])
            t_pdfree = t_y4
            t_yfree = t_y4
            t_st = pb.op("sync", lambda s, i=i: s.dma_start(
                out=y4[i * 128:(i + 1) * 128, :], in_=xf[:]),
                waits=[t_y4], dma=True)
            t_y4free[p] = t_st
            t_xffree[0] = t_st
            t_y4st.append(t_st)
        pb.op("sync", lambda s: s.nop(), waits=t_y4st[-2:])
        pb.barrier()
        pht.close()
    glob.close()





def emit_gather_rows(nc, pb, src_dram, dst_dram, ownidx_in, nch):
    al = Alloc(nc)
    idxs = al.sb("gidx", [128, nch], I32)
    buf = al.sb("gbuf", [128, 2 * D], F32)
    t_i = pb.op("gpsimd", lambda g: g.dma_start(out=idxs[:], in_=ownidx_in[:]), dma=True)
    t_pf = [None, None]
    for o in range(nch):
        p = o % 2
        t_g = pb.op("gpsimd", lambda g, o=o, p=p: g.indirect_dma_start(
            buf[:, p * D:(p + 1) * D], None, src_dram[:],
            IndirectOffsetOnAxis(idxs[:, o:o + 1], 0)),
            waits=[t_i, t_pf[p]], dma=True)
        t_s = pb.op("sync", lambda s, o=o, p=p: s.dma_start(
            out=dst_dram[o * 128:(o + 1) * 128, :], in_=buf[:, p * D:(p + 1) * D]),
            waits=[t_g], dma=True)
        t_pf[p] = t_s
    pb.barrier()
    al.close()


def build_fused():
    nc = bass.Bass("TRN2", target_bir_lowering=False)
    NQC = 32
    NKC = NQC + 2
    I = 2816
    ext = lambda n, sh, dt=F32: nc.dram_tensor(n, sh, dt, kind="ExternalInput").ap()
    io = {
        'x_pad': ext("x_pad", [NKC * 128, D]),
        'qkv_wT': ext("qkv_wT", [D, 3 * D]),
        'qkv_b': ext("qkv_b", [1, 3 * D]),
        'out_wT': ext("out_wT", [D, D]),
        'out_b': ext("out_b", [1, D]),
        'band_mask': ext("band_mask", [128, NQC * 3 * 128], BF16),
        'identm': ext("identm", [128, 128], BF16),
        'g_bc': ext("g_bc", [128, D]),
        'b_bc': ext("b_bc", [128, D]),
        'rW': ext("rW", [128, 8 * 8]),
        'rb': ext("rb", [1, 8]),
        'W1': ext("W1", [8, D, 2048]),
        'b1t': ext("b1t", [128, 8 * 16]),
        'W2': ext("W2", [8, 2048, D]),
        'b2r': ext("b2r", [1, 8 * D]),
        'w_q': ext("w_q", [D, D]),
        'w_kv': ext("w_kv", [D, 2 * D]),
        'w_out': ext("w_out", [D, D]),
        'kcs': ext("kcs", [128, 32 * 64]),
        'qcs': ext("qcs", [128, 16 * 64]),
        'cmask': ext("cmask", [128, 4 * 8 * 128], BF16),
        'ownidx': ext("ownidx", [128, 16], I32),
        'w_gate': ext("w_gate", [I, D]),
        'w_up': ext("w_up", [I, D]),
        'w_down': ext("w_down", [D, I]),
        'y4': nc.dram_tensor("y4", [2048, D], F32, kind="ExternalOutput").ap(),
        'y1': nc.dram_tensor("y1_scr", [4096, D], F32).ap(),
        'y2': nc.dram_tensor("y2_scr", [4096, D], F32).ap(),
        'y2own': nc.dram_tensor("y2own_scr", [2048, D], F32).ap(),
        'y3': nc.dram_tensor("y3_scr", [2048, D], F32).ap(),
    }
    with nc.Block() as blk:
        pb = PB(nc)
        emit_l0a(nc, pb, blk, io, NQC)
        io2 = dict(io)
        io2['y1_in'] = io['y1']
        emit_l0f(nc, pb, blk, io2, NQC, npass=8)
        emit_gather_rows(nc, pb, io['y2'], io['y2own'], io['ownidx'], 16)
        I = 2816
        wq_dram = nc.dram_tensor("wq_scr", [2 * I + D, max(D, I)], BF16).ap()
        phw = Alloc(nc)
        scrw = {}
        box = {}

        def mid_hook(tails):
            scrw['wc'] = phw.sb("wc", [128, 2 * 1408], F32, side='right')
            scrw['wb'] = phw.sb("wb", [128, 2 * 1408], BF16, side='right')
            scrw['bm'] = phw.sb("qbm", [128, 3 * 88], F32, side='right')
            scrw['ax'] = phw.sb("qax", [128, 1408], F32, side='right')
            scrw['idx'] = phw.sb("qidx", [128, 1408], F32, side='right')
            scrw['s2'] = phw.sb("qs2", [128, 1408], F32, side='right')
            cq = phw.sb("qc", [128, 2], F32, side='right')
            scrw['c4'] = cq[:, 0:1]
            scrw['c2'] = cq[:, 1:2]
            t1 = pb.op("gpsimd", lambda g: g.memset(cq[:, 0:1], -4.0))
            scrw['ctok'] = pb.op("gpsimd", lambda g: g.memset(cq[:, 1:2], -2.0))
            box['tails'] = tails
        io3 = dict(io)
        io3['y2_seq'] = io['y2']
        io3['y2_own'] = io['y2own']
        io4 = dict(io)
        io4['y3_in'] = io['y3']
        t_wst = []
        bgen = gen_l1f_weights(nc, pb, io4, scrw, wq_dram, t_wst)
        emit_l1a(nc, pb, blk, io3, nqg=4, skc=32, barrier=False, mid_hook=mid_hook,
                 bg=bgen)
        for _ in bgen:
            pass
        pb.barrier()
        phw.close()
        emit_l1f(nc, pb, blk, io4, 16, t_wst=t_wst, wq_dram=wq_dram)
        for e in ENGINES:
            n = len(pb.streams[e])
            print(f"fused stream {e}: {n} ops")
        pb.emit(blk)
    return nc


_programs = {}


def _get_prog(name):
    if name not in _programs:
        if name == 'l0a':
            _programs[name] = build_l0a(16)
        elif name == 'l0f':
            _programs[name] = build_l0f(16, 2)
        elif name == 'l1a':
            _programs[name] = build_l1a(4)
        elif name == 'l1f':
            _programs[name] = build_l1f(16)
    return _programs[name]


_IDENT = None


def kernel_unfused(L0_attn_x, L0_attn_qkv_w, L0_attn_qkv_b, L0_attn_out_w, L0_attn_out_b,
           L0_ffn_ln_g, L0_ffn_ln_b, L0_ffn_rW, L0_ffn_rb, L0_ffn_W1, L0_ffn_b1,
           L0_ffn_W2, L0_ffn_b2, L1_attn_w_q, L1_attn_w_kv, L1_attn_w_out,
           L1_ffn_w_gate, L1_ffn_w_up, L1_ffn_w_down):
    global _IDENT
    if _IDENT is None:
        _IDENT = bf(np.eye(128, dtype=np.float32))
    x = f32(L0_attn_x)
    B, N, Dm = x.shape

    nc = _get_prog('l0a')
    qkv_wT = f32(L0_attn_qkv_w.T)
    out_wT = f32(L0_attn_out_w.T)
    qkv_b = f32(np.asarray(L0_attn_qkv_b)[None, :])
    out_b = f32(np.asarray(L0_attn_out_b)[None, :])
    in_maps = []
    for core in range(8):
        b, h = core // 2, core % 2
        d = l0a_inputs(x, core)
        d.update({"qkv_wT": qkv_wT, "qkv_b": qkv_b, "out_wT": out_wT, "out_b": out_b,
                  "band_mask": l0a_masks(16, h * 2048), "identm": _IDENT})
        in_maps.append(d)
    r1 = run_bass_kernel_spmd(nc, in_maps, core_ids=list(range(8)))

    nc = _get_prog('l0f')
    g_bc = f32(np.broadcast_to(np.asarray(L0_ffn_ln_g), (128, Dm)))
    b_bc = f32(np.broadcast_to(np.asarray(L0_ffn_ln_b), (128, Dm)))
    rW = f32(np.asarray(L0_ffn_rW).reshape(8, 128, 8).transpose(1, 0, 2).reshape(128, 64))
    rb = f32(np.asarray(L0_ffn_rb)[None, :])
    W1 = f32(L0_ffn_W1)
    b1t = f32(np.asarray(L0_ffn_b1).reshape(8, 16, 128).transpose(2, 0, 1).reshape(128, 128))
    W2 = f32(L0_ffn_W2)
    b2r = f32(np.asarray(L0_ffn_b2).reshape(1, 8 * 1024))
    in_maps = []
    for core in range(8):
        in_maps.append({"y1_in": r1.results[core]["y1"], "g_bc": g_bc, "b_bc": b_bc,
                        "rW": rW, "rb": rb, "W1": W1, "b1t": b1t, "W2": W2, "b2r": b2r,
                        "identm": _IDENT})
    r2 = run_bass_kernel_spmd(nc, in_maps, core_ids=list(range(8)))

    y2 = np.empty((B, N, Dm), dtype=np.float32)
    for core in range(8):
        b, h = core // 2, core % 2
        y2[b, h * 2048:(h + 1) * 2048] = r2.results[core]["y2"]

    nc = _get_prog('l1a')
    w_q = f32(L1_attn_w_q)
    w_kv = f32(L1_attn_w_kv)
    w_out = f32(L1_attn_w_out)
    kcs = l1a_rope_tables(np.arange(N).astype(np.float64))
    qcs_h = [l1a_rope_tables(own_rows(h).astype(np.float64)) for h in (0, 1)]
    msk_h = [l1a_masks(h) for h in (0, 1)]
    in_maps = []
    for core in range(8):
        b, h = core // 2, core % 2
        in_maps.append({"y2_seq": y2[b], "y2_own": f32(y2[b][own_rows(h)]),
                        "w_q": w_q, "w_kv": w_kv, "w_out": w_out, "kcs": kcs,
                        "qcs": qcs_h[h], "cmask": msk_h[h], "identm": _IDENT})
    r3 = run_bass_kernel_spmd(nc, in_maps, core_ids=list(range(8)))

    nc = _get_prog('l1f')
    wg = f32(L1_ffn_w_gate)
    wu = f32(L1_ffn_w_up)
    wd = f32(L1_ffn_w_down)
    in_maps = []
    for core in range(8):
        in_maps.append({"y3_in": r3.results[core]["y3"], "w_gate": wg, "w_up": wu,
                        "w_down": wd, "identm": _IDENT})
    r4 = run_bass_kernel_spmd(nc, in_maps, core_ids=list(range(8)))

    out = np.empty((B, N, Dm), dtype=np.float32)
    for core in range(8):
        b, h = core // 2, core % 2
        out[b][own_rows(h)] = r4.results[core]["y4"]
    return out


def kernel(L0_attn_x, L0_attn_qkv_w, L0_attn_qkv_b, L0_attn_out_w, L0_attn_out_b,
           L0_ffn_ln_g, L0_ffn_ln_b, L0_ffn_rW, L0_ffn_rb, L0_ffn_W1, L0_ffn_b1,
           L0_ffn_W2, L0_ffn_b2, L1_attn_w_q, L1_attn_w_kv, L1_attn_w_out,
           L1_ffn_w_gate, L1_ffn_w_up, L1_ffn_w_down):
    global _IDENT
    if _IDENT is None:
        _IDENT = bf(np.eye(128, dtype=np.float32))
    if 'fused' not in _programs:
        _programs['fused'] = build_fused()
    nc = _programs['fused']
    x = f32(L0_attn_x)
    B, N, Dm = x.shape

    shared = {
        "qkv_wT": f32(np.asarray(L0_attn_qkv_w).T),
        "qkv_b": f32(np.asarray(L0_attn_qkv_b)[None, :]),
        "out_wT": f32(np.asarray(L0_attn_out_w).T),
        "out_b": f32(np.asarray(L0_attn_out_b)[None, :]),
        "band_mask": l0a_masks(32, 0),
        "identm": _IDENT,
        "g_bc": f32(np.broadcast_to(np.asarray(L0_ffn_ln_g), (128, Dm))),
        "b_bc": f32(np.broadcast_to(np.asarray(L0_ffn_ln_b), (128, Dm))),
        "rW": f32(np.asarray(L0_ffn_rW).reshape(8, 128, 8).transpose(1, 0, 2).reshape(128, 64)),
        "rb": f32(np.asarray(L0_ffn_rb)[None, :]),
        "W1": f32(L0_ffn_W1),
        "b1t": f32(np.asarray(L0_ffn_b1).reshape(8, 16, 128).transpose(2, 0, 1).reshape(128, 128)),
        "W2": f32(L0_ffn_W2),
        "b2r": f32(np.asarray(L0_ffn_b2).reshape(1, 8 * 1024)),
        "w_q": f32(L1_attn_w_q),
        "w_kv": f32(L1_attn_w_kv),
        "w_out": f32(L1_attn_w_out),
        "kcs": l1a_rope_tables(np.arange(N).astype(np.float64)),
        "w_gate": f32(L1_ffn_w_gate),
        "w_up": f32(L1_ffn_w_up),
        "w_down": f32(L1_ffn_w_down),
    }
    qcs_h = [l1a_rope_tables(own_rows(h).astype(np.float64)) for h in (0, 1)]
    msk_h = [l1a_masks(h) for h in (0, 1)]
    idx_h = []
    for h in (0, 1):
        rows = own_rows(h).astype(np.int32)
        idx_h.append(np.ascontiguousarray(rows.reshape(16, 128).T))
    in_maps = []
    for core in range(8):
        b, h = core // 2, core % 2
        xp = np.zeros((34 * 128, Dm), dtype=np.float32)
        xp[128:128 + N] = x[b]
        d = dict(shared)
        d.update({"x_pad": xp, "qcs": qcs_h[h], "cmask": msk_h[h], "ownidx": idx_h[h]})
        in_maps.append(d)
    res = run_bass_kernel_spmd(nc, in_maps, core_ids=list(range(8)))
    out = np.empty((B, N, Dm), dtype=np.float32)
    for core in range(8):
        b, h = core // 2, core % 2
        out[b][own_rows(h)] = res.results[core]["y4"]
    return out


# version 15 (final)
# speedup vs baseline: 1.0901x; 1.0293x over previous
import numpy as np
import ml_dtypes
import concourse.bass as bass
import concourse.mybir as mybir
from concourse.bass_utils import run_bass_kernel_spmd
from concourse.bass_core_rust import IndirectOffsetOnAxis

F32 = mybir.dt.float32
I32 = mybir.dt.int32
BF16 = mybir.dt.bfloat16
AF = mybir.ActivationFunctionType
ALU = mybir.AluOpType
AX = mybir.AxisListType

D = 1024
SEQ = 4096
LN_EPS = 1e-5
QK_EPS = 1e-6
SM_OFF = 10.0
MOE_LN_EPS = 1e-5

ENGINES = ("tensor", "scalar", "vector", "gpsimd", "sync")


class Tok:
    __slots__ = ("eng", "sem", "seq", "inc", "flag", "value")

    def __init__(self, eng, sem, seq, inc, value=0):
        self.eng, self.sem, self.seq, self.inc = eng, sem, seq, inc
        self.flag, self.value = False, value


NDMA_SEMS = 4


class PB:

    def __init__(self, nc):
        self.nc = nc
        self.streams = {e: [] for e in ENGINES}
        self.dma_idx = {}
        self.pending = {e: [] for e in ENGINES}
        self.last_dma = {}

    def op(self, eng, fn, waits=(), dma=False):
        ws = [w for w in waits if w is not None]
        if self.pending[eng]:
            ws.extend(self.pending[eng])
            self.pending[eng] = []
        if dma:
            j = self.dma_idx[eng] = self.dma_idx.get(eng, -1) + 1
            sem = f"{eng}dma{j % NDMA_SEMS}"
            tok = Tok(eng, sem, len(self.streams[eng]), 16, 16 * (j // NDMA_SEMS + 1))
            tok.flag = True
            self.last_dma[(eng, sem)] = tok
            if j >= NDMA_SEMS:
                prev = Tok(eng, sem, -1, 16, 16 * (j // NDMA_SEMS))
                prev.flag = True
                ws.append(prev)
        else:
            sem = eng
            tok = Tok(eng, sem, len(self.streams[eng]), 1)
        for w in ws:
            if w.eng != eng or w.inc == 16:
                w.flag = True
        self.streams[eng].append((fn, ws, tok))
        return tok

    def barrier(self):
        toks = []
        for e in ENGINES:
            if self.streams[e]:
                toks.append(self.streams[e][-1][2])
        toks.extend(self.last_dma.values())
        for e in ENGINES:
            self.pending[e] = [t for t in toks if t.eng != e or t.inc == 16]

    def emit(self, block):
        sem_names = set()
        for e in ENGINES:
            for (_, ws, tok) in self.streams[e]:
                if tok.flag:
                    sem_names.add(tok.sem)
                for w in ws:
                    if w.flag:
                        sem_names.add(w.sem)
        self.cum = {}
        for e in ENGINES:
            cnt, cums = 0, []
            for (_, _, tok) in self.streams[e]:
                if tok.flag and tok.inc == 1:
                    cnt += 1
                cums.append(cnt)
            self.cum[e] = cums
        import contextlib
        self._stack = contextlib.ExitStack()
        sems = {s: self._stack.enter_context(self.nc.semaphore(f"pb_{s}"))
                for s in sorted(sem_names)}
        self.sems = sems

        def emitter(eng):
            def run(h):
                waited = {}
                drained = 0
                for i, (fn, ws, tok) in enumerate(self.streams[eng]):
                    need_drain = False
                    for w in ws:
                        if w.inc == 16:
                            if w.value > waited.get(w.sem, 0):
                                h.wait_ge(sems[w.sem], w.value)
                                waited[w.sem] = w.value
                        elif w.eng == eng:
                            if w.seq >= drained:
                                need_drain = True
                        else:
                            val = self.cum[w.eng][w.seq]
                            if val > waited.get(w.eng, 0):
                                h.wait_ge(sems[w.eng], val)
                                waited[w.eng] = val
                    if need_drain:
                        h.drain()
                        drained = i
                    inst = fn(h)
                    if tok.flag:
                        inst.then_inc(sems[tok.sem], tok.inc)
            return run

        block.tensor(emitter("tensor"))
        block.scalar(emitter("scalar"))
        block.vector(emitter("vector"))
        block.gpsimd(emitter("gpsimd"))
        block.sync(emitter("sync"))


def bf(x):
    return np.ascontiguousarray(x).astype(ml_dtypes.bfloat16)


def f32(x):
    return np.ascontiguousarray(x, dtype=np.float32)



_ALLOC_N = [0]


class Alloc:

    def __init__(self, nc):
        import contextlib
        self.nc = nc
        self.stack = contextlib.ExitStack()

    def _u(self, name):
        _ALLOC_N[0] += 1
        return f"{name}_{_ALLOC_N[0]}"

    def sb(self, name, shape, dtype, side=None):
        return self.stack.enter_context(
            self.nc.sbuf_tensor(self._u(name), shape, dtype, side=side)
            if side else self.nc.sbuf_tensor(self._u(name), shape, dtype))

    def ps(self, name, shape, dtype):
        return self.stack.enter_context(self.nc.psum_tensor(self._u(name), shape, dtype))

    def close(self):
        self.stack.close()


def build_l0a(nqc):
    nc = bass.Bass("TRN2", target_bir_lowering=False)
    NKC = nqc + 2
    io = {
        'x_pad': nc.dram_tensor("x_pad", [NKC * 128, D], F32, kind="ExternalInput").ap(),
        'qkv_wT': nc.dram_tensor("qkv_wT", [D, 3 * D], F32, kind="ExternalInput").ap(),
        'qkv_b': nc.dram_tensor("qkv_b", [1, 3 * D], F32, kind="ExternalInput").ap(),
        'out_wT': nc.dram_tensor("out_wT", [D, D], F32, kind="ExternalInput").ap(),
        'out_b': nc.dram_tensor("out_b", [1, D], F32, kind="ExternalInput").ap(),
        'band_mask': nc.dram_tensor("band_mask", [128, nqc * 3 * 128], BF16,
                                    kind="ExternalInput").ap(),
        'identm': nc.dram_tensor("identm", [128, 128], BF16, kind="ExternalInput").ap(),
        'y1': nc.dram_tensor("y1", [nqc * 128, D], F32, kind="ExternalOutput").ap(),
    }
    with nc.Block() as blk:
        pb = PB(nc)
        emit_l0a(nc, pb, blk, io, nqc)
        pb.emit(blk)
    return nc


def emit_l0a(nc, pb, blk, io, nqc, tag=''):
    NKC = nqc + 2
    x_pad, qkv_wT, qkv_b, out_wT, out_b, band_mask, ident_in, y1 = (
        io['x_pad'], io['qkv_wT'], io['qkv_b'], io['out_wT'], io['out_b'],
        io['band_mask'], io['identm'], io['y1'])
    qk_dram = nc.dram_tensor("qkv_scr" + tag, [NKC * 128, 3 * D], BF16).ap()

    glob = Alloc(nc)
    ident = glob.sb("ident", [128, 128], BF16)
    masks = glob.sb("masks", [128, nqc * 3 * 128], BF16)
    onesrow = glob.sb("onesrow", [1, 128], BF16)
    onesf = glob.sb("onesf", [1, 64], F32)
    consts = glob.sb("consts", [128, 4], F32)
    t_ones = pb.op("gpsimd", lambda g: g.memset(onesrow[:], 1.0))
    t_onesf = pb.op("gpsimd", lambda g: g.memset(onesf[:], 1.0))
    pb.op("gpsimd", lambda g: g.memset(consts[:, 0:1], LN_EPS))
    pb.op("gpsimd", lambda g: g.memset(consts[:, 1:2], 1e-12))
    t_lne = pb.op("gpsimd", lambda g: g.memset(consts[:, 2:3], -0.5))
    t_eps = t_lne
    t_id = pb.op("gpsimd", lambda g: g.dma_start(out=ident[:], in_=ident_in[:]), dma=True)
    t_msk = pb.op("gpsimd", lambda g: g.dma_start(out=masks[:], in_=band_mask[:]), dma=True)

    ph = Alloc(nc)
    wqkv = ph.sb("wqkv", [128, 8 * 3 * D], BF16)
    brow = ph.sb("brow", [1, 3 * D], BF16)
    xc = ph.sb("xc", [128, 2 * D], F32)
    lnc = ph.sb("lnc", [128, 2 * D], BF16)
    lnT = ph.sb("lnT", [128, 2 * 8 * 128], BF16)
    qkf = ph.sb("qkf", [128, 2048], F32)
    sqs = ph.sb("sqs", [128, 2048], F32)
    st_ = ph.sb("st_", [128, 16], F32)
    nrm = ph.sb("nrm", [128, 128], F32)
    qkn = ph.sb("qkn", [128, 2 * 3072], BF16)
    pT_ = [ph.ps("pT0", [128, 1024], BF16), ph.ps("pT1", [128, 1024], BF16)]
    pqkv = ph.ps("pqkv", [128, 2048], F32)
    pv = ph.ps("pv", [128, 1024], F32)

    t_w = [pb.op("gpsimd", lambda g, j=j: g.dma_start(
        out=wqkv[:, j * 3072:(j + 1) * 3072],
        in_=qkv_wT[j * 128:(j + 1) * 128, :]), dma=True) for j in range(8)]
    t_b = pb.op("gpsimd", lambda g: g.dma_start(out=brow[:], in_=qkv_b[:]), dma=True)

    t_qkst = []
    SAB = ph.sb("SAB", [128, 4 * NKC], F32)
    SA, SB = SAB[:, 0:NKC], SAB[:, NKC:2 * NKC]
    MEAN, RSTD = SAB[:, 2 * NKC:3 * NKC], SAB[:, 3 * NKC:4 * NKC]
    t_sfree = [None, None]
    GS = 8
    t_gstats = {}
    for i in range(NKC):
        p = i % 2
        xs = xc[:, p * D:(p + 1) * D]
        t_x = pb.op("sync", lambda s, i=i, xs=xs: s.dma_start(
            out=xs, in_=x_pad[i * 128:(i + 1) * 128, :]),
            waits=[t_sfree[p]], dma=True)
        t_sfree[p] = emit_sums(pb, xs, SA[:, i:i + 1], SB[:, i:i + 1], sqs,
                               [t_x, t_sfree[1 - p]])
        if i % GS == GS - 1 or i == NKC - 1:
            g = i // GS
            lo, hi = g * GS, i + 1
            t_gstats[g] = emit_stats_batch(
                pb, SA[:, lo:hi], SB[:, lo:hi], MEAN[:, lo:hi], RSTD[:, lo:hi],
                consts[:, 0:1], consts[:, 2:3], [t_sfree[0], t_sfree[1], t_lne])
    t_xfree = None
    t_lnTfree = [None, None]
    t_lncfree = [None, None]
    t_qkffree = [None, None]
    t_pvfree = None
    t_pqkvfree = None
    t_qknfree = [None, None]
    for i in range(NKC):
        p = i % 2
        xs = xc[:, p * D:(p + 1) * D]
        lns = lnc[:, p * D:(p + 1) * D]
        NR = nrm[:, p * 64:(p + 1) * 64]
        t_x = pb.op("sync", lambda s, i=i, xs=xs: s.dma_start(
            out=xs, in_=x_pad[i * 128:(i + 1) * 128, :]),
            waits=[t_xfree] if i >= 2 else [t_sfree[0], t_sfree[1]], dma=True)
        t_ln = pb.op("vector", lambda v, xs=xs, lns=lns, i=i: v.tensor_scalar(
            lns, xs, MEAN[:, i:i + 1], RSTD[:, i:i + 1], ALU.subtract, ALU.mult),
            waits=[t_x, t_gstats[i // GS], t_lncfree[p]])
        t_xfree = t_ln
        t_lnTs = []
        t_tp = None
        for j in range(8):
            pj = (i * 8 + j) % 2
            t_tp = pb.op("tensor", lambda t, lns=lns, j=j, pj=pj: t.transpose(
                pT_[pj][:, 0:128], lns[:, j * 128:(j + 1) * 128], ident[:]),
                waits=[t_ln, t_id, t_lnTfree[pj]])
            t_ev = pb.op("vector", lambda v, j=j, p=p, pj=pj: v.tensor_copy(
                lnT[:, (p * 8 + j) * 128:(p * 8 + j + 1) * 128], pT_[pj][:, 0:128]),
                waits=[t_tp])
            t_lnTfree[pj] = t_ev
            t_lnTs.append(t_ev)
        t_lncfree[p] = t_tp

        def qk_mms(t, p=p):
            for piece in range(4):
                for j in range(8):
                    t.matmul(pqkv[:, piece * 512:(piece + 1) * 512],
                             lnT[:, (p * 8 + j) * 128:(p * 8 + j + 1) * 128],
                             wqkv[:, j * 3072 + piece * 512:j * 3072 + (piece + 1) * 512],
                             start=(j == 0), stop=False)
                last = t.matmul(pqkv[:, piece * 512:(piece + 1) * 512], onesrow[0:1, :],
                                brow[0:1, piece * 512:(piece + 1) * 512],
                                start=False, stop=True)
            return last
        t_qk = pb.op("tensor", qk_mms, waits=t_lnTs + t_w + [t_b, t_ones, t_pqkvfree])

        def v_mms(t, p=p):
            for piece in range(2):
                for j in range(8):
                    t.matmul(pv[:, piece * 512:(piece + 1) * 512],
                             lnT[:, (p * 8 + j) * 128:(p * 8 + j + 1) * 128],
                             wqkv[:, j * 3072 + 2048 + piece * 512:
                                  j * 3072 + 2048 + (piece + 1) * 512],
                             start=(j == 0), stop=False)
                last = t.matmul(pv[:, piece * 512:(piece + 1) * 512], onesrow[0:1, :],
                                brow[0:1, 2048 + piece * 512:2048 + (piece + 1) * 512],
                                start=False, stop=True)
            return last
        t_v = pb.op("tensor", v_mms, waits=[t_pvfree])
        t_qkev = pb.op("vector", lambda v: v.tensor_copy(qkf[:], pqkv[:]),
                       waits=[t_qk, t_qkffree[0], t_qkffree[1]])
        t_pqkvfree = t_qkev
        t_vev = pb.op("vector", lambda v, p=p: v.tensor_copy(
            qkn[:, p * 3072 + 2048:(p + 1) * 3072], pv[:]),
            waits=[t_v, t_qknfree[p]])
        t_pvfree = t_vev
        t_sqp = pb.op("scalar", lambda s: s.activation(sqs[:], qkf[:], AF.Square),
                      waits=[t_qkev])
        t_red = pb.op("vector", lambda v, NR=NR: v.tensor_reduce(
            NR[:, 0:32], sqs[:].rearrange("p (h e) -> p h e", e=64), AX.X, ALU.add),
            waits=[t_sqp])
        t_nsq = pb.op("scalar", lambda s, NR=NR: s.activation(
            NR[:, 32:64], NR[:, 0:32], AF.Ln, bias=consts[:, 1:2]), waits=[t_red, t_lne])
        t_nr = pb.op("scalar", lambda s, NR=NR: s.activation(
            NR[:, 0:32], NR[:, 32:64], AF.Exp, scale=consts[:, 2:3]), waits=[t_nsq])
        t_qknv = pb.op("vector", lambda v, p=p, NR=NR: v.tensor_tensor(
            qkn[:, p * 3072:p * 3072 + 2048].rearrange("p (h e) -> p h e", e=64),
            qkf[:].rearrange("p (h e) -> p h e", e=64),
            NR[:, 0:32].unsqueeze(2).broadcast_to([128, 32, 64]), ALU.mult),
            waits=[t_nr, t_qknfree[p]])
        t_qkffree = [t_sqp, t_qknv]
        t_st = pb.op("sync", lambda s, i=i, p=p: s.dma_start(
            out=qk_dram[i * 128:(i + 1) * 128, :], in_=qkn[:, p * 3072:(p + 1) * 3072]),
            waits=[t_qknv, t_vev], dma=True)
        t_qknfree[p] = t_st
        t_qkst.append(t_st)
    pb.barrier()
    ph.close()

    for r0 in range(0, nqc, 16):
        nqh = min(16, nqc - r0)
        nkh = nqh + 2
        ph2 = Alloc(nc)
        kT = ph2.sb("kT", [128, 8 * 18 * 128], BF16)
        qT = ph2.sb("qT", [128, 8 * 16 * 128], BF16)
        v_aug = ph2.sb("v_aug", [128, 18 * 16 * 65], BF16)
        wout = ph2.sb("wout", [128, 8 * D], BF16)
        obrow = ph2.sb("obrow", [1, D], BF16)
        expt = ph2.sb("expt", [128, 2 * 512], BF16)
        rl = ph2.sb("rl", [1, 2 * 512], F32)
        rlbc = ph2.sb("rlbc", [64, 2 * 512], F32)
        outT = ph2.sb("outT", [128, 2 * 8 * 128], BF16)
        xre = ph2.sb("xre", [128, 2 * D], F32)
        y1c = ph2.sb("y1c", [128, 2 * D], F32)
        psc = ph2.ps("psc", [128, 2 * 1024], F32)
        pav = ph2.ps("pav", [65, 2 * 512], F32)
        pbc = ph2.ps("pbc", [64, 512], F32)
        pproj = psc[:, 0:1024]

        t_kT = [pb.op("sync", lambda s, j=j, r0=r0, nkh=nkh: s.dma_start(
            out=kT[:, j * 18 * 128:j * 18 * 128 + nkh * 128],
            in_=qk_dram[r0 * 128:(r0 + nkh) * 128, D + j * 128:D + (j + 1) * 128],
            transpose=True), waits=t_qkst[:r0 + nkh], dma=True) for j in range(8)]
        t_qT = [pb.op("sync", lambda s, j=j, r0=r0, nqh=nqh: s.dma_start(
            out=qT[:, j * 16 * 128:j * 16 * 128 + nqh * 128],
            in_=qk_dram[(r0 + 1) * 128:(r0 + 1 + nqh) * 128, j * 128:(j + 1) * 128],
            transpose=True), waits=t_qkst[:r0 + 1 + nqh], dma=True) for j in range(8)]
        t_vl, t_vone = [], []
        for c in range(nkh):
            t_ = pb.op("gpsimd", lambda g, c=c, r0=r0: g.dma_start(
                out=v_aug[:, c * 1040:(c + 1) * 1040]
                .rearrange("p (h e) -> p h e", e=65)[:, :, 0:64],
                in_=qk_dram[(r0 + c) * 128:(r0 + c + 1) * 128, 2 * D:3 * D]
                .rearrange("p (h e) -> p h e", e=64)),
                waits=t_qkst[:r0 + nkh], dma=True)
            t_vl.append(t_)
            t_vone.append(pb.op("gpsimd", lambda g, c=c: g.memset(
                v_aug[:, c * 1040 + 64:(c + 1) * 1040:65], 1.0), waits=[t_]))
        t_wo = [pb.op("gpsimd", lambda g, j=j: g.dma_start(
            out=wout[:, j * D:(j + 1) * D],
            in_=out_wT[j * 128:(j + 1) * 128, :]), dma=True) for j in range(8)]
        t_ob = pb.op("gpsimd", lambda g: g.dma_start(out=obrow[:], in_=out_b[:]), dma=True)

        t_expfree = [None, None]
        t_pprojfree = None
        t_pscfree = [None, None]
        t_pavfree = [None, None]
        t_rlf = None
        t_bcf = None
        t_y1 = [None, None]
        for qc in range(nqh):
            gqc = r0 + qc
            p = qc % 2
            t_xr = pb.op("sync", lambda s, gqc=gqc, p=p: s.dma_start(
                out=xre[:, p * D:(p + 1) * D],
                in_=x_pad[(gqc + 1) * 128:(gqc + 2) * 128, :]),
                waits=[t_y1[p]], dma=True)
            t_fins = []
            for hg in range(4):
                u = qc * 4 + hg
                pu = u % 2
                t_avs = []
                for kc in range(3):
                    kchunk = qc + kc
                    pp = (u * 3 + kc) % 2

                    def sc_mms(t, hg=hg, pp=pp, kchunk=kchunk, qc=qc):
                        for hh in range(4):
                            h = hg * 4 + hh
                            tile, row0 = h // 2, 64 * (h % 2)
                            col = pp * 1024 + (hh % 2) * 512 + (hh // 2) * 128
                            last = t.matmul(
                                psc[:, col:col + 128],
                                kT[row0:row0 + 64,
                                   tile * 18 * 128 + kchunk * 128:
                                   tile * 18 * 128 + (kchunk + 1) * 128],
                                qT[row0:row0 + 64,
                                   tile * 16 * 128 + qc * 128:
                                   tile * 16 * 128 + (qc + 1) * 128],
                                start=(hh < 2), stop=(hh >= 2))
                        return last
                    t_sc = pb.op("tensor", sc_mms, waits=t_kT + t_qT + [t_pscfree[pp]])
                    t_exp = pb.op("scalar", lambda s, pp=pp: s.activation(
                        expt[:, pp * 512:(pp + 1) * 512],
                        psc[:, pp * 1024:(pp + 1) * 1024]
                        .rearrange("p (b c) -> p b c", c=512)[:, :, 0:256], AF.Exp),
                        waits=[t_sc, t_expfree[pp]])
                    t_pscfree[pp] = t_exp
                    t_m = pb.op("vector", lambda v, pp=pp, gqc=gqc, kc=kc: v.tensor_tensor(
                        expt[:, pp * 512:(pp + 1) * 512].rearrange("p (h q) -> p h q", q=128),
                        expt[:, pp * 512:(pp + 1) * 512].rearrange("p (h q) -> p h q", q=128),
                        masks[:, (gqc * 3 + kc) * 128:(gqc * 3 + kc + 1) * 128]
                        .unsqueeze(1).broadcast_to([128, 4, 128]), ALU.mult),
                        waits=[t_exp, t_msk])

                    def av_mms(t, hg=hg, pp=pp, pu=pu, kchunk=kchunk, kc=kc):
                        for hh in range(4):
                            h = hg * 4 + hh
                            ec = ((hh % 2) * 2 + hh // 2) * 128
                            last = t.matmul(
                                pav[:, pu * 512 + hh * 128:pu * 512 + (hh + 1) * 128],
                                v_aug[:, kchunk * 1040 + h * 65:
                                      kchunk * 1040 + (h + 1) * 65],
                                expt[:, pp * 512 + ec:pp * 512 + ec + 128],
                                start=(kc == 0 and hh == 0), stop=(kc == 2 and hh == 3))
                        return last
                    t_av = pb.op("tensor", av_mms,
                                 waits=[t_m, t_pavfree[pu]] + t_vone if kc == 0 else [t_m])
                    t_expfree[pp] = t_av
                    t_avs.append(t_av)
                t_rl = pb.op("scalar", lambda s, pu=pu: s.activation(
                    rl[0:1, pu * 512:(pu + 1) * 512], pav[64:65, pu * 512:(pu + 1) * 512],
                    AF.Ln), waits=[t_avs[2], t_rlf])
                t_rl = pb.op("scalar", lambda s, pu=pu: s.activation(
                    rl[0:1, pu * 512:(pu + 1) * 512], rl[0:1, pu * 512:(pu + 1) * 512],
                    AF.Exp, scale=-1.0), waits=[t_rl])
                t_bc = pb.op("tensor", lambda t, pu=pu: t.matmul(
                    pbc[:], onesf[0:1, 0:64],
                    rl[0:1, pu * 512:(pu + 1) * 512], start=True, stop=True),
                    waits=[t_rl, t_bcf, t_onesf])
                t_rlf = t_bc
                t_bcev = pb.op("vector", lambda v, pu=pu: v.tensor_copy(
                    rlbc[:, pu * 512:(pu + 1) * 512], pbc[:]), waits=[t_bc])
                t_bcf = t_bcev

                def fin(v, hg=hg, pu=pu, p=p):
                    for hh in range(4):
                        h = hg * 4 + hh
                        tile, row0 = h // 2, 64 * (h % 2)
                        last = v.scalar_tensor_tensor(
                            outT[row0:row0 + 64, (p * 8 + tile) * 128:(p * 8 + tile + 1) * 128],
                            pav[0:64, pu * 512 + hh * 128:pu * 512 + (hh + 1) * 128], 1.0,
                            rlbc[:, pu * 512 + hh * 128:pu * 512 + (hh + 1) * 128],
                            ALU.mult, ALU.mult)
                    return last
                t_fin = pb.op("vector", fin, waits=[t_bcev] + ([t_y1[p]] if hg == 0 else []))
                t_pavfree[pu] = t_fin
                t_fins.append(t_fin)

            def proj_mms(t, p=p):
                for piece in range(2):
                    for j in range(8):
                        t.matmul(pproj[:, piece * 512:(piece + 1) * 512],
                                 outT[:, (p * 8 + j) * 128:(p * 8 + j + 1) * 128],
                                 wout[:, j * D + piece * 512:j * D + (piece + 1) * 512],
                                 start=(j == 0), stop=False)
                    last = t.matmul(pproj[:, piece * 512:(piece + 1) * 512], onesrow[0:1, :],
                                    obrow[0:1, piece * 512:(piece + 1) * 512],
                                    start=False, stop=True)
                return last
            t_proj = pb.op("tensor", proj_mms,
                           waits=t_fins + t_wo + [t_ob, t_pprojfree,
                                                  t_pscfree[0], t_pscfree[1]])
            t_pscfree[0] = None
            t_y1c = pb.op("vector", lambda v, p=p: v.scalar_tensor_tensor(
                y1c[:, p * D:(p + 1) * D], pproj[:], 1.0,
                xre[:, p * D:(p + 1) * D], ALU.mult, ALU.add),
                waits=[t_proj, t_xr])
            t_pprojfree = t_y1c
            t_pscfree[0] = t_y1c
            t_st = pb.op("sync", lambda s, gqc=gqc, p=p: s.dma_start(
                out=y1[gqc * 128:(gqc + 1) * 128, :], in_=y1c[:, p * D:(p + 1) * D]),
                waits=[t_y1c], dma=True)
            t_y1[p] = t_st
        pb.op("sync", lambda s: s.nop(), waits=[t_y1[0], t_y1[1]])
        pb.barrier()
        ph2.close()
    glob.close()



def l0a_inputs(x, core):
    b, h = core // 2, core % 2
    q0 = h * 2048
    lo, hi = q0 - 128, q0 + 2048 + 128
    xp = np.zeros((2048 + 256, D), dtype=np.float32)
    s, e = max(lo, 0), min(hi, SEQ)
    xp[s - lo:e - lo] = x[b, s:e]
    return {"x_pad": xp}


def l0a_masks(nqc, q0):
    m = np.zeros((nqc, 3, 128, 128), dtype=np.float32)
    for qc in range(nqc):
        gq = q0 + qc * 128 + np.arange(128)[None, :]
        for kc in range(3):
            gk = q0 + (qc + kc - 1) * 128 + np.arange(128)[:, None]
            diff = gk - gq
            ok = (np.abs(diff) <= 16) & (diff % 2 == 0) & (gk >= 0) & (gk < SEQ)
            m[qc, kc] = ok.astype(np.float32)
    return bf(m.reshape(nqc * 3, 128, 128).transpose(1, 0, 2).reshape(128, nqc * 3 * 128))


def emit_sums(pb, xs, sa_col, sb_col, sqs, waits):
    t1 = pb.op("scalar", lambda s: s.activation(
        sqs[:, 0:D], xs, AF.Square, accum_out=sa_col), waits=waits)
    t2 = pb.op("scalar", lambda s: s.activation(
        sqs[:, 0:D], xs, AF.Copy, accum_out=sb_col), waits=[t1])
    return t2


def emit_stats_batch(pb, SA, SB, MEAN, RSTD, eps_col, mhalf_col, waits):
    t = pb.op("vector", lambda v: v.tensor_scalar_mul(MEAN, SB, 1.0 / D), waits=waits)
    t = pb.op("vector", lambda v: v.scalar_tensor_tensor(
        RSTD, MEAN, 1.0, MEAN, ALU.mult, ALU.mult), waits=[t])
    t = pb.op("vector", lambda v: v.scalar_tensor_tensor(
        RSTD, SA, 1.0 / D, RSTD, ALU.mult, ALU.subtract), waits=[t])
    t = pb.op("scalar", lambda s: s.activation(RSTD, RSTD, AF.Ln, bias=eps_col),
              waits=[t])
    t = pb.op("scalar", lambda s: s.activation(RSTD, RSTD, AF.Exp, scale=mhalf_col),
              waits=[t])
    return t


def emit_ln_stats(pb, sc_eng_bufs, xs, S, consts_eps_col, waits, eps_tok=None,
                  mhalf=None):
    (sqs,) = sc_eng_bufs
    t_s1 = pb.op("scalar", lambda s: s.activation(
        sqs[:, 0:D], xs, AF.Square, accum_out=S[:, 1:2]), waits=waits)
    t_s0 = pb.op("scalar", lambda s: s.activation(
        sqs[:, 0:D], xs, AF.Copy, accum_out=S[:, 0:1]), waits=[t_s1])
    t_mv = pb.op("vector", lambda v: v.tensor_scalar_mul(S[:, 2:3], S[:, 0:1], 1.0 / D),
                 waits=[t_s0])
    t_mv2 = pb.op("vector", lambda v: v.scalar_tensor_tensor(
        S[:, 3:4], S[:, 2:3], 1.0, S[:, 2:3], ALU.mult, ALU.mult), waits=[t_mv])
    t_var = pb.op("vector", lambda v: v.scalar_tensor_tensor(
        S[:, 3:4], S[:, 1:2], 1.0 / D, S[:, 3:4], ALU.mult, ALU.subtract), waits=[t_mv2])
    t_sq = pb.op("scalar", lambda s: s.activation(
        S[:, 4:5], S[:, 3:4], AF.Ln, bias=consts_eps_col), waits=[t_var, eps_tok])
    t_r = pb.op("scalar", lambda s: s.activation(
        S[:, 5:6], S[:, 4:5], AF.Exp, scale=mhalf), waits=[t_sq])
    return t_var, t_r



def build_l0f(nt_chunks, npass=2):
    nc = bass.Bass("TRN2", target_bir_lowering=False)
    NT = nt_chunks * 128
    io = {
        'y1_in': nc.dram_tensor("y1_in", [NT, D], F32, kind="ExternalInput").ap(),
        'g_bc': nc.dram_tensor("g_bc", [128, D], F32, kind="ExternalInput").ap(),
        'b_bc': nc.dram_tensor("b_bc", [128, D], F32, kind="ExternalInput").ap(),
        'rW': nc.dram_tensor("rW", [128, 8 * 8], F32, kind="ExternalInput").ap(),
        'rb': nc.dram_tensor("rb", [1, 8], F32, kind="ExternalInput").ap(),
        'W1': nc.dram_tensor("W1", [8, D, 2048], F32, kind="ExternalInput").ap(),
        'b1t': nc.dram_tensor("b1t", [128, 8 * 16], F32, kind="ExternalInput").ap(),
        'W2': nc.dram_tensor("W2", [8, 2048, D], F32, kind="ExternalInput").ap(),
        'b2r': nc.dram_tensor("b2r", [1, 8 * D], F32, kind="ExternalInput").ap(),
        'identm': nc.dram_tensor("identm", [128, 128], BF16, kind="ExternalInput").ap(),
        'y2': nc.dram_tensor("y2", [NT, D], F32, kind="ExternalOutput").ap(),
    }
    with nc.Block() as blk:
        pb = PB(nc)
        emit_l0f(nc, pb, blk, io, nt_chunks, npass)
        pb.emit(blk)
    return nc


def emit_l0f(nc, pb, blk, io, nt_chunks, npass=2, tag=''):
    NT = nt_chunks * 128
    (y1_in, g_bc, b_bc, rW, rb, W1, b1t, W2, b2r, ident_in, y2) = (
        io['y1_in'], io['g_bc'], io['b_bc'], io['rW'], io['rb'], io['W1'],
        io['b1t'], io['W2'], io['b2r'], io['identm'], io['y2'])
    w1_bf = nc.dram_tensor("w1_bf" + tag, [8, D, 2048], BF16).ap()
    w2_bf = nc.dram_tensor("w2_bf" + tag, [8, 2048, D], BF16).ap()

    pc = nt_chunks // npass
    TGP = (pc * 128) // 512

    glob = Alloc(nc)
    ident = glob.sb("ident", [128, 128], BF16)
    gb = glob.sb("gb", [128, 2 * D], F32)
    b1s = glob.sb("b1s", [128, 8 * 16], F32)
    b2row = glob.sb("b2row", [1, 8 * D], BF16)
    rWs = glob.sb("rWs", [128, 8 * 8], BF16)
    rbs = glob.sb("rbs", [1, 8], BF16)
    onesrow = glob.sb("onesrow", [1, 128], BF16)
    consts = glob.sb("consts", [128, 4], F32)
    stats = glob.sb("stats", [128, nt_chunks * 4], F32)
    wts = glob.sb("wts", [128, nt_chunks * 8], F32)
    zT = glob.sb("zT", [128, 8 * NT], BF16)
    acc = glob.sb("acc", [128, pc * D], F32)
    SAB = glob.sb("SAB", [128, 7 * nt_chunks], F32)
    if True:
        t_ones = pb.op("gpsimd", lambda g: g.memset(onesrow[:], 1.0))
        pb.op("gpsimd", lambda g: g.memset(consts[:, 0:1], LN_EPS))
        pb.op("gpsimd", lambda g: g.memset(consts[:, 1:2], MOE_LN_EPS))
        t_eps = pb.op("gpsimd", lambda g: g.memset(consts[:, 2:3], -0.5))
        t_id = pb.op("gpsimd", lambda g: g.dma_start(out=ident[:], in_=ident_in[:]), dma=True)
        t_gb = pb.op("gpsimd", lambda g: g.dma_start(out=gb[:, 0:D], in_=g_bc[:]), dma=True)
        t_bb = pb.op("gpsimd", lambda g: g.dma_start(out=gb[:, D:2 * D], in_=b_bc[:]), dma=True)
        t_b1 = pb.op("gpsimd", lambda g: g.dma_start(out=b1s[:], in_=b1t[:]), dma=True)
        t_b2 = pb.op("gpsimd", lambda g: g.dma_start(out=b2row[:], in_=b2r[:]), dma=True)
        t_rw = pb.op("gpsimd", lambda g: g.dma_start(out=rWs[:], in_=rW[:]), dma=True)
        t_rb = pb.op("gpsimd", lambda g: g.dma_start(out=rbs[:], in_=rb[:]), dma=True)

        ph = Alloc(nc)
        yc = ph.sb("yc", [128, 2 * D], F32)
        sqs = ph.sb("sqs", [128, D], F32)
        S8 = ph.sb("S8", [128, 16], F32)
        az = ph.sb("az", [128, 2 * D], F32)
        zc = ph.sb("zc", [128, 2 * D], BF16)
        es = ph.sb("es", [128, 2 * 8], F32)
        r8 = ph.sb("r8", [128, 2 * 8], F32)
        pTz = [ph.ps("pTz0", [128, 1024], BF16), ph.ps("pTz1", [128, 1024], BF16)]
        plog = ph.ps("plog", [128, 1024], F32)

        NT_C = nt_chunks
        SA, SB = SAB[:, 0:NT_C], SAB[:, NT_C:2 * NT_C]
        MEAN, R1 = SAB[:, 2 * NT_C:3 * NT_C], SAB[:, 3 * NT_C:4 * NT_C]
        SC12 = SAB[:, 4 * NT_C:5 * NT_C]
        P0, P1 = SAB[:, 5 * NT_C:6 * NT_C], SAB[:, 6 * NT_C:7 * NT_C]
        lg_all = ph.sb("lg_all", [128, NT_C * 8], F32)
        t_sfree = [None, None]
        for i in range(NT_C):
            p = i % 2
            ys = yc[:, p * D:(p + 1) * D]
            t_y = pb.op("sync", lambda s, i=i, ys=ys: s.dma_start(
                out=ys, in_=y1_in[i * 128:(i + 1) * 128, :]),
                waits=[t_sfree[p]], dma=True)
            t_sfree[p] = emit_sums(pb, ys, SA[:, i:i + 1], SB[:, i:i + 1], sqs,
                                   [t_y, t_sfree[1 - p]])
        t_st1 = emit_stats_batch(pb, SA, SB, MEAN, R1, consts[:, 0:1],
                                 consts[:, 2:3], [t_sfree[0], t_sfree[1], t_eps])
        t_b1_ = pb.op("vector", lambda v: v.tensor_scalar_add(P0, R1, 1.0), waits=[t_st1])
        t_b2_ = pb.op("vector", lambda v: v.tensor_tensor(P1, MEAN, R1, ALU.mult),
                      waits=[t_b1_])
        t_b3_ = pb.op("vector", lambda v: v.scalar_tensor_tensor(
            SC12, MEAN, 1.0, MEAN, ALU.mult, ALU.mult), waits=[t_b2_])
        t_b4_ = pb.op("vector", lambda v: v.scalar_tensor_tensor(
            SC12, SA, 1.0 / D, SC12, ALU.mult, ALU.subtract), waits=[t_b3_])
        t_b5_ = pb.op("vector", lambda v: v.tensor_tensor(SA, R1, R1, ALU.mult),
                      waits=[t_b4_])
        t_b6_ = pb.op("vector", lambda v: v.tensor_tensor(SC12, SC12, SA, ALU.mult),
                      waits=[t_b5_])
        t_b7_ = pb.op("scalar", lambda s: s.activation(SC12, SC12, AF.Ln,
                                                       bias=consts[:, 1:2]), waits=[t_b6_])
        t_b8_ = pb.op("scalar", lambda s: s.activation(SC12, SC12, AF.Exp,
                                                       scale=consts[:, 2:3]), waits=[t_b7_])
        t_stats = pb.op("vector", lambda v: v.tensor_tensor(SC12, SC12, R1, ALU.mult),
                        waits=[t_b8_])
        t_yfree = None
        t_lnTfree = [None, None]
        t_zfree = [None, None]
        t_azfree = [None, None]
        t_lgfree = None
        t_plogfree = [None, None]
        for i in range(NT_C):
            p = i % 2
            ys = yc[:, p * D:(p + 1) * D]
            t_y = pb.op("sync", lambda s, i=i, ys=ys: s.dma_start(
                out=ys, in_=y1_in[i * 128:(i + 1) * 128, :]),
                waits=[t_yfree] if i >= 2 else [t_sfree[0], t_sfree[1]], dma=True)
            t_az = pb.op("vector", lambda v, ys=ys, i=i, p=p: v.tensor_scalar(
                az[:, p * D:(p + 1) * D], ys, MEAN[:, i:i + 1], SC12[:, i:i + 1],
                ALU.subtract, ALU.mult), waits=[t_y, t_stats, t_azfree[p]])
            t_yfree = t_az
            t_z1 = pb.op("vector", lambda v, p=p: v.tensor_tensor(
                az[:, p * D:(p + 1) * D], az[:, p * D:(p + 1) * D], gb[:, 0:D], ALU.mult),
                waits=[t_az, t_gb])
            t_z = pb.op("vector", lambda v, p=p: v.tensor_tensor(
                zc[:, p * D:(p + 1) * D], az[:, p * D:(p + 1) * D], gb[:, D:2 * D], ALU.add),
                waits=[t_z1, t_bb, t_zfree[p]])
            t_azfree[p] = t_z
            t_zTs = []
            t_tp = None
            for j in range(8):
                pj = (i * 8 + j) % 2
                t_tp = pb.op("tensor", lambda t, p=p, j=j, pj=pj: t.transpose(
                    pTz[pj][:, 0:128], zc[:, p * D + j * 128:p * D + (j + 1) * 128], ident[:]),
                    waits=[t_z, t_id, t_lnTfree[pj]])
                t_ev = pb.op("vector", lambda v, i=i, j=j, pj=pj: v.tensor_copy(
                    zT[:, j * NT + i * 128:j * NT + (i + 1) * 128], pTz[pj][:, 0:128]),
                    waits=[t_tp])
                t_lnTfree[pj] = t_ev
                t_zTs.append(t_ev)
            t_zfree[p] = t_tp

            def r_mms(t, i=i, p=p):
                for j in range(8):
                    t.matmul(plog[:, p * 512:p * 512 + 8],
                             zT[:, j * NT + i * 128:j * NT + (i + 1) * 128],
                             rWs[:, j * 8:(j + 1) * 8], start=(j == 0), stop=False)
                return t.matmul(plog[:, p * 512:p * 512 + 8], onesrow[0:1, :],
                                rbs[0:1, :], start=False, stop=True)
            t_lg = pb.op("tensor", r_mms,
                         waits=t_zTs + [t_rw, t_rb, t_ones, t_plogfree[p]])
            t_lge = pb.op("vector", lambda v, i=i, p=p: v.tensor_copy(
                lg_all[:, i * 8:(i + 1) * 8], plog[:, p * 512:p * 512 + 8]),
                waits=[t_lg, t_lgfree])
            t_plogfree[p] = t_lge
        L3 = lg_all[:].rearrange("p (c e) -> p c e", e=8)
        W3 = wts[:].rearrange("p (c e) -> p c e", e=8)
        t_d1 = pb.op("vector", lambda v: v.tensor_reduce(
            SA[:, 0:NT_C], L3, AX.X, ALU.max, negate=True),
            waits=[t_plogfree[0], t_plogfree[1]])
        t_d2 = pb.op("vector", lambda v: v.tensor_tensor(
            L3, L3, SA[:, 0:NT_C].unsqueeze(2).broadcast_to([128, NT_C, 8]), ALU.add),
            waits=[t_d1])
        t_d3 = pb.op("scalar", lambda s: s.activation(lg_all[:], lg_all[:], AF.Exp),
                     waits=[t_d2])
        t_d4 = pb.op("vector", lambda v: v.tensor_reduce(
            SA[:, 0:NT_C], L3, AX.X, ALU.max), waits=[t_d3])
        t_d5 = pb.op("vector", lambda v: v.tensor_tensor(
            W3, L3, SA[:, 0:NT_C].unsqueeze(2).broadcast_to([128, NT_C, 8]), ALU.is_ge),
            waits=[t_d4])
        t_d6 = pb.op("vector", lambda v: v.scalar_tensor_tensor(
            wts[:], wts[:], -1e30, lg_all[:], ALU.mult, ALU.add), waits=[t_d5])
        t_d7 = pb.op("vector", lambda v: v.tensor_reduce(
            SB[:, 0:NT_C], W3, AX.X, ALU.max), waits=[t_d6])
        t_d8 = pb.op("vector", lambda v: v.tensor_tensor(
            W3, L3, SB[:, 0:NT_C].unsqueeze(2).broadcast_to([128, NT_C, 8]), ALU.is_ge),
            waits=[t_d7])
        t_d9 = pb.op("vector", lambda v: v.tensor_tensor(
            wts[:], wts[:], lg_all[:], ALU.mult), waits=[t_d8])
        t_da = pb.op("vector", lambda v: v.tensor_reduce(
            SB[:, 0:NT_C], W3, AX.X, ALU.add), waits=[t_d9])
        t_db_ = pb.op("scalar", lambda s: s.activation(SB[:, 0:NT_C], SB[:, 0:NT_C],
                                                       AF.Ln), waits=[t_da])
        t_dc = pb.op("scalar", lambda s: s.activation(SB[:, 0:NT_C], SB[:, 0:NT_C],
                                                      AF.Exp, scale=-1.0), waits=[t_db_])
        t_dd = pb.op("vector", lambda v: v.tensor_tensor(
            W3, W3, SB[:, 0:NT_C].unsqueeze(2).broadcast_to([128, NT_C, 8]), ALU.mult),
            waits=[t_dc])
        pb.barrier()
        ph.close()

        ph2 = Alloc(nc)
        NWB = 4
        wbuf = [ph2.sb(f"wbuf{k}", [128, 8 * 1024], BF16) for k in range(NWB)]
        hb = ph2.sb("hb", [128, 16 * 512], BF16)
        yre = ph2.sb("yre", [128, D], F32)
        y2c = ph2.sb("y2c", [128, 2 * D], F32)
        phid = ph2.ps("phid", [128, 2048], F32)
        po = ph2.ps("po", [128, 2 * 1024], F32)

        t_wload = []
        t_wuse = []
        t_wecho = {}
        cache_tok = {}
        nloads = npass * 8 * 4

        def emit_wload(k):
            P, rem = divmod(k, 32)
            e, q = divmod(rem, 4)
            buf = wbuf[k % NWB]
            waits = []
            if k >= NWB:
                waits.append(t_wuse[k - NWB])
                waits.extend(t_wecho.get(k - NWB, []))
            if P == 0:
                if q < 2:
                    srcs = [W1[e, j * 128:(j + 1) * 128, q * 1024:(q + 1) * 1024]
                            for j in range(8)]
                    dsts = [w1_bf[e, j * 128:(j + 1) * 128, q * 1024:(q + 1) * 1024]
                            for j in range(8)]
                else:
                    h = q - 2
                    srcs = [W2[e, h * 1024 + j * 128:h * 1024 + (j + 1) * 128, :]
                            for j in range(8)]
                    dsts = [w2_bf[e, h * 1024 + j * 128:h * 1024 + (j + 1) * 128, :]
                            for j in range(8)]
                toks = [pb.op("gpsimd", lambda g, j=j, buf=buf, src=srcs[j]: g.dma_start(
                    out=buf[:, j * 1024:(j + 1) * 1024], in_=src), waits=waits, dma=True)
                    for j in range(8)]
                if npass > 1:
                    echo = [pb.op("sync", lambda s, j=j, buf=buf, dst=dsts[j]: s.dma_start(
                        out=dst, in_=buf[:, j * 1024:(j + 1) * 1024]),
                        waits=[toks[j]], dma=True) for j in range(8)]
                    cache_tok[(e, q)] = echo
                    t_wecho[k] = echo
            else:
                if q < 2:
                    srcs = [w1_bf[e, j * 128:(j + 1) * 128, q * 1024:(q + 1) * 1024]
                            for j in range(8)]
                else:
                    h = q - 2
                    srcs = [w2_bf[e, h * 1024 + j * 128:h * 1024 + (j + 1) * 128, :]
                            for j in range(8)]
                waits = waits + cache_tok[(e, q)]
                toks = [pb.op("sync", lambda s, j=j, buf=buf, src=srcs[j]: s.dma_start(
                    out=buf[:, j * 1024:(j + 1) * 1024], in_=src), waits=waits, dma=True)
                    for j in range(8)]
            t_wload.append(toks)
            t_wuse.append(None)

        for k in range(min(NWB, nloads)):
            emit_wload(k)

        t_accfree = None
        t_hbfree = [None]
        t_pofree = [None, None]
        t_phidfree = [None] * 16
        t_yrefree = None
        t_y2free = [None, None]
        t_y2st = []
        for P in range(npass):
            c0 = P * pc
            t_az0 = pb.op("vector", lambda v: v.memset(acc[:], 0.0),
                          waits=[t_accfree])
            t_accfree = None
            t_evacs = []
            for e in range(8):
                kbase = P * 32 + e * 4
                for tg in range(TGP):
                    tp = 0
                    HB = hb[:]
                    t_gelus = []
                    for fo in range(16):
                        half = fo // 8
                        fol = fo % 8
                        kW1 = kbase + half
                        bank = fo % 4

                        def w1_mms(t, e=e, tg=tg, fo=fo, fol=fol, kW1=kW1, bank=bank,
                                   c0=c0, P=P):
                            buf = wbuf[kW1 % NWB]
                            for j in range(8):
                                last = t.matmul(
                                    phid[:, bank * 512:(bank + 1) * 512],
                                    buf[:, j * 1024 + fol * 128:j * 1024 + (fol + 1) * 128],
                                    zT[:, j * NT + c0 * 128 + tg * 512:
                                       j * NT + c0 * 128 + (tg + 1) * 512],
                                    start=(j == 0), stop=(j == 7))
                            return last
                        t_mm = pb.op("tensor", w1_mms,
                                     waits=t_wload[kW1] + [t_phidfree[bank],
                                                           t_gelus[-4] if fo >= 4 else None])
                        t_g = pb.op("scalar", lambda s, fo=fo, bank=bank, e=e, HB=HB:
                                    s.activation(HB[:, fo * 512:(fo + 1) * 512],
                                                 phid[:, bank * 512:(bank + 1) * 512],
                                                 AF.Gelu, bias=b1s[:, e * 16 + fo:e * 16 + fo + 1]),
                                    waits=[t_mm, t_b1, t_hbfree[tp]] if fo == 0 else [t_mm])
                        t_phidfree[bank] = t_g
                        t_gelus.append(t_g)
                    if e == 0 and tg < 2:
                        pass
                    t_wuse[kbase] = t_gelus[7]
                    t_wuse[kbase + 1] = t_gelus[15]
                    t_w2ev = []
                    for t128 in range(4):
                        il = tg * 4 + t128
                        pop = (tg * 4 + t128) % 2

                        def w2_mms(t, e=e, kbase=kbase, t128=t128, pop=pop, HB=HB):
                            for piece in range(2):
                                for i16 in range(16):
                                    buf = wbuf[(kbase + 2 + i16 // 8) % NWB]
                                    t.matmul(
                                        po[:, pop * 1024 + piece * 512:
                                           pop * 1024 + (piece + 1) * 512],
                                        HB[:, i16 * 512 + t128 * 128:i16 * 512 + t128 * 128 + 128],
                                        buf[:, (i16 % 8) * 1024 + piece * 512:
                                            (i16 % 8) * 1024 + (piece + 1) * 512],
                                        start=(i16 == 0), stop=False)
                                last = t.matmul(
                                    po[:, pop * 1024 + piece * 512:pop * 1024 + (piece + 1) * 512],
                                    onesrow[0:1, :],
                                    b2row[0:1, e * D + piece * 512:e * D + (piece + 1) * 512],
                                    start=False, stop=True)
                            return last
                        t_w2 = pb.op("tensor", w2_mms,
                                     waits=[t_gelus[15], t_pofree[pop], t_b2]
                                     + t_wload[kbase + 2] + t_wload[kbase + 3])

                        def evac(v, e=e, il=il, pop=pop, c0=c0):
                            i_glob = c0 + il
                            for piece in range(2):
                                last = v.scalar_tensor_tensor(
                                    acc[:, il * D + piece * 512:il * D + (piece + 1) * 512],
                                    po[:, pop * 1024 + piece * 512:
                                       pop * 1024 + (piece + 1) * 512],
                                    wts[:, i_glob * 8 + e:i_glob * 8 + e + 1],
                                    acc[:, il * D + piece * 512:il * D + (piece + 1) * 512],
                                    ALU.mult, ALU.add)
                            return last
                        t_ev = pb.op("vector", evac, waits=[t_w2, t_az0])
                        t_pofree[pop] = t_ev
                        t_evacs.append(t_ev)
                    t_hbfree[tp] = t_w2
                t_wuse[kbase + 2] = t_w2
                t_wuse[kbase + 3] = t_w2
                for k in range(kbase + NWB, min(kbase + 4 + NWB, nloads)):
                    if len(t_wload) <= k:
                        emit_wload(k)
            for il in range(pc):
                i = c0 + il
                p = il % 2
                t_yre = pb.op("sync", lambda s, i=i: s.dma_start(
                    out=yre[:], in_=y1_in[i * 128:(i + 1) * 128, :]),
                    waits=[t_yrefree], dma=True)
                t_t1 = pb.op("vector", lambda v, p=p, i=i: v.tensor_scalar(
                    y2c[:, p * D:(p + 1) * D], yre[:],
                    P0[:, i:i + 1], P1[:, i:i + 1], ALU.mult, ALU.subtract),
                    waits=[t_yre, t_y2free[p]] + t_evacs)
                t_t2 = pb.op("vector", lambda v, p=p, il=il: v.tensor_tensor(
                    y2c[:, p * D:(p + 1) * D], y2c[:, p * D:(p + 1) * D],
                    acc[:, il * D:(il + 1) * D], ALU.add), waits=[t_t1])
                t_yrefree = t_t2
                t_st = pb.op("sync", lambda s, i=i, p=p: s.dma_start(
                    out=y2[i * 128:(i + 1) * 128, :], in_=y2c[:, p * D:(p + 1) * D]),
                    waits=[t_t2], dma=True)
                t_y2free[p] = t_st
                t_y2st.append(t_st)
            t_accfree = t_y2st[-1]
        pb.op("sync", lambda s: s.nop(), waits=t_y2st[-2:])
        pb.barrier()
        ph2.close()
    glob.close()



def ceil_div(a, b):
    return -(-a // b)


def build_l1a(nqg=4, skc=None):
    if skc is None:
        skc = nqg * 8
    nc = bass.Bass("TRN2", target_bir_lowering=False)
    NQ = nqg * 512
    NS = skc * 128
    io = {
        'y2_seq': nc.dram_tensor("y2_seq", [NS, D], F32, kind="ExternalInput").ap(),
        'y2_own': nc.dram_tensor("y2_own", [NQ, D], F32, kind="ExternalInput").ap(),
        'w_q': nc.dram_tensor("w_q", [D, D], F32, kind="ExternalInput").ap(),
        'w_kv': nc.dram_tensor("w_kv", [D, 2 * D], F32, kind="ExternalInput").ap(),
        'w_out': nc.dram_tensor("w_out", [D, D], F32, kind="ExternalInput").ap(),
        'kcs': nc.dram_tensor("kcs", [128, skc * 64], F32, kind="ExternalInput").ap(),
        'qcs': nc.dram_tensor("qcs", [128, (NQ // 128) * 64], F32, kind="ExternalInput").ap(),
        'cmask': nc.dram_tensor("cmask", [128, nqg * 8 * 128], BF16, kind="ExternalInput").ap(),
        'identm': nc.dram_tensor("identm", [128, 128], BF16, kind="ExternalInput").ap(),
        'y3': nc.dram_tensor("y3", [NQ, D], F32, kind="ExternalOutput").ap(),
    }
    with nc.Block() as blk:
        pb = PB(nc)
        emit_l1a(nc, pb, blk, io, nqg, skc)
        pb.emit(blk)
    return nc


def emit_l1a(nc, pb, blk, io, nqg=4, skc=None, tag='', barrier=True, mid_hook=None, bg=None):
    if skc is None:
        skc = nqg * 8
    NQ = nqg * 512
    NS = skc * 128
    (y2_seq, y2_own, w_q, w_kv, w_out, kcs, qcs, cmask, ident_in, y3) = (
        io['y2_seq'], io['y2_own'], io['w_q'], io['w_kv'], io['w_out'],
        io['kcs'], io['qcs'], io['cmask'], io['identm'], io['y3'])
    qk_dram = nc.dram_tensor("l1qk_scr" + tag, [NS, D], BF16).ap()
    q_dram = nc.dram_tensor("q_scr" + tag, [NQ, D], BF16).ap()
    o_dram = nc.dram_tensor("o_scr" + tag, [D, NQ], BF16).ap()

    glob = Alloc(nc)
    ident = glob.sb("ident", [128, 128], BF16)
    onesrow = glob.sb("onesrow", [1, 128], BF16)
    onesf = glob.sb("onesf", [1, 64], F32)
    consts = glob.sb("consts", [128, 4], F32)
    v_aug = glob.sb("v_aug", [128, skc * 16 * 65], BF16)
    kT = glob.sb("kT", [128, 8 * NS], BF16)
    if True:
        t_ones = pb.op("gpsimd", lambda g: g.memset(onesrow[:], 1.0))
        t_onesf = pb.op("gpsimd", lambda g: g.memset(onesf[:], 1.0))
        pb.op("gpsimd", lambda g: g.memset(consts[:, 0:1], LN_EPS))
        t_lne = pb.op("gpsimd", lambda g: g.memset(consts[:, 2:3], -0.5))
        t_off = pb.op("gpsimd", lambda g: g.memset(consts[:, 1:2], -SM_OFF))
        t_id = pb.op("gpsimd", lambda g: g.dma_start(out=ident[:], in_=ident_in[:]), dma=True)

        ph = Alloc(nc)
        wkv = ph.sb("wkv", [128, 8 * 2 * D], BF16)
        css = ph.sb("css", [128, 2 * 64], F32)
        yc = ph.sb("yc", [128, 2 * D], F32)
        lnc = ph.sb("lnc", [128, 2 * D], BF16)
        lnT = ph.sb("lnT", [128, 2 * 8 * 128], BF16)
        sqs = ph.sb("sqs", [128, D], F32)
        S8 = ph.sb("S8", [128, 16], F32)
        kf = ph.sb("kf", [128, 2 * D], F32)
        tmp = ph.sb("tmp", [128, 512], F32)
        kb = ph.sb("kb", [128, 2 * D], BF16)
        pT = [ph.ps("pT0", [128, 1024], BF16), ph.ps("pT1", [128, 1024], BF16)]
        pkv = ph.ps("pkv", [128, 2048], F32)
        pq = ph.ps("pq", [128, 1024], F32)

        t_wkv = [pb.op("gpsimd", lambda g, j=j: g.dma_start(
            out=wkv[:, j * 2048:(j + 1) * 2048],
            in_=w_kv[j * 128:(j + 1) * 128, :]), dma=True) for j in range(8)]
        t_wq = []
        t_css = [None, None]

        rope_chain = [None]

        def rope_inplace(pb_, X, CS, c, waits):
            X3 = X.rearrange("p (h e) -> p h e", e=64)
            C = CS[:, 0:32]
            Sn = CS[:, 32:64]

            def b(col0, n):
                return lambda q: q
            t1 = pb_.op("vector", lambda v: v.tensor_tensor(
                tmp[:, 0:256].rearrange("p (h e) -> p h e", e=16),
                X3[:, :, 16:32],
                Sn[:, 0:16].unsqueeze(1).broadcast_to([128, 16, 16]), ALU.mult),
                waits=list(waits) + [rope_chain[0]])
            t2 = pb_.op("vector", lambda v: v.tensor_tensor(
                tmp[:, 256:512].rearrange("p (h e) -> p h e", e=16),
                X3[:, :, 0:16],
                Sn[:, 16:32].unsqueeze(1).broadcast_to([128, 16, 16]), ALU.mult),
                waits=waits)
            t3 = pb_.op("vector", lambda v: v.tensor_tensor(
                X3[:, :, 0:16], X3[:, :, 0:16],
                C[:, 0:16].unsqueeze(1).broadcast_to([128, 16, 16]), ALU.mult),
                waits=[t1, t2])
            t4 = pb_.op("vector", lambda v: v.tensor_tensor(
                X3[:, :, 16:32], X3[:, :, 16:32],
                C[:, 16:32].unsqueeze(1).broadcast_to([128, 16, 16]), ALU.mult),
                waits=[t3])
            t5 = pb_.op("vector", lambda v: v.tensor_tensor(
                X3[:, :, 0:16], X3[:, :, 0:16],
                tmp[:, 0:256].rearrange("p (h e) -> p h e", e=16), ALU.subtract),
                waits=[t4])
            t6 = pb_.op("vector", lambda v: v.tensor_tensor(
                X3[:, :, 16:32], X3[:, :, 16:32],
                tmp[:, 256:512].rearrange("p (h e) -> p h e", e=16), ALU.add),
                waits=[t5])
            rope_chain[0] = t6
            return t6

        t_kst = []
        t_qst = []
        NCH = skc + NQ // 128
        SAB = ph.sb("SAB", [128, 4 * (NCH)], F32)
        SA, SB = SAB[:, 0:NCH], SAB[:, NCH:2 * NCH]
        MEAN, RSTD = SAB[:, 2 * NCH:3 * NCH], SAB[:, 3 * NCH:4 * NCH]
        t_sfree = [None, None]
        GS = 8
        t_gstats = {}
        for i in range(NCH):
            p = i % 2
            ys = yc[:, p * D:(p + 1) * D]
            src_ = y2_seq[i * 128:(i + 1) * 128, :] if i < skc else \
                y2_own[(i - skc) * 128:(i - skc + 1) * 128, :]
            t_y = pb.op("sync", lambda s, src_=src_, ys=ys: s.dma_start(out=ys, in_=src_),
                        waits=[t_sfree[p]], dma=True)
            t_sfree[p] = emit_sums(pb, ys, SA[:, i:i + 1], SB[:, i:i + 1], sqs,
                                   [t_y, t_sfree[1 - p]])
            if i % GS == GS - 1 or i == NCH - 1:
                g = i // GS
                lo, hi = g * GS, i + 1
                t_gstats[g] = emit_stats_batch(
                    pb, SA[:, lo:hi], SB[:, lo:hi], MEAN[:, lo:hi], RSTD[:, lo:hi],
                    consts[:, 0:1], consts[:, 2:3],
                    [t_sfree[0], t_sfree[1], t_lne])
        t_yfree = None
        t_lnTfree = [None, None]
        t_lncfree = [None, None]
        t_kffree = [None, None]
        t_kbfree = [None, None]
        t_pkvfree = None
        t_pqfree = [None, None]
        t_vone = None
        for i in range(NCH):
            isq = i < skc
            io = i - skc
            p = i % 2
            ys = yc[:, p * D:(p + 1) * D]
            lns = lnc[:, p * D:(p + 1) * D]
            src_ = y2_seq[i * 128:(i + 1) * 128, :] if isq else \
                y2_own[io * 128:(io + 1) * 128, :]
            t_y = pb.op("sync", lambda s, src_=src_, ys=ys: s.dma_start(out=ys, in_=src_),
                        waits=[t_yfree] if i >= 2 else [t_sfree[0], t_sfree[1]], dma=True)
            t_ln = pb.op("vector", lambda v, ys=ys, lns=lns, i=i: v.tensor_scalar(
                lns, ys, MEAN[:, i:i + 1], RSTD[:, i:i + 1], ALU.subtract, ALU.mult),
                waits=[t_y, t_gstats[i // GS], t_lncfree[p]])
            t_yfree = t_ln
            t_lnTs = []
            t_tp = None
            for j in range(8):
                pj = (i * 8 + j) % 2
                t_tp = pb.op("tensor", lambda t, lns=lns, j=j, pj=pj: t.transpose(
                    pT[pj][:, 0:128], lns[:, j * 128:(j + 1) * 128], ident[:]),
                    waits=[t_ln, t_id, t_lnTfree[pj]])
                t_ev = pb.op("vector", lambda v, j=j, p=p, pj=pj: v.tensor_copy(
                    lnT[:, (p * 8 + j) * 128:(p * 8 + j + 1) * 128], pT[pj][:, 0:128]),
                    waits=[t_tp])
                t_lnTfree[pj] = t_ev
                t_lnTs.append(t_ev)
            t_lncfree[p] = t_tp
            if isq:
                def kv_mms(t, p=p):
                    pass
                def kv_mms(t, p=p):
                    for piece in range(4):
                        for j in range(8):
                            last = t.matmul(
                                pkv[:, piece * 512:(piece + 1) * 512],
                                lnT[:, (p * 8 + j) * 128:(p * 8 + j + 1) * 128],
                                wkv[:, j * 2048 + piece * 512:j * 2048 + (piece + 1) * 512],
                                start=(j == 0), stop=(j == 7))
                    return last
                t_kv = pb.op("tensor", kv_mms, waits=t_lnTs + t_wkv + [t_pkvfree])
                if i == skc - 1:
                    t_kv_last = t_kv
                t_vev = pb.op("vector", lambda v, i=i: v.tensor_copy(
                    v_aug[:, i * 1040:(i + 1) * 1040].rearrange("p (h e) -> p h e", e=65)[:, :, 0:64],
                    pkv[:, 1024:2048].rearrange("p (h e) -> p h e", e=64)), waits=[t_kv])
                t_vone = pb.op("vector", lambda v, i=i: v.memset(
                    v_aug[:, i * 1040 + 64:(i + 1) * 1040:65], 1.0), waits=[t_vev])
                t_kev = pb.op("vector", lambda v, p=p: v.tensor_copy(
                    kf[:, p * D:(p + 1) * D], pkv[:, 0:1024]),
                    waits=[t_kv, t_kffree[p]])
                t_pkvfree = t_vev if False else t_kev
                t_cl = pb.op("scalar", lambda g, i=i, p=p: g.dma_start(
                    out=css[:, p * 64:(p + 1) * 64], in_=kcs[:, i * 64:(i + 1) * 64]),
                    waits=[t_css[p]], dma=True)
                t_rope = rope_inplace(pb, kf[:, p * D:(p + 1) * D],
                                      css[:, p * 64:(p + 1) * 64], i, [t_kev, t_cl])
                t_css[p] = t_rope
                t_kb = pb.op("vector", lambda v, p=p: v.tensor_copy(
                    kb[:, p * D:(p + 1) * D], kf[:, p * D:(p + 1) * D]),
                    waits=[t_rope, t_kbfree[p]])
                t_kffree[p] = t_kb
                t_st = pb.op("sync", lambda s, i=i, p=p: s.dma_start(
                    out=qk_dram[i * 128:(i + 1) * 128, 0:1024], in_=kb[:, p * D:(p + 1) * D]),
                    waits=[t_kb], dma=True)
                t_kbfree[p] = t_st
                t_kst.append(t_st)
            else:
                if io == 0:
                    t_wq = [pb.op("gpsimd", lambda g, j=j: g.dma_start(
                        out=wkv[:, j * 2048:j * 2048 + 1024],
                        in_=w_q[j * 128:(j + 1) * 128, :]),
                        waits=[t_kv_last], dma=True) for j in range(8)]

                def q_mms(t, p=p):
                    for piece in range(2):
                        for j in range(8):
                            last = t.matmul(
                                pq[:, piece * 512:(piece + 1) * 512],
                                lnT[:, (p * 8 + j) * 128:(p * 8 + j + 1) * 128],
                                wkv[:, j * 2048 + piece * 512:j * 2048 + (piece + 1) * 512],
                                start=(j == 0), stop=(j == 7))
                    return last
                t_q = pb.op("tensor", q_mms, waits=t_lnTs + t_wq + [t_pqfree[0]])
                t_qev = pb.op("vector", lambda v, p=p: v.tensor_scalar_mul(
                    kf[:, p * D:(p + 1) * D], pq[:], 0.125),
                    waits=[t_q, t_kffree[p]])
                t_pqfree[0] = t_qev
                t_cl = pb.op("scalar", lambda g, io=io, p=p: g.dma_start(
                    out=css[:, p * 64:(p + 1) * 64], in_=qcs[:, io * 64:(io + 1) * 64]),
                    waits=[t_css[p]], dma=True)
                t_rope = rope_inplace(pb, kf[:, p * D:(p + 1) * D],
                                      css[:, p * 64:(p + 1) * 64], io, [t_qev, t_cl])
                t_css[p] = t_rope
                t_kb = pb.op("vector", lambda v, p=p: v.tensor_copy(
                    kb[:, p * D:(p + 1) * D], kf[:, p * D:(p + 1) * D]),
                    waits=[t_rope, t_kbfree[p]])
                t_kffree[p] = t_kb
                t_st = pb.op("sync", lambda s, io=io, p=p: s.dma_start(
                    out=q_dram[io * 128:(io + 1) * 128, :], in_=kb[:, p * D:(p + 1) * D]),
                    waits=[t_kb], dma=True)
                t_kbfree[p] = t_st
                t_qst.append(t_st)
        t_kT = [pb.op("sync", lambda s, j=j: s.dma_start(
            out=kT[:, j * NS:(j + 1) * NS],
            in_=qk_dram[:, j * 128:(j + 1) * 128], transpose=True),
            waits=t_kst, dma=True) for j in range(8)]
        pb.barrier()
        ph.close()
        if mid_hook is not None:
            tails = [pb.streams[e][-1][2] for e in ENGINES if pb.streams[e]]
            mid_hook(tails)

        ph3 = Alloc(nc)
        qTg = ph3.sb("qTg", [128, 2 * 8 * 512], BF16)
        mk = ph3.sb("mk", [128, 2 * 8 * 128], BF16)
        expt = ph3.sb("expt", [128, 2 * 1024], BF16)
        rl = ph3.sb("rl", [1, 3 * 512], F32)
        rlbc = ph3.sb("rlbc", [64, 3 * 512], F32)
        oT = ph3.sb("oT", [128, 3 * 512], BF16)
        psc = ph3.ps("psc", [128, 2 * 1024], F32)
        pav = ph3.ps("pav", [65, 2 * 512], F32)
        pbc = ph3.ps("pbc", [64, 512], F32)

        t_qT = [[None] * 8 for _ in range(2)]
        t_mkl = [None, None]
        t_qTfree = [[None] * 8 for _ in range(2)]
        t_mkfree = [None, None]
        t_pscfree = [None, None]
        t_expfree = [None, None]
        t_pavfree = [None, None]
        pb.pair_ctr = 0
        t_rlf = None
        t_bcf = None
        t_oTfree = [None, None]
        t_ost = []
        for qg in range(nqg):
            gp = qg % 2
            t_qT[gp] = [pb.op("sync", lambda s, j=j, qg=qg, gp=gp: s.dma_start(
                out=qTg[:, (gp * 8 + j) * 512:(gp * 8 + j + 1) * 512],
                in_=q_dram[qg * 512:(qg + 1) * 512, j * 128:(j + 1) * 128], transpose=True),
                waits=t_qst + [t_qTfree[gp][j]], dma=True) for j in range(8)]
            t_mkl[gp] = pb.op("scalar", lambda g, qg=qg, gp=gp: g.dma_start(
                out=mk[:, gp * 1024:(gp + 1) * 1024],
                in_=cmask[:, qg * 1024:(qg + 1) * 1024]),
                waits=[t_mkfree[gp]], dma=True)
            qg_last_avs = []

            def head_pipeline(h, qg=qg, gp=gp):
                nonlocal t_rlf, t_bcf
                u = qg * 16 + h
                pu = u % 2
                tile, row0 = h // 2, 64 * (h % 2)
                nkc = 8 * qg + 8
                t_avs = []
                for k0 in range(0, nkc, 2):
                    pr = (pb.pair_ctr) % 2
                    pb.pair_ctr += 1
                    kcs_ = [kc for kc in (k0, k0 + 1) if kc < nkc]
                    t_mms = []
                    cols = []
                    for ii, kc in enumerate(kcs_):
                        thresh = max(0, min(4, ceil_div(kc - 1 - 8 * qg, 2)))
                        ncol = (4 - thresh) * 128
                        col0 = ii * 512 + thresh * 128
                        cols.append((kc, col0, ncol, thresh))
                        t_sc = pb.op("tensor", lambda t, tile=tile, row0=row0, kc=kc,
                                     qg=qg, gp=gp, pr=pr, col0=col0, ncol=ncol,
                                     ii=ii: t.matmul(
                                         psc[:, pr * 1024 + col0:pr * 1024 + col0 + ncol],
                                         kT[row0:row0 + 64, tile * NS + kc * 128:
                                            tile * NS + (kc + 1) * 128],
                                         qTg[row0:row0 + 64,
                                             (gp * 8 + tile) * 512 + col0 - ii * 512:
                                             (gp * 8 + tile) * 512 + col0 - ii * 512 + ncol],
                                         start=True, stop=True),
                                     waits=t_kT + [t_qT[gp][tile], t_pscfree[pr]])
                        t_mms.append(t_sc)
                    spans = []
                    for (kc, col0, ncol, thresh) in cols:
                        if spans and spans[-1][1] == col0:
                            spans[-1][1] = col0 + ncol
                        else:
                            spans.append([col0, col0 + ncol])
                    t_exp = None
                    for (e0, e1) in spans:
                        t_exp = pb.op("scalar", lambda s, pr=pr, e0=e0, e1=e1: s.activation(
                            expt[:, pr * 1024 + e0:pr * 1024 + e1],
                            psc[:, pr * 1024 + e0:pr * 1024 + e1],
                            AF.Exp, bias=consts[:, 1:2]),
                            waits=t_mms + [t_off, t_expfree[pr]])
                    t_pscfree[pr] = t_exp
                    t_last = t_exp
                    for (kc, col0, ncol, thresh) in cols:
                        if kc >= 8 * qg:
                            t_m = pb.op("vector", lambda v, pr=pr, gp=gp, kc=kc, qg=qg,
                                        col0=col0: v.tensor_tensor(
                                            expt[:, pr * 1024 + col0:pr * 1024 + col0 + 128],
                                            expt[:, pr * 1024 + col0:pr * 1024 + col0 + 128],
                                            mk[:, gp * 1024 + (kc - 8 * qg) * 128:
                                               gp * 1024 + (kc - 8 * qg) * 128 + 128],
                                            ALU.mult),
                                        waits=[t_exp, t_mkl[gp]])
                        else:
                            t_m = t_exp
                        t_av = pb.op("tensor", lambda t, kc=kc, h=h, pr=pr, pu=pu,
                                     col0=col0, ncol=ncol: t.matmul(
                                         pav[:, pu * 512 + col0 % 512:
                                             pu * 512 + col0 % 512 + ncol],
                                         v_aug[:, kc * 1040 + h * 65:kc * 1040 + (h + 1) * 65],
                                         expt[:, pr * 1024 + col0:pr * 1024 + col0 + ncol],
                                         start=(kc == 0), stop=(kc == nkc - 1)),
                                     waits=[t_m, t_vone, t_pavfree[pu]] if kc == 0 else [t_m])
                        t_last = t_av
                        t_avs.append(t_av)
                    t_expfree[pr] = t_last
                    yield
                t_rl = pb.op("scalar", lambda s, pu=pu: s.activation(
                    rl[0:1, pu * 512:(pu + 1) * 512], pav[64:65, pu * 512:(pu + 1) * 512],
                    AF.Ln), waits=[t_avs[-1], t_rlf])
                t_rl = pb.op("scalar", lambda s, pu=pu: s.activation(
                    rl[0:1, pu * 512:(pu + 1) * 512], rl[0:1, pu * 512:(pu + 1) * 512],
                    AF.Exp, scale=-1.0), waits=[t_rl])
                t_bc = pb.op("tensor", lambda t, pu=pu: t.matmul(
                    pbc[:], onesf[0:1, 0:64], rl[0:1, pu * 512:(pu + 1) * 512],
                    start=True, stop=True), waits=[t_rl, t_bcf, t_onesf])
                t_rlf = t_bc
                t_bcev = pb.op("vector", lambda v, pu=pu: v.tensor_copy(
                    rlbc[:, pu * 512:(pu + 1) * 512], pbc[:]), waits=[t_bc])
                t_bcf = t_bcev
                t_fin = pb.op("vector", lambda v, pu=pu: v.scalar_tensor_tensor(
                    oT[row0:row0 + 64, pu * 512:(pu + 1) * 512] if False else
                    oT[0:64, pu * 512:(pu + 1) * 512],
                    pav[0:64, pu * 512:(pu + 1) * 512], 1.0,
                    rlbc[:, pu * 512:(pu + 1) * 512], ALU.mult, ALU.mult),
                    waits=[t_bcev, t_oTfree[pu]])
                t_pavfree[pu] = t_fin
                t_o = pb.op("sync", lambda s, h=h, qg=qg, pu=pu: s.dma_start(
                    out=o_dram[h * 64:(h + 1) * 64, qg * 512:(qg + 1) * 512],
                    in_=oT[0:64, pu * 512:(pu + 1) * 512]), waits=[t_fin], dma=True)
                t_oTfree[pu] = t_o
                t_ost.append(t_o)
                qg_last_avs.append(t_avs[-1])
                if bg is not None:
                    next(bg, None)

            for h0 in range(0, 16, 2):
                gens = [head_pipeline(h0), head_pipeline(h0 + 1)]
                while gens:
                    gens = [g for g in gens if next(g, 'done') != 'done']
                    if bg is not None:
                        for _ in range(3):
                            next(bg, None)
            for j in range(8):
                t_qTfree[gp][j] = qg_last_avs[-1]
            t_mkfree[gp] = qg_last_avs[-1]
        pb.barrier()
        ph3.close()

        ph4 = Alloc(nc)
        wos = ph4.sb("wos", [128, 8 * D], BF16)
        oTl = ph4.sb("oTl", [128, 2 * 8 * 128], BF16)
        yre = ph4.sb("yre", [128, 2 * D], F32)
        y3c = ph4.sb("y3c", [128, 2 * D], F32)
        pproj = ph4.ps("pproj", [128, 1024], F32)
        t_wo = [pb.op("gpsimd", lambda g, j=j: g.dma_start(
            out=wos[:, j * D:(j + 1) * D], in_=w_out[j * 128:(j + 1) * 128, :]),
            dma=True) for j in range(8)]
        t_y3 = [None, None]
        t_oTl = [None, None]
        t_pprojfree = None
        for qc in range(NQ // 128):
            p = qc % 2
            t_ol = [pb.op("scalar", lambda g, j=j, qc=qc, p=p: g.dma_start(
                out=oTl[:, (p * 8 + j) * 128:(p * 8 + j + 1) * 128],
                in_=o_dram[j * 128:(j + 1) * 128, qc * 128:(qc + 1) * 128]),
                waits=t_ost + [t_oTl[p]], dma=True) for j in range(8)]
            t_xr = pb.op("sync", lambda s, qc=qc, p=p: s.dma_start(
                out=yre[:, p * D:(p + 1) * D], in_=y2_own[qc * 128:(qc + 1) * 128, :]),
                waits=[t_y3[p]], dma=True)

            def proj_mms(t, p=p):
                for piece in range(2):
                    for j in range(8):
                        last = t.matmul(pproj[:, piece * 512:(piece + 1) * 512],
                                        oTl[:, (p * 8 + j) * 128:(p * 8 + j + 1) * 128],
                                        wos[:, j * D + piece * 512:j * D + (piece + 1) * 512],
                                        start=(j == 0), stop=(j == 7))
                return last
            t_proj = pb.op("tensor", proj_mms, waits=t_ol + t_wo + [t_pprojfree])
            t_y3c = pb.op("vector", lambda v, p=p: v.scalar_tensor_tensor(
                y3c[:, p * D:(p + 1) * D], pproj[:], 1.0,
                yre[:, p * D:(p + 1) * D], ALU.mult, ALU.add), waits=[t_proj, t_xr])
            t_pprojfree = t_y3c
            t_st = pb.op("sync", lambda s, qc=qc, p=p: s.dma_start(
                out=y3[qc * 128:(qc + 1) * 128, :], in_=y3c[:, p * D:(p + 1) * D]),
                waits=[t_y3c], dma=True)
            t_y3[p] = t_st
            t_oTl[p] = t_proj
        pb.op("sync", lambda s: s.nop(), waits=[t_y3[0], t_y3[1]])
        if barrier:
            pb.barrier()
        ph4.close()
    glob.close()


def l1a_rope_tables(positions):
    ROT = 32
    inv = 1.0 / (10000.0 ** (np.arange(0, ROT, 2) / ROT))
    f = positions[:, None] * inv[None, :]
    fr = np.concatenate([f, f], axis=-1)
    cs = np.concatenate([np.cos(fr), np.sin(fr)], axis=-1)
    n = len(positions) // 128
    return f32(cs.reshape(n, 128, 64).transpose(1, 0, 2).reshape(128, n * 64))


def l1a_masks(h, nqg=4):
    out = np.zeros((nqg, 8, 128, 128), dtype=np.float32)
    for qg in range(nqg):
        for dk in range(8):
            kc = 8 * qg + dk
            thresh = max(0, min(4, ceil_div(kc - 1 - 8 * qg, 2)))
            j0 = thresh
            gqc = 8 * qg + 2 * j0 + h
            kpos = kc * 128 + np.arange(128)[:, None]
            qpos = gqc * 128 + np.arange(128)[None, :]
            out[qg, dk] = (kpos <= qpos).astype(np.float32)
            for j in range(thresh + 1, 4):
                gq = 8 * qg + 2 * j + h
                assert kc * 128 + 127 <= gq * 128, (qg, kc, j, h)
    return bf(out.reshape(nqg * 8, 128, 128).transpose(1, 0, 2).reshape(128, nqg * 8 * 128))


def own_rows(h, noc=16):
    idx = []
    for o in range(noc):
        g = 2 * o + h
        idx.extend(range(g * 128, (g + 1) * 128))
    return np.array(idx)



def gen_quant(pb, eng, X, W, scr, waits, out, box):
    for step in _quant_steps(pb, eng, X, W, scr, waits, out, box):
        yield


def emit_quant(pb, eng, X, W, scr, waits, out=None):
    box = {}
    for _ in _quant_steps(pb, eng, X, W, scr, waits, out, box):
        pass
    return box['tok']


def _quant_steps(pb, eng, X, W, scr, waits, out, box):
    NB = W // 16
    bm, sc, rsc = scr['bm'][:, 0:NB], scr['bm'][:, NB:2 * NB], scr['bm'][:, 2 * NB:3 * NB]
    ax, idx = scr['ax'][:, 0:W], scr['idx'][:, 0:W]
    s2 = scr['s2'][:, 0:W]
    dst = out if out is not None else X
    dst3 = dst.rearrange("p (n b) -> p n b", b=16)
    ax3 = ax.rearrange("p (n b) -> p n b", b=16)
    idx3 = idx.rearrange("p (n b) -> p n b", b=16)
    mids = [0.25, 0.75, 1.25, 1.75, 2.5, 3.5, 5.0]
    ea = pb.op("scalar", lambda s: s.activation(ax, X, AF.Abs), waits=waits)
    yield
    es = pb.op("scalar", lambda s: s.activation(s2, X, AF.Sign), waits=waits)
    yield
    e = pb.op(eng, lambda v: v.tensor_reduce(bm, ax3, AX.X, ALU.max), waits=[ea])
    yield
    e = pb.op(eng, lambda v: v.tensor_scalar(sc, bm, 1.0 / 6.0, 1e-12,
                                             ALU.mult, ALU.max), waits=[e])
    er = pb.op("vector", lambda v: v.reciprocal(rsc, sc), waits=[e])
    yield
    e = pb.op(eng, lambda v: v.tensor_tensor(
        ax3, ax3, rsc.unsqueeze(2).broadcast_to([128, NB, 16]), ALU.mult), waits=[er])
    yield
    e = pb.op(eng, lambda v: v.tensor_scalar(idx, ax, mids[0], None, ALU.is_gt),
              waits=[e])
    yield
    for m in mids[1:]:
        e = pb.op(eng, lambda v, m=m: v.scalar_tensor_tensor(
            idx, ax, m, idx, ALU.is_gt, ALU.add), waits=[e])
        yield
    e1 = pb.op("scalar", lambda s: s.activation(ax, idx, AF.Relu, bias=scr['c4']),
               waits=[e, scr['ctok']])
    yield
    e3 = pb.op(eng, lambda v: v.tensor_scalar_mul(idx, idx, 0.5), waits=[e1])
    yield
    e4 = pb.op(eng, lambda v: v.scalar_tensor_tensor(idx, ax, 0.5, idx,
                                                     ALU.mult, ALU.add), waits=[e3])
    yield
    e2 = pb.op("scalar", lambda s: s.activation(ax, ax, AF.Relu, bias=scr['c2']),
               waits=[e4])
    yield
    e5 = pb.op(eng, lambda v: v.tensor_tensor(idx, idx, ax, ALU.add), waits=[e2])
    yield
    e6 = pb.op(eng, lambda v: v.tensor_tensor(idx, idx, s2, ALU.mult), waits=[e5, es])
    yield
    e7 = pb.op(eng, lambda v: v.tensor_tensor(
        dst3, idx3, sc.unsqueeze(2).broadcast_to([128, NB, 16]), ALU.mult), waits=[e6])
    box['tok'] = e7


def build_l1f(ntc=16):
    nc = bass.Bass("TRN2", target_bir_lowering=False)
    NT = ntc * 128
    I = 2816
    io = {
        'y3_in': nc.dram_tensor("y3_in", [NT, D], F32, kind="ExternalInput").ap(),
        'w_gate': nc.dram_tensor("w_gate", [I, D], F32, kind="ExternalInput").ap(),
        'w_up': nc.dram_tensor("w_up", [I, D], F32, kind="ExternalInput").ap(),
        'w_down': nc.dram_tensor("w_down", [D, I], F32, kind="ExternalInput").ap(),
        'identm': nc.dram_tensor("identm", [128, 128], BF16, kind="ExternalInput").ap(),
        'y4': nc.dram_tensor("y4", [NT, D], F32, kind="ExternalOutput").ap(),
    }
    with nc.Block() as blk:
        pb = PB(nc)
        emit_l1f(nc, pb, blk, io, ntc)
        pb.emit(blk)
    return nc


def gen_l1f_weights(nc, pb, io, scr, wq_dram, t_wst, tag='', waits0=()):
    I = 2816
    NI = I // 128
    w_gate, w_up, w_down = io['w_gate'], io['w_up'], io['w_down']
    wc, wb = scr['wc'], scr['wb']
    t_wqchain = None
    t_wcfree = [None, None]
    t_wbfree = [None, None]
    jobs = [(w_gate, 0, NI, 1024, 0), (w_up, I, NI, 1024, 0),
            (w_down, 2 * I, 8, 1408, 0), (w_down, 2 * I, 8, 1408, 1408)]
    for (wsrc, row0, nch, wid, co) in jobs:
        for ch in range(nch):
            p = ch % 2
            WC = wc[:, p * 1408:p * 1408 + wid]
            t_l = pb.op("sync", lambda s, wsrc=wsrc, ch=ch, WC=WC, wid=wid, co=co:
                        s.dma_start(out=WC, in_=wsrc[ch * 128:(ch + 1) * 128, co:co + wid]),
                        waits=[t_wcfree[p]] + list(waits0), dma=True)
            box = {}
            for _ in gen_quant(pb, "vector", WC, wid, scr,
                               [t_l, t_wqchain, t_wbfree[p]],
                               wb[:, p * 1408:p * 1408 + wid], box):
                yield
            t_q = box['tok']
            t_wqchain = t_q
            t_wcfree[p] = t_q
            t_s = pb.op("sync", lambda s, row0=row0, ch=ch, p=p, wid=wid, co=co:
                        s.dma_start(out=wq_dram[row0 + ch * 128:row0 + (ch + 1) * 128,
                                                co:co + wid],
                                    in_=wb[:, p * 1408:p * 1408 + wid]),
                        waits=[t_q], dma=True)
            t_wbfree[p] = t_s
            t_wst.append(t_s)
            yield


def emit_l1f_weights(nc, pb, io, scr, wq_dram, tag='', waits0=()):
    t_wst = []
    for _ in gen_l1f_weights(nc, pb, io, scr, wq_dram, t_wst, tag, waits0):
        pass
    return t_wst


def emit_l1f(nc, pb, blk, io, ntc=16, tag='', t_wst=None, wq_dram=None):
    NT = ntc * 128
    I = 2816
    NI = I // 128
    (y3_in, w_gate, w_up, w_down, ident_in, y4) = (
        io['y3_in'], io['w_gate'], io['w_up'], io['w_down'], io['identm'], io['y4'])
    if wq_dram is None:
        wq_dram = nc.dram_tensor("wq_scr" + tag, [2 * I + D, max(D, I)], BF16).ap()

    glob = Alloc(nc)
    ident = glob.sb("ident", [128, 128], BF16)
    consts = glob.sb("consts", [128, 4], F32)
    wgT = glob.sb("wgT", [128, 8 * I], BF16)
    wuT = glob.sb("wuT", [128, 8 * I], BF16)
    wdT = glob.sb("wdT", [128, NI * D], BF16)
    if True:
        pb.op("gpsimd", lambda g: g.memset(consts[:, 0:1], LN_EPS))
        t_lne = pb.op("gpsimd", lambda g: g.memset(consts[:, 2:3], -0.5))
        t_id = pb.op("gpsimd", lambda g: g.dma_start(out=ident[:], in_=ident_in[:]), dma=True)

        if t_wst is None:
            phw = Alloc(nc)
            scrw = {
                'wc': phw.sb("wc", [128, 2 * 1408], F32),
                'wb': phw.sb("wb", [128, 2 * 1408], BF16),
                'bm': phw.sb("qbm", [128, 3 * 88], F32),
                'ax': phw.sb("qax", [128, 1408], F32),
                'idx': phw.sb("qidx", [128, 1408], F32),
                's2': phw.sb("qs2", [128, 1408], F32),
            }
            cq = phw.sb("qc", [128, 2], F32)
            scrw['c4'] = cq[:, 0:1]
            scrw['c2'] = cq[:, 1:2]
            pb.op("gpsimd", lambda g: g.memset(cq[:, 0:1], -4.0))
            scrw['ctok'] = pb.op("gpsimd", lambda g: g.memset(cq[:, 1:2], -2.0))
            t_wst = emit_l1f_weights(nc, pb, io, scrw, wq_dram, tag)
            pb.barrier()
            phw.close()
        t_wgT = [pb.op("sync", lambda s, j=j: s.dma_start(
            out=wgT[:, j * I:(j + 1) * I],
            in_=wq_dram[0:I, j * 128:(j + 1) * 128], transpose=True),
            waits=t_wst, dma=True) for j in range(8)]
        t_wuT = [pb.op("sync", lambda s, j=j: s.dma_start(
            out=wuT[:, j * I:(j + 1) * I],
            in_=wq_dram[I:2 * I, j * 128:(j + 1) * 128], transpose=True),
            waits=t_wst, dma=True) for j in range(8)]
        t_wdT = [pb.op("sync", lambda s, j=j: s.dma_start(
            out=wdT[:, j * D:(j + 1) * D],
            in_=wq_dram[2 * I:2 * I + D, j * 128:(j + 1) * 128], transpose=True),
            waits=t_wst, dma=True) for j in range(NI)]

        pht = Alloc(nc)
        yc = pht.sb("yc", [128, D], F32)
        sqs = pht.sb("sqs", [128, D], F32)
        S8 = pht.sb("S8", [128, 16], F32)
        xf = pht.sb("xf", [128, D], F32)
        xqT = pht.sb("xqT", [128, 8 * 128], BF16)
        hq = pht.sb("hq", [128, I], F32)
        hqb = pht.sb("hqb", [128, I], BF16)
        qbm2 = pht.sb("qbm2", [128, 3 * (I // 16)], F32)
        qax = pht.sb("qax", [128, I], F32)
        qidx2 = pht.sb("qidx2", [128, I], F32)
        qs2 = pht.sb("qs2", [128, I], F32)
        xbm = pht.sb("xbm", [128, 3 * 64], F32)
        xidx = pht.sb("xidx", [128, D], F32)
        xs2 = pht.sb("xs2", [128, D], F32)
        cq2 = pht.sb("cq2", [128, 2], F32)
        pT = [pht.ps("pT0", [128, 1024], BF16), pht.ps("pT1", [128, 1024], BF16)]
        pgu = pht.ps("pgu", [128, 2048], F32)
        pd = pht.ps("pd", [128, 1024], F32)

        scr2 = {'bm': qbm2, 'ax': qax, 'idx': qidx2, 's2': qs2,
                'c4': cq2[:, 0:1], 'c2': cq2[:, 1:2]}
        pb.op("gpsimd", lambda g: g.memset(cq2[:, 0:1], -4.0))
        scr2['ctok'] = pb.op("gpsimd", lambda g: g.memset(cq2[:, 1:2], -2.0))
        scrx = {'bm': xbm, 'ax': sqs, 'idx': xidx, 's2': xs2,
                'c4': cq2[:, 0:1], 'c2': cq2[:, 1:2], 'ctok': scr2['ctok']}
        segs = [(0, 1024), (1024, 1024), (2048, 768)]
        SABt = pht.sb("SABt", [128, 4 * ntc], F32)
        SAt, SBt = SABt[:, 0:ntc], SABt[:, ntc:2 * ntc]
        MEANt, RSTDt = SABt[:, 2 * ntc:3 * ntc], SABt[:, 3 * ntc:4 * ntc]
        t_sfree = [None, None]
        for i in range(ntc):
            p = 0
            ys = yc[:]
            t_y = pb.op("sync", lambda s, i=i, ys=ys: s.dma_start(
                out=ys, in_=y3_in[i * 128:(i + 1) * 128, :]),
                waits=[t_sfree[0]], dma=True)
            t_sfree[0] = emit_sums(pb, ys, SAt[:, i:i + 1], SBt[:, i:i + 1], sqs,
                                   [t_y, t_sfree[0]])
        t_stats = emit_stats_batch(pb, SAt, SBt, MEANt, RSTDt, consts[:, 0:1],
                                   consts[:, 2:3], [t_sfree[0], t_sfree[1], t_lne])
        t_yfree = None
        t_qchain = None
        t_xchain = None
        t_xqTfree = [None]
        t_hqT_free = None
        t_lnTfree = [None, None]
        t_xffree = [None]
        t_hqbfree = None
        t_hqfree = [None]
        t_pgufree = [None, None]
        t_pdfree = None
        t_y4free = [None, None]
        t_y4st = []
        for i in range(ntc):
            p = i % 2
            ys = yc[:]
            XF = xf[:]
            t_y = pb.op("sync", lambda s, i=i, ys=ys: s.dma_start(
                out=ys, in_=y3_in[i * 128:(i + 1) * 128, :]),
                waits=[t_yfree] if i >= 1 else [t_sfree[0]], dma=True)
            t_ln = pb.op("vector", lambda v, ys=ys, XF=XF, i=i: v.tensor_scalar(
                XF, ys, MEANt[:, i:i + 1], RSTDt[:, i:i + 1], ALU.subtract, ALU.mult),
                waits=[t_y, t_stats, t_xffree[0]])
            t_xq = emit_quant(pb, "vector", XF, D, scrx, [t_ln, t_xchain, t_hqbfree],
                              out=hqb[:, 0:D])
            t_xchain = t_xq
            t_cb = t_xq
            t_xqTs = []
            t_tp = None
            for j in range(8):
                pj = (i * 8 + j) % 2
                t_tp = pb.op("tensor", lambda t, j=j, pj=pj: t.transpose(
                    pT[pj][:, 0:128], hqb[:, j * 128:(j + 1) * 128], ident[:]),
                    waits=[t_cb, t_id, t_lnTfree[pj]])
                t_ev = pb.op("vector", lambda v, j=j, pj=pj: v.tensor_copy(
                    xqT[:, j * 128:(j + 1) * 128], pT[pj][:, 0:128]),
                    waits=[t_tp] + ([t_xqTfree[0]] if j == 0 else []))
                t_lnTfree[pj] = t_ev
                t_xqTs.append(t_ev)
            t_hqbfree = t_tp
            t_hqs = []
            for (s0, sw) in segs:
                sp = (s0 // 1024) % 2

                def gu_mms(t, s0=s0, sw=sw, W_=None, off=0):
                    for half in range(ceil_div(sw, 512)):
                        cw = min(512, sw - half * 512)
                        for j in range(8):
                            last = t.matmul(
                                pgu[:, off + half * 512:off + half * 512 + cw],
                                xqT[:, j * 128:(j + 1) * 128],
                                W_[:, j * I + s0 + half * 512:j * I + s0 + half * 512 + cw],
                                start=(j == 0), stop=(j == 7))
                    return last
                t_g = pb.op("tensor", lambda t, f=gu_mms: f(t, W_=wgT, off=0),
                            waits=t_xqTs + t_wgT + [t_pgufree[0]])
                t_u = pb.op("tensor", lambda t, f=gu_mms: f(t, W_=wuT, off=1024),
                            waits=t_xqTs + t_wuT + [t_pgufree[1]])
                t_xqTfree[0] = t_u
                t_si = pb.op("scalar", lambda s, s0=s0, sw=sw: s.activation(
                    hq[:, s0:s0 + sw], pgu[:, 0:sw], AF.Silu),
                    waits=[t_g, t_hqfree[0]])
                t_pgufree[0] = t_si
                t_h = pb.op("vector", lambda v, s0=s0, sw=sw: v.scalar_tensor_tensor(
                    hq[:, s0:s0 + sw], pgu[:, 1024:1024 + sw], 1.0,
                    hq[:, s0:s0 + sw], ALU.mult, ALU.mult),
                    waits=[t_si, t_u])
                t_pgufree[1] = t_h
                t_hqs.append(t_h)
            t_qh = emit_quant(pb, "vector", hq[:, 0:I], I, scr2,
                              t_hqs + [t_qchain, t_hqbfree], out=hqb[:, 0:I])
            t_qchain = t_qh
            t_cbh = t_qh
            t_hqTs = []
            for j in range(NI):
                pj = j % 2
                t_tp2 = pb.op("tensor", lambda t, j=j, pj=pj: t.transpose(
                    pT[pj][:, 128:256], hqb[:, j * 128:(j + 1) * 128], ident[:]),
                    waits=[t_cbh, t_lnTfree[pj]])
                t_ev2 = pb.op("vector", lambda v, j=j, pj=pj: v.tensor_copy(
                    hq[:].bitcast(BF16)[:, j * 128:(j + 1) * 128],
                    pT[pj][:, 128:256]),
                    waits=[t_tp2] + ([t_hqT_free] if j == 0 else []))
                t_lnTfree[pj] = t_ev2
                t_hqTs.append(t_ev2)
            t_hqbfree = t_tp2
            def d_mms(t):
                for piece in range(2):
                    for j in range(NI):
                        last = t.matmul(
                            pd[:, piece * 512:(piece + 1) * 512],
                            hq[:].bitcast(BF16)[:, j * 128:(j + 1) * 128],
                            wdT[:, j * D + piece * 512:j * D + (piece + 1) * 512],
                            start=(j == 0), stop=(j == NI - 1))
                return last
            t_d = pb.op("tensor", d_mms, waits=t_hqTs + t_wdT + [t_pdfree])
            t_hqT_free = t_d
            t_hqfree[0] = t_d
            t_y4 = pb.op("vector", lambda v, ys=ys: v.scalar_tensor_tensor(
                xf[:], pd[:], 1.0, ys, ALU.mult, ALU.add),
                waits=[t_d, t_y4free[p], t_xffree[0]])
            t_pdfree = t_y4
            t_yfree = t_y4
            t_st = pb.op("sync", lambda s, i=i: s.dma_start(
                out=y4[i * 128:(i + 1) * 128, :], in_=xf[:]),
                waits=[t_y4], dma=True)
            t_y4free[p] = t_st
            t_xffree[0] = t_st
            t_y4st.append(t_st)
        pb.op("sync", lambda s: s.nop(), waits=t_y4st[-2:])
        pb.barrier()
        pht.close()
    glob.close()





def emit_gather_rows(nc, pb, src_dram, dst_dram, ownidx_in, nch):
    al = Alloc(nc)
    idxs = al.sb("gidx", [128, nch], I32)
    buf = al.sb("gbuf", [128, 2 * D], F32)
    t_i = pb.op("gpsimd", lambda g: g.dma_start(out=idxs[:], in_=ownidx_in[:]), dma=True)
    t_pf = [None, None]
    for o in range(nch):
        p = o % 2
        t_g = pb.op("gpsimd", lambda g, o=o, p=p: g.indirect_dma_start(
            buf[:, p * D:(p + 1) * D], None, src_dram[:],
            IndirectOffsetOnAxis(idxs[:, o:o + 1], 0)),
            waits=[t_i, t_pf[p]], dma=True)
        t_s = pb.op("sync", lambda s, o=o, p=p: s.dma_start(
            out=dst_dram[o * 128:(o + 1) * 128, :], in_=buf[:, p * D:(p + 1) * D]),
            waits=[t_g], dma=True)
        t_pf[p] = t_s
    pb.barrier()
    al.close()


def build_fused():
    nc = bass.Bass("TRN2", target_bir_lowering=False)
    NQC = 32
    NKC = NQC + 2
    I = 2816
    ext = lambda n, sh, dt=F32: nc.dram_tensor(n, sh, dt, kind="ExternalInput").ap()
    io = {
        'x_pad': ext("x_pad", [NKC * 128, D]),
        'qkv_wT': ext("qkv_wT", [D, 3 * D]),
        'qkv_b': ext("qkv_b", [1, 3 * D]),
        'out_wT': ext("out_wT", [D, D]),
        'out_b': ext("out_b", [1, D]),
        'band_mask': ext("band_mask", [128, NQC * 3 * 128], BF16),
        'identm': ext("identm", [128, 128], BF16),
        'g_bc': ext("g_bc", [128, D]),
        'b_bc': ext("b_bc", [128, D]),
        'rW': ext("rW", [128, 8 * 8]),
        'rb': ext("rb", [1, 8]),
        'W1': ext("W1", [8, D, 2048]),
        'b1t': ext("b1t", [128, 8 * 16]),
        'W2': ext("W2", [8, 2048, D]),
        'b2r': ext("b2r", [1, 8 * D]),
        'w_q': ext("w_q", [D, D]),
        'w_kv': ext("w_kv", [D, 2 * D]),
        'w_out': ext("w_out", [D, D]),
        'kcs': ext("kcs", [128, 32 * 64]),
        'qcs': ext("qcs", [128, 16 * 64]),
        'cmask': ext("cmask", [128, 4 * 8 * 128], BF16),
        'ownidx': ext("ownidx", [128, 16], I32),
        'w_gate': ext("w_gate", [I, D]),
        'w_up': ext("w_up", [I, D]),
        'w_down': ext("w_down", [D, I]),
        'y4': nc.dram_tensor("y4", [2048, D], F32, kind="ExternalOutput").ap(),
        'y1': nc.dram_tensor("y1_scr", [4096, D], F32).ap(),
        'y2': nc.dram_tensor("y2_scr", [4096, D], F32).ap(),
        'y2own': nc.dram_tensor("y2own_scr", [2048, D], F32).ap(),
        'y3': nc.dram_tensor("y3_scr", [2048, D], F32).ap(),
    }
    with nc.Block() as blk:
        pb = PB(nc)
        emit_l0a(nc, pb, blk, io, NQC)
        io2 = dict(io)
        io2['y1_in'] = io['y1']
        emit_l0f(nc, pb, blk, io2, NQC, npass=8)
        emit_gather_rows(nc, pb, io['y2'], io['y2own'], io['ownidx'], 16)
        I = 2816
        wq_dram = nc.dram_tensor("wq_scr", [2 * I + D, max(D, I)], BF16).ap()
        phw = Alloc(nc)
        scrw = {}
        box = {}

        def mid_hook(tails):
            scrw['wc'] = phw.sb("wc", [128, 2 * 1408], F32, side='right')
            scrw['wb'] = phw.sb("wb", [128, 2 * 1408], BF16, side='right')
            scrw['bm'] = phw.sb("qbm", [128, 3 * 88], F32, side='right')
            scrw['ax'] = phw.sb("qax", [128, 1408], F32, side='right')
            scrw['idx'] = phw.sb("qidx", [128, 1408], F32, side='right')
            scrw['s2'] = phw.sb("qs2", [128, 1408], F32, side='right')
            cq = phw.sb("qc", [128, 2], F32, side='right')
            scrw['c4'] = cq[:, 0:1]
            scrw['c2'] = cq[:, 1:2]
            t1 = pb.op("gpsimd", lambda g: g.memset(cq[:, 0:1], -4.0))
            scrw['ctok'] = pb.op("gpsimd", lambda g: g.memset(cq[:, 1:2], -2.0))
            box['tails'] = tails
        io3 = dict(io)
        io3['y2_seq'] = io['y2']
        io3['y2_own'] = io['y2own']
        io4 = dict(io)
        io4['y3_in'] = io['y3']
        t_wst = []
        bgen = gen_l1f_weights(nc, pb, io4, scrw, wq_dram, t_wst)
        emit_l1a(nc, pb, blk, io3, nqg=4, skc=32, barrier=False, mid_hook=mid_hook,
                 bg=bgen)
        for _ in bgen:
            pass
        pb.barrier()
        phw.close()
        emit_l1f(nc, pb, blk, io4, 16, t_wst=t_wst, wq_dram=wq_dram)
        for e in ENGINES:
            n = len(pb.streams[e])
            print(f"fused stream {e}: {n} ops")
        pb.emit(blk)
    return nc


_programs = {}


def _get_prog(name):
    if name not in _programs:
        if name == 'l0a':
            _programs[name] = build_l0a(16)
        elif name == 'l0f':
            _programs[name] = build_l0f(16, 2)
        elif name == 'l1a':
            _programs[name] = build_l1a(4)
        elif name == 'l1f':
            _programs[name] = build_l1f(16)
    return _programs[name]


_IDENT = None


def kernel_unfused(L0_attn_x, L0_attn_qkv_w, L0_attn_qkv_b, L0_attn_out_w, L0_attn_out_b,
           L0_ffn_ln_g, L0_ffn_ln_b, L0_ffn_rW, L0_ffn_rb, L0_ffn_W1, L0_ffn_b1,
           L0_ffn_W2, L0_ffn_b2, L1_attn_w_q, L1_attn_w_kv, L1_attn_w_out,
           L1_ffn_w_gate, L1_ffn_w_up, L1_ffn_w_down):
    global _IDENT
    if _IDENT is None:
        _IDENT = bf(np.eye(128, dtype=np.float32))
    x = f32(L0_attn_x)
    B, N, Dm = x.shape

    nc = _get_prog('l0a')
    qkv_wT = f32(L0_attn_qkv_w.T)
    out_wT = f32(L0_attn_out_w.T)
    qkv_b = f32(np.asarray(L0_attn_qkv_b)[None, :])
    out_b = f32(np.asarray(L0_attn_out_b)[None, :])
    in_maps = []
    for core in range(8):
        b, h = core // 2, core % 2
        d = l0a_inputs(x, core)
        d.update({"qkv_wT": qkv_wT, "qkv_b": qkv_b, "out_wT": out_wT, "out_b": out_b,
                  "band_mask": l0a_masks(16, h * 2048), "identm": _IDENT})
        in_maps.append(d)
    r1 = run_bass_kernel_spmd(nc, in_maps, core_ids=list(range(8)))

    nc = _get_prog('l0f')
    g_bc = f32(np.broadcast_to(np.asarray(L0_ffn_ln_g), (128, Dm)))
    b_bc = f32(np.broadcast_to(np.asarray(L0_ffn_ln_b), (128, Dm)))
    rW = f32(np.asarray(L0_ffn_rW).reshape(8, 128, 8).transpose(1, 0, 2).reshape(128, 64))
    rb = f32(np.asarray(L0_ffn_rb)[None, :])
    W1 = f32(L0_ffn_W1)
    b1t = f32(np.asarray(L0_ffn_b1).reshape(8, 16, 128).transpose(2, 0, 1).reshape(128, 128))
    W2 = f32(L0_ffn_W2)
    b2r = f32(np.asarray(L0_ffn_b2).reshape(1, 8 * 1024))
    in_maps = []
    for core in range(8):
        in_maps.append({"y1_in": r1.results[core]["y1"], "g_bc": g_bc, "b_bc": b_bc,
                        "rW": rW, "rb": rb, "W1": W1, "b1t": b1t, "W2": W2, "b2r": b2r,
                        "identm": _IDENT})
    r2 = run_bass_kernel_spmd(nc, in_maps, core_ids=list(range(8)))

    y2 = np.empty((B, N, Dm), dtype=np.float32)
    for core in range(8):
        b, h = core // 2, core % 2
        y2[b, h * 2048:(h + 1) * 2048] = r2.results[core]["y2"]

    nc = _get_prog('l1a')
    w_q = f32(L1_attn_w_q)
    w_kv = f32(L1_attn_w_kv)
    w_out = f32(L1_attn_w_out)
    kcs = l1a_rope_tables(np.arange(N).astype(np.float64))
    qcs_h = [l1a_rope_tables(own_rows(h).astype(np.float64)) for h in (0, 1)]
    msk_h = [l1a_masks(h) for h in (0, 1)]
    in_maps = []
    for core in range(8):
        b, h = core // 2, core % 2
        in_maps.append({"y2_seq": y2[b], "y2_own": f32(y2[b][own_rows(h)]),
                        "w_q": w_q, "w_kv": w_kv, "w_out": w_out, "kcs": kcs,
                        "qcs": qcs_h[h], "cmask": msk_h[h], "identm": _IDENT})
    r3 = run_bass_kernel_spmd(nc, in_maps, core_ids=list(range(8)))

    nc = _get_prog('l1f')
    wg = f32(L1_ffn_w_gate)
    wu = f32(L1_ffn_w_up)
    wd = f32(L1_ffn_w_down)
    in_maps = []
    for core in range(8):
        in_maps.append({"y3_in": r3.results[core]["y3"], "w_gate": wg, "w_up": wu,
                        "w_down": wd, "identm": _IDENT})
    r4 = run_bass_kernel_spmd(nc, in_maps, core_ids=list(range(8)))

    out = np.empty((B, N, Dm), dtype=np.float32)
    for core in range(8):
        b, h = core // 2, core % 2
        out[b][own_rows(h)] = r4.results[core]["y4"]
    return out


def kernel(L0_attn_x, L0_attn_qkv_w, L0_attn_qkv_b, L0_attn_out_w, L0_attn_out_b,
           L0_ffn_ln_g, L0_ffn_ln_b, L0_ffn_rW, L0_ffn_rb, L0_ffn_W1, L0_ffn_b1,
           L0_ffn_W2, L0_ffn_b2, L1_attn_w_q, L1_attn_w_kv, L1_attn_w_out,
           L1_ffn_w_gate, L1_ffn_w_up, L1_ffn_w_down):
    global _IDENT
    if _IDENT is None:
        _IDENT = bf(np.eye(128, dtype=np.float32))
    if 'fused' not in _programs:
        _programs['fused'] = build_fused()
    nc = _programs['fused']
    x = f32(L0_attn_x)
    B, N, Dm = x.shape

    shared = {
        "qkv_wT": f32(np.asarray(L0_attn_qkv_w).T),
        "qkv_b": f32(np.asarray(L0_attn_qkv_b)[None, :]),
        "out_wT": f32(np.asarray(L0_attn_out_w).T),
        "out_b": f32(np.asarray(L0_attn_out_b)[None, :]),
        "band_mask": l0a_masks(32, 0),
        "identm": _IDENT,
        "g_bc": f32(np.broadcast_to(np.asarray(L0_ffn_ln_g), (128, Dm))),
        "b_bc": f32(np.broadcast_to(np.asarray(L0_ffn_ln_b), (128, Dm))),
        "rW": f32(np.asarray(L0_ffn_rW).reshape(8, 128, 8).transpose(1, 0, 2).reshape(128, 64)),
        "rb": f32(np.asarray(L0_ffn_rb)[None, :]),
        "W1": f32(L0_ffn_W1),
        "b1t": f32(np.asarray(L0_ffn_b1).reshape(8, 16, 128).transpose(2, 0, 1).reshape(128, 128)),
        "W2": f32(L0_ffn_W2),
        "b2r": f32(np.asarray(L0_ffn_b2).reshape(1, 8 * 1024)),
        "w_q": f32(L1_attn_w_q),
        "w_kv": f32(L1_attn_w_kv),
        "w_out": f32(L1_attn_w_out),
        "kcs": l1a_rope_tables(np.arange(N).astype(np.float64)),
        "w_gate": f32(L1_ffn_w_gate),
        "w_up": f32(L1_ffn_w_up),
        "w_down": f32(L1_ffn_w_down),
    }
    qcs_h = [l1a_rope_tables(own_rows(h).astype(np.float64)) for h in (0, 1)]
    msk_h = [l1a_masks(h) for h in (0, 1)]
    idx_h = []
    for h in (0, 1):
        rows = own_rows(h).astype(np.int32)
        idx_h.append(np.ascontiguousarray(rows.reshape(16, 128).T))
    in_maps = []
    for core in range(8):
        b, h = core // 2, core % 2
        xp = np.zeros((34 * 128, Dm), dtype=np.float32)
        xp[128:128 + N] = x[b]
        d = dict(shared)
        d.update({"x_pad": xp, "qcs": qcs_h[h], "cmask": msk_h[h], "ownidx": idx_h[h]})
        in_maps.append(d)
    res = run_bass_kernel_spmd(nc, in_maps, core_ids=list(range(8)))
    out = np.empty((B, N, Dm), dtype=np.float32)
    for core in range(8):
        b, h = core // 2, core % 2
        out[b][own_rows(h)] = res.results[core]["y4"]
    return out
